# version 03
# baseline (speedup 1.0000x reference)
import numpy as np
import ml_dtypes
import concourse.bass as bass
import concourse.mybir as mybir
from concourse.bass_utils import run_bass_kernel_spmd

F32 = mybir.dt.float32
BF16 = mybir.dt.bfloat16
AF = mybir.ActivationFunctionType
ALU = mybir.AluOpType
AX = mybir.AxisListType

D = 2048
NT_D = 16
T_OWN = 1024
NT_OWN = 8
HALO = 512
TH = T_OWN + HALO
NT_TH = 12
H = 16
DH = 128
R = 16
F_MOE = 1024
NT_F = 8
NE, NSH = 16, 2
HID = 5461
HID_P = 5504
NT_HID = 43
CHUNK = 128
NCH = 8
NCORES = 8
INV_SQRT_DH = float(1.0 / np.sqrt(128.0))
LORA_SCALE = 1.0 / 16.0

BF = ml_dtypes.bfloat16
NO_LORA = False


def _slopes():
    start = 2.0 ** (-2.0 ** (-(np.log2(H) - 3)))
    return np.array([start ** (i + 1) for i in range(H)], dtype=np.float64)



def host_consts(core):
    c = {}
    rbase = core * T_OWN
    half = DH // 2
    th = 1.0 / 10000.0 ** (np.arange(half, dtype=np.float64) / half)
    pos = np.arange(rbase - HALO, rbase + T_OWN, dtype=np.float64)
    pos = np.clip(pos, 0, None)
    fr = pos[:, None] * th[None, :]
    emb = np.concatenate([fr, fr], -1)
    c["ropecos"] = np.ascontiguousarray(np.cos(emb).T.astype(BF))
    c["ropesin"] = np.ascontiguousarray(np.sin(emb).T.astype(BF))
    masks = np.zeros((5, 128, 640), np.float32)
    for slot in range(5):
        qt = slot if (core == 0 and slot < 4) else None
        bcol = np.arange(640)[None, :]
        arow = np.arange(128)[:, None]
        ok = (bcol - arow >= 1) & (bcol - arow <= 512)
        if qt is not None:
            gq = qt * 128 + arow
            gk = qt * 128 + bcol - 512
            ok = ok & (gk >= 0) & (gk <= gq)
        masks[slot] = np.where(ok, 0.0, -1e30)
    c["mask5"] = masks
    sl = _slopes()
    idc = np.arange(CHUNK, dtype=np.float64)
    diff = idc[:, None] - idc[None, :]
    dd = np.where(diff >= 0, np.exp(-sl[:, None, None] * diff), 0.0)
    c["diagdecT"] = np.ascontiguousarray(dd.transpose(0, 2, 1).astype(np.float32))
    c["qdecay"] = np.exp(-sl[:, None] * (idc + 1.0)[None, :]).astype(np.float32).reshape(1, H * CHUNK)
    c["kdecay"] = np.exp(-sl[:, None] * (CHUNK - 1.0 - idc)[None, :]).astype(np.float32).reshape(1, H * CHUNK)
    bd = np.exp(-sl * CHUNK)
    Wc = np.zeros((8, H), np.float32)
    for b_ in range(core):
        Wc[b_] = (bd ** (8 * (core - 1 - b_))).astype(np.float32)
    c["Wc"] = np.ascontiguousarray(Wc.reshape(1, 8 * H))
    c["idxrow"] = np.arange(rbase, rbase + T_OWN, dtype=np.float32).reshape(1, T_OWN)
    return c


def shard_inputs(inputs):
    x = np.ascontiguousarray(np.asarray(inputs["L0_attn_x"])[0])
    TT = lambda a: np.ascontiguousarray(np.asarray(a).T)
    base = {}
    base["wqT"] = TT(inputs["L0_attn_wq"])
    base["wkT"] = TT(inputs["L0_attn_wk"])
    base["wvT"] = TT(inputs["L0_attn_wv"])
    base["woT"] = TT(inputs["L0_attn_wo"])
    base["nwfm"] = np.ascontiguousarray(np.asarray(inputs["L0_attn_norm_w"]).reshape(NT_D, 128).T)
    for nm in ("qA", "kA", "vA", "oA"):
        base[nm + "T"] = TT(inputs["L0_attn_" + nm])
    for nm in ("qB", "kB", "vB", "oB"):
        base[nm + "T"] = TT(inputs["L0_attn_" + nm])
    base["wa"] = np.asarray(inputs["L0_ffn_wa"])
    for nm in ("rg", "rw1", "rw2"):
        base[nm] = np.asarray(inputs["L0_ffn_" + nm])
    for nm in ("sg", "sw1", "sw2"):
        base[nm] = np.asarray(inputs["L0_ffn_" + nm])
    base["rgbT"] = TT(inputs["L0_ffn_rgb"])
    base["rb1T"] = TT(inputs["L0_ffn_rb1"])
    base["rb2"] = np.asarray(inputs["L0_ffn_rb2"])
    base["sgbT"] = TT(inputs["L0_ffn_sgb"])
    base["sb1T"] = TT(inputs["L0_ffn_sb1"])
    base["sb2"] = np.asarray(inputs["L0_ffn_sb2"])
    base["wqkvT"] = TT(inputs["L1_attn_w_qkv"])
    base["wprojT"] = TT(inputs["L1_attn_w_proj"])
    base["thetarow"] = np.ascontiguousarray(np.asarray(inputs["L1_attn_theta"]).reshape(1, H * 128))
    base["w1T"] = TT(inputs["L1_ffn_w1"])
    base["w3T"] = TT(inputs["L1_ffn_w3"])
    base["w2T"] = TT(inputs["L1_ffn_w2"])
    maps = []
    for core in range(NCORES):
        m = dict(base)
        rb = core * T_OWN
        xh = np.zeros((TH, D), np.float32)
        lo = max(0, rb - HALO)
        xh[HALO - (rb - lo):] = x[lo:rb + T_OWN]
        m["xh"] = xh
        m.update(host_consts(core))
        maps.append(m)
    return maps



class Sem:
    def __init__(self, h):
        self.h = h
        self.v = 0


class Ser:

    def __init__(self, b, name):
        self.s = b.sem(name)

    def step(self, inst, eng):
        inst.then_inc(self.s.h, 1)
        self.s.v += 1
        eng.wait_ge(self.s.h, self.s.v)


class Builder:
    def __init__(self, want_dbg=()):
        nc = bass.Bass("TRN2", target_bir_lowering=False, debug=False)
        self.nc = nc
        self.sboff = 0
        self.want_dbg = set(want_dbg)
        self.dbg_outs = {}
        self._pool = []
        self._pool_idx = 0
        self._phase_sems = []

        def inp(name, shape, dtype=F32):
            return nc.dram_tensor(name, list(shape), dtype, kind="ExternalInput").ap()

        i = {}
        i["xh"] = inp("xh", (TH, D))
        for nm in ("wqT", "wkT", "wvT", "woT"):
            i[nm] = inp(nm, (D, D))
        i["nwfm"] = inp("nwfm", (128, NT_D))
        for nm in ("qAT", "kAT", "vAT", "oAT"):
            i[nm] = inp(nm, (D, R))
        for nm in ("qBT", "kBT", "vBT", "oBT"):
            i[nm] = inp(nm, (R, D))
        i["wa"] = inp("wa", (D, NE))
        for nm in ("rg", "rw1"):
            i[nm] = inp(nm, (NE, D, F_MOE))
        i["rw2"] = inp("rw2", (NE, F_MOE, D))
        for nm in ("sg", "sw1"):
            i[nm] = inp(nm, (NSH, D, F_MOE))
        i["sw2"] = inp("sw2", (NSH, F_MOE, D))
        i["rgbT"] = inp("rgbT", (F_MOE, NE))
        i["rb1T"] = inp("rb1T", (F_MOE, NE))
        i["rb2"] = inp("rb2", (NE, D))
        i["sgbT"] = inp("sgbT", (F_MOE, NSH))
        i["sb1T"] = inp("sb1T", (F_MOE, NSH))
        i["sb2"] = inp("sb2", (NSH, D))
        i["ropecos"] = inp("ropecos", (128, TH), BF16)
        i["ropesin"] = inp("ropesin", (128, TH), BF16)
        i["mask5"] = inp("mask5", (5, 128, 640))
        i["wqkvT"] = inp("wqkvT", (D, 3 * D))
        i["wprojT"] = inp("wprojT", (D, D))
        i["thetarow"] = inp("thetarow", (1, H * 128))
        i["diagdecT"] = inp("diagdecT", (H, CHUNK, CHUNK))
        i["qdecay"] = inp("qdecay", (1, H * CHUNK))
        i["kdecay"] = inp("kdecay", (1, H * CHUNK))
        i["idxrow"] = inp("idxrow", (1, T_OWN))
        i["w1T"] = inp("w1T", (D, HID))
        i["w3T"] = inp("w3T", (D, HID))
        i["w2T"] = inp("w2T", (HID, D))
        self.i = i

        o = {}
        o["x1"] = nc.dram_tensor("x1", [T_OWN, D], F32, kind="ExternalOutput").ap()
        o["g2"] = nc.dram_tensor("g2", [T_OWN, D], BF16, kind="ExternalOutput").ap()
        o["ylocal"] = nc.dram_tensor("ylocal", [D, T_OWN], BF16, kind="ExternalOutput").ap()
        o["qdw"] = nc.dram_tensor("qdw", [H * 256, T_OWN], BF16, kind="ExternalOutput").ap()
        o["S_c"] = nc.dram_tensor("S_c", [H * 256, DH], F32, kind="ExternalOutput").ap()
        self.o = o

        self.yattn_d = nc.dram_tensor("yattn_d", [NT_D, 128, T_OWN], BF16, kind="Internal").ap()
        self.qf_d = nc.dram_tensor("qf_d", [H, 128, T_OWN], BF16, kind="Internal").ap()
        self.kf_d = nc.dram_tensor("kf_d", [H, 128, TH], BF16, kind="Internal").ap()
        self.vt_d = nc.dram_tensor("vt_d", [NT_TH, 128, D], BF16, kind="Internal").ap()
        self.w1q_d = nc.dram_tensor("w1q_d", [D, HID_P], BF16, kind="Internal").ap()
        self.w3q_d = nc.dram_tensor("w3q_d", [D, HID_P], BF16, kind="Internal").ap()
        self.w2q_d = nc.dram_tensor("w2q_d", [HID_P, D], BF16, kind="Internal").ap()

        self.ident_d = nc.inline_tensor(np.asarray(np.eye(128, dtype=np.float32), dtype=BF), "identbf").ap()
        self.identf_d = nc.inline_tensor(np.eye(128, dtype=np.float32), "identf32").ap()
        self.ones1_d = nc.inline_tensor(np.ones((1, 128), dtype=BF), "ones1bf").ap()
        o2 = np.zeros((128, 512), np.float32)
        o2[0:2, :] = 1.0
        self.ones2_d = nc.inline_tensor(np.asarray(o2, dtype=BF), "ones2bf").ap()
        Rm = np.zeros((128, 128), np.float32)
        for d_ in range(64):
            Rm[d_, d_ + 64] = -1.0
            Rm[d_ + 64, d_] = 1.0
        self.RT_d = nc.inline_tensor(np.asarray(Rm.T, dtype=BF), "ropeRT").ap()
        sel16 = np.zeros((128, NE * 128), np.float32)
        for e in range(NE):
            sel16[e, e * 128:(e + 1) * 128] = 1.0
        self.esel_d = nc.inline_tensor(np.asarray(sel16, dtype=BF), "esel").ap()
        sel12 = np.zeros((NT_TH, NT_TH * 128), np.float32)
        for t in range(NT_TH):
            sel12[t, t * 128:(t + 1) * 128] = 1.0
        self.tsel_d = nc.inline_tensor(sel12, "tsel12").ap()

        self.P = [nc.alloc_psum_tensor(f"P{k}", [128, 512], F32).ap() for k in range(8)]

        self.xn = [self.salloc(f"xn{t}", [128, TH], BF16) for t in range(NT_D)]
        self.rrb = self.salloc("rrb", [128, TH], F32)
        self.rrc = self.salloc("rrc", [128, NT_TH], F32)
        self.nwf = self.salloc("nwf", [128, NT_D], F32)
        self.nwfq = self.salloc("nwfq", [128, NT_D], F32)
        self.smll = self.salloc("smll", [128, 2048], BF16)
        self.BTm = {nm: self.salloc("bt_" + nm, [128, D], BF16)
                    for nm in ("qB", "kB", "vB", "oB")}
        self.zb = self.salloc("zb", [128, TH], BF16)
        self.ztmp = self.salloc("ztmp", [128, 16], BF16)
        self.off_zb = self.sboff
        self.combT = self.salloc("combT", [128, T_OWN], BF16)
        self.ident = self.salloc("ident", [128, 128], BF16)
        self.identf = self.salloc("identf", [128, 128], F32)
        self.consts = self.salloc("consts", [128, 8], F32)
        self.PERSIST = self.sboff

        sm = self.smll
        self.AT = {nm: sm[:, k * 256:(k + 1) * 256]
                   for k, nm in enumerate(("qA", "kA", "vA", "oA"))}
        self.waf = sm[:, 1024:1280]

        nc.const_aps.register(F32, 1e-5, self.consts[:, 0:1])
        nc.const_aps.register(F32, 1e-6, self.consts[:, 1:2])
        self._const_vals = [(0, 1e-5), (1, 1e-6)]

    SB_BASE = 16512
    SB_TOP = 229344

    def salloc(self, name, shape, dtype, at=None):
        per = int(np.prod(shape[1:])) * mybir.dt.size(dtype)
        per = (per + 31) & ~31
        if at is None:
            at = self.sboff
            self.sboff = at + per
        assert self.SB_BASE + at + per <= self.SB_TOP, (name, at, per)
        ap = self.nc.alloc_sbuf_tensor_at(name, list(shape), dtype,
                                          offset=self.SB_BASE + at)
        return ap.ap()

    def sem(self, name, g=False):
        if g:
            return Sem(self.nc.alloc_semaphore(name))
        if self._pool_idx >= len(self._pool):
            self._pool.append(self.nc.alloc_semaphore(f"pool{len(self._pool)}"))
        h = self._pool[self._pool_idx]
        self._pool_idx += 1
        s = Sem(h)
        self._phase_sems.append(s)
        return s

    def end_phase(self):
        nc = self.nc
        sems = list(self._phase_sems)
        if sems:
            with nc.Block() as blk:
                @blk.gpsimd
                def _(g):
                    for s in sems:
                        g.sem_clear(s.h)
        self._phase_sems = []
        self._pool_idx = 0

    def dbg(self, name, shape, dtype=F32):
        if name in self.dbg_outs:
            return self.dbg_outs[name]
        ap = self.nc.dram_tensor("dbg_" + name, list(shape), dtype, kind="ExternalOutput").ap()
        self.dbg_outs[name] = ap
        return ap



def emit_consts(b):
    nc = b.nc
    ldg = b.sem("c_ldg", g=True)
    lds = b.sem("c_lds")
    mzs = b.sem("c_mzs")
    with nc.Block() as blk:
        @blk.gpsimd
        def _(g):
            for col, val in b._const_vals:
                g.memset(b.consts[:, col:col + 1], val)
            g.memset(b.zb, 0.0)
            mz = None
            for nm in ("qB", "kB", "vB", "oB"):
                mz = g.memset(b.BTm[nm], 0.0)
            mz.then_inc(mzs.h, 1)
            g.wait_ge(mzs.h, 1)
            for nm in ("qA", "kA", "vA", "oA"):
                g.dma_start(out=b.AT[nm].rearrange("p (t r) -> p t r", t=NT_D),
                            in_=b.i[nm + "T"].rearrange("(t p) r -> p t r", p=128)).then_inc(ldg.h, 16)
            for nm in ("qB", "kB", "vB", "oB"):
                g.dma_start(out=b.BTm[nm][0:16, :], in_=b.i[nm + "T"]).then_inc(ldg.h, 16)
            g.dma_start(out=b.waf.rearrange("p (t e) -> p t e", t=NT_D),
                        in_=b.i["wa"].rearrange("(t p) e -> p t e", p=128)).then_inc(ldg.h, 16)
            ldg.v += 16 * 9

        @blk.sync
        def _(sync):
            sync.dma_start(out=b.nwf, in_=b.i["nwfm"]).then_inc(lds.h, 16)
            sync.dma_start(out=b.ident, in_=b.ident_d).then_inc(lds.h, 16)
            sync.dma_start(out=b.identf, in_=b.identf_d).then_inc(lds.h, 16)
            lds.v += 48

        @blk.vector
        def _(v):
            v.wait_ge(ldg.h, ldg.v)
            v.wait_ge(lds.h, lds.v)
            for nm in ("qA", "kA", "vA"):
                a3 = b.AT[nm].rearrange("p (t r) -> p t r", t=NT_D)
                for dt in range(NT_D):
                    v.tensor_scalar_mul(a3[:, dt], a3[:, dt], b.nwf[:, dt:dt + 1])
            v.tensor_scalar_mul(b.BTm["qB"], b.BTm["qB"], LORA_SCALE * INV_SQRT_DH)
            for nm in ("kB", "vB", "oB"):
                v.tensor_scalar_mul(b.BTm[nm], b.BTm[nm], LORA_SCALE)
            v.tensor_scalar_mul(b.nwfq, b.nwf, INV_SQRT_DH)
    b.end_phase()

def emit_ln0(b):
    nc = b.nc
    LD0, LD1, LDC, SQ, VAR, RSTD, XN, Q2, RR, TP, EV = (b.sem("n" + s) for s in
        ("LD0", "LD1", "LDC", "SQ", "VAR", "RSTD", "XN", "Q2", "RR", "TP", "EV"))
    LDx = (LD0, LD1)
    off = b.PERSIST
    xt = [b.salloc(f"l0x{k}", [128, D], F32, at=off + k * 8192) for k in range(2)]
    sqd = b.salloc("l0sq", [128, D], F32, at=off + 16384)
    xnt = b.salloc("l0xnt", [128, D], BF16, at=off + 16384 + 8192)
    stt = b.salloc("l0stat", [128, 16], F32, at=off + 16384 + 8192 + 4096)
    rrT = b.salloc("l0rrT", [NT_TH, 128], F32, at=off + 16384 + 8192 + 4096 + 64)
    tsel = b.salloc("l0tsel", [NT_TH, NT_TH * 128], F32, at=off + 16384 + 8192 + 4096 + 64 + 512)

    xin = b.i["xh"].rearrange("(t p) d -> t p d", p=128)
    C1 = float(1.0 / D)
    NT = NT_TH
    sttrace = b.salloc("l0sttr", [128, NT_TH * 16], F32,
                       at=off + 16384 + 8192 + 4096 + 64 + 512 + 6144)

    with nc.Block() as blk:
        @blk.sync
        def _(sync):
            sync.dma_start(out=tsel, in_=b.tsel_d).then_inc(LDC.h, 16)
            LDC.v = 16
            for t in range(NT):
                if t >= 2:
                    sync.wait_ge(XN.h, t - 1)
                    sync.wait_ge(SQ.h, t - 1)
                sync.dma_start(out=xt[t % 2], in_=xin[t]).then_inc(LDx[t % 2].h, 16)
            LD0.v = 16 * ((NT + 1) // 2)
            LD1.v = 16 * (NT // 2)

        @blk.scalar
        def _(s):
            ss = Ser(b, "nSS")
            for t in range(NT):
                s.wait_ge(LDx[t % 2].h, 16 * (t // 2 + 1))
                if t > 0:
                    s.wait_ge(VAR.h, t)
                s.activation(sqd, xt[t % 2], AF.Square,
                             accum_out=stt[:, 1:2]).then_inc(SQ.h, 1)
                s.wait_ge(VAR.h, t + 1)
                ss.step(s.activation(stt[:, 5:6], stt[:, 4:5], AF.Ln, bias=1e-5), s)
                s.activation(stt[:, 6:7], stt[:, 5:6], AF.Exp,
                             scale=-0.5).then_inc(RSTD.h, 1)
                s.wait_ge(Q2.h, t + 1)
                ss.step(s.activation(stt[:, 9:10], stt[:, 8:9], AF.Ln, bias=1e-6), s)
                s.activation(b.rrc[:, t:t + 1], stt[:, 9:10], AF.Exp,
                             scale=-0.5).then_inc(RR.h, 1)
            SQ.v = RSTD.v = RR.v = NT

        @blk.vector
        def _(v):
            sv = Ser(b, "nSV")
            for t in range(NT):
                v.wait_ge(LDx[t % 2].h, 16 * (t // 2 + 1))
                v.wait_ge(SQ.h, t + 1)
                x_ = xt[t % 2]
                sv.step(v.reduce_sum(stt[:, 0:1], x_, AX.X), v)
                sv.step(v.tensor_scalar_mul(stt[:, 2:3], stt[:, 0:1], C1), v)
                sv.step(v.tensor_scalar_mul(stt[:, 3:4], stt[:, 1:2], C1), v)
                sv.step(v.tensor_tensor(stt[:, 4:5], stt[:, 2:3], stt[:, 2:3], ALU.mult), v)
                v.tensor_tensor(stt[:, 4:5], stt[:, 3:4], stt[:, 4:5],
                                ALU.subtract).then_inc(VAR.h, 1)
                v.wait_ge(RSTD.h, t + 1)
                if t > 0:
                    v.wait_ge(TP.h, t * NT_D)
                v.tensor_scalar(xnt, x_, stt[:, 2:3], stt[:, 6:7],
                                ALU.subtract, ALU.mult).then_inc(XN.h, 1)
                sv.step(v.tensor_tensor(stt[:, 7:8], stt[:, 6:7], stt[:, 6:7], ALU.mult), v)
                v.tensor_tensor(stt[:, 8:9], stt[:, 4:5], stt[:, 7:8],
                                ALU.mult).then_inc(Q2.h, 1)
                for dt in range(NT_D):
                    v.wait_ge(TP.h, t * NT_D + dt + 1)
                    pb = b.P[dt % 4].bitcast(BF16)
                    v.tensor_copy(b.xn[dt][:, t * 128:(t + 1) * 128], pb[:, 0:128]).then_inc(EV.h, 1)
            VAR.v = XN.v = Q2.v = NT
            EV.v = NT * NT_D

        @blk.tensor
        def _(t_):
            for t in range(NT):
                t_.wait_ge(XN.h, t + 1)
                for dt in range(NT_D):
                    if t * NT_D + dt >= 4:
                        t_.wait_ge(EV.h, t * NT_D + dt - 3)
                    pb = b.P[dt % 4].bitcast(BF16)
                    t_.transpose(pb[:, 0:128], xnt[:, dt * 128:(dt + 1) * 128],
                                 b.ident).then_inc(TP.h, 1)
            TP.v = NT * NT_D

    TP2, EV2 = b.sem("nTP2"), b.sem("nEV2")
    with nc.Block() as blk:
        @blk.tensor
        def _(t_):
            t_.transpose(b.P[4][0:NT_TH, 0:128], b.rrc[:, 0:NT_TH], b.identf).then_inc(TP2.h, 1)

        @blk.vector
        def _(v):
            v.wait_ge(TP2.h, 1)
            v.tensor_copy(rrT, b.P[4][0:NT_TH, 0:128]).then_inc(EV2.h, 1)

        @blk.tensor
        def _(t_):
            t_.wait_ge(EV2.h, 1)
            t_.wait_ge(LDC.h, 16)
            for t in range(NT_TH):
                t_.matmul(b.P[5][:, 0:128], tsel[:, t * 128:(t + 1) * 128], rrT,
                          start=True, stop=True).then_inc(TP2.h, 1)
                t_.wait_ge(EV2.h, 2 + t)

        @blk.vector
        def _(v):
            for t in range(NT_TH):
                v.wait_ge(TP2.h, 2 + t)
                v.tensor_copy(b.rrb[:, t * 128:(t + 1) * 128], b.P[5][:, 0:128]).then_inc(EV2.h, 1)
            EV2.v = 1 + NT_TH
            TP2.v = 1 + NT_TH

    if "xn0" in b.want_dbg:
        dxn = b.dbg("xn0", (NT_D, 128, TH), BF16)
        drr = b.dbg("rrb", (128, TH))
        ST = b.sem("nST")
        with nc.Block() as blk:
            @blk.sync
            def _(sync):
                for dt in range(NT_D):
                    sync.dma_start(out=dxn[dt], in_=b.xn[dt]).then_inc(ST.h, 16)
                sync.dma_start(out=drr, in_=b.rrb).then_inc(ST.h, 16)
                ST.v = 16 * (NT_D + 1)
                sync.wait_ge(ST.h, ST.v)
    b.end_phase()

def _emit_z(b, Anm, chunks, ZP, ZE1, TPz, ZE):
    subs = [(cb + st * 128) for cb in chunks for st in range(4)]

    def tensor_part(t_):
        pzT = b.P[7].bitcast(BF16)
        for si, tok0 in enumerate(subs):
            if si > 0:
                t_.wait_ge(ZE1.h, si)
            a3 = b.AT[Anm].rearrange("p (t r) -> p t r", t=NT_D)
            for dt in range(NT_D):
                mmz = t_.matmul(b.P[6][:, 0:16], b.xn[dt][:, tok0:tok0 + 128],
                                a3[:, dt], start=(dt == 0), stop=(dt == NT_D - 1))
            mmz.then_inc(ZP.h, 1)
            t_.wait_ge(ZE1.h, si + 1)
            if si > 0:
                t_.wait_ge(ZE.h, si)
            t_.transpose(pzT[0:16, 0:128], b.ztmp, b.ident).then_inc(TPz.h, 1)
        ZP.v += len(subs)
        TPz.v += len(subs)

    def vector_part(v):
        pzT = b.P[7].bitcast(BF16)
        for si, tok0 in enumerate(subs):
            v.wait_ge(ZP.h, si + 1)
            v.tensor_scalar_mul(b.ztmp, b.P[6][:, 0:16],
                                b.rrc[:, tok0 // 128:tok0 // 128 + 1]).then_inc(ZE1.h, 1)
            v.wait_ge(TPz.h, si + 1)
            v.tensor_copy(b.zb[0:16, tok0:tok0 + 128], pzT[0:16, 0:128]).then_inc(ZE.h, 1)
        ZE1.v += len(subs)
        ZE.v += len(subs)

    return tensor_part, vector_part, len(subs)


def emit_proj_qk(b, which):
    nc = b.nc
    off = b.PERSIST
    isq = which == "q"
    wT, Anm, Bnm = ("wqT", "qA", "qB") if isq else ("wkT", "kA", "kB")
    dst = b.qf_d if isq else b.kf_d
    nck = 2 if isq else 3
    cb0 = HALO if isq else 0
    nwcol = b.nwfq if isq else b.nwf

    wbuf = [b.salloc(f"{which}w{k}", [128, NT_D * 256], BF16, at=off + k * 8192)
            for k in range(2)]
    ropec = b.salloc(which + "ropec", [128, TH], BF16, at=off + 16384)
    ropes = b.salloc(which + "ropes", [128, TH], BF16, at=off + 16384 + 3072)
    usc = b.salloc(which + "usc", [128, 512], F32, at=off + 16384 + 6144)
    rotb = b.salloc(which + "rot", [128, 512], F32, at=off + 16384 + 6144 + 2048)
    tmA = b.salloc(which + "tmA", [128, 512], F32, at=off + 16384 + 6144 + 4096)
    tmB = b.salloc(which + "tmB", [128, 512], F32, at=off + 16384 + 6144 + 6144)
    outb = [b.salloc(f"{which}o{k}", [128, 512], BF16,
                     at=off + 16384 + 6144 + 8192 + k * 1024) for k in range(2)]

    WL = [b.sem(which + "WL0", g=True), b.sem(which + "WL1", g=True)]
    RL = b.sem(which + "RL")
    FD = b.sem(which + "FD")
    MM = b.sem(which + "MM")
    ZP, ZE1 = b.sem(which + "ZP"), b.sem(which + "ZE1")
    TPz, ZE = b.sem(which + "TPz"), b.sem(which + "ZE")
    EV = b.sem(which + "EV")
    OE = b.sem(which + "OE")
    SO = [b.sem(which + "SO0"), b.sem(which + "SO1")]
    sv = Ser(b, which + "SV")
    zt, zv, nsubs = _emit_z(b, Anm, [cb0 + c * 512 for c in range(nck)], ZP, ZE1, TPz, ZE)

    gpo = 2 * nck
    ngroups = 8 * gpo

    with nc.Block() as blk:
        @blk.sync
        def _(sync):
            sync.dma_start(out=ropec, in_=b.i["ropecos"]).then_inc(RL.h, 16)
            sync.dma_start(out=ropes, in_=b.i["ropesin"]).then_inc(RL.h, 16)
            RL.v = 32
            for gi in range(ngroups):
                og, ot, ck = gi // gpo, (gi % gpo) // nck, gi % nck
                head = og * 2 + ot
                slot = gi % 2
                sync.wait_ge(OE.h, gi + 1)
                sync.dma_start(out=dst[head][:, ck * 512:(ck + 1) * 512],
                               in_=outb[slot]).then_inc(SO[slot].h, 16)
                SO[slot].v += 16
            sync.wait_ge(SO[0].h, SO[0].v)
            sync.wait_ge(SO[1].h, SO[1].v)

        @blk.gpsimd
        def _(g):
            for og in range(8):
                slot = og % 2
                if og >= 2:
                    g.wait_ge(MM.h, (og - 1) * gpo)
                g.dma_start(out=wbuf[slot].rearrange("p (t o) -> p t o", t=NT_D),
                            in_=b.i[wT][:, og * 256:(og + 1) * 256]
                            .rearrange("(t p) o -> p t o", p=128)).then_inc(WL[slot].h, 16)
                WL[slot].v += 16

        @blk.scalar
        def _(s_):
            for og in range(8):
                slot = og % 2
                s_.wait_ge(WL[slot].h, 16 * (og // 2 + 1))
                w3 = wbuf[slot].rearrange("p (t o) -> p t o", t=NT_D)
                for dt in range(NT_D):
                    m = s_.mul(w3[:, dt], w3[:, dt], nwcol[:, dt:dt + 1])
                m.then_inc(FD.h, 1)
                FD.v += 1

        @blk.tensor
        def _(t_):
            if not NO_LORA:
                zt(t_)
                t_.wait_ge(ZE.h, nsubs)
            for gi in range(ngroups):
                og, ot, ck = gi // gpo, (gi % gpo) // nck, gi % nck
                head = og * 2 + ot
                slot = og % 2
                cb = cb0 + ck * 512
                if gi % gpo == 0:
                    t_.wait_ge(FD.h, og + 1)
                if gi >= 2:
                    t_.wait_ge(EV.h, gi - 1)
                pp = b.P[gi % 2]
                w3 = wbuf[slot].rearrange("p (t o) -> p t o", t=NT_D)
                for dt in range(NT_D):
                    t_.matmul(pp, w3[:, dt, ot * 128:(ot + 1) * 128],
                              b.xn[dt][:, cb:cb + 512], start=(dt == 0),
                              stop=(NO_LORA and dt == NT_D - 1))
                if not NO_LORA:
                    t_.matmul(pp, b.BTm[Bnm][:, head * 128:(head + 1) * 128],
                              b.zb[:, cb:cb + 512],
                              start=False, stop=True).then_inc(MM.h, 1)
                else:
                    t_.nop().then_inc(MM.h, 1)
                MM.v += 1

        @blk.vector
        def _(v):
            if not NO_LORA:
                zv(v)
            v.wait_ge(RL.h, 32)
            for gi in range(ngroups):
                og, ot, ck = gi // gpo, (gi % gpo) // nck, gi % nck
                slot = gi % 2
                cb = cb0 + ck * 512
                pp = b.P[gi % 2]
                v.wait_ge(MM.h, gi + 1)
                if gi >= 2:
                    v.wait_ge(SO[slot].h, 16 * (gi // 2))
                sv.step(v.tensor_tensor(usc, pp, b.rrb[:, cb:cb + 512], ALU.mult), v)
                nc.vector.engine_nop().then_inc(EV.h, 1)
                EV.v += 1
                v.tensor_scalar_mul(rotb[0:64, :], usc[64:128, :], -1.0)
                sv.step(v.tensor_scalar_mul(rotb[64:128, :], usc[0:64, :], 1.0), v)
                sv.step(v.tensor_tensor(tmA, usc, ropec[:, cb:cb + 512], ALU.mult), v)
                sv.step(v.tensor_tensor(tmB, rotb, ropes[:, cb:cb + 512], ALU.mult), v)
                v.tensor_tensor(outb[slot], tmA, tmB, ALU.add).then_inc(OE.h, 1)
                OE.v += 1
    assert MM.v == ngroups
    b.end_phase()

def emit_proj_v(b):
    nc = b.nc
    off = b.PERSIST
    wbuf = [b.salloc(f"vw{k}", [128, NT_D * 512], BF16, at=off + k * 16384) for k in range(2)]
    outb = [b.salloc(f"vo{k}", [128, 512], BF16, at=off + 32768 + k * 1024) for k in range(2)]
    WL = [b.sem("vWL0", g=True), b.sem("vWL1", g=True)]
    FD = b.sem("vFD")
    MM = b.sem("vMM")
    ZP, ZE1 = b.sem("vZP"), b.sem("vZE1")
    TPz, ZE = b.sem("vTPz"), b.sem("vZE")
    EVv = b.sem("vEV")
    OE = b.sem("vOE")
    SO = [b.sem("vSO0"), b.sem("vSO1")]
    zt, zv, nsubs = _emit_z(b, "vA", [c * 512 for c in range(3)], ZP, ZE1, TPz, ZE)
    ngroups = 4 * NT_TH

    with nc.Block() as blk:
        @blk.sync
        def _(sync):
            for gi in range(ngroups):
                dvc, tt = gi // NT_TH, gi % NT_TH
                slot = gi % 2
                sync.wait_ge(OE.h, gi + 1)
                sync.dma_start(out=b.vt_d[tt][:, dvc * 512:(dvc + 1) * 512],
                               in_=outb[slot]).then_inc(SO[slot].h, 16)
                SO[slot].v += 16
            sync.wait_ge(SO[0].h, SO[0].v)
            sync.wait_ge(SO[1].h, SO[1].v)

        @blk.gpsimd
        def _(g):
            for dvc in range(4):
                slot = dvc % 2
                if dvc >= 2:
                    g.wait_ge(MM.h, (dvc - 1) * NT_TH)
                g.dma_start(out=wbuf[slot].rearrange("p (t o) -> p t o", t=NT_D),
                            in_=b.i["wvT"][:, dvc * 512:(dvc + 1) * 512]
                            .rearrange("(t p) o -> p t o", p=128)).then_inc(WL[slot].h, 16)
                WL[slot].v += 16

        @blk.scalar
        def _(s_):
            for dvc in range(4):
                slot = dvc % 2
                s_.wait_ge(WL[slot].h, 16 * (dvc // 2 + 1))
                w3 = wbuf[slot].rearrange("p (t o) -> p t o", t=NT_D)
                for dt in range(NT_D):
                    m = s_.mul(w3[:, dt], w3[:, dt], b.nwf[:, dt:dt + 1])
                m.then_inc(FD.h, 1)
                FD.v += 1

        @blk.tensor
        def _(t_):
            zt(t_)
            t_.wait_ge(ZE.h, nsubs)
            for gi in range(ngroups):
                dvc, tt = gi // NT_TH, gi % NT_TH
                slot = dvc % 2
                if gi % NT_TH == 0:
                    t_.wait_ge(FD.h, dvc + 1)
                if gi >= 2:
                    t_.wait_ge(EVv.h, gi - 1)
                pp = b.P[gi % 2]
                w3 = wbuf[slot].rearrange("p (t o) -> p t o", t=NT_D)
                for dt in range(NT_D):
                    t_.matmul(pp, b.xn[dt][:, tt * 128:(tt + 1) * 128],
                              w3[:, dt], start=(dt == 0), stop=False)
                t_.matmul(pp, b.zb[:, tt * 128:(tt + 1) * 128],
                          b.BTm["vB"][:, dvc * 512:(dvc + 1) * 512],
                          start=False, stop=True).then_inc(MM.h, 1)
                MM.v += 1

        @blk.vector
        def _(v):
            zv(v)
            for gi in range(ngroups):
                dvc, tt = gi // NT_TH, gi % NT_TH
                slot = gi % 2
                pp = b.P[gi % 2]
                v.wait_ge(MM.h, gi + 1)
                if gi >= 2:
                    v.wait_ge(SO[slot].h, 16 * (gi // 2))
                v.tensor_scalar_mul(outb[slot], pp, b.rrc[:, tt:tt + 1]).then_inc(OE.h, 1)
                OE.v += 1
                nc.vector.engine_nop().then_inc(EVv.h, 1)
                EVv.v += 1
    assert MM.v == ngroups
    b.end_phase()

def emit_attn(b):
    nc = b.nc
    off = b.PERSIST
    Kh = [b.salloc(f"aK{k}", [128, TH], BF16, at=off + k * 3072) for k in range(2)]
    Vh = [b.salloc(f"aV{k}", [128, TH], BF16, at=off + 6144 + k * 3072) for k in range(2)]
    Qh = [b.salloc(f"aQ{k}", [128, T_OWN], BF16, at=off + 12288 + k * 2048) for k in range(2)]
    msk = b.salloc("amsk", [128, 5 * 640], F32, at=off + 16384)
    s_sb = b.salloc("assb", [128, 640], F32, at=off + 16384 + 12800)
    pf = b.salloc("apf", [128, 640], F32, at=off + 16384 + 12800 + 2560)
    pbf = b.salloc("apbf", [128, 640], BF16, at=off + 16384 + 12800 + 5120)
    PTs = b.salloc("aPT", [128, 640], BF16, at=off + 16384 + 12800 + 5120 + 1280)
    ysb = [b.salloc(f"ay{k}", [128, 128], BF16, at=off + 16384 + 12800 + 5120 + 2560 + k * 256)
           for k in range(2)]
    ast = b.salloc("aast", [128, 4], F32, at=off + 16384 + 12800 + 5120 + 2560 + 512)

    KL = [b.sem("aKL0", g=True), b.sem("aKL1", g=True)]
    ML = b.sem("aML")
    MMD = b.sem("aMMD")
    SCP = b.sem("aSCP")
    SCE = b.sem("aSCE")
    NM = b.sem("aNM")
    EXP = b.sem("aEXP")
    PBF = b.sem("aPBF")
    TPP = b.sem("aTPP")
    PTE = b.sem("aPTE")
    OV = b.sem("aOV")
    YE = b.sem("aYE")
    YS = [b.sem("aYS0"), b.sem("aYS1")]
    sv = Ser(b, "aSV")

    NP = H * NT_OWN

    with nc.Block() as blk:
        @blk.sync
        def _(sync):
            for ms in range(5):
                sync.dma_start(out=msk[:, ms * 640:(ms + 1) * 640],
                               in_=b.i["mask5"][ms]).then_inc(ML.h, 16)
            ML.v = 80
            for i in range(NP):
                h, qt = i // NT_OWN, i % NT_OWN
                slot = i % 2
                sync.wait_ge(YE.h, i + 1)
                sync.dma_start(out=b.yattn_d[h][:, qt * 128:(qt + 1) * 128],
                               in_=ysb[slot]).then_inc(YS[slot].h, 16)
                YS[slot].v += 16
            sync.wait_ge(YS[0].h, YS[0].v)
            sync.wait_ge(YS[1].h, YS[1].v)

        @blk.gpsimd
        def _(g):
            for h in range(H):
                slot = h % 2
                if h >= 2:
                    g.wait_ge(OV.h, NT_OWN * (h - 1))
                g.dma_start(out=Kh[slot], in_=b.kf_d[h]).then_inc(KL[slot].h, 16)
                g.dma_start(out=Vh[slot].rearrange("p (t c) -> p t c", t=NT_TH),
                            in_=b.vt_d[:, :, h * 128:(h + 1) * 128]
                            .rearrange("t p c -> p t c")).then_inc(KL[slot].h, 16)
                g.dma_start(out=Qh[slot], in_=b.qf_d[h]).then_inc(KL[slot].h, 16)
                KL[slot].v += 48

        @blk.tensor
        def _(t_):
            for i in range(NP):
                h, qt = i // NT_OWN, i % NT_OWN
                slot = h % 2
                if qt == 0:
                    t_.wait_ge(KL[slot].h, 48 * (h // 2 + 1))
                if i > 0:
                    t_.wait_ge(SCE.h, i)
                t_.matmul(b.P[0], Qh[slot][:, qt * 128:(qt + 1) * 128],
                          Kh[slot][:, qt * 128:qt * 128 + 512], start=True, stop=True)
                t_.matmul(b.P[1][:, 0:128], Qh[slot][:, qt * 128:(qt + 1) * 128],
                          Kh[slot][:, qt * 128 + 512:qt * 128 + 640],
                          start=True, stop=True).then_inc(SCP.h, 1)
                SCP.v += 1
                t_.wait_ge(PBF.h, i + 1)
                for j in range(5):
                    if 5 * i + j >= 2:
                        t_.wait_ge(PTE.h, 5 * i + j - 1)
                    pb = b.P[2 + (j % 2)].bitcast(BF16)
                    t_.transpose(pb[:, 0:128], pbf[:, j * 128:(j + 1) * 128],
                                 b.ident).then_inc(TPP.h, 1)
                    TPP.v += 1
                if i >= 2:
                    t_.wait_ge(YE.h, i - 1)
                t_.wait_ge(PTE.h, 5 * i + 5)
                po = b.P[4 + (i % 2)]
                for j in range(5):
                    mm = t_.matmul(po[:, 0:128], Vh[slot][:, (qt + j) * 128:(qt + j + 1) * 128],
                                   PTs[:, j * 128:(j + 1) * 128],
                                   start=(j == 0), stop=(j == 4))
                mm.then_inc(OV.h, 1)
                OV.v += 1

        @blk.vector
        def _(v):
            v.wait_ge(ML.h, 80)
            for i in range(NP):
                h, qt = i // NT_OWN, i % NT_OWN
                mslot = min(qt, 4)
                if i >= 1:
                    v.wait_ge(EXP.h, i)
                v.wait_ge(SCP.h, i + 1)
                v.tensor_tensor(s_sb[:, 0:512], b.P[0], msk[:, mslot * 640:mslot * 640 + 512],
                                ALU.add)
                v.tensor_tensor(s_sb[:, 512:640], b.P[1][:, 0:128],
                                msk[:, mslot * 640 + 512:(mslot + 1) * 640],
                                ALU.add).then_inc(SCE.h, 1)
                SCE.v += 1
                v.wait_ge(SCE.h, SCE.v)
                sv.step(v.reduce_max(ast[:, 0:1], s_sb, AX.X, negate=True), v)
                nc.vector.engine_nop().then_inc(NM.h, 1)
                NM.v += 1
                v.wait_ge(EXP.h, i + 1)
                sv.step(v.reciprocal(ast[:, 2:3], ast[:, 1:2]), v)
                v.tensor_scalar_mul(pbf, pf, ast[:, 2:3]).then_inc(PBF.h, 1)
                PBF.v += 1
                for j in range(5):
                    v.wait_ge(TPP.h, 5 * i + j + 1)
                    pb = b.P[2 + (j % 2)].bitcast(BF16)
                    v.tensor_copy(PTs[:, j * 128:(j + 1) * 128], pb[:, 0:128]).then_inc(PTE.h, 1)
                    PTE.v += 1
                v.wait_ge(OV.h, i + 1)
                slot = i % 2
                if i >= 2:
                    v.wait_ge(YS[slot].h, 16 * (i // 2))
                v.tensor_copy(ysb[slot], b.P[4 + (i % 2)][:, 0:128]).then_inc(YE.h, 1)
                YE.v += 1

        @blk.scalar
        def _(s_):
            for i in range(NP):
                s_.wait_ge(NM.h, i + 1)
                if i >= 1:
                    s_.wait_ge(PBF.h, i)
                s_.activation(pf, s_sb, AF.Exp, bias=ast[:, 0:1],
                              accum_out=ast[:, 1:2]).then_inc(EXP.h, 1)
                EXP.v += 1
    b.end_phase()

def _dump_yattn(b):
    nc = b.nc
    d = b.dbg("yat", (NT_D, 128, T_OWN), BF16)
    ST = b.sem("dyST")
    with nc.Block() as blk:
        @blk.sync
        def _(sync):
            sync.dma_start(out=d, in_=b.yattn_d).then_inc(ST.h, 16)
            ST.v = 16
            sync.wait_ge(ST.h, ST.v)
    b.end_phase()

def emit_oproj(b):
    nc = b.nc
    b.s_fm = b.salloc("s_fm", [128, NT_D * T_OWN], BF16)
    off = b.sboff
    ybuf = b.salloc("oyb", [128, NT_D * T_OWN], BF16, at=off)
    wbuf = [b.salloc(f"ow{k}", [128, NT_D * 256], BF16, at=off + 32768 + k * 8192)
            for k in range(2)]
    y3 = ybuf.rearrange("p (t c) -> p t c", t=NT_D)
    s3 = b.s_fm.rearrange("p (t c) -> p t c", t=NT_D)

    YL = b.sem("oYL", g=True)
    WL = [b.sem("oWL0", g=True), b.sem("oWL1", g=True)]
    MM = b.sem("oMM")
    ZP, ZE1 = b.sem("oZP"), b.sem("oZE1")
    TPz, ZE = b.sem("oTPz"), b.sem("oZE")
    EV = b.sem("oEV")
    ngroups = 8 * 2 * 2

    with nc.Block() as blk:
        @blk.gpsimd
        def _(g):
            g.dma_start(out=y3, in_=b.yattn_d.rearrange("t p c -> p t c")).then_inc(YL.h, 16)
            YL.v += 16
            for og in range(8):
                slot = og % 2
                if og >= 2:
                    g.wait_ge(MM.h, (og - 1) * 4)
                g.dma_start(out=wbuf[slot].rearrange("p (t o) -> p t o", t=NT_D),
                            in_=b.i["woT"][:, og * 256:(og + 1) * 256]
                            .rearrange("(t p) o -> p t o", p=128)).then_inc(WL[slot].h, 16)
                WL[slot].v += 16

        @blk.tensor
        def _(t_):
            t_.wait_ge(YL.h, 16)
            a3 = b.AT["oA"].rearrange("p (t r) -> p t r", t=NT_D)
            pzT = b.P[7].bitcast(BF16)
            for si in range(8):
                tok0 = si * 128
                if si > 0:
                    t_.wait_ge(ZE1.h, si)
                for dt in range(NT_D):
                    mmz = t_.matmul(b.P[6][:, 0:16], y3[:, dt, tok0:tok0 + 128],
                                    a3[:, dt], start=(dt == 0), stop=(dt == NT_D - 1))
                mmz.then_inc(ZP.h, 1)
                ZP.v += 1
                t_.wait_ge(ZE1.h, si + 1)
                if si > 0:
                    t_.wait_ge(ZE.h, si)
                t_.transpose(pzT[0:16, 0:128], b.ztmp, b.ident).then_inc(TPz.h, 1)
                TPz.v += 1
            t_.wait_ge(ZE.h, 8)
            for gi in range(ngroups):
                og, ot, ck = gi // 4, (gi % 4) // 2, gi % 2
                head = og * 2 + ot
                slot = og % 2
                if gi % 4 == 0:
                    t_.wait_ge(WL[slot].h, 16 * (og // 2 + 1))
                if gi >= 2:
                    t_.wait_ge(EV.h, gi - 1)
                pp = b.P[gi % 2]
                w3 = wbuf[slot].rearrange("p (t o) -> p t o", t=NT_D)
                for dt in range(NT_D):
                    t_.matmul(pp, w3[:, dt, ot * 128:(ot + 1) * 128],
                              y3[:, dt, ck * 512:(ck + 1) * 512], start=(dt == 0), stop=False)
                t_.matmul(pp, b.BTm["oB"][:, head * 128:(head + 1) * 128],
                          b.zb[:, ck * 512:(ck + 1) * 512],
                          start=False, stop=True).then_inc(MM.h, 1)
                MM.v += 1

        @blk.vector
        def _(v):
            pzT = b.P[7].bitcast(BF16)
            for si in range(8):
                v.wait_ge(ZP.h, si + 1)
                v.tensor_copy(b.ztmp, b.P[6][:, 0:16]).then_inc(ZE1.h, 1)
                ZE1.v += 1
                v.wait_ge(TPz.h, si + 1)
                v.tensor_copy(b.zb[0:16, si * 128:(si + 1) * 128],
                              pzT[0:16, 0:128]).then_inc(ZE.h, 1)
                ZE.v += 1
            for gi in range(ngroups):
                og, ot, ck = gi // 4, (gi % 4) // 2, gi % 2
                head = og * 2 + ot
                pp = b.P[gi % 2]
                v.wait_ge(MM.h, gi + 1)
                v.tensor_tensor(s3[:, head, ck * 512:(ck + 1) * 512], pp,
                                b.xn[head][:, HALO + ck * 512:HALO + (ck + 1) * 512],
                                ALU.add).then_inc(EV.h, 1)
                EV.v += 1
    b.end_phase()

def _dump_s(b):
    nc = b.nc
    d = b.dbg("s0", (128, NT_D * T_OWN), BF16)
    ST = b.sem("dsST")
    with nc.Block() as blk:
        @blk.sync
        def _(sync):
            sync.dma_start(out=d, in_=b.s_fm).then_inc(ST.h, 16)
            ST.v = 16
            sync.wait_ge(ST.h, ST.v)
    b.end_phase()

def emit_router(b):
    nc = b.nc
    off = 64 * 1024 + 512
    aff = b.salloc("raff", [128, 16], F32, at=off)
    eqt = b.salloc("reqt", [128, 16], F32, at=off + 64)
    tbd = b.salloc("rtbd", [128, 16], F32, at=off + 128)
    rst = b.salloc("rrst", [128, 16], F32, at=off + 192)
    ctm = b.salloc("rctm", [128, 16], BF16, at=off + 256)
    es = b.salloc("resel", [128, NE * 128], BF16)
    b._moe_es = es

    EL = b.sem("rEL")
    ELM = b.sem("rELM")
    AP_ = b.sem("rAP")
    SG = b.sem("rSG")
    NM1 = b.sem("rNM1")
    EQS = b.sem("rEQS")
    MS = b.sem("rMS")
    SG2 = b.sem("rSG2")
    CMB = b.sem("rCMB")
    TPC = b.sem("rTPC")
    CEV = b.sem("rCEV")
    sv = Ser(b, "rSV")
    ss = Ser(b, "rSS")

    with nc.Block() as blk:
        @blk.sync
        def _(sync):
            sync.dma_start(out=es, in_=b.esel_d).then_inc(EL.h, 16)
            EL.v = 16

        @blk.gpsimd
        def _(g):
            g.memset(b.combT, 0.0).then_inc(ELM.h, 1)
            ELM.v += 1

        @blk.tensor
        def _(t_):
            t_.wait_ge(EL.h, EL.v)
            t_.wait_ge(ELM.h, ELM.v)
            pcT = b.P[7].bitcast(BF16)
            for tt in range(NT_OWN):
                if tt > 0:
                    t_.wait_ge(SG.h, tt)
                for dt in range(NT_D):
                    mm = t_.matmul(b.P[5][:, 0:16],
                                   b.xn[dt][:, HALO + tt * 128:HALO + (tt + 1) * 128],
                                   b.waf.rearrange("p (t e) -> p t e", t=NT_D)[:, dt],
                                   start=(dt == 0), stop=(dt == NT_D - 1))
                mm.then_inc(AP_.h, 1)
                AP_.v += 1
                t_.wait_ge(CMB.h, tt + 1)
                if tt > 0:
                    t_.wait_ge(CEV.h, tt)
                t_.transpose(pcT[0:16, 0:128], ctm, b.ident).then_inc(TPC.h, 1)
                TPC.v += 1

        @blk.scalar
        def _(s_):
            for tt in range(NT_OWN):
                s_.wait_ge(AP_.h, tt + 1)
                if tt > 0:
                    s_.wait_ge(CMB.h, tt)
                ss.step(s_.activation(aff, b.P[5][:, 0:16], AF.Sigmoid), s_)
                s_.nop().then_inc(SG.h, 1)
                SG.v += 1
                s_.wait_ge(NM1.h, tt + 1)
                ss.step(s_.activation(eqt, aff, AF.Sign, bias=rst[:, 0:1]), s_)
                s_.nop().then_inc(EQS.h, 1)
                EQS.v += 1
                s_.wait_ge(MS.h, tt + 1)
                ss.step(s_.activation(tbd, aff, AF.Sign, bias=rst[:, 9:10]), s_)
                s_.nop().then_inc(SG2.h, 1)
                SG2.v += 1

        @blk.vector
        def _(v):
            for tt in range(NT_OWN):
                v.wait_ge(SG.h, tt + 1)
                sv.step(v.reduce_max(rst[:, 0:1], aff, AX.X, negate=True), v)
                v.reduce_max(rst[:, 1:2], aff, AX.X).then_inc(NM1.h, 1)
                NM1.v += 1
                v.wait_ge(EQS.h, tt + 1)
                sv.step(v.tensor_scalar_add(eqt, eqt, 1.0), v)
                sv.step(v.reduce_sum(rst[:, 5:6], eqt, AX.X), v)
                sv.step(v.scalar_tensor_tensor(tbd, eqt, 1e30, aff, ALU.mult, ALU.subtract), v)
                sv.step(v.tensor_reduce(rst[:, 6:7], tbd, AX.X, ALU.min, negate=True), v)
                sv.step(v.tensor_scalar(rst[:, 7:8], rst[:, 5:6], -1.5, 0.0, ALU.add, ALU.max), v)
                sv.step(v.tensor_scalar_min(rst[:, 7:8], rst[:, 7:8], 1.0), v)
                sv.step(v.tensor_tensor(rst[:, 8:9], rst[:, 1:2], rst[:, 6:7], ALU.subtract), v)
                sv.step(v.tensor_tensor(rst[:, 8:9], rst[:, 8:9], rst[:, 7:8], ALU.mult), v)
                sv.step(v.tensor_tensor(rst[:, 8:9], rst[:, 6:7], rst[:, 8:9], ALU.add), v)
                sv.step(v.tensor_scalar_mul(rst[:, 9:10], rst[:, 8:9], -1.0), v)
                sv.step(v.tensor_tensor(rst[:, 11:12], rst[:, 1:2], rst[:, 8:9], ALU.add), v)
                v.reciprocal(rst[:, 12:13], rst[:, 11:12]).then_inc(MS.h, 1)
                MS.v += 1
                v.wait_ge(SG2.h, tt + 1)
                sv.step(v.tensor_scalar(tbd, tbd, 1.0, 1.0, ALU.add, ALU.min), v)
                sv.step(v.tensor_tensor(tbd, aff, tbd, ALU.mult), v)
                v.tensor_scalar_mul(ctm, tbd, rst[:, 12:13]).then_inc(CMB.h, 1)
                CMB.v += 1
                v.wait_ge(TPC.h, tt + 1)
                pcT = b.P[7].bitcast(BF16)
                v.tensor_copy(b.combT[0:16, tt * 128:(tt + 1) * 128],
                              pcT[0:16, 0:128]).then_inc(CEV.h, 1)
                CEV.v += 1
    b.end_phase()

def emit_moe(b):
    nc = b.nc
    b.xnc = b.salloc("xnc", [128, NT_D * T_OWN], BF16)
    xc3 = b.xnc.rearrange("p (t c) -> p t c", t=NT_D)
    CPX = b.sem("mCPX")
    with nc.Block() as blk:
        @blk.vector
        def _(v):
            for dt in range(NT_D):
                cp = v.tensor_copy(xc3[:, dt], b.xn[dt][:, HALO:HALO + T_OWN])
            cp.then_inc(CPX.h, 1)
            CPX.v = 1

    acc = b.salloc("macc", [128, NT_D * T_OWN], F32, at=0)
    b.acc = acc
    a3 = acc.rearrange("p (t c) -> p t c", t=NT_D)
    off2 = 65536
    gbuf = b.salloc("mgel", [128, 512], F32, at=off2)
    tmp2 = b.salloc("mtmp2", [128, 512], F32, at=off2 + 2048)
    rgb3 = b.salloc("mrgb", [128, NT_F * NE], F32, at=off2 + 4096)
    rb13 = b.salloc("mrb1", [128, NT_F * NE], F32, at=off2 + 4608)
    sgb3 = b.salloc("msgb", [128, NT_F * NSH], F32, at=off2 + 5120)
    sb13 = b.salloc("msb1", [128, NT_F * NSH], F32, at=off2 + 5184)
    rb2p = b.salloc("mrb2p", [128, D], BF16, at=off2 + 5248)
    sb2p = b.salloc("msb2p", [128, D], BF16, at=off2 + 5248 + 4096)
    assert off2 + 5248 + 8192 <= 79072
    o2p = b.salloc("mo2p", [128, 512], BF16)
    cbb = b.salloc("mcbb", [128, T_OWN], BF16)
    b._moe_offS = b.sboff
    wg = [b.salloc(f"mwg{k}", [128, NT_D * 128], BF16) for k in range(2)]
    w1b = [b.salloc(f"mw1{k}", [128, NT_D * 128], BF16) for k in range(2)]
    w2b = [b.salloc(f"mw2{k}", [128, NT_F * 256], BF16) for k in range(2)]
    yb = b.salloc("myb", [128, NT_F * T_OWN], BF16)
    y3 = yb.rearrange("p (t c) -> p t c", t=NT_F)
    assert b.sboff <= 206000, b.sboff

    NEX = NE + NSH
    WLG = [b.sem("mWLG0", g=True), b.sem("mWLG1", g=True)]
    WL2 = [b.sem("mWL20", g=True), b.sem("mWL21", g=True)]
    BL = b.sem("mBL", g=True)
    CB = b.sem("mCB")
    CBE = b.sem("mCBE")
    HG = b.sem("mHG")
    H1 = b.sem("mH1")
    GE = b.sem("mGE")
    YG = b.sem("mYG")
    YE = b.sem("mYE")
    W2M = b.sem("mW2M")
    AE = b.sem("mAE")
    MZ2 = b.sem("mMZ2")
    sv = Ser(b, "mSV")

    def eW(e):
        if e < NE:
            return "rg", "rw1", "rw2", rgb3, rb13, e
        return "sg", "sw1", "sw2", sgb3, sb13, e - NE

    with nc.Block() as blk:
        @blk.gpsimd
        def _(g):
            g.dma_start(out=rgb3.rearrange("p (t e) -> p t e", t=NT_F),
                        in_=b.i["rgbT"].rearrange("(t p) e -> p t e", p=128)).then_inc(BL.h, 16)
            g.dma_start(out=rb13.rearrange("p (t e) -> p t e", t=NT_F),
                        in_=b.i["rb1T"].rearrange("(t p) e -> p t e", p=128)).then_inc(BL.h, 16)
            g.dma_start(out=sgb3.rearrange("p (t e) -> p t e", t=NT_F),
                        in_=b.i["sgbT"].rearrange("(t p) e -> p t e", p=128)).then_inc(BL.h, 16)
            g.dma_start(out=sb13.rearrange("p (t e) -> p t e", t=NT_F),
                        in_=b.i["sb1T"].rearrange("(t p) e -> p t e", p=128)).then_inc(BL.h, 16)
            g.memset(rb2p, 0.0)
            mz = g.memset(sb2p, 0.0)
            mz.then_inc(MZ2.h, 1)
            g.wait_ge(MZ2.h, 1)
            g.wait_ge(BL.h, 64)
            g.dma_start(out=rb2p[0:16, :], in_=b.i["rb2"]).then_inc(BL.h, 16)
            g.dma_start(out=sb2p[0:2, :], in_=b.i["sb2"]).then_inc(BL.h, 16)
            g.dma_start(out=o2p, in_=b.ones2_d).then_inc(BL.h, 16)
            BL.v = 64 + 48
            MZ2.v = 1
            for e in range(NEX):
                ng, n1, n2, _, _, ei = eW(e)
                for sl in range(8):
                    ls = e * 8 + sl
                    slot = ls % 2
                    if ls >= 2:
                        g.wait_ge(H1.h, 2 * (ls - 1))
                    g.dma_start(out=wg[slot].rearrange("p (t o) -> p t o", t=NT_D),
                                in_=b.i[ng][ei][:, sl * 128:(sl + 1) * 128]
                                .rearrange("(t p) o -> p t o", p=128)).then_inc(WLG[slot].h, 16)
                    g.dma_start(out=w1b[slot].rearrange("p (t o) -> p t o", t=NT_D),
                                in_=b.i[n1][ei][:, sl * 128:(sl + 1) * 128]
                                .rearrange("(t p) o -> p t o", p=128)).then_inc(WLG[slot].h, 16)
                    WLG[slot].v += 32
                for sl in range(8):
                    ls = e * 8 + sl
                    slot = ls % 2
                    if ls >= 2:
                        g.wait_ge(W2M.h, 32 + 4 * (ls - 1))
                    g.dma_start(out=w2b[slot].rearrange("p (t o) -> p t o", t=NT_F),
                                in_=b.i[n2][ei][:, sl * 256:(sl + 1) * 256]
                                .rearrange("(t p) o -> p t o", p=128)).then_inc(WL2[slot].h, 16)
                    WL2[slot].v += 16

        @blk.tensor
        def _(t_):
            t_.wait_ge(CPX.h, 1)
            t_.wait_ge(BL.h, BL.v)
            t_.wait_ge(MZ2.h, 1)
            w3g = [w.rearrange("p (t o) -> p t o", t=NT_D) for w in wg]
            w31 = [w.rearrange("p (t o) -> p t o", t=NT_D) for w in w1b]
            w32 = [w.rearrange("p (t o) -> p t o", t=NT_F) for w in w2b]
            for gi in range(32):
                dt, ch = gi // 2, gi % 2
                if gi >= 2:
                    t_.wait_ge(AE.h, gi - 1)
                pp = b.P[4 + (gi % 2)]
                t_.matmul(pp, rb2p[:, dt * 128:(dt + 1) * 128],
                          b.combT[:, ch * 512:(ch + 1) * 512], start=True, stop=False)
                t_.matmul(pp, sb2p[:, dt * 128:(dt + 1) * 128],
                          o2p, start=False, stop=True).then_inc(W2M.h, 1)
                W2M.v += 1
            for e in range(NEX):
                if e < NE:
                    for ch in range(2):
                        if 2 * e + ch >= 1:
                            t_.wait_ge(CBE.h, 2 * e + ch)
                        if e >= 1 and ch == 0:
                            t_.wait_ge(YE.h, 16 * e)
                        t_.matmul(b.P[6], b._moe_es[:, e * 128:(e + 1) * 128],
                                  b.combT[:, ch * 512:(ch + 1) * 512],
                                  start=True, stop=True).then_inc(CB.h, 1)
                        CB.v += 1
                for gi in range(16):
                    ft, ch = gi // 2, gi % 2
                    ls = e * 8 + ft
                    slot = ls % 2
                    gg = e * 16 + gi
                    t_.wait_ge(WLG[slot].h, 32 * (ls // 2 + 1))
                    if gg >= 2:
                        t_.wait_ge(GE.h, gg - 1)
                        t_.wait_ge(YG.h, gg - 1)
                    pg = b.P[0 + (gg % 2)]
                    p1 = b.P[2 + (gg % 2)]
                    for dt in range(NT_D):
                        mmg = t_.matmul(pg, w3g[slot][:, dt],
                                        xc3[:, dt, ch * 512:(ch + 1) * 512],
                                        start=(dt == 0), stop=(dt == NT_D - 1))
                    mmg.then_inc(HG.h, 1)
                    HG.v += 1
                    for dt in range(NT_D):
                        mm1 = t_.matmul(p1, w31[slot][:, dt],
                                        xc3[:, dt, ch * 512:(ch + 1) * 512],
                                        start=(dt == 0), stop=(dt == NT_D - 1))
                    mm1.then_inc(H1.h, 1)
                    H1.v += 1
                t_.wait_ge(YE.h, 16 * (e + 1))
                for wgi in range(32):
                    dt, ch = wgi // 2, wgi % 2
                    sl = dt // 2
                    ls2 = e * 8 + sl
                    slot = ls2 % 2
                    gw = 32 + e * 32 + wgi
                    t_.wait_ge(WL2[slot].h, 16 * (ls2 // 2 + 1))
                    if gw >= 2:
                        t_.wait_ge(AE.h, gw - 1)
                    pp = b.P[4 + (gw % 2)]
                    for ft in range(NT_F):
                        mmw = t_.matmul(pp, w32[slot][:, ft, (dt % 2) * 128:(dt % 2 + 1) * 128],
                                        y3[:, ft, ch * 512:(ch + 1) * 512],
                                        start=(ft == 0), stop=(ft == NT_F - 1))
                    mmw.then_inc(W2M.h, 1)
                    W2M.v += 1

        @blk.scalar
        def _(s_):
            for e in range(NEX):
                _, _, _, gb3, b13, ei = eW(e)
                for gi in range(16):
                    ft, ch = gi // 2, gi % 2
                    gg = e * 16 + gi
                    s_.wait_ge(HG.h, gg + 1)
                    if gg >= 1:
                        s_.wait_ge(YG.h, gg)
                    pg = b.P[0 + (gg % 2)]
                    s_.activation(gbuf, pg, AF.Gelu,
                                  bias=gb3.rearrange("p (t e) -> p t e", t=NT_F)[:, ft, ei:ei + 1]
                                  ).then_inc(GE.h, 1)
                    GE.v += 1

        @blk.vector
        def _(v):
            for gi0 in range(32):
                v.wait_ge(W2M.h, gi0 + 1)
                dt, ch = gi0 // 2, gi0 % 2
                pp = b.P[4 + (gi0 % 2)]
                v.tensor_copy(a3[:, dt, ch * 512:(ch + 1) * 512], pp).then_inc(AE.h, 1)
                AE.v += 1
            for e in range(NEX):
                _, _, _, gb3, b13, ei = eW(e)
                if e < NE:
                    for ch in range(2):
                        v.wait_ge(CB.h, 2 * e + ch + 1)
                        v.tensor_copy(cbb[:, ch * 512:(ch + 1) * 512],
                                      b.P[6]).then_inc(CBE.h, 1)
                        CBE.v += 1
                for gi in range(16):
                    ft, ch = gi // 2, gi % 2
                    gg = e * 16 + gi
                    v.wait_ge(GE.h, gg + 1)
                    v.wait_ge(H1.h, gg + 1)
                    if gg >= 1:
                        v.wait_ge(YE.h, gg)
                    p1 = b.P[2 + (gg % 2)]
                    sv.step(v.scalar_tensor_tensor(
                        tmp2, p1, b13.rearrange("p (t e) -> p t e", t=NT_F)[:, ft, ei:ei + 1],
                        gbuf, ALU.add, ALU.mult), v)
                    nc.vector.engine_nop().then_inc(YG.h, 1)
                    YG.v += 1
                    if e < NE:
                        v.tensor_tensor(y3[:, ft, ch * 512:(ch + 1) * 512], tmp2,
                                        cbb[:, ch * 512:(ch + 1) * 512],
                                        ALU.mult).then_inc(YE.h, 1)
                    else:
                        v.tensor_copy(y3[:, ft, ch * 512:(ch + 1) * 512],
                                      tmp2).then_inc(YE.h, 1)
                    YE.v += 1
                for wgi in range(32):
                    gw = 32 + e * 32 + wgi
                    dt, ch = wgi // 2, wgi % 2
                    v.wait_ge(W2M.h, gw + 1)
                    pp = b.P[4 + (gw % 2)]
                    v.tensor_tensor(a3[:, dt, ch * 512:(ch + 1) * 512], pp,
                                    a3[:, dt, ch * 512:(ch + 1) * 512],
                                    ALU.add).then_inc(AE.h, 1)
                    AE.v += 1
    b.end_phase()

def _dump_acc(b):
    nc = b.nc
    d = b.dbg("acc", (128, NT_D * T_OWN))
    dc = b.dbg("combT", (128, T_OWN), BF16)
    ST = b.sem("daST")
    with nc.Block() as blk:
        @blk.sync
        def _(sync):
            sync.dma_start(out=d, in_=b.acc).then_inc(ST.h, 16)
            sync.dma_start(out=dc, in_=b.combT).then_inc(ST.h, 16)
            ST.v = 32
            sync.wait_ge(ST.h, ST.v)
    b.end_phase()

def emit_combine_ln1(b):
    nc = b.nc
    moff = b._moe_offS
    prod = b.salloc("cprod", [128, NT_D * T_OWN], BF16, at=moff)
    p3 = prod.rearrange("p (t c) -> p t c", t=NT_D)
    gtmp = b.salloc("cgt", [128, T_OWN], F32, at=moff + 32768)
    xnt = b.salloc("cxnt", [128, D], BF16, at=moff + 32768 + 4096)
    stt = b.salloc("cstat", [128, 16], F32, at=65536)
    ptm = b.salloc("cptm", [128, D], BF16, at=65536 + 128)
    x1t = b.salloc("cx1", [128, D], F32, at=65536 + 128 + 4096)
    assert 65536 + 128 + 4096 + 8192 <= 79072
    xt = [b.salloc(f"cx{k}", [128, D], F32, at=32768 + k * 8192) for k in range(2)]
    sqd = b.salloc("csq", [128, D], F32, at=49152)
    b.xn1 = [b.salloc(f"cxn1_{t}", [128, T_OWN], BF16, at=t * 2048) for t in range(NT_D)]
    assert moff + 32768 + 4096 + 4096 <= 206000

    XL = [b.sem("cXL0"), b.sem("cXL1")]
    PRD = b.sem("cPRD")
    TPD = b.sem("cTPD")
    EVD = b.sem("cEVD")
    X1R = b.sem("cX1R")
    X1S = [b.sem("cX1S0"), b.sem("cX1S1")]
    SQ2 = b.sem("cSQ2")
    VAR2 = b.sem("cVAR2")
    RSTD2 = b.sem("cRSTD2")
    XN2 = b.sem("cXN2")
    TP2 = b.sem("cTP2")
    EV2 = b.sem("cEV2")
    sv = Ser(b, "cSV")
    ss = Ser(b, "cSS")
    C1 = float(1.0 / D)
    xin = b.i["xh"].rearrange("(t p) d -> t p d", p=128)

    with nc.Block() as blk:
        @blk.scalar
        def _(s_):
            for dt in range(NT_D):
                s_.wait_ge(PRD.h, 3 * dt + 1)
                ss.step(s_.activation(gtmp, gtmp, AF.Silu), s_)
                s_.nop().then_inc(PRD.h, 1)
            for t in range(NT_OWN):
                s_.wait_ge(X1R.h, t + 1)
                if t > 0:
                    s_.wait_ge(VAR2.h, t)
                s_.activation(sqd, x1t, AF.Square, accum_out=stt[:, 1:2]).then_inc(SQ2.h, 1)
                s_.wait_ge(VAR2.h, t + 1)
                ss.step(s_.activation(stt[:, 5:6], stt[:, 4:5], AF.Ln, bias=1e-5), s_)
                s_.activation(stt[:, 6:7], stt[:, 5:6], AF.Exp, scale=-0.5).then_inc(RSTD2.h, 1)
            SQ2.v = RSTD2.v = NT_OWN

        @blk.vector
        def _(v):
            for dt in range(NT_D):
                if dt > 0:
                    v.wait_ge(PRD.h, 3 * dt)
                v.tensor_tensor(gtmp, b.acc.rearrange("p (t c) -> p t c", t=NT_D)[:, dt],
                                b.xnc.rearrange("p (t c) -> p t c", t=NT_D)[:, dt],
                                ALU.add).then_inc(PRD.h, 1)
                v.wait_ge(PRD.h, 3 * dt + 2)
                v.tensor_tensor(p3[:, dt], gtmp, b.s_fm.rearrange("p (t c) -> p t c", t=NT_D)[:, dt],
                                ALU.mult).then_inc(PRD.h, 1)
            PRD.v = 3 * NT_D
            for t in range(NT_OWN):
                for dt in range(NT_D):
                    v.wait_ge(TPD.h, t * NT_D + dt + 1)
                    pb = b.P[dt % 2].bitcast(BF16)
                    v.tensor_copy(ptm[:, dt * 128:(dt + 1) * 128], pb[:, 0:128]).then_inc(EVD.h, 1)
                v.wait_ge(EVD.h, (t + 1) * NT_D)
                v.wait_ge(XL[t % 2].h, 16 * (t // 2 + 1))
                if t >= 1:
                    v.wait_ge(X1S[0].h, 16 * t)
                sv.step(v.tensor_tensor(x1t, xt[t % 2], ptm, ALU.add), v)
                nc.vector.engine_nop().then_inc(X1R.h, 1)
                v.wait_ge(SQ2.h, t + 1)
                sv.step(v.reduce_sum(stt[:, 0:1], x1t, AX.X), v)
                sv.step(v.tensor_scalar_mul(stt[:, 2:3], stt[:, 0:1], C1), v)
                sv.step(v.tensor_scalar_mul(stt[:, 3:4], stt[:, 1:2], C1), v)
                sv.step(v.tensor_tensor(stt[:, 4:5], stt[:, 2:3], stt[:, 2:3], ALU.mult), v)
                v.tensor_tensor(stt[:, 4:5], stt[:, 3:4], stt[:, 4:5],
                                ALU.subtract).then_inc(VAR2.h, 1)
                v.wait_ge(RSTD2.h, t + 1)
                if t > 0:
                    v.wait_ge(TP2.h, t * NT_D)
                v.tensor_scalar(xnt, x1t, stt[:, 2:3], stt[:, 6:7],
                                ALU.subtract, ALU.mult).then_inc(XN2.h, 1)
                for dt in range(NT_D):
                    v.wait_ge(TP2.h, t * NT_D + dt + 1)
                    pb = b.P[2 + dt % 2].bitcast(BF16)
                    v.tensor_copy(b.xn1[dt][:, t * 128:(t + 1) * 128], pb[:, 0:128]).then_inc(EV2.h, 1)
            X1R.v = NT_OWN
            VAR2.v = XN2.v = NT_OWN
            EVD.v = EV2.v = NT_OWN * NT_D

        @blk.tensor
        def _(t_):
            t_.wait_ge(PRD.h, 3 * NT_D)
            for t in range(NT_OWN):
                for dt in range(NT_D):
                    li = t * NT_D + dt
                    if li >= 2:
                        t_.wait_ge(EVD.h, li - 1)
                    pb = b.P[dt % 2].bitcast(BF16)
                    t_.transpose(pb[:, 0:128], p3[:, dt, t * 128:(t + 1) * 128],
                                 b.ident).then_inc(TPD.h, 1)
                t_.wait_ge(XN2.h, t + 1)
                for dt in range(NT_D):
                    li = t * NT_D + dt
                    if li >= 2:
                        t_.wait_ge(EV2.h, li - 1)
                    pb = b.P[2 + dt % 2].bitcast(BF16)
                    t_.transpose(pb[:, 0:128], xnt[:, dt * 128:(dt + 1) * 128],
                                 b.ident).then_inc(TP2.h, 1)
            TPD.v = NT_OWN * NT_D
            TP2.v = NT_OWN * NT_D

        @blk.sync
        def _(sync):
            sync.wait_ge(PRD.h, 3 * NT_D)
            for t in range(NT_OWN):
                if t >= 2:
                    sync.wait_ge(XN2.h, t - 1)
                    sync.dma_start(out=b.o["x1"][(t - 2) * 128:(t - 1) * 128, :],
                                   in_=x1t).then_inc(X1S[0].h, 16)
                    X1S[0].v += 16
                sync.dma_start(out=xt[t % 2], in_=xin[4 + t]).then_inc(XL[t % 2].h, 16)
                XL[t % 2].v += 16
            for t in (NT_OWN - 2, NT_OWN - 1):
                sync.wait_ge(XN2.h, t + 1)
                sync.dma_start(out=b.o["x1"][t * 128:(t + 1) * 128, :],
                               in_=x1t).then_inc(X1S[0].h, 16)
                X1S[0].v += 16
            sync.wait_ge(X1S[0].h, X1S[0].v)
    b.end_phase()

def _dump_xn1(b):
    nc = b.nc
    d = b.dbg("xn1", (NT_D, 128, T_OWN), BF16)
    ST = b.sem("dx1ST")
    with nc.Block() as blk:
        @blk.sync
        def _(sync):
            for dt in range(NT_D):
                sync.dma_start(out=d[dt], in_=b.xn1[dt]).then_inc(ST.h, 16)
            ST.v = 16 * NT_D
            sync.wait_ge(ST.h, ST.v)



TWO_PI = float(2.0 * np.pi)
INV_2PI = float(1.0 / TWO_PI)
MAGIC_RND = 12582912.0
PI_F = float(np.float32(np.pi))
PI_HALF = float(np.float32(np.pi / 2))
N_WELEM = HID * D


def _cw_split():
    def chop(x):
        u = np.frombuffer(np.float32(x).tobytes(), np.uint32)[0] & 0xFFFFF000
        return float(np.frombuffer(np.uint32(u).tobytes(), np.float32)[0])
    hi = chop(TWO_PI)
    mid = chop(TWO_PI - hi)
    lo = float(np.float32(TWO_PI - hi - mid))
    return hi, mid, lo


C2PI_HI, C2PI_MID, C2PI_LO = _cw_split()


def emit_l1qkv(b):
    nc = b.nc
    wbuf = [b.salloc(f"jw{k}", [128, NT_D * 256], BF16, at=32768 + k * 8192) for k in range(2)]
    outb = [b.salloc(f"jo{k}", [128, 512], BF16, at=49152 + k * 1024) for k in range(2)]
    wvb = [b.salloc(f"jv{k}", [128, NT_D * 512], BF16, at=81920 + k * 16384) for k in range(2)]

    for which, colbase, dst in (("q", 0, b.qf_d), ("k", D, b.kf_d)):
        WL = [b.sem("j" + which + "WL0", g=True), b.sem("j" + which + "WL1", g=True)]
        MM = b.sem("j" + which + "MM")
        EV = b.sem("j" + which + "EV")
        SO = [b.sem("j" + which + "SO0"), b.sem("j" + which + "SO1")]
        gpo = 4
        ngroups = 8 * gpo
        with nc.Block() as blk:
            @blk.gpsimd
            def _(g):
                for og in range(8):
                    slot = og % 2
                    if og >= 2:
                        g.wait_ge(MM.h, (og - 1) * gpo)
                    g.dma_start(out=wbuf[slot].rearrange("p (t o) -> p t o", t=NT_D),
                                in_=b.i["wqkvT"][:, colbase + og * 256:colbase + (og + 1) * 256]
                                .rearrange("(t p) o -> p t o", p=128)).then_inc(WL[slot].h, 16)
                    WL[slot].v += 16

            @blk.tensor
            def _(t_):
                for gi in range(ngroups):
                    og, ot, ck = gi // gpo, (gi % gpo) // 2, gi % 2
                    slot = og % 2
                    if gi % gpo == 0:
                        t_.wait_ge(WL[slot].h, 16 * (og // 2 + 1))
                    if gi >= 2:
                        t_.wait_ge(EV.h, gi - 1)
                    pp = b.P[gi % 2]
                    w3 = wbuf[slot].rearrange("p (t o) -> p t o", t=NT_D)
                    for dt in range(NT_D):
                        mm = t_.matmul(pp, w3[:, dt, ot * 128:(ot + 1) * 128],
                                       b.xn1[dt][:, ck * 512:(ck + 1) * 512],
                                       start=(dt == 0), stop=(dt == NT_D - 1))
                    mm.then_inc(MM.h, 1)
                    MM.v += 1

            @blk.vector
            def _(v):
                for gi in range(ngroups):
                    slot = gi % 2
                    v.wait_ge(MM.h, gi + 1)
                    if gi >= 2:
                        v.wait_ge(SO[slot].h, 16 * (gi // 2))
                    v.tensor_copy(outb[slot], b.P[gi % 2]).then_inc(EV.h, 1)
                    EV.v += 1

            @blk.sync
            def _(sync):
                for gi in range(ngroups):
                    og, ot, ck = gi // gpo, (gi % gpo) // 2, gi % 2
                    head = og * 2 + ot
                    slot = gi % 2
                    sync.wait_ge(EV.h, gi + 1)
                    sync.dma_start(out=dst[head][:, ck * 512:(ck + 1) * 512],
                                   in_=outb[slot]).then_inc(SO[slot].h, 16)
                    SO[slot].v += 16
                sync.wait_ge(SO[0].h, SO[0].v)
                sync.wait_ge(SO[1].h, SO[1].v)
        b.end_phase()

    WLV = [b.sem("jvWL0", g=True), b.sem("jvWL1", g=True)]
    MMV = b.sem("jvMM")
    EVV = b.sem("jvEV")
    SOV = [b.sem("jvSO0"), b.sem("jvSO1")]
    ngroups = 4 * NT_OWN
    with nc.Block() as blk:
        @blk.gpsimd
        def _(g):
            for dvc in range(4):
                slot = dvc % 2
                if dvc >= 2:
                    g.wait_ge(MMV.h, (dvc - 1) * NT_OWN)
                g.dma_start(out=wvb[slot].rearrange("p (t o) -> p t o", t=NT_D),
                            in_=b.i["wqkvT"][:, 2 * D + dvc * 512:2 * D + (dvc + 1) * 512]
                            .rearrange("(t p) o -> p t o", p=128)).then_inc(WLV[slot].h, 16)
                WLV[slot].v += 16

        @blk.tensor
        def _(t_):
            for gi in range(ngroups):
                dvc, tt = gi // NT_OWN, gi % NT_OWN
                slot = dvc % 2
                if gi % NT_OWN == 0:
                    t_.wait_ge(WLV[slot].h, 16 * (dvc // 2 + 1))
                if gi >= 2:
                    t_.wait_ge(EVV.h, gi - 1)
                pp = b.P[gi % 2]
                w3 = wvb[slot].rearrange("p (t o) -> p t o", t=NT_D)
                for dt in range(NT_D):
                    mm = t_.matmul(pp, b.xn1[dt][:, tt * 128:(tt + 1) * 128],
                                   w3[:, dt], start=(dt == 0), stop=(dt == NT_D - 1))
                mm.then_inc(MMV.h, 1)
                MMV.v += 1

        @blk.vector
        def _(v):
            for gi in range(ngroups):
                slot = gi % 2
                v.wait_ge(MMV.h, gi + 1)
                if gi >= 2:
                    v.wait_ge(SOV[slot].h, 16 * (gi // 2))
                v.tensor_copy(outb[slot], b.P[gi % 2]).then_inc(EVV.h, 1)
                EVV.v += 1

        @blk.sync
        def _(sync):
            for gi in range(ngroups):
                dvc, tt = gi // NT_OWN, gi % NT_OWN
                slot = gi % 2
                sync.wait_ge(EVV.h, gi + 1)
                sync.dma_start(out=b.vt_d[tt][:, dvc * 512:(dvc + 1) * 512],
                               in_=outb[slot]).then_inc(SOV[slot].h, 16)
                SOV[slot].v += 16
            sync.wait_ge(SOV[0].h, SOV[0].v)
            sync.wait_ge(SOV[1].h, SOV[1].v)
    b.end_phase()


def emit_l1tables(b):
    nc = b.nc
    b.cosT = b.salloc("cosT", [128, H * T_OWN], BF16, at=114688)
    b.sinT = b.salloc("sinT", [128, H * T_OWN], BF16, at=147456)
    th_sb = b.salloc("tth", [1, H * 128], F32, at=32768)
    idx_sb = b.salloc("tidx", [1, T_OWN], F32, at=40960)
    t1 = b.salloc("tt1", [128, 512], F32, at=45056)
    t2 = b.salloc("tt2", [128, 512], F32, at=47104)
    LD = b.sem("tLD")
    ANG = b.sem("tANG")
    RED = b.sem("tRED")
    SIN = b.sem("tSIN")
    sv = Ser(b, "tSV")

    with nc.Block() as blk:
        @blk.sync
        def _(sync):
            sync.dma_start(out=th_sb, in_=b.i["thetarow"]).then_inc(LD.h, 16)
            sync.dma_start(out=idx_sb, in_=b.i["idxrow"]).then_inc(LD.h, 16)
            LD.v = 32

        @blk.tensor
        def _(t_):
            t_.wait_ge(LD.h, 32)
            for h in range(H):
                if h > 0:
                    t_.wait_ge(RED.h, 4 * h)
                t_.matmul(b.P[6], th_sb[:, h * 128:(h + 1) * 128],
                          idx_sb[:, 0:512], start=True, stop=True)
                t_.matmul(b.P[7], th_sb[:, h * 128:(h + 1) * 128],
                          idx_sb[:, 512:1024], start=True,
                          stop=True).then_inc(ANG.h, 1)
                ANG.v += 1

        @blk.vector
        def _(v):
            for h in range(H):
                v.wait_ge(ANG.h, h + 1)
                for trig in range(2):
                    for cc in range(2):
                        gidx = (h * 2 + trig) * 2 + cc
                        ps = b.P[6 + cc]
                        if gidx > 0:
                            v.wait_ge(SIN.h, gidx)
                        bias = 0.25 if trig == 1 else 0.0
                        sv.step(v.tensor_scalar(t1, ps, INV_2PI, bias,
                                                ALU.mult, ALU.add), v)
                        sv.step(v.tensor_scalar_add(t1, t1, MAGIC_RND), v)
                        sv.step(v.tensor_scalar_add(t1, t1, -MAGIC_RND), v)
                        sv.step(v.scalar_tensor_tensor(t2, t1, -C2PI_HI, ps,
                                                       ALU.mult, ALU.add), v)
                        sv.step(v.scalar_tensor_tensor(t2, t1, -C2PI_MID, t2,
                                                       ALU.mult, ALU.add), v)
                        sv.step(v.scalar_tensor_tensor(t2, t1, -C2PI_LO, t2,
                                                       ALU.mult, ALU.add), v)
                        if trig == 1:
                            sv.step(v.tensor_scalar_add(t2, t2, PI_HALF), v)
                        v.tensor_scalar(t2, t2, PI_F, -PI_F,
                                        ALU.min, ALU.max).then_inc(RED.h, 1)
                        RED.v += 1

        @blk.scalar
        def _(s_):
            for h in range(H):
                for trig in range(2):
                    for cc in range(2):
                        gidx = (h * 2 + trig) * 2 + cc
                        s_.wait_ge(RED.h, gidx + 1)
                        base = h * T_OWN + cc * 512
                        dstv = (b.sinT if trig == 0 else b.cosT)[:, base:base + 512]
                        s_.activation(dstv, t2, AF.Sin).then_inc(SIN.h, 1)
                        SIN.v += 1
    b.end_phase()


def emit_l1scan(b):
    nc = b.nc
    qh = b.salloc("sqh", [128, T_OWN], BF16, at=32768)
    kh = b.salloc("skh", [128, T_OWN], BF16, at=34816)
    vh = b.salloc("svh", [128, NCH * 128], BF16, at=36864)
    qdA = b.salloc("sqdA", [128, T_OWN], BF16, at=38912)
    qdB = b.salloc("sqdB", [128, T_OWN], BF16, at=40960)
    kdA = b.salloc("skdA", [128, T_OWN], BF16, at=43008)
    kdB = b.salloc("skdB", [128, T_OWN], BF16, at=45056)
    qwA = b.salloc("sqwA", [128, T_OWN], BF16, at=47104)
    qwB = b.salloc("sqwB", [128, T_OWN], BF16, at=49152)
    dec = b.salloc("sdec", [128, 128], F32, at=51200)
    qdecb = b.salloc("sqdecb", [128, 128], F32, at=51712)
    kdecb = b.salloc("skdecb", [128, 128], F32, at=52224)
    qrow = b.salloc("sqrow", [1, H * 128], F32, at=52736)
    krow = b.salloc("skrow", [1, H * 128], F32, at=60928)
    ones1f = b.salloc("sones1f", [1, 128], F32, at=69120)
    Pb = b.salloc("sPb", [128, 128], BF16, at=69632)
    kwA = b.salloc("skwA", [128, 128], BF16, at=69888)
    kwB = b.salloc("skwB", [128, 128], BF16, at=70144)
    kwAT = b.salloc("skwAT", [128, 128], BF16, at=70400)
    kwBT = b.salloc("skwBT", [128, 128], BF16, at=70656)
    stA = b.salloc("sstA", [128, 128], F32, at=70912)
    stB = b.salloc("sstB", [128, 128], F32, at=71424)
    stAb = b.salloc("sstAb", [128, 128], BF16, at=71936)
    stBb = b.salloc("sstBb", [128, 128], BF16, at=72192)
    yob = b.salloc("syob", [128, 128], BF16, at=72448)
    slopes = np.exp(-_slopes() * CHUNK)

    QHL = b.sem("scQHL", g=True)
    KHL = b.sem("scKHL", g=True)
    VHL = b.sem("scVHL", g=True)
    MS1 = b.sem("scMS1")
    CL = b.sem("scCL")
    DECL = b.sem("scDECL")
    QWS = b.sem("scQWS")
    SCST = b.sem("scSCST")
    YS = b.sem("scYS")
    QD = b.sem("scQD")
    SCP = b.sem("scSCP")
    PBE = b.sem("scPBE")
    KW = b.sem("scKW")
    TPK = b.sem("scTPK")
    KTE = b.sem("scKTE")
    OV = b.sem("scOV")
    YEv = b.sem("scYE")
    UPD = b.sem("scUPD")
    STB = b.sem("scSTB")
    SCS = b.sem("scSCS")
    sv = Ser(b, "scSV")

    with nc.Block() as blk:
        @blk.gpsimd
        def _(g):
            g.memset(ones1f, 1.0).then_inc(MS1.h, 1)
            MS1.v = 1
            for h in range(H):
                if h >= 1:
                    g.wait_ge(QD.h, 5 * h)
                g.dma_start(out=qh, in_=b.qf_d[h]).then_inc(QHL.h, 16)
                g.dma_start(out=kh, in_=b.kf_d[h][:, 0:T_OWN]).then_inc(KHL.h, 16)
                QHL.v += 16
                KHL.v += 16
                if h >= 1:
                    g.wait_ge(SCP.h, 17 * h)
                g.dma_start(out=vh.rearrange("p (t c) -> p t c", t=NCH),
                            in_=b.vt_d[0:NCH, :, h * 128:(h + 1) * 128]
                            .rearrange("t p c -> p t c")).then_inc(VHL.h, 16)
                VHL.v += 16

        @blk.sync
        def _(sync):
            sync.dma_start(out=qrow, in_=b.i["qdecay"]).then_inc(CL.h, 16)
            sync.dma_start(out=krow, in_=b.i["kdecay"]).then_inc(CL.h, 16)
            CL.v = 32
            for h in range(H):
                sync.dma_start(out=dec, in_=b.i["diagdecT"][h]).then_inc(DECL.h, 16)
                DECL.v += 16
                for j in range(NCH):
                    sync.wait_ge(YEv.h, h * NCH + j + 1)
                    sync.dma_start(out=b.o["ylocal"][h * 128:(h + 1) * 128,
                                                     j * 128:(j + 1) * 128],
                                   in_=yob).then_inc(YS.h, 16)
                    YS.v += 16
                sync.wait_ge(QD.h, 5 * (h + 1))
                sync.dma_start(out=b.o["qdw"][h * 256:h * 256 + 128, :],
                               in_=qwA).then_inc(QWS.h, 16)
                sync.dma_start(out=b.o["qdw"][h * 256 + 128:h * 256 + 256, :],
                               in_=qwB).then_inc(QWS.h, 16)
                QWS.v += 32
                sync.wait_ge(SCS.h, h + 1)
                sync.dma_start(out=b.o["S_c"][h * 256:h * 256 + 128, :],
                               in_=stA).then_inc(SCST.h, 16)
                sync.dma_start(out=b.o["S_c"][h * 256 + 128:h * 256 + 256, :],
                               in_=stB).then_inc(SCST.h, 16)
                SCST.v += 32
            sync.wait_ge(YS.h, YS.v)
            sync.wait_ge(QWS.h, QWS.v)
            sync.wait_ge(SCST.h, SCST.v)

        @blk.tensor
        def _(t_):
            pk = b.P[4].bitcast(BF16)
            t_.wait_ge(CL.h, 32)
            t_.wait_ge(MS1.h, 1)
            for h in range(H):
                if h:
                    t_.wait_ge(QD.h, 5 * h)
                t_.matmul(b.P[5][:, 0:128], ones1f, qrow[:, h * 128:(h + 1) * 128],
                          start=True, stop=True)
                t_.matmul(b.P[5][:, 128:256], ones1f, krow[:, h * 128:(h + 1) * 128],
                          start=True, stop=True).then_inc(SCP.h, 1)
                SCP.v += 1
                t_.wait_ge(QD.h, 5 * h + 5)
                t_.wait_ge(VHL.h, 16 * (h + 1))
                for j in range(NCH):
                    jj = h * NCH + j
                    sl = slice(j * 128, (j + 1) * 128)
                    if jj:
                        t_.wait_ge(PBE.h, jj)
                    t_.matmul(b.P[0][:, 0:128], kdA[:, sl], qdA[:, sl],
                              start=True, stop=False)
                    t_.matmul(b.P[0][:, 0:128], kdB[:, sl], qdB[:, sl],
                              start=False, stop=True).then_inc(SCP.h, 1)
                    SCP.v += 1
                    t_.wait_ge(KW.h, 2 * jj + 2)
                    if jj:
                        t_.wait_ge(KTE.h, 2 * jj)
                    t_.transpose(pk[:, 0:128], kwA, b.ident)
                    t_.transpose(pk[:, 128:256], kwB, b.ident).then_inc(TPK.h, 1)
                    TPK.v += 1
                    t_.wait_ge(PBE.h, jj + 1)
                    t_.wait_ge(STB.h, 25 * h + 3 * j + 1)
                    if jj:
                        t_.wait_ge(YEv.h, jj)
                    t_.matmul(b.P[1][:, 0:128], vh[:, sl], Pb, start=True, stop=False)
                    t_.matmul(b.P[1][:, 0:128], stAb, qwA[:, sl], start=False, stop=False)
                    t_.matmul(b.P[1][:, 0:128], stBb, qwB[:, sl],
                              start=False, stop=True).then_inc(OV.h, 1)
                    OV.v += 1
                    t_.wait_ge(KTE.h, 2 * jj + 2)
                    if jj:
                        t_.wait_ge(UPD.h, 2 * jj)
                    t_.matmul(b.P[2][:, 0:128], kwAT, vh[:, sl], start=True, stop=True)
                    t_.matmul(b.P[3][:, 0:128], kwBT, vh[:, sl],
                              start=True, stop=True).then_inc(SCP.h, 1)
                    SCP.v += 1

        @blk.vector
        def _(v):
            pk = b.P[4].bitcast(BF16)
            for h in range(H):
                bd = float(slopes[h])
                cs = b.cosT[:, h * T_OWN:(h + 1) * T_OWN]
                sn = b.sinT[:, h * T_OWN:(h + 1) * T_OWN]
                v.wait_ge(DECL.h, 16 * (h + 1))
                v.wait_ge(SCP.h, 17 * h + 1)
                sv.step(v.tensor_copy(qdecb, b.P[5][:, 0:128]), v)
                sv.step(v.tensor_copy(kdecb, b.P[5][:, 128:256]), v)
                v.wait_ge(QHL.h, 16 * (h + 1))
                v.tensor_tensor(qdA, qh, cs, ALU.mult).then_inc(QD.h, 1)
                v.tensor_tensor(qdB, qh, sn, ALU.mult).then_inc(QD.h, 1)
                v.wait_ge(KHL.h, 16 * (h + 1))
                v.tensor_tensor(kdA, kh, cs, ALU.mult).then_inc(QD.h, 1)
                sv.step(v.tensor_tensor(kdB, kh, sn, ALU.mult), v)
                if h:
                    v.wait_ge(QWS.h, 32 * h)
                for j in range(NCH):
                    sl = slice(j * 128, (j + 1) * 128)
                    v.tensor_tensor(qwA[:, sl], qdA[:, sl], qdecb, ALU.mult)
                    ins = v.tensor_tensor(qwB[:, sl], qdB[:, sl], qdecb, ALU.mult)
                    if j == NCH - 1:
                        sv.step(ins, v)
                v.engine_nop().then_inc(QD.h, 2)
                QD.v += 5
                if h:
                    v.wait_ge(SCST.h, 32 * h)
                v.memset(stA, 0.0)
                v.memset(stB, 0.0)
                v.memset(stAb, 0.0)
                sv.step(v.memset(stBb, 0.0), v)
                v.engine_nop().then_inc(STB.h, 1)
                STB.v += 1
                for j in range(NCH):
                    jj = h * NCH + j
                    sl = slice(j * 128, (j + 1) * 128)
                    v.wait_ge(SCP.h, 17 * h + 2 + 2 * j)
                    if jj:
                        v.wait_ge(OV.h, jj)
                    v.tensor_tensor(Pb, b.P[0][:, 0:128], dec, ALU.mult).then_inc(PBE.h, 1)
                    PBE.v += 1
                    sv.step(v.tensor_tensor(kwA, kdA[:, sl], kdecb, ALU.mult), v)
                    v.tensor_tensor(kwB, kdB[:, sl], kdecb, ALU.mult).then_inc(KW.h, 2)
                    KW.v += 2
                    v.wait_ge(TPK.h, jj + 1)
                    v.tensor_copy(kwAT, pk[:, 0:128])
                    v.tensor_copy(kwBT, pk[:, 128:256]).then_inc(KTE.h, 2)
                    KTE.v += 2
                    v.wait_ge(OV.h, jj + 1)
                    if jj:
                        v.wait_ge(YS.h, 16 * jj)
                    v.tensor_copy(yob, b.P[1][:, 0:128]).then_inc(YEv.h, 1)
                    YEv.v += 1
                    v.wait_ge(SCP.h, 17 * h + 3 + 2 * j)
                    sv.step(v.tensor_scalar_mul(stA, stA, bd), v)
                    sv.step(v.tensor_tensor(stA, b.P[2][:, 0:128], stA, ALU.add), v)
                    sv.step(v.tensor_scalar_mul(stB, stB, bd), v)
                    sv.step(v.tensor_tensor(stB, b.P[3][:, 0:128], stB, ALU.add), v)
                    v.engine_nop().then_inc(UPD.h, 2)
                    UPD.v += 2
                    if j < NCH - 1:
                        sv.step(v.tensor_copy(stAb, stA), v)
                        sv.step(v.tensor_copy(stBb, stB), v)
                        v.engine_nop().then_inc(STB.h, 3)
                        STB.v += 3
                    else:
                        v.engine_nop().then_inc(STB.h, 3)
                        STB.v += 3
                        v.engine_nop().then_inc(SCS.h, 1)
                        SCS.v += 1
    b.end_phase()


def emit_l1quant(b):
    nc = b.nc
    wf = [b.salloc(f"uw{k}", [128, HID_P], F32, at=32768 + k * 22016) for k in range(2)]
    t2a = b.salloc("ut2a", [128, HID], BF16, at=81920)
    t2b = b.salloc("ut2b", [128, HID], BF16, at=92864)
    wqb = [b.salloc(f"uwqb{k}", [128, HID_P], BF16, at=103808 + k * 11008)
           for k in range(2)]
    accs = b.salloc("uaccs", [128, 4], F32, at=76800)
    tmpc = b.salloc("utmpc", [128, 1], F32, at=76960)
    onesc = b.salloc("uonesc", [128, 1], F32, at=77024)
    onesr = b.salloc("uonesr", [1, 128], F32, at=77088)
    tot = b.salloc("utot", [1, 4], F32, at=77600)
    b.sq_sb = b.salloc("usq", [128, 12], F32, at=180224)

    mats = []
    for mi, nm in enumerate(("w1T", "w3T", "w2T")):
        src = b.i[nm]
        blocks = []
        if nm != "w2T":
            for bt in range(NT_D):
                blocks.append((src[bt * 128:(bt + 1) * 128, :], 128, HID))
        else:
            for bt in range(NT_HID):
                r0 = bt * 128
                rows = min(128, HID - r0)
                blocks.append((src[r0:r0 + rows, :], rows, D))
        mats.append((mi, nm, blocks))

    MS = b.sem("uMS")
    WLD = [b.sem("uWLD0"), b.sem("uWLD1")]
    RD = b.sem("uRD")
    TOT = b.sem("uTOT")
    SQ = b.sem("uSQ")
    SG = b.sem("uSG")
    AD = b.sem("uAD")
    ST = [b.sem("uST0"), b.sem("uST1")]
    sv = Ser(b, "uSV")
    ss = Ser(b, "uSS")

    nblk = sum(len(m[2]) for m in mats)

    with nc.Block() as blk:
        @blk.gpsimd
        def _(g):
            g.memset(accs, 0.0)
            g.memset(onesc, 1.0)
            g.memset(tmpc, 0.0)
            g.memset(onesr, 1.0).then_inc(MS.h, 1)
            MS.v = 1

        @blk.sync
        def _(sync):
            bi = 0
            for mi, nm, blocks in mats:
                for (ap, rows, width) in blocks:
                    slot = bi % 2
                    if bi >= 2:
                        sync.wait_ge(RD.h, bi - 1)
                    sync.dma_start(out=wf[slot][0:rows, 0:width],
                                   in_=ap).then_inc(WLD[slot].h, 16)
                    WLD[slot].v += 16
                    bi += 1

        @blk.vector
        def _(v):
            v.wait_ge(MS.h, 1)
            bi = 0
            for mi, nm, blocks in mats:
                for (ap, rows, width) in blocks:
                    slot = bi % 2
                    v.wait_ge(WLD[slot].h, 16 * (bi // 2 + 1))
                    if rows < 128:
                        sv.step(v.memset(tmpc, 0.0), v)
                    sv.step(v.tensor_reduce(tmpc[0:rows, :], wf[slot][0:rows, 0:width],
                                            AX.X, ALU.add,
                                            apply_absolute_value=True), v)
                    acol = accs[:, mi:mi + 1]
                    v.tensor_tensor(acol, tmpc, acol, ALU.add).then_inc(RD.h, 1)
                    RD.v += 1
                    v.wait_ge(RD.h, RD.v)
                    bi += 1
            v.wait_ge(TOT.h, 1)
            sv.step(v.tensor_copy(tot[:, 0:3], b.P[6][0:1, 0:3]), v)
            v.engine_nop().then_inc(RD.h, 1)
            RD.v += 1
            v.wait_ge(TOT.h, 2)
            half = 0.5 / N_WELEM
            for mi in range(3):
                pcol = b.P[7][:, mi:mi + 1]
                v.tensor_scalar_mul(b.sq_sb[:, mi:mi + 1], pcol, half)
                v.tensor_scalar_mul(b.sq_sb[:, 3 + mi:4 + mi], pcol, -half)
                ins = v.tensor_scalar_mul(b.sq_sb[:, 6 + mi:7 + mi], pcol, half)
                sv.step(ins, v)
            v.engine_nop().then_inc(SQ.h, 1)
            SQ.v += 1

        @blk.tensor
        def _(t_):
            t_.wait_ge(RD.h, nblk)
            for mi in range(3):
                mm = t_.matmul(b.P[6][0:1, mi:mi + 1], accs[:, mi:mi + 1], onesc,
                               start=True, stop=True)
            mm.then_inc(TOT.h, 1)
            TOT.v += 1
            t_.wait_ge(RD.h, nblk + 1)
            t_.matmul(b.P[7][:, 0:3], onesr, tot[:, 0:3],
                      start=True, stop=True).then_inc(TOT.h, 1)
            TOT.v += 1
            t_.wait_ge(SQ.h, 1)
    b.end_phase()

    MSB = b.sem("uMSB")
    WLB = [b.sem("uWLB0"), b.sem("uWLB1")]
    SGB = b.sem("uSGB")
    ADB = b.sem("uADB")
    STB2 = [b.sem("uSTB0"), b.sem("uSTB1")]
    ZP = b.sem("uZP")
    svb = Ser(b, "uSVB")
    ssb = Ser(b, "uSSB")
    dsts = {"w1T": b.w1q_d, "w3T": b.w3q_d, "w2T": b.w2q_d}

    with nc.Block() as blk:
        @blk.gpsimd
        def _(g):
            g.memset(wqb[0][:, HID:HID_P], 0.0)
            g.memset(wqb[1][:, HID:HID_P], 0.0).then_inc(MSB.h, 1)
            MSB.v = 1

        @blk.sync
        def _(sync):
            flat = []
            for mi, nm, blocks in mats:
                r0 = 0
                for (ap, rows, width) in blocks:
                    if nm != "w2T":
                        dd = dsts[nm][r0:r0 + rows, :]
                    else:
                        dd = dsts[nm][r0:r0 + rows, 0:width]
                    flat.append((ap, dd, rows, width, nm))
                    r0 += rows

            def store(bi):
                ap, dd, rows, width, nm = flat[bi]
                pslot = bi % 2
                sync.wait_ge(ADB.h, bi + 1)
                srcq = (wqb[pslot][0:rows, :] if nm != "w2T"
                        else wqb[pslot][0:rows, 0:width])
                sync.dma_start(out=dd, in_=srcq).then_inc(STB2[pslot].h, 16)
                STB2[pslot].v += 16

            for bi, (ap, dd, rows, width, nm) in enumerate(flat):
                slot = bi % 2
                if bi >= 2:
                    sync.wait_ge(SGB.h, 2 * (bi - 1))
                sync.dma_start(out=wf[slot][0:rows, 0:width],
                               in_=ap).then_inc(WLB[slot].h, 16)
                WLB[slot].v += 16
                if bi >= 1:
                    store(bi - 1)
            store(len(flat) - 1)
            sync.wait_ge(ZP.h, 1)
            sync.dma_start(out=b.w2q_d[HID:HID_P, :],
                           in_=t2a[0:HID_P - HID, 0:D]).then_inc(STB2[0].h, 16)
            STB2[0].v += 16
            sync.wait_ge(STB2[0].h, STB2[0].v)
            sync.wait_ge(STB2[1].h, STB2[1].v)

        @blk.scalar
        def _(s_):
            s_.wait_ge(MSB.h, 1)
            bi = 0
            for mi, nm, blocks in mats:
                nb = b.sq_sb[:, 3 + mi:4 + mi]
                pb = b.sq_sb[:, 6 + mi:7 + mi]
                for (ap, rows, width) in blocks:
                    slot = bi % 2
                    s_.wait_ge(WLB[slot].h, 16 * (bi // 2 + 1))
                    if bi >= 1:
                        s_.wait_ge(ADB.h, bi)
                    ss_in = wf[slot][0:rows, 0:width]
                    ssb.step(s_.activation(t2a[0:rows, 0:width], ss_in, AF.Sign,
                                           bias=nb[0:rows, :]), s_)
                    s_.activation(t2b[0:rows, 0:width], ss_in, AF.Sign,
                                  bias=pb[0:rows, :]).then_inc(SGB.h, 2)
                    SGB.v += 2
                    bi += 1

        @blk.vector
        def _(v):
            bi = 0
            for mi, nm, blocks in mats:
                for (ap, rows, width) in blocks:
                    pslot = bi % 2
                    v.wait_ge(SGB.h, 2 * (bi + 1))
                    if bi >= 2:
                        v.wait_ge(STB2[pslot].h, 16 * (bi // 2))
                    v.tensor_tensor(wqb[pslot][0:rows, 0:width], t2a[0:rows, 0:width],
                                    t2b[0:rows, 0:width], ALU.add).then_inc(ADB.h, 1)
                    ADB.v += 1
                    bi += 1
            v.wait_ge(ADB.h, nblk)
            svb.step(v.memset(t2a[0:HID_P - HID, 0:D], 0.0), v)
            v.engine_nop().then_inc(ZP.h, 1)
            ZP.v += 1
    b.end_phase()


def emit_l1ffn(b):
    nc = b.nc
    wc1 = [b.salloc(f"fw1{k}", [128, D], BF16, at=32768 + k * 4096) for k in range(2)]
    wc3 = [b.salloc(f"fw3{k}", [128, D], BF16, at=40960 + k * 4096) for k in range(2)]
    gt = b.salloc("fgt", [128, NT_HID * T_OWN], BF16, at=81920)
    tg1 = b.salloc("ftg1", [128, 512], F32, at=49152)
    tg1s = b.salloc("ftg1s", [128, 512], F32, at=51200)
    tg3 = b.salloc("ftg3", [128, 512], F32, at=53248)
    w2s = b.salloc("fw2s", [128, NT_HID * 512], BF16, at=32768)
    tmpf = b.salloc("ftmpf", [128, 512], F32, at=76800)
    g2b = [b.salloc(f"fg2b{k}", [128, 512], BF16, at=78848 + k * 1024) for k in range(2)]
    gt3 = gt.rearrange("p (t c) -> p t c", t=NT_HID)

    WL = [b.sem("fWL0", g=True), b.sem("fWL1", g=True)]
    MM = b.sem("fMM")
    GE = b.sem("fGE")
    T1 = b.sem("fT1")
    T3 = b.sem("fT3")
    SIL = b.sem("fSIL")

    s1c = b.sq_sb[:, 0:1]
    s3c = b.sq_sb[:, 1:2]
    s2c = b.sq_sb[:, 2:3]

    with nc.Block() as blk:
        @blk.gpsimd
        def _(g):
            for ht in range(NT_HID):
                slot = ht % 2
                if ht >= 2:
                    g.wait_ge(MM.h, 4 * (ht - 1))
                g.dma_start(out=wc1[slot].rearrange("p (t c) -> p t c", t=NT_D),
                            in_=b.w1q_d[:, ht * 128:(ht + 1) * 128]
                            .rearrange("(t p) c -> p t c", p=128)).then_inc(WL[slot].h, 16)
                g.dma_start(out=wc3[slot].rearrange("p (t c) -> p t c", t=NT_D),
                            in_=b.w3q_d[:, ht * 128:(ht + 1) * 128]
                            .rearrange("(t p) c -> p t c", p=128)).then_inc(WL[slot].h, 16)
                WL[slot].v += 32

        @blk.tensor
        def _(t_):
            for ht in range(NT_HID):
                slot = ht % 2
                t_.wait_ge(WL[slot].h, 32 * (ht // 2 + 1))
                if ht >= 1:
                    t_.wait_ge(T1.h, 2 * ht)
                w13 = wc1[slot].rearrange("p (t c) -> p t c", t=NT_D)
                w33 = wc3[slot].rearrange("p (t c) -> p t c", t=NT_D)
                for half in range(2):
                    hs = slice(half * 512, (half + 1) * 512)
                    for dt in range(NT_D):
                        t_.matmul(b.P[half], w13[:, dt], b.xn1[dt][:, hs],
                                  start=(dt == 0), stop=(dt == NT_D - 1))
                if ht >= 1:
                    t_.wait_ge(T3.h, 2 * ht)
                for half in range(2):
                    hs = slice(half * 512, (half + 1) * 512)
                    for dt in range(NT_D):
                        mm = t_.matmul(b.P[2 + half], w33[:, dt], b.xn1[dt][:, hs],
                                       start=(dt == 0), stop=(dt == NT_D - 1))
                mm.then_inc(MM.h, 4)
                MM.v += 4

        @blk.vector
        def _(v):
            for ht in range(NT_HID):
                v.wait_ge(MM.h, 4 * (ht + 1))
                for half in range(2):
                    idx = 2 * ht + half
                    v.tensor_scalar_mul(tg1, b.P[half], s1c).then_inc(T1.h, 1)
                    T1.v += 1
                    v.tensor_scalar_mul(tg3, b.P[2 + half], s3c).then_inc(T3.h, 1)
                    T3.v += 1
                    v.wait_ge(T3.h, T3.v)
                    v.wait_ge(SIL.h, idx + 1)
                    v.tensor_tensor(gt3[:, ht, half * 512:(half + 1) * 512],
                                    tg1s, tg3, ALU.mult).then_inc(GE.h, 1)
                    GE.v += 1
                    v.wait_ge(GE.h, GE.v)

        @blk.scalar
        def _(s_):
            for ht in range(NT_HID):
                for half in range(2):
                    idx = 2 * ht + half
                    s_.wait_ge(T1.h, idx + 1)
                    s_.activation(tg1s, tg1, AF.Silu).then_inc(SIL.h, 1)
                    SIL.v += 1
    b.end_phase()

    WL2 = b.sem("f2WL", g=True)
    MM2 = b.sem("f2MM")
    EV2 = b.sem("f2EV")
    SL2 = b.sem("f2SL")
    SO2 = [b.sem("f2SO0"), b.sem("f2SO1")]
    sv2 = Ser(b, "f2SV")
    w2s3 = w2s.rearrange("p (t c) -> p t c", t=NT_HID)

    with nc.Block() as blk:
        @blk.gpsimd
        def _(g):
            for og in range(4):
                if og >= 1:
                    g.wait_ge(MM2.h, NT_OWN * og)
                g.dma_start(out=w2s3,
                            in_=b.w2q_d[:, og * 512:(og + 1) * 512]
                            .rearrange("(t p) c -> p t c", p=128)).then_inc(WL2.h, 16)
                WL2.v += 16

        @blk.tensor
        def _(t_):
            for og in range(4):
                t_.wait_ge(WL2.h, 16 * (og + 1))
                for tk in range(NT_OWN):
                    gi = og * NT_OWN + tk
                    if gi >= 2:
                        t_.wait_ge(EV2.h, gi - 1)
                    pp = b.P[gi % 2]
                    for ht in range(NT_HID):
                        mm = t_.matmul(pp, gt3[:, ht, tk * 128:(tk + 1) * 128],
                                       w2s3[:, ht], start=(ht == 0),
                                       stop=(ht == NT_HID - 1))
                    mm.then_inc(MM2.h, 1)
                    MM2.v += 1

        @blk.vector
        def _(v):
            for gi in range(4 * NT_OWN):
                v.wait_ge(MM2.h, gi + 1)
                if gi >= 1:
                    v.wait_ge(SL2.h, gi)
                v.tensor_scalar_mul(tmpf, b.P[gi % 2], s2c).then_inc(EV2.h, 1)
                EV2.v += 1

        @blk.scalar
        def _(s_):
            for gi in range(4 * NT_OWN):
                slot = gi % 2
                s_.wait_ge(EV2.h, gi + 1)
                if gi >= 2:
                    s_.wait_ge(SO2[slot].h, 16 * (gi // 2))
                s_.activation(g2b[slot], tmpf, AF.Silu).then_inc(SL2.h, 1)
                SL2.v += 1

        @blk.sync
        def _(sync):
            for gi in range(4 * NT_OWN):
                og, tk = gi // NT_OWN, gi % NT_OWN
                slot = gi % 2
                sync.wait_ge(SL2.h, gi + 1)
                sync.dma_start(out=b.o["g2"][tk * 128:(tk + 1) * 128,
                                             og * 512:(og + 1) * 512],
                               in_=g2b[slot]).then_inc(SO2[slot].h, 16)
                SO2[slot].v += 16
            sync.wait_ge(SO2[0].h, SO2[0].v)
            sync.wait_ge(SO2[1].h, SO2[1].v)
    b.end_phase()


def _dump_tables(b):
    nc = b.nc
    dc = b.dbg("cosT", (128, H * T_OWN), BF16)
    ds = b.dbg("sinT", (128, H * T_OWN), BF16)
    ST = b.sem("dtST")
    with nc.Block() as blk:
        @blk.sync
        def _(sync):
            sync.dma_start(out=dc, in_=b.cosT).then_inc(ST.h, 16)
            sync.dma_start(out=ds, in_=b.sinT).then_inc(ST.h, 16)
            sync.wait_ge(ST.h, 32)
    b.end_phase()


def _dump_quant(b):
    nc = b.nc
    d1 = b.dbg("w1q", (D, HID_P), BF16)
    d2 = b.dbg("w2q", (HID_P, D), BF16)
    ST = b.sem("dqqST")
    with nc.Block() as blk:
        @blk.sync
        def _(sync):
            sync.dma_start(out=d1, in_=b.w1q_d).then_inc(ST.h, 16)
            sync.dma_start(out=d2, in_=b.w2q_d).then_inc(ST.h, 16)
            sync.wait_ge(ST.h, 32)
    b.end_phase()



class BuilderB:
    SB_BASE = Builder.SB_BASE
    SB_TOP = Builder.SB_TOP
    salloc = Builder.salloc
    sem = Builder.sem
    end_phase = Builder.end_phase

    def __init__(self):
        nc = bass.Bass("TRN2", target_bir_lowering=False, debug=False)
        self.nc = nc
        self.sboff = 0
        self._pool = []
        self._pool_idx = 0
        self._phase_sems = []

        def inp(name, shape, dtype=F32):
            return nc.dram_tensor(name, list(shape), dtype, kind="ExternalInput").ap()

        i = {}
        i["S_all"] = inp("S_all", (8, H * 256, DH))
        i["Wc"] = inp("Wc", (1, 8 * H))
        i["x1d"] = inp("x1d", (T_OWN, D))
        i["g2d"] = inp("g2d", (T_OWN, D), BF16)
        i["yld"] = inp("yld", (D, T_OWN), BF16)
        i["qdwd"] = inp("qdwd", (H * 256, T_OWN), BF16)
        i["wprojT"] = inp("wprojT", (D, D))
        self.i = i
        self.o = {"out": nc.dram_tensor("out", [T_OWN, D], F32,
                                        kind="ExternalOutput").ap()}
        self.P = [nc.alloc_psum_tensor(f"P{k}", [128, 512], F32).ap() for k in range(8)]
        self.yfin = [self.salloc(f"byf{h}", [128, T_OWN], BF16, at=h * 2048)
                     for h in range(H)]


def emit_B1(b):
    nc = b.nc
    sS = b.salloc("bsS", [128, 2048], F32, at=32768)
    stiA = b.salloc("bstiA", [128, 128], F32, at=40960)
    stiB = b.salloc("bstiB", [128, 128], F32, at=41472)
    tmpt = b.salloc("btmpt", [128, 128], F32, at=41984)
    stiAb = b.salloc("bstiAb", [128, 128], BF16, at=42496)
    stiBb = b.salloc("bstiBb", [128, 128], BF16, at=42752)
    qdwA = b.salloc("bqdwA", [128, T_OWN], BF16, at=43008)
    qdwB = b.salloc("bqdwB", [128, T_OWN], BF16, at=45056)
    ylh = b.salloc("bylh", [128, T_OWN], BF16, at=47104)
    wcr = b.salloc("bwcr", [1, 8 * H], F32, at=49152)
    onesr = b.salloc("bonesr", [1, 128], F32, at=49664)
    slopes = np.exp(-_slopes() * CHUNK)

    MS = b.sem("bMS")
    WCL = b.sem("bWCL")
    BCT = b.sem("bBCT")
    SLD = b.sem("bSLD")
    QL = b.sem("bQL")
    YLL = b.sem("bYLL")
    STI = b.sem("bSTI")
    O2 = b.sem("bO2")
    YF = b.sem("bYF")
    sv = Ser(b, "bSV")

    with nc.Block() as blk:
        @blk.gpsimd
        def _(g):
            g.memset(onesr, 1.0).then_inc(MS.h, 1)
            MS.v = 1

        @blk.sync
        def _(sync):
            sync.dma_start(out=wcr, in_=b.i["Wc"]).then_inc(WCL.h, 16)
            WCL.v = 16
            for h in range(H):
                if h:
                    sync.wait_ge(STI.h, h)
                sync.dma_start(out=sS[:, 0:1024].rearrange("d (g v) -> d g v", g=8),
                               in_=b.i["S_all"][:, h * 256:h * 256 + 128, :]
                               .rearrange("b d v -> d b v")).then_inc(SLD.h, 16)
                sync.dma_start(out=sS[:, 1024:2048].rearrange("d (g v) -> d g v", g=8),
                               in_=b.i["S_all"][:, h * 256 + 128:h * 256 + 256, :]
                               .rearrange("b d v -> d b v")).then_inc(SLD.h, 16)
                SLD.v += 32
                if h:
                    sync.wait_ge(O2.h, 2 * h)
                sync.dma_start(out=qdwA,
                               in_=b.i["qdwd"][h * 256:h * 256 + 128, :]).then_inc(QL.h, 16)
                sync.dma_start(out=qdwB,
                               in_=b.i["qdwd"][h * 256 + 128:h * 256 + 256, :]).then_inc(QL.h, 16)
                QL.v += 32
                if h:
                    sync.wait_ge(YF.h, 8 * h)
                sync.dma_start(out=ylh,
                               in_=b.i["yld"][h * 128:(h + 1) * 128, :]).then_inc(YLL.h, 16)
                YLL.v += 16

        @blk.tensor
        def _(t_):
            t_.wait_ge(MS.h, 1)
            t_.wait_ge(WCL.h, 16)
            t_.matmul(b.P[6][:, 0:128], onesr, wcr,
                      start=True, stop=True).then_inc(BCT.h, 1)
            BCT.v += 1
            for h in range(H):
                t_.wait_ge(STI.h, h + 1)
                t_.wait_ge(QL.h, 32 * (h + 1))
                for nh in range(2):
                    if h + nh:
                        t_.wait_ge(YF.h, 8 * h + 4 * nh)
                    ts = slice(nh * 512, (nh + 1) * 512)
                    t_.matmul(b.P[nh], stiAb, qdwA[:, ts], start=True, stop=False)
                    t_.matmul(b.P[nh], stiBb, qdwB[:, ts],
                              start=False, stop=True).then_inc(O2.h, 1)
                    O2.v += 1

        @blk.vector
        def _(v):
            v.wait_ge(BCT.h, 1)
            for h in range(H):
                v.wait_ge(SLD.h, 32 * (h + 1))
                if h:
                    v.wait_ge(O2.h, 2 * h)
                v.memset(stiA, 0.0)
                sv.step(v.memset(stiB, 0.0), v)
                for b_ in range(8):
                    col = b.P[6][:, b_ * H + h:b_ * H + h + 1]
                    sv.step(v.tensor_scalar_mul(tmpt, sS[:, b_ * 128:(b_ + 1) * 128],
                                                col), v)
                    sv.step(v.tensor_tensor(stiA, tmpt, stiA, ALU.add), v)
                    sv.step(v.tensor_scalar_mul(
                        tmpt, sS[:, 1024 + b_ * 128:1024 + (b_ + 1) * 128], col), v)
                    sv.step(v.tensor_tensor(stiB, tmpt, stiB, ALU.add), v)
                v.tensor_copy(stiAb, stiA)
                sv.step(v.tensor_copy(stiBb, stiB), v)
                v.engine_nop().then_inc(STI.h, 1)
                STI.v += 1
                v.wait_ge(YLL.h, 16 * (h + 1))
                bd = float(slopes[h])
                for nh in range(2):
                    v.wait_ge(O2.h, 2 * h + nh + 1)
                    for jq in range(4):
                        j = nh * 4 + jq
                        v.scalar_tensor_tensor(
                            b.yfin[h][:, j * 128:(j + 1) * 128],
                            b.P[nh][:, jq * 128:(jq + 1) * 128], float(bd ** j),
                            ylh[:, j * 128:(j + 1) * 128],
                            ALU.mult, ALU.add).then_inc(YF.h, 1)
                        YF.v += 1
    b.end_phase()


def emit_B2(b):
    nc = b.nc
    wp = [b.salloc(f"bwp{k}", [128, H * 512], BF16, at=53248 + k * 16384)
          for k in range(2)]
    x1s = [b.salloc(f"bx1s{k}", [128, 512], F32, at=86016 + k * 2048) for k in range(2)]
    g2s = [b.salloc(f"bg2s{k}", [128, 512], BF16, at=90112 + k * 1024) for k in range(2)]
    outs = [b.salloc(f"bouts{k}", [128, 512], F32, at=92160 + k * 2048) for k in range(2)]

    WPL = [b.sem("b2WPL0", g=True), b.sem("b2WPL1", g=True)]
    XL = [b.sem("b2XL0"), b.sem("b2XL1")]
    GL = [b.sem("b2GL0"), b.sem("b2GL1")]
    MMp = b.sem("b2MM")
    EVp = b.sem("b2EV")
    OS = [b.sem("b2OS0"), b.sem("b2OS1")]
    sv = Ser(b, "b2SV")
    NGI = 4 * NT_OWN

    with nc.Block() as blk:
        @blk.gpsimd
        def _(g):
            for og in range(4):
                slot = og % 2
                if og >= 2:
                    g.wait_ge(MMp.h, NT_OWN * (og - 1))
                g.dma_start(out=wp[slot].rearrange("p (t c) -> p t c", t=NT_D),
                            in_=b.i["wprojT"][:, og * 512:(og + 1) * 512]
                            .rearrange("(t p) c -> p t c", p=128)).then_inc(WPL[slot].h, 16)
                WPL[slot].v += 16

        @blk.tensor
        def _(t_):
            for gi in range(NGI):
                og, tk = gi // NT_OWN, gi % NT_OWN
                wslot = og % 2
                if gi % NT_OWN == 0:
                    t_.wait_ge(WPL[wslot].h, 16 * (og // 2 + 1))
                if gi >= 2:
                    t_.wait_ge(EVp.h, gi - 1)
                pp = b.P[gi % 2]
                wp3 = wp[wslot].rearrange("p (t c) -> p t c", t=H)
                for h in range(H):
                    mm = t_.matmul(pp, b.yfin[h][:, tk * 128:(tk + 1) * 128],
                                   wp3[:, h], start=(h == 0), stop=(h == H - 1))
                mm.then_inc(MMp.h, 1)
                MMp.v += 1

        @blk.sync
        def _(sync):
            for gi in range(NGI):
                og, tk = gi // NT_OWN, gi % NT_OWN
                slot = gi % 2
                if gi >= 2:
                    sync.wait_ge(EVp.h, gi - 1)
                sync.dma_start(out=x1s[slot],
                               in_=b.i["x1d"][tk * 128:(tk + 1) * 128,
                                              og * 512:(og + 1) * 512]).then_inc(XL[slot].h, 16)
                XL[slot].v += 16
                sync.dma_start(out=g2s[slot],
                               in_=b.i["g2d"][tk * 128:(tk + 1) * 128,
                                              og * 512:(og + 1) * 512]).then_inc(GL[slot].h, 16)
                GL[slot].v += 16
                if gi >= 1:
                    pg = gi - 1
                    pog, ptk = pg // NT_OWN, pg % NT_OWN
                    sync.wait_ge(EVp.h, pg + 1)
                    sync.dma_start(out=b.o["out"][ptk * 128:(ptk + 1) * 128,
                                                  pog * 512:(pog + 1) * 512],
                                   in_=outs[pg % 2]).then_inc(OS[pg % 2].h, 16)
                    OS[pg % 2].v += 16
            pg = NGI - 1
            pog, ptk = pg // NT_OWN, pg % NT_OWN
            sync.wait_ge(EVp.h, pg + 1)
            sync.dma_start(out=b.o["out"][ptk * 128:(ptk + 1) * 128,
                                          pog * 512:(pog + 1) * 512],
                           in_=outs[pg % 2]).then_inc(OS[pg % 2].h, 16)
            OS[pg % 2].v += 16
            sync.wait_ge(OS[0].h, OS[0].v)
            sync.wait_ge(OS[1].h, OS[1].v)

        @blk.vector
        def _(v):
            for gi in range(NGI):
                slot = gi % 2
                v.wait_ge(MMp.h, gi + 1)
                v.wait_ge(XL[slot].h, 16 * (gi // 2 + 1))
                v.wait_ge(GL[slot].h, 16 * (gi // 2 + 1))
                if gi >= 2:
                    v.wait_ge(OS[slot].h, 16 * (gi // 2))
                sv.step(v.tensor_tensor(outs[slot], b.P[gi % 2], g2s[slot],
                                        ALU.mult), v)
                v.tensor_tensor(outs[slot], outs[slot], x1s[slot],
                                ALU.add).then_inc(EVp.h, 1)
                EVp.v += 1
    b.end_phase()


def build_devB():
    b = BuilderB()
    with b.nc.named_scope("B1"):
        emit_B1(b)
    with b.nc.named_scope("B2"):
        emit_B2(b)
    return b


def build_devA(upto="ln0", want_dbg=()):
    b = Builder(want_dbg=want_dbg)
    nc = b.nc
    with nc.named_scope("consts"):
        emit_consts(b)
    with nc.named_scope("ln0"):
        emit_ln0(b)
    if upto == "ln0":
        return b
    with nc.named_scope("projq"):
        emit_proj_qk(b, "q")
    with nc.named_scope("projk"):
        emit_proj_qk(b, "k")
    with nc.named_scope("projv"):
        emit_proj_v(b)
    if "qkv" in b.want_dbg:
        _dump_qkv(b)
    if upto == "qkv":
        return b
    with nc.named_scope("attn"):
        emit_attn(b)
    if "yat" in b.want_dbg:
        _dump_yattn(b)
    if upto == "attn":
        return b
    with nc.named_scope("oproj"):
        emit_oproj(b)
    if "s0" in b.want_dbg:
        _dump_s(b)
    if upto == "oproj":
        return b
    with nc.named_scope("router"):
        emit_router(b)
    with nc.named_scope("moe"):
        emit_moe(b)
    if "acc" in b.want_dbg:
        _dump_acc(b)
    if upto == "moe":
        return b
    with nc.named_scope("combine"):
        emit_combine_ln1(b)
    if "xn1" in b.want_dbg:
        _dump_xn1(b)
    if upto == "ln1":
        return b
    with nc.named_scope("l1qkv"):
        emit_l1qkv(b)
    if "jqkv" in b.want_dbg:
        _dump_qkv(b)
    if upto == "l1qkv":
        return b
    with nc.named_scope("l1tables"):
        emit_l1tables(b)
    if "tab" in b.want_dbg:
        _dump_tables(b)
    if upto == "tables":
        return b
    with nc.named_scope("l1scan"):
        emit_l1scan(b)
    if upto == "scan":
        return b
    with nc.named_scope("l1quant"):
        emit_l1quant(b)
    if "wq" in b.want_dbg:
        _dump_quant(b)
    if upto == "quant":
        return b
    with nc.named_scope("l1ffn"):
        emit_l1ffn(b)
    if upto == "ffn":
        return b
    raise ValueError(upto)


def _dump_qkv(b, parts=("q", "k", "v")):
    nc = b.nc
    srcs = {"q": (b.qf_d, (H, 128, T_OWN)), "k": (b.kf_d, (H, 128, TH)),
            "v": (b.vt_d, (NT_TH, 128, D))}
    ST = b.sem("dqST")
    with nc.Block() as blk:
        @blk.sync
        def _(sync):
            for p in parts:
                src, shp = srcs[p]
                d = b.dbg(p + "f", shp, BF16)
                sync.dma_start(out=d, in_=src).then_inc(ST.h, 16)
                ST.v += 16
            sync.wait_ge(ST.h, ST.v)
    b.end_phase()

def kernel(**inputs):
    maps = shard_inputs(inputs)
    bA = build_devA(upto="ffn")
    resA = run_bass_kernel_spmd(bA.nc, maps, core_ids=list(range(NCORES)))
    S_all = np.ascontiguousarray(
        np.stack([np.asarray(resA.results[c]["S_c"], np.float32)
                  for c in range(NCORES)]))
    wp = maps[0]["wprojT"]
    mapsB = []
    for c in range(NCORES):
        r = resA.results[c]
        mapsB.append({"S_all": S_all, "Wc": host_consts(c)["Wc"],
                      "x1d": np.asarray(r["x1"]), "g2d": np.asarray(r["g2"]),
                      "yld": np.asarray(r["ylocal"]),
                      "qdwd": np.asarray(r["qdw"]), "wprojT": wp})
    bB = build_devB()
    resB = run_bass_kernel_spmd(bB.nc, mapsB, core_ids=list(range(NCORES)))
    out = np.concatenate([np.asarray(resB.results[c]["out"], np.float32)
                          for c in range(NCORES)], axis=0)
    return out.reshape(1, NCORES * T_OWN, D)


# version 04
# speedup vs baseline: 1.1278x; 1.1278x over previous
import numpy as np
import ml_dtypes
import concourse.bass as bass
import concourse.mybir as mybir
from concourse.bass_utils import run_bass_kernel_spmd

F32 = mybir.dt.float32
BF16 = mybir.dt.bfloat16
AF = mybir.ActivationFunctionType
ALU = mybir.AluOpType
AX = mybir.AxisListType

D = 2048
NT_D = 16
T_OWN = 1024
NT_OWN = 8
HALO = 512
TH = T_OWN + HALO
NT_TH = 12
H = 16
DH = 128
R = 16
F_MOE = 1024
NT_F = 8
NE, NSH = 16, 2
HID = 5461
HID_P = 5504
NT_HID = 43
CHUNK = 128
NCH = 8
NCORES = 8
INV_SQRT_DH = float(1.0 / np.sqrt(128.0))
LORA_SCALE = 1.0 / 16.0

BF = ml_dtypes.bfloat16
NO_LORA = False


def _slopes():
    start = 2.0 ** (-2.0 ** (-(np.log2(H) - 3)))
    return np.array([start ** (i + 1) for i in range(H)], dtype=np.float64)



def host_consts(core):
    c = {}
    rbase = core * T_OWN
    half = DH // 2
    th = 1.0 / 10000.0 ** (np.arange(half, dtype=np.float64) / half)
    pos = np.arange(rbase - HALO, rbase + T_OWN, dtype=np.float64)
    pos = np.clip(pos, 0, None)
    fr = pos[:, None] * th[None, :]
    emb = np.concatenate([fr, fr], -1)
    c["ropecos"] = np.ascontiguousarray(np.cos(emb).T.astype(BF))
    c["ropesin"] = np.ascontiguousarray(np.sin(emb).T.astype(BF))
    masks = np.zeros((5, 128, 640), np.float32)
    for slot in range(5):
        qt = slot if (core == 0 and slot < 4) else None
        bcol = np.arange(640)[None, :]
        arow = np.arange(128)[:, None]
        ok = (bcol - arow >= 1) & (bcol - arow <= 512)
        if qt is not None:
            gq = qt * 128 + arow
            gk = qt * 128 + bcol - 512
            ok = ok & (gk >= 0) & (gk <= gq)
        masks[slot] = np.where(ok, 0.0, -1e30)
    c["mask5"] = masks
    sl = _slopes()
    idc = np.arange(CHUNK, dtype=np.float64)
    diff = idc[:, None] - idc[None, :]
    dd = np.where(diff >= 0, np.exp(-sl[:, None, None] * diff), 0.0)
    c["diagdecT"] = np.ascontiguousarray(dd.transpose(0, 2, 1).astype(np.float32))
    c["qdecay"] = np.exp(-sl[:, None] * (idc + 1.0)[None, :]).astype(np.float32).reshape(1, H * CHUNK)
    c["kdecay"] = np.exp(-sl[:, None] * (CHUNK - 1.0 - idc)[None, :]).astype(np.float32).reshape(1, H * CHUNK)
    bd = np.exp(-sl * CHUNK)
    Wc = np.zeros((8, H), np.float32)
    for b_ in range(core):
        Wc[b_] = (bd ** (8 * (core - 1 - b_))).astype(np.float32)
    c["Wc"] = np.ascontiguousarray(Wc.reshape(1, 8 * H))
    c["idxrow"] = np.arange(rbase, rbase + T_OWN, dtype=np.float32).reshape(1, T_OWN)
    return c


def shard_inputs(inputs):
    x = np.ascontiguousarray(np.asarray(inputs["L0_attn_x"])[0])
    TT = lambda a: np.ascontiguousarray(np.asarray(a).T)
    base = {}
    base["wqT"] = TT(inputs["L0_attn_wq"])
    base["wkT"] = TT(inputs["L0_attn_wk"])
    base["wvT"] = TT(inputs["L0_attn_wv"])
    base["woT"] = TT(inputs["L0_attn_wo"])
    base["nwfm"] = np.ascontiguousarray(np.asarray(inputs["L0_attn_norm_w"]).reshape(NT_D, 128).T)
    for nm in ("qA", "kA", "vA", "oA"):
        base[nm + "T"] = TT(inputs["L0_attn_" + nm])
    for nm in ("qB", "kB", "vB", "oB"):
        base[nm + "T"] = TT(inputs["L0_attn_" + nm])
    base["wa"] = np.asarray(inputs["L0_ffn_wa"])
    for nm in ("rg", "rw1", "rw2"):
        base[nm] = np.asarray(inputs["L0_ffn_" + nm])
    for nm in ("sg", "sw1", "sw2"):
        base[nm] = np.asarray(inputs["L0_ffn_" + nm])
    base["rgbT"] = TT(inputs["L0_ffn_rgb"])
    base["rb1T"] = TT(inputs["L0_ffn_rb1"])
    base["rb2"] = np.asarray(inputs["L0_ffn_rb2"])
    base["sgbT"] = TT(inputs["L0_ffn_sgb"])
    base["sb1T"] = TT(inputs["L0_ffn_sb1"])
    base["sb2"] = np.asarray(inputs["L0_ffn_sb2"])
    base["wqkvT"] = TT(inputs["L1_attn_w_qkv"])
    base["wprojT"] = TT(inputs["L1_attn_w_proj"])
    base["thetarow"] = np.ascontiguousarray(np.asarray(inputs["L1_attn_theta"]).reshape(1, H * 128))
    base["w1T"] = TT(inputs["L1_ffn_w1"])
    base["w3T"] = TT(inputs["L1_ffn_w3"])
    base["w2T"] = TT(inputs["L1_ffn_w2"])
    maps = []
    for core in range(NCORES):
        m = dict(base)
        rb = core * T_OWN
        xh = np.zeros((TH, D), np.float32)
        lo = max(0, rb - HALO)
        xh[HALO - (rb - lo):] = x[lo:rb + T_OWN]
        m["xh"] = xh
        m.update(host_consts(core))
        maps.append(m)
    return maps



class Sem:
    def __init__(self, h):
        self.h = h
        self.v = 0


class Ser:

    def __init__(self, b, name):
        self.s = b.sem(name)

    def step(self, inst, eng):
        inst.then_inc(self.s.h, 1)
        self.s.v += 1
        eng.wait_ge(self.s.h, self.s.v)


class Builder:
    def __init__(self, want_dbg=()):
        nc = bass.Bass("TRN2", target_bir_lowering=False, debug=False)
        self.nc = nc
        self.sboff = 0
        self.want_dbg = set(want_dbg)
        self.dbg_outs = {}
        self._pool = []
        self._pool_idx = 0
        self._phase_sems = []

        def inp(name, shape, dtype=F32):
            return nc.dram_tensor(name, list(shape), dtype, kind="ExternalInput").ap()

        i = {}
        i["xh"] = inp("xh", (TH, D))
        for nm in ("wqT", "wkT", "wvT", "woT"):
            i[nm] = inp(nm, (D, D))
        i["nwfm"] = inp("nwfm", (128, NT_D))
        for nm in ("qAT", "kAT", "vAT", "oAT"):
            i[nm] = inp(nm, (D, R))
        for nm in ("qBT", "kBT", "vBT", "oBT"):
            i[nm] = inp(nm, (R, D))
        i["wa"] = inp("wa", (D, NE))
        for nm in ("rg", "rw1"):
            i[nm] = inp(nm, (NE, D, F_MOE))
        i["rw2"] = inp("rw2", (NE, F_MOE, D))
        for nm in ("sg", "sw1"):
            i[nm] = inp(nm, (NSH, D, F_MOE))
        i["sw2"] = inp("sw2", (NSH, F_MOE, D))
        i["rgbT"] = inp("rgbT", (F_MOE, NE))
        i["rb1T"] = inp("rb1T", (F_MOE, NE))
        i["rb2"] = inp("rb2", (NE, D))
        i["sgbT"] = inp("sgbT", (F_MOE, NSH))
        i["sb1T"] = inp("sb1T", (F_MOE, NSH))
        i["sb2"] = inp("sb2", (NSH, D))
        i["ropecos"] = inp("ropecos", (128, TH), BF16)
        i["ropesin"] = inp("ropesin", (128, TH), BF16)
        i["mask5"] = inp("mask5", (5, 128, 640))
        i["wqkvT"] = inp("wqkvT", (D, 3 * D))
        i["wprojT"] = inp("wprojT", (D, D))
        i["thetarow"] = inp("thetarow", (1, H * 128))
        i["diagdecT"] = inp("diagdecT", (H, CHUNK, CHUNK))
        i["qdecay"] = inp("qdecay", (1, H * CHUNK))
        i["kdecay"] = inp("kdecay", (1, H * CHUNK))
        i["idxrow"] = inp("idxrow", (1, T_OWN))
        i["w1T"] = inp("w1T", (D, HID))
        i["w3T"] = inp("w3T", (D, HID))
        i["w2T"] = inp("w2T", (HID, D))
        self.i = i

        o = {}
        o["x1"] = nc.dram_tensor("x1", [T_OWN, D], F32, kind="ExternalOutput").ap()
        o["g2"] = nc.dram_tensor("g2", [T_OWN, D], BF16, kind="ExternalOutput").ap()
        o["ylocal"] = nc.dram_tensor("ylocal", [D, T_OWN], BF16, kind="ExternalOutput").ap()
        o["qdw"] = nc.dram_tensor("qdw", [H * 256, T_OWN], BF16, kind="ExternalOutput").ap()
        o["S_c"] = nc.dram_tensor("S_c", [H * 256, DH], F32, kind="ExternalOutput").ap()
        self.o = o

        self.yattn_d = nc.dram_tensor("yattn_d", [NT_D, 128, T_OWN], BF16, kind="Internal").ap()
        self.qf_d = nc.dram_tensor("qf_d", [H, 128, T_OWN], BF16, kind="Internal").ap()
        self.kf_d = nc.dram_tensor("kf_d", [H, 128, TH], BF16, kind="Internal").ap()
        self.vt_d = nc.dram_tensor("vt_d", [NT_TH, 128, D], BF16, kind="Internal").ap()
        self.w1q_d = nc.dram_tensor("w1q_d", [D, HID_P], BF16, kind="Internal").ap()
        self.w3q_d = nc.dram_tensor("w3q_d", [D, HID_P], BF16, kind="Internal").ap()
        self.w2q_d = nc.dram_tensor("w2q_d", [HID_P, D], BF16, kind="Internal").ap()

        self.ident_d = nc.inline_tensor(np.asarray(np.eye(128, dtype=np.float32), dtype=BF), "identbf").ap()
        self.identf_d = nc.inline_tensor(np.eye(128, dtype=np.float32), "identf32").ap()
        self.ones1_d = nc.inline_tensor(np.ones((1, 128), dtype=BF), "ones1bf").ap()
        o2 = np.zeros((128, 512), np.float32)
        o2[0:2, :] = 1.0
        self.ones2_d = nc.inline_tensor(np.asarray(o2, dtype=BF), "ones2bf").ap()
        Rm = np.zeros((128, 128), np.float32)
        for d_ in range(64):
            Rm[d_, d_ + 64] = -1.0
            Rm[d_ + 64, d_] = 1.0
        self.RT_d = nc.inline_tensor(np.asarray(Rm.T, dtype=BF), "ropeRT").ap()
        sel16 = np.zeros((128, NE * 128), np.float32)
        for e in range(NE):
            sel16[e, e * 128:(e + 1) * 128] = 1.0
        self.esel_d = nc.inline_tensor(np.asarray(sel16, dtype=BF), "esel").ap()
        sel12 = np.zeros((NT_TH, NT_TH * 128), np.float32)
        for t in range(NT_TH):
            sel12[t, t * 128:(t + 1) * 128] = 1.0
        self.tsel_d = nc.inline_tensor(sel12, "tsel12").ap()

        self.P = [nc.alloc_psum_tensor(f"P{k}", [128, 512], F32).ap() for k in range(8)]

        self.xn = [self.salloc(f"xn{t}", [128, TH], BF16) for t in range(NT_D)]
        self.rrb = self.salloc("rrb", [128, TH], F32)
        self.rrc = self.salloc("rrc", [128, NT_TH], F32)
        self.nwf = self.salloc("nwf", [128, NT_D], F32)
        self.nwfq = self.salloc("nwfq", [128, NT_D], F32)
        self.smll = self.salloc("smll", [128, 2048], BF16)
        self.BTm = {nm: self.salloc("bt_" + nm, [128, D], BF16)
                    for nm in ("qB", "kB", "vB", "oB")}
        self.zb = self.salloc("zb", [128, TH], BF16)
        self.ztmp = self.salloc("ztmp", [128, 16], BF16)
        self.off_zb = self.sboff
        self.combT = self.salloc("combT", [128, T_OWN], BF16)
        self.ident = self.salloc("ident", [128, 128], BF16)
        self.identf = self.salloc("identf", [128, 128], F32)
        self.consts = self.salloc("consts", [128, 8], F32)
        self.PERSIST = self.sboff

        sm = self.smll
        self.AT = {nm: sm[:, k * 256:(k + 1) * 256]
                   for k, nm in enumerate(("qA", "kA", "vA", "oA"))}
        self.waf = sm[:, 1024:1280]

        nc.const_aps.register(F32, 1e-5, self.consts[:, 0:1])
        nc.const_aps.register(F32, 1e-6, self.consts[:, 1:2])
        self._const_vals = [(0, 1e-5), (1, 1e-6)]

    SB_BASE = 16512
    SB_TOP = 229344

    def salloc(self, name, shape, dtype, at=None):
        per = int(np.prod(shape[1:])) * mybir.dt.size(dtype)
        per = (per + 31) & ~31
        if at is None:
            at = self.sboff
            self.sboff = at + per
        assert self.SB_BASE + at + per <= self.SB_TOP, (name, at, per)
        ap = self.nc.alloc_sbuf_tensor_at(name, list(shape), dtype,
                                          offset=self.SB_BASE + at)
        return ap.ap()

    def sem(self, name, g=False):
        if g:
            return Sem(self.nc.alloc_semaphore(name))
        if self._pool_idx >= len(self._pool):
            self._pool.append(self.nc.alloc_semaphore(f"pool{len(self._pool)}"))
        h = self._pool[self._pool_idx]
        self._pool_idx += 1
        s = Sem(h)
        self._phase_sems.append(s)
        return s

    def end_phase(self):
        nc = self.nc
        sems = list(self._phase_sems)
        if sems:
            with nc.Block() as blk:
                @blk.gpsimd
                def _(g):
                    for s in sems:
                        g.sem_clear(s.h)
        self._phase_sems = []
        self._pool_idx = 0

    def dbg(self, name, shape, dtype=F32):
        if name in self.dbg_outs:
            return self.dbg_outs[name]
        ap = self.nc.dram_tensor("dbg_" + name, list(shape), dtype, kind="ExternalOutput").ap()
        self.dbg_outs[name] = ap
        return ap



def emit_consts(b):
    nc = b.nc
    ldg = b.sem("c_ldg", g=True)
    lds = b.sem("c_lds")
    mzs = b.sem("c_mzs")
    with nc.Block() as blk:
        @blk.gpsimd
        def _(g):
            for col, val in b._const_vals:
                g.memset(b.consts[:, col:col + 1], val)
            g.memset(b.zb, 0.0)
            mz = None
            for nm in ("qB", "kB", "vB", "oB"):
                mz = g.memset(b.BTm[nm], 0.0)
            mz.then_inc(mzs.h, 1)
            g.wait_ge(mzs.h, 1)
            for nm in ("qA", "kA", "vA", "oA"):
                g.dma_start(out=b.AT[nm].rearrange("p (t r) -> p t r", t=NT_D),
                            in_=b.i[nm + "T"].rearrange("(t p) r -> p t r", p=128)).then_inc(ldg.h, 16)
            for nm in ("qB", "kB", "vB", "oB"):
                g.dma_start(out=b.BTm[nm][0:16, :], in_=b.i[nm + "T"]).then_inc(ldg.h, 16)
            g.dma_start(out=b.waf.rearrange("p (t e) -> p t e", t=NT_D),
                        in_=b.i["wa"].rearrange("(t p) e -> p t e", p=128)).then_inc(ldg.h, 16)
            ldg.v += 16 * 9

        @blk.sync
        def _(sync):
            sync.dma_start(out=b.nwf, in_=b.i["nwfm"]).then_inc(lds.h, 16)
            sync.dma_start(out=b.ident, in_=b.ident_d).then_inc(lds.h, 16)
            sync.dma_start(out=b.identf, in_=b.identf_d).then_inc(lds.h, 16)
            lds.v += 48

        @blk.vector
        def _(v):
            v.wait_ge(ldg.h, ldg.v)
            v.wait_ge(lds.h, lds.v)
            for nm in ("qA", "kA", "vA"):
                a3 = b.AT[nm].rearrange("p (t r) -> p t r", t=NT_D)
                for dt in range(NT_D):
                    v.tensor_scalar_mul(a3[:, dt], a3[:, dt], b.nwf[:, dt:dt + 1])
            v.tensor_scalar_mul(b.BTm["qB"], b.BTm["qB"], LORA_SCALE * INV_SQRT_DH)
            for nm in ("kB", "vB", "oB"):
                v.tensor_scalar_mul(b.BTm[nm], b.BTm[nm], LORA_SCALE)
            v.tensor_scalar_mul(b.nwfq, b.nwf, INV_SQRT_DH)
    b.end_phase()

def emit_ln0(b):
    nc = b.nc
    LD0, LD1, LDC, SQ, VAR, RSTD, XN, Q2, RR, TP, EV = (b.sem("n" + s) for s in
        ("LD0", "LD1", "LDC", "SQ", "VAR", "RSTD", "XN", "Q2", "RR", "TP", "EV"))
    LDx = (LD0, LD1)
    off = b.PERSIST
    xt = [b.salloc(f"l0x{k}", [128, D], F32, at=off + k * 8192) for k in range(2)]
    sqd = b.salloc("l0sq", [128, D], F32, at=off + 16384)
    xnt = b.salloc("l0xnt", [128, D], BF16, at=off + 16384 + 8192)
    stt = b.salloc("l0stat", [128, 16], F32, at=off + 16384 + 8192 + 4096)
    rrT = b.salloc("l0rrT", [NT_TH, 128], F32, at=off + 16384 + 8192 + 4096 + 64)
    tsel = b.salloc("l0tsel", [NT_TH, NT_TH * 128], F32, at=off + 16384 + 8192 + 4096 + 64 + 512)

    xin = b.i["xh"].rearrange("(t p) d -> t p d", p=128)
    C1 = float(1.0 / D)
    NT = NT_TH
    sttrace = b.salloc("l0sttr", [128, NT_TH * 16], F32,
                       at=off + 16384 + 8192 + 4096 + 64 + 512 + 6144)

    with nc.Block() as blk:
        @blk.sync
        def _(sync):
            sync.dma_start(out=tsel, in_=b.tsel_d).then_inc(LDC.h, 16)
            LDC.v = 16
            for t in range(NT):
                if t >= 2:
                    sync.wait_ge(XN.h, t - 1)
                    sync.wait_ge(SQ.h, t - 1)
                sync.dma_start(out=xt[t % 2], in_=xin[t]).then_inc(LDx[t % 2].h, 16)
            LD0.v = 16 * ((NT + 1) // 2)
            LD1.v = 16 * (NT // 2)

        @blk.scalar
        def _(s):
            ss = Ser(b, "nSS")
            for t in range(NT):
                s.wait_ge(LDx[t % 2].h, 16 * (t // 2 + 1))
                if t > 0:
                    s.wait_ge(VAR.h, t)
                s.activation(sqd, xt[t % 2], AF.Square,
                             accum_out=stt[:, 1:2]).then_inc(SQ.h, 1)
                s.wait_ge(VAR.h, t + 1)
                ss.step(s.activation(stt[:, 5:6], stt[:, 4:5], AF.Ln, bias=1e-5), s)
                s.activation(stt[:, 6:7], stt[:, 5:6], AF.Exp,
                             scale=-0.5).then_inc(RSTD.h, 1)
                s.wait_ge(Q2.h, t + 1)
                ss.step(s.activation(stt[:, 9:10], stt[:, 8:9], AF.Ln, bias=1e-6), s)
                s.activation(b.rrc[:, t:t + 1], stt[:, 9:10], AF.Exp,
                             scale=-0.5).then_inc(RR.h, 1)
            SQ.v = RSTD.v = RR.v = NT

        @blk.vector
        def _(v):
            sv = Ser(b, "nSV")
            for t in range(NT):
                v.wait_ge(LDx[t % 2].h, 16 * (t // 2 + 1))
                v.wait_ge(SQ.h, t + 1)
                x_ = xt[t % 2]
                sv.step(v.reduce_sum(stt[:, 0:1], x_, AX.X), v)
                sv.step(v.tensor_scalar_mul(stt[:, 2:3], stt[:, 0:1], C1), v)
                sv.step(v.tensor_scalar_mul(stt[:, 3:4], stt[:, 1:2], C1), v)
                sv.step(v.tensor_tensor(stt[:, 4:5], stt[:, 2:3], stt[:, 2:3], ALU.mult), v)
                v.tensor_tensor(stt[:, 4:5], stt[:, 3:4], stt[:, 4:5],
                                ALU.subtract).then_inc(VAR.h, 1)
                v.wait_ge(RSTD.h, t + 1)
                if t > 0:
                    v.wait_ge(TP.h, t * NT_D)
                v.tensor_scalar(xnt, x_, stt[:, 2:3], stt[:, 6:7],
                                ALU.subtract, ALU.mult).then_inc(XN.h, 1)
                sv.step(v.tensor_tensor(stt[:, 7:8], stt[:, 6:7], stt[:, 6:7], ALU.mult), v)
                v.tensor_tensor(stt[:, 8:9], stt[:, 4:5], stt[:, 7:8],
                                ALU.mult).then_inc(Q2.h, 1)
                for dt in range(NT_D):
                    v.wait_ge(TP.h, t * NT_D + dt + 1)
                    pb = b.P[dt % 4].bitcast(BF16)
                    v.tensor_copy(b.xn[dt][:, t * 128:(t + 1) * 128], pb[:, 0:128]).then_inc(EV.h, 1)
            VAR.v = XN.v = Q2.v = NT
            EV.v = NT * NT_D

        @blk.tensor
        def _(t_):
            for t in range(NT):
                t_.wait_ge(XN.h, t + 1)
                for dt in range(NT_D):
                    if t * NT_D + dt >= 4:
                        t_.wait_ge(EV.h, t * NT_D + dt - 3)
                    pb = b.P[dt % 4].bitcast(BF16)
                    t_.transpose(pb[:, 0:128], xnt[:, dt * 128:(dt + 1) * 128],
                                 b.ident).then_inc(TP.h, 1)
            TP.v = NT * NT_D

    TP2, EV2 = b.sem("nTP2"), b.sem("nEV2")
    with nc.Block() as blk:
        @blk.tensor
        def _(t_):
            t_.transpose(b.P[4][0:NT_TH, 0:128], b.rrc[:, 0:NT_TH], b.identf).then_inc(TP2.h, 1)

        @blk.vector
        def _(v):
            v.wait_ge(TP2.h, 1)
            v.tensor_copy(rrT, b.P[4][0:NT_TH, 0:128]).then_inc(EV2.h, 1)

        @blk.tensor
        def _(t_):
            t_.wait_ge(EV2.h, 1)
            t_.wait_ge(LDC.h, 16)
            for t in range(NT_TH):
                t_.matmul(b.P[5][:, 0:128], tsel[:, t * 128:(t + 1) * 128], rrT,
                          start=True, stop=True).then_inc(TP2.h, 1)
                t_.wait_ge(EV2.h, 2 + t)

        @blk.vector
        def _(v):
            for t in range(NT_TH):
                v.wait_ge(TP2.h, 2 + t)
                v.tensor_copy(b.rrb[:, t * 128:(t + 1) * 128], b.P[5][:, 0:128]).then_inc(EV2.h, 1)
            EV2.v = 1 + NT_TH
            TP2.v = 1 + NT_TH

    if "xn0" in b.want_dbg:
        dxn = b.dbg("xn0", (NT_D, 128, TH), BF16)
        drr = b.dbg("rrb", (128, TH))
        ST = b.sem("nST")
        with nc.Block() as blk:
            @blk.sync
            def _(sync):
                for dt in range(NT_D):
                    sync.dma_start(out=dxn[dt], in_=b.xn[dt]).then_inc(ST.h, 16)
                sync.dma_start(out=drr, in_=b.rrb).then_inc(ST.h, 16)
                ST.v = 16 * (NT_D + 1)
                sync.wait_ge(ST.h, ST.v)
    b.end_phase()

def _emit_z(b, Anm, chunks, ZP, ZE1, TPz, ZE):
    subs = [(cb + st * 128) for cb in chunks for st in range(4)]

    def tensor_part(t_):
        pzT = b.P[7].bitcast(BF16)
        for si, tok0 in enumerate(subs):
            if si > 0:
                t_.wait_ge(ZE1.h, si)
            a3 = b.AT[Anm].rearrange("p (t r) -> p t r", t=NT_D)
            for dt in range(NT_D):
                mmz = t_.matmul(b.P[6][:, 0:16], b.xn[dt][:, tok0:tok0 + 128],
                                a3[:, dt], start=(dt == 0), stop=(dt == NT_D - 1))
            mmz.then_inc(ZP.h, 1)
            t_.wait_ge(ZE1.h, si + 1)
            if si > 0:
                t_.wait_ge(ZE.h, si)
            t_.transpose(pzT[0:16, 0:128], b.ztmp, b.ident).then_inc(TPz.h, 1)
        ZP.v += len(subs)
        TPz.v += len(subs)

    def vector_part(v):
        pzT = b.P[7].bitcast(BF16)
        for si, tok0 in enumerate(subs):
            v.wait_ge(ZP.h, si + 1)
            v.tensor_scalar_mul(b.ztmp, b.P[6][:, 0:16],
                                b.rrc[:, tok0 // 128:tok0 // 128 + 1]).then_inc(ZE1.h, 1)
            v.wait_ge(TPz.h, si + 1)
            v.tensor_copy(b.zb[0:16, tok0:tok0 + 128], pzT[0:16, 0:128]).then_inc(ZE.h, 1)
        ZE1.v += len(subs)
        ZE.v += len(subs)

    return tensor_part, vector_part, len(subs)


def emit_proj_qk(b, which):
    nc = b.nc
    off = b.PERSIST
    isq = which == "q"
    wT, Anm, Bnm = ("wqT", "qA", "qB") if isq else ("wkT", "kA", "kB")
    dst = b.qf_d if isq else b.kf_d
    nck = 2 if isq else 3
    cb0 = HALO if isq else 0
    nwcol = b.nwfq if isq else b.nwf

    wbuf = [b.salloc(f"{which}w{k}", [128, NT_D * 256], BF16, at=off + k * 8192)
            for k in range(2)]
    ropec = b.salloc(which + "ropec", [128, TH], BF16, at=off + 16384)
    ropes = b.salloc(which + "ropes", [128, TH], BF16, at=off + 16384 + 3072)
    usc = b.salloc(which + "usc", [128, 512], F32, at=off + 16384 + 6144)
    rotb = b.salloc(which + "rot", [128, 512], F32, at=off + 16384 + 6144 + 2048)
    tmA = b.salloc(which + "tmA", [128, 512], F32, at=off + 16384 + 6144 + 4096)
    tmB = b.salloc(which + "tmB", [128, 512], F32, at=off + 16384 + 6144 + 6144)
    outb = [b.salloc(f"{which}o{k}", [128, 512], BF16,
                     at=off + 16384 + 6144 + 8192 + k * 1024) for k in range(2)]

    WL = [b.sem(which + "WL0", g=True), b.sem(which + "WL1", g=True)]
    RL = b.sem(which + "RL")
    FD = b.sem(which + "FD")
    MM = b.sem(which + "MM")
    ZP, ZE1 = b.sem(which + "ZP"), b.sem(which + "ZE1")
    TPz, ZE = b.sem(which + "TPz"), b.sem(which + "ZE")
    EV = b.sem(which + "EV")
    OE = b.sem(which + "OE")
    SO = [b.sem(which + "SO0"), b.sem(which + "SO1")]
    sv = Ser(b, which + "SV")
    zt, zv, nsubs = _emit_z(b, Anm, [cb0 + c * 512 for c in range(nck)], ZP, ZE1, TPz, ZE)

    gpo = 2 * nck
    ngroups = 8 * gpo

    with nc.Block() as blk:
        @blk.sync
        def _(sync):
            sync.dma_start(out=ropec, in_=b.i["ropecos"]).then_inc(RL.h, 16)
            sync.dma_start(out=ropes, in_=b.i["ropesin"]).then_inc(RL.h, 16)
            RL.v = 32
            for gi in range(ngroups):
                og, ot, ck = gi // gpo, (gi % gpo) // nck, gi % nck
                head = og * 2 + ot
                slot = gi % 2
                sync.wait_ge(OE.h, gi + 1)
                sync.dma_start(out=dst[head][:, ck * 512:(ck + 1) * 512],
                               in_=outb[slot]).then_inc(SO[slot].h, 16)
                SO[slot].v += 16
            sync.wait_ge(SO[0].h, SO[0].v)
            sync.wait_ge(SO[1].h, SO[1].v)

        @blk.gpsimd
        def _(g):
            for og in range(8):
                slot = og % 2
                if og >= 2:
                    g.wait_ge(MM.h, (og - 1) * gpo)
                g.dma_start(out=wbuf[slot].rearrange("p (t o) -> p t o", t=NT_D),
                            in_=b.i[wT][:, og * 256:(og + 1) * 256]
                            .rearrange("(t p) o -> p t o", p=128)).then_inc(WL[slot].h, 16)
                WL[slot].v += 16

        @blk.scalar
        def _(s_):
            for og in range(8):
                slot = og % 2
                s_.wait_ge(WL[slot].h, 16 * (og // 2 + 1))
                w3 = wbuf[slot].rearrange("p (t o) -> p t o", t=NT_D)
                for dt in range(NT_D):
                    m = s_.mul(w3[:, dt], w3[:, dt], nwcol[:, dt:dt + 1])
                m.then_inc(FD.h, 1)
                FD.v += 1

        @blk.tensor
        def _(t_):
            if not NO_LORA:
                zt(t_)
                t_.wait_ge(ZE.h, nsubs)
            for gi in range(ngroups):
                og, ot, ck = gi // gpo, (gi % gpo) // nck, gi % nck
                head = og * 2 + ot
                slot = og % 2
                cb = cb0 + ck * 512
                if gi % gpo == 0:
                    t_.wait_ge(FD.h, og + 1)
                if gi >= 2:
                    t_.wait_ge(EV.h, gi - 1)
                pp = b.P[gi % 2]
                w3 = wbuf[slot].rearrange("p (t o) -> p t o", t=NT_D)
                for dt in range(NT_D):
                    t_.matmul(pp, w3[:, dt, ot * 128:(ot + 1) * 128],
                              b.xn[dt][:, cb:cb + 512], start=(dt == 0),
                              stop=(NO_LORA and dt == NT_D - 1))
                if not NO_LORA:
                    t_.matmul(pp, b.BTm[Bnm][:, head * 128:(head + 1) * 128],
                              b.zb[:, cb:cb + 512],
                              start=False, stop=True).then_inc(MM.h, 1)
                else:
                    t_.nop().then_inc(MM.h, 1)
                MM.v += 1

        @blk.vector
        def _(v):
            if not NO_LORA:
                zv(v)
            v.wait_ge(RL.h, 32)
            for gi in range(ngroups):
                og, ot, ck = gi // gpo, (gi % gpo) // nck, gi % nck
                slot = gi % 2
                cb = cb0 + ck * 512
                pp = b.P[gi % 2]
                v.wait_ge(MM.h, gi + 1)
                if gi >= 2:
                    v.wait_ge(SO[slot].h, 16 * (gi // 2))
                sv.step(v.tensor_tensor(usc, pp, b.rrb[:, cb:cb + 512], ALU.mult), v)
                nc.vector.engine_nop().then_inc(EV.h, 1)
                EV.v += 1
                v.tensor_scalar_mul(rotb[0:64, :], usc[64:128, :], -1.0)
                sv.step(v.tensor_scalar_mul(rotb[64:128, :], usc[0:64, :], 1.0), v)
                sv.step(v.tensor_tensor(tmA, usc, ropec[:, cb:cb + 512], ALU.mult), v)
                sv.step(v.tensor_tensor(tmB, rotb, ropes[:, cb:cb + 512], ALU.mult), v)
                v.tensor_tensor(outb[slot], tmA, tmB, ALU.add).then_inc(OE.h, 1)
                OE.v += 1
    assert MM.v == ngroups
    b.end_phase()

def emit_proj_v(b):
    nc = b.nc
    off = b.PERSIST
    wbuf = [b.salloc(f"vw{k}", [128, NT_D * 512], BF16, at=off + k * 16384) for k in range(2)]
    outb = [b.salloc(f"vo{k}", [128, 512], BF16, at=off + 32768 + k * 1024) for k in range(2)]
    WL = [b.sem("vWL0", g=True), b.sem("vWL1", g=True)]
    FD = b.sem("vFD")
    MM = b.sem("vMM")
    ZP, ZE1 = b.sem("vZP"), b.sem("vZE1")
    TPz, ZE = b.sem("vTPz"), b.sem("vZE")
    EVv = b.sem("vEV")
    OE = b.sem("vOE")
    SO = [b.sem("vSO0"), b.sem("vSO1")]
    zt, zv, nsubs = _emit_z(b, "vA", [c * 512 for c in range(3)], ZP, ZE1, TPz, ZE)
    ngroups = 4 * NT_TH

    with nc.Block() as blk:
        @blk.sync
        def _(sync):
            for gi in range(ngroups):
                dvc, tt = gi // NT_TH, gi % NT_TH
                slot = gi % 2
                sync.wait_ge(OE.h, gi + 1)
                sync.dma_start(out=b.vt_d[tt][:, dvc * 512:(dvc + 1) * 512],
                               in_=outb[slot]).then_inc(SO[slot].h, 16)
                SO[slot].v += 16
            sync.wait_ge(SO[0].h, SO[0].v)
            sync.wait_ge(SO[1].h, SO[1].v)

        @blk.gpsimd
        def _(g):
            for dvc in range(4):
                slot = dvc % 2
                if dvc >= 2:
                    g.wait_ge(MM.h, (dvc - 1) * NT_TH)
                g.dma_start(out=wbuf[slot].rearrange("p (t o) -> p t o", t=NT_D),
                            in_=b.i["wvT"][:, dvc * 512:(dvc + 1) * 512]
                            .rearrange("(t p) o -> p t o", p=128)).then_inc(WL[slot].h, 16)
                WL[slot].v += 16

        @blk.scalar
        def _(s_):
            for dvc in range(4):
                slot = dvc % 2
                s_.wait_ge(WL[slot].h, 16 * (dvc // 2 + 1))
                w3 = wbuf[slot].rearrange("p (t o) -> p t o", t=NT_D)
                for dt in range(NT_D):
                    m = s_.mul(w3[:, dt], w3[:, dt], b.nwf[:, dt:dt + 1])
                m.then_inc(FD.h, 1)
                FD.v += 1

        @blk.tensor
        def _(t_):
            zt(t_)
            t_.wait_ge(ZE.h, nsubs)
            for gi in range(ngroups):
                dvc, tt = gi // NT_TH, gi % NT_TH
                slot = dvc % 2
                if gi % NT_TH == 0:
                    t_.wait_ge(FD.h, dvc + 1)
                if gi >= 2:
                    t_.wait_ge(EVv.h, gi - 1)
                pp = b.P[gi % 2]
                w3 = wbuf[slot].rearrange("p (t o) -> p t o", t=NT_D)
                for dt in range(NT_D):
                    t_.matmul(pp, b.xn[dt][:, tt * 128:(tt + 1) * 128],
                              w3[:, dt], start=(dt == 0), stop=False)
                t_.matmul(pp, b.zb[:, tt * 128:(tt + 1) * 128],
                          b.BTm["vB"][:, dvc * 512:(dvc + 1) * 512],
                          start=False, stop=True).then_inc(MM.h, 1)
                MM.v += 1

        @blk.vector
        def _(v):
            zv(v)
            for gi in range(ngroups):
                dvc, tt = gi // NT_TH, gi % NT_TH
                slot = gi % 2
                pp = b.P[gi % 2]
                v.wait_ge(MM.h, gi + 1)
                if gi >= 2:
                    v.wait_ge(SO[slot].h, 16 * (gi // 2))
                v.tensor_scalar_mul(outb[slot], pp, b.rrc[:, tt:tt + 1]).then_inc(OE.h, 1)
                OE.v += 1
                nc.vector.engine_nop().then_inc(EVv.h, 1)
                EVv.v += 1
    assert MM.v == ngroups
    b.end_phase()

def emit_attn(b):
    nc = b.nc
    off = b.PERSIST
    Kh = [b.salloc(f"aK{k}", [128, TH], BF16, at=off + k * 3072) for k in range(2)]
    Vh = [b.salloc(f"aV{k}", [128, TH], BF16, at=off + 6144 + k * 3072) for k in range(2)]
    Qh = [b.salloc(f"aQ{k}", [128, T_OWN], BF16, at=off + 12288 + k * 2048) for k in range(2)]
    msk = b.salloc("amsk", [128, 5 * 640], F32, at=off + 16384)
    s_sb = b.salloc("assb", [128, 640], F32, at=off + 16384 + 12800)
    pf = b.salloc("apf", [128, 640], F32, at=off + 16384 + 12800 + 2560)
    pbf = b.salloc("apbf", [128, 640], BF16, at=off + 16384 + 12800 + 5120)
    PTs = b.salloc("aPT", [128, 640], BF16, at=off + 16384 + 12800 + 5120 + 1280)
    ysb = [b.salloc(f"ay{k}", [128, 128], BF16, at=off + 16384 + 12800 + 5120 + 2560 + k * 256)
           for k in range(2)]
    ast = b.salloc("aast", [128, 4], F32, at=off + 16384 + 12800 + 5120 + 2560 + 512)

    KL = [b.sem("aKL0", g=True), b.sem("aKL1", g=True)]
    ML = b.sem("aML")
    MMD = b.sem("aMMD")
    SCP = b.sem("aSCP")
    SCE = b.sem("aSCE")
    NM = b.sem("aNM")
    EXP = b.sem("aEXP")
    PBF = b.sem("aPBF")
    TPP = b.sem("aTPP")
    PTE = b.sem("aPTE")
    OV = b.sem("aOV")
    YE = b.sem("aYE")
    YS = [b.sem("aYS0"), b.sem("aYS1")]
    sv = Ser(b, "aSV")

    NP = H * NT_OWN

    with nc.Block() as blk:
        @blk.sync
        def _(sync):
            for ms in range(5):
                sync.dma_start(out=msk[:, ms * 640:(ms + 1) * 640],
                               in_=b.i["mask5"][ms]).then_inc(ML.h, 16)
            ML.v = 80
            for i in range(NP):
                h, qt = i // NT_OWN, i % NT_OWN
                slot = i % 2
                sync.wait_ge(YE.h, i + 1)
                sync.dma_start(out=b.yattn_d[h][:, qt * 128:(qt + 1) * 128],
                               in_=ysb[slot]).then_inc(YS[slot].h, 16)
                YS[slot].v += 16
            sync.wait_ge(YS[0].h, YS[0].v)
            sync.wait_ge(YS[1].h, YS[1].v)

        @blk.gpsimd
        def _(g):
            for h in range(H):
                slot = h % 2
                if h >= 2:
                    g.wait_ge(OV.h, NT_OWN * (h - 1))
                g.dma_start(out=Kh[slot], in_=b.kf_d[h]).then_inc(KL[slot].h, 16)
                g.dma_start(out=Vh[slot].rearrange("p (t c) -> p t c", t=NT_TH),
                            in_=b.vt_d[:, :, h * 128:(h + 1) * 128]
                            .rearrange("t p c -> p t c")).then_inc(KL[slot].h, 16)
                g.dma_start(out=Qh[slot], in_=b.qf_d[h]).then_inc(KL[slot].h, 16)
                KL[slot].v += 48

        @blk.tensor
        def _(t_):
            for i in range(NP):
                h, qt = i // NT_OWN, i % NT_OWN
                slot = h % 2
                if qt == 0:
                    t_.wait_ge(KL[slot].h, 48 * (h // 2 + 1))
                if i > 0:
                    t_.wait_ge(SCE.h, i)
                t_.matmul(b.P[0], Qh[slot][:, qt * 128:(qt + 1) * 128],
                          Kh[slot][:, qt * 128:qt * 128 + 512], start=True, stop=True)
                t_.matmul(b.P[1][:, 0:128], Qh[slot][:, qt * 128:(qt + 1) * 128],
                          Kh[slot][:, qt * 128 + 512:qt * 128 + 640],
                          start=True, stop=True).then_inc(SCP.h, 1)
                SCP.v += 1
                t_.wait_ge(PBF.h, i + 1)
                for j in range(5):
                    if 5 * i + j >= 2:
                        t_.wait_ge(PTE.h, 5 * i + j - 1)
                    pb = b.P[2 + (j % 2)].bitcast(BF16)
                    t_.transpose(pb[:, 0:128], pbf[:, j * 128:(j + 1) * 128],
                                 b.ident).then_inc(TPP.h, 1)
                    TPP.v += 1
                if i >= 2:
                    t_.wait_ge(YE.h, i - 1)
                t_.wait_ge(PTE.h, 5 * i + 5)
                po = b.P[4 + (i % 2)]
                for j in range(5):
                    mm = t_.matmul(po[:, 0:128], Vh[slot][:, (qt + j) * 128:(qt + j + 1) * 128],
                                   PTs[:, j * 128:(j + 1) * 128],
                                   start=(j == 0), stop=(j == 4))
                mm.then_inc(OV.h, 1)
                OV.v += 1

        @blk.vector
        def _(v):
            v.wait_ge(ML.h, 80)
            for i in range(NP):
                h, qt = i // NT_OWN, i % NT_OWN
                mslot = min(qt, 4)
                if i >= 1:
                    v.wait_ge(EXP.h, i)
                v.wait_ge(SCP.h, i + 1)
                v.tensor_tensor(s_sb[:, 0:512], b.P[0], msk[:, mslot * 640:mslot * 640 + 512],
                                ALU.add)
                v.tensor_tensor(s_sb[:, 512:640], b.P[1][:, 0:128],
                                msk[:, mslot * 640 + 512:(mslot + 1) * 640],
                                ALU.add).then_inc(SCE.h, 1)
                SCE.v += 1
                v.wait_ge(SCE.h, SCE.v)
                sv.step(v.reduce_max(ast[:, 0:1], s_sb, AX.X, negate=True), v)
                nc.vector.engine_nop().then_inc(NM.h, 1)
                NM.v += 1
                v.wait_ge(EXP.h, i + 1)
                sv.step(v.reciprocal(ast[:, 2:3], ast[:, 1:2]), v)
                v.tensor_scalar_mul(pbf, pf, ast[:, 2:3]).then_inc(PBF.h, 1)
                PBF.v += 1
                for j in range(5):
                    v.wait_ge(TPP.h, 5 * i + j + 1)
                    pb = b.P[2 + (j % 2)].bitcast(BF16)
                    v.tensor_copy(PTs[:, j * 128:(j + 1) * 128], pb[:, 0:128]).then_inc(PTE.h, 1)
                    PTE.v += 1
                v.wait_ge(OV.h, i + 1)
                slot = i % 2
                if i >= 2:
                    v.wait_ge(YS[slot].h, 16 * (i // 2))
                v.tensor_copy(ysb[slot], b.P[4 + (i % 2)][:, 0:128]).then_inc(YE.h, 1)
                YE.v += 1

        @blk.scalar
        def _(s_):
            for i in range(NP):
                s_.wait_ge(NM.h, i + 1)
                if i >= 1:
                    s_.wait_ge(PBF.h, i)
                s_.activation(pf, s_sb, AF.Exp, bias=ast[:, 0:1],
                              accum_out=ast[:, 1:2]).then_inc(EXP.h, 1)
                EXP.v += 1
    b.end_phase()

def _dump_yattn(b):
    nc = b.nc
    d = b.dbg("yat", (NT_D, 128, T_OWN), BF16)
    ST = b.sem("dyST")
    with nc.Block() as blk:
        @blk.sync
        def _(sync):
            sync.dma_start(out=d, in_=b.yattn_d).then_inc(ST.h, 16)
            ST.v = 16
            sync.wait_ge(ST.h, ST.v)
    b.end_phase()

def emit_oproj(b):
    nc = b.nc
    b.s_fm = b.salloc("s_fm", [128, NT_D * T_OWN], BF16)
    off = b.sboff
    ybuf = b.salloc("oyb", [128, NT_D * T_OWN], BF16, at=off)
    wbuf = [b.salloc(f"ow{k}", [128, NT_D * 256], BF16, at=off + 32768 + k * 8192)
            for k in range(2)]
    y3 = ybuf.rearrange("p (t c) -> p t c", t=NT_D)
    s3 = b.s_fm.rearrange("p (t c) -> p t c", t=NT_D)

    YL = b.sem("oYL", g=True)
    WL = [b.sem("oWL0", g=True), b.sem("oWL1", g=True)]
    MM = b.sem("oMM")
    ZP, ZE1 = b.sem("oZP"), b.sem("oZE1")
    TPz, ZE = b.sem("oTPz"), b.sem("oZE")
    EV = b.sem("oEV")
    ngroups = 8 * 2 * 2

    with nc.Block() as blk:
        @blk.gpsimd
        def _(g):
            g.dma_start(out=y3, in_=b.yattn_d.rearrange("t p c -> p t c")).then_inc(YL.h, 16)
            YL.v += 16
            for og in range(8):
                slot = og % 2
                if og >= 2:
                    g.wait_ge(MM.h, (og - 1) * 4)
                g.dma_start(out=wbuf[slot].rearrange("p (t o) -> p t o", t=NT_D),
                            in_=b.i["woT"][:, og * 256:(og + 1) * 256]
                            .rearrange("(t p) o -> p t o", p=128)).then_inc(WL[slot].h, 16)
                WL[slot].v += 16

        @blk.tensor
        def _(t_):
            t_.wait_ge(YL.h, 16)
            a3 = b.AT["oA"].rearrange("p (t r) -> p t r", t=NT_D)
            pzT = b.P[7].bitcast(BF16)
            for si in range(8):
                tok0 = si * 128
                if si > 0:
                    t_.wait_ge(ZE1.h, si)
                for dt in range(NT_D):
                    mmz = t_.matmul(b.P[6][:, 0:16], y3[:, dt, tok0:tok0 + 128],
                                    a3[:, dt], start=(dt == 0), stop=(dt == NT_D - 1))
                mmz.then_inc(ZP.h, 1)
                ZP.v += 1
                t_.wait_ge(ZE1.h, si + 1)
                if si > 0:
                    t_.wait_ge(ZE.h, si)
                t_.transpose(pzT[0:16, 0:128], b.ztmp, b.ident).then_inc(TPz.h, 1)
                TPz.v += 1
            t_.wait_ge(ZE.h, 8)
            for gi in range(ngroups):
                og, ot, ck = gi // 4, (gi % 4) // 2, gi % 2
                head = og * 2 + ot
                slot = og % 2
                if gi % 4 == 0:
                    t_.wait_ge(WL[slot].h, 16 * (og // 2 + 1))
                if gi >= 2:
                    t_.wait_ge(EV.h, gi - 1)
                pp = b.P[gi % 2]
                w3 = wbuf[slot].rearrange("p (t o) -> p t o", t=NT_D)
                for dt in range(NT_D):
                    t_.matmul(pp, w3[:, dt, ot * 128:(ot + 1) * 128],
                              y3[:, dt, ck * 512:(ck + 1) * 512], start=(dt == 0), stop=False)
                t_.matmul(pp, b.BTm["oB"][:, head * 128:(head + 1) * 128],
                          b.zb[:, ck * 512:(ck + 1) * 512],
                          start=False, stop=True).then_inc(MM.h, 1)
                MM.v += 1

        @blk.vector
        def _(v):
            pzT = b.P[7].bitcast(BF16)
            for si in range(8):
                v.wait_ge(ZP.h, si + 1)
                v.tensor_copy(b.ztmp, b.P[6][:, 0:16]).then_inc(ZE1.h, 1)
                ZE1.v += 1
                v.wait_ge(TPz.h, si + 1)
                v.tensor_copy(b.zb[0:16, si * 128:(si + 1) * 128],
                              pzT[0:16, 0:128]).then_inc(ZE.h, 1)
                ZE.v += 1
            for gi in range(ngroups):
                og, ot, ck = gi // 4, (gi % 4) // 2, gi % 2
                head = og * 2 + ot
                pp = b.P[gi % 2]
                v.wait_ge(MM.h, gi + 1)
                v.tensor_tensor(s3[:, head, ck * 512:(ck + 1) * 512], pp,
                                b.xn[head][:, HALO + ck * 512:HALO + (ck + 1) * 512],
                                ALU.add).then_inc(EV.h, 1)
                EV.v += 1
    b.end_phase()

def _dump_s(b):
    nc = b.nc
    d = b.dbg("s0", (128, NT_D * T_OWN), BF16)
    ST = b.sem("dsST")
    with nc.Block() as blk:
        @blk.sync
        def _(sync):
            sync.dma_start(out=d, in_=b.s_fm).then_inc(ST.h, 16)
            ST.v = 16
            sync.wait_ge(ST.h, ST.v)
    b.end_phase()

def emit_router(b):
    nc = b.nc
    off = 64 * 1024 + 512
    aff = b.salloc("raff", [128, 16], F32, at=off)
    eqt = b.salloc("reqt", [128, 16], F32, at=off + 64)
    tbd = b.salloc("rtbd", [128, 16], F32, at=off + 128)
    rst = b.salloc("rrst", [128, 16], F32, at=off + 192)
    ctm = b.salloc("rctm", [128, 16], BF16, at=off + 256)
    es = b.salloc("resel", [128, NE * 128], BF16)
    b._moe_es = es

    EL = b.sem("rEL")
    ELM = b.sem("rELM")
    AP_ = b.sem("rAP")
    SG = b.sem("rSG")
    NM1 = b.sem("rNM1")
    EQS = b.sem("rEQS")
    MS = b.sem("rMS")
    SG2 = b.sem("rSG2")
    CMB = b.sem("rCMB")
    TPC = b.sem("rTPC")
    CEV = b.sem("rCEV")
    sv = Ser(b, "rSV")
    ss = Ser(b, "rSS")

    with nc.Block() as blk:
        @blk.sync
        def _(sync):
            sync.dma_start(out=es, in_=b.esel_d).then_inc(EL.h, 16)
            EL.v = 16

        @blk.gpsimd
        def _(g):
            g.memset(b.combT, 0.0).then_inc(ELM.h, 1)
            ELM.v += 1

        @blk.tensor
        def _(t_):
            t_.wait_ge(EL.h, EL.v)
            t_.wait_ge(ELM.h, ELM.v)
            pcT = b.P[7].bitcast(BF16)
            for tt in range(NT_OWN):
                if tt > 0:
                    t_.wait_ge(SG.h, tt)
                for dt in range(NT_D):
                    mm = t_.matmul(b.P[5][:, 0:16],
                                   b.xn[dt][:, HALO + tt * 128:HALO + (tt + 1) * 128],
                                   b.waf.rearrange("p (t e) -> p t e", t=NT_D)[:, dt],
                                   start=(dt == 0), stop=(dt == NT_D - 1))
                mm.then_inc(AP_.h, 1)
                AP_.v += 1
                t_.wait_ge(CMB.h, tt + 1)
                if tt > 0:
                    t_.wait_ge(CEV.h, tt)
                t_.transpose(pcT[0:16, 0:128], ctm, b.ident).then_inc(TPC.h, 1)
                TPC.v += 1

        @blk.scalar
        def _(s_):
            for tt in range(NT_OWN):
                s_.wait_ge(AP_.h, tt + 1)
                if tt > 0:
                    s_.wait_ge(CMB.h, tt)
                ss.step(s_.activation(aff, b.P[5][:, 0:16], AF.Sigmoid), s_)
                s_.nop().then_inc(SG.h, 1)
                SG.v += 1
                s_.wait_ge(NM1.h, tt + 1)
                ss.step(s_.activation(eqt, aff, AF.Sign, bias=rst[:, 0:1]), s_)
                s_.nop().then_inc(EQS.h, 1)
                EQS.v += 1
                s_.wait_ge(MS.h, tt + 1)
                ss.step(s_.activation(tbd, aff, AF.Sign, bias=rst[:, 9:10]), s_)
                s_.nop().then_inc(SG2.h, 1)
                SG2.v += 1

        @blk.vector
        def _(v):
            for tt in range(NT_OWN):
                v.wait_ge(SG.h, tt + 1)
                sv.step(v.reduce_max(rst[:, 0:1], aff, AX.X, negate=True), v)
                v.reduce_max(rst[:, 1:2], aff, AX.X).then_inc(NM1.h, 1)
                NM1.v += 1
                v.wait_ge(EQS.h, tt + 1)
                sv.step(v.tensor_scalar_add(eqt, eqt, 1.0), v)
                sv.step(v.reduce_sum(rst[:, 5:6], eqt, AX.X), v)
                sv.step(v.scalar_tensor_tensor(tbd, eqt, 1e30, aff, ALU.mult, ALU.subtract), v)
                sv.step(v.tensor_reduce(rst[:, 6:7], tbd, AX.X, ALU.min, negate=True), v)
                sv.step(v.tensor_scalar(rst[:, 7:8], rst[:, 5:6], -1.5, 0.0, ALU.add, ALU.max), v)
                sv.step(v.tensor_scalar_min(rst[:, 7:8], rst[:, 7:8], 1.0), v)
                sv.step(v.tensor_tensor(rst[:, 8:9], rst[:, 1:2], rst[:, 6:7], ALU.subtract), v)
                sv.step(v.tensor_tensor(rst[:, 8:9], rst[:, 8:9], rst[:, 7:8], ALU.mult), v)
                sv.step(v.tensor_tensor(rst[:, 8:9], rst[:, 6:7], rst[:, 8:9], ALU.add), v)
                sv.step(v.tensor_scalar_mul(rst[:, 9:10], rst[:, 8:9], -1.0), v)
                sv.step(v.tensor_tensor(rst[:, 11:12], rst[:, 1:2], rst[:, 8:9], ALU.add), v)
                v.reciprocal(rst[:, 12:13], rst[:, 11:12]).then_inc(MS.h, 1)
                MS.v += 1
                v.wait_ge(SG2.h, tt + 1)
                sv.step(v.tensor_scalar(tbd, tbd, 1.0, 1.0, ALU.add, ALU.min), v)
                sv.step(v.tensor_tensor(tbd, aff, tbd, ALU.mult), v)
                v.tensor_scalar_mul(ctm, tbd, rst[:, 12:13]).then_inc(CMB.h, 1)
                CMB.v += 1
                v.wait_ge(TPC.h, tt + 1)
                pcT = b.P[7].bitcast(BF16)
                v.tensor_copy(b.combT[0:16, tt * 128:(tt + 1) * 128],
                              pcT[0:16, 0:128]).then_inc(CEV.h, 1)
                CEV.v += 1
    b.end_phase()

def emit_moe(b):
    nc = b.nc
    b.xnc = b.salloc("xnc", [128, NT_D * T_OWN], BF16)
    xc3 = b.xnc.rearrange("p (t c) -> p t c", t=NT_D)
    CPX = b.sem("mCPX")
    with nc.Block() as blk:
        @blk.vector
        def _(v):
            for dt in range(NT_D):
                cp = v.tensor_copy(xc3[:, dt], b.xn[dt][:, HALO:HALO + T_OWN])
            cp.then_inc(CPX.h, 1)
            CPX.v = 1

    acc = b.salloc("macc", [128, NT_D * T_OWN], F32, at=0)
    b.acc = acc
    a3 = acc.rearrange("p (t c) -> p t c", t=NT_D)
    off2 = 65536
    gbuf = b.salloc("mgel", [128, 512], F32, at=off2)
    tmp2 = b.salloc("mtmp2", [128, 512], F32, at=off2 + 2048)
    rgb3 = b.salloc("mrgb", [128, NT_F * NE], F32, at=off2 + 4096)
    rb13 = b.salloc("mrb1", [128, NT_F * NE], F32, at=off2 + 4608)
    sgb3 = b.salloc("msgb", [128, NT_F * NSH], F32, at=off2 + 5120)
    sb13 = b.salloc("msb1", [128, NT_F * NSH], F32, at=off2 + 5184)
    rb2p = b.salloc("mrb2p", [128, D], BF16, at=off2 + 5248)
    sb2p = b.salloc("msb2p", [128, D], BF16, at=off2 + 5248 + 4096)
    assert off2 + 5248 + 8192 <= 79072
    o2p = b.salloc("mo2p", [128, 512], BF16)
    cbb = b.salloc("mcbb", [128, T_OWN], BF16)
    b._moe_offS = b.sboff
    wg = [b.salloc(f"mwg{k}", [128, NT_D * 128], BF16) for k in range(2)]
    w1b = [b.salloc(f"mw1{k}", [128, NT_D * 128], BF16) for k in range(2)]
    w2b = [b.salloc(f"mw2{k}", [128, NT_F * 256], BF16) for k in range(2)]
    yb = b.salloc("myb", [128, NT_F * T_OWN], BF16)
    y3 = yb.rearrange("p (t c) -> p t c", t=NT_F)
    assert b.sboff <= 206000, b.sboff

    NEX = NE + NSH
    WLG = [b.sem("mWLG0", g=True), b.sem("mWLG1", g=True)]
    WL2 = [b.sem("mWL20", g=True), b.sem("mWL21", g=True)]
    BL = b.sem("mBL", g=True)
    CB = b.sem("mCB")
    CBE = b.sem("mCBE")
    HG = b.sem("mHG")
    H1 = b.sem("mH1")
    GE = b.sem("mGE")
    YG = b.sem("mYG")
    YE = b.sem("mYE")
    W2M = b.sem("mW2M")
    AE = b.sem("mAE")
    MZ2 = b.sem("mMZ2")
    sv = Ser(b, "mSV")

    def eW(e):
        if e < NE:
            return "rg", "rw1", "rw2", rgb3, rb13, e
        return "sg", "sw1", "sw2", sgb3, sb13, e - NE

    with nc.Block() as blk:
        @blk.gpsimd
        def _(g):
            g.dma_start(out=rgb3.rearrange("p (t e) -> p t e", t=NT_F),
                        in_=b.i["rgbT"].rearrange("(t p) e -> p t e", p=128)).then_inc(BL.h, 16)
            g.dma_start(out=rb13.rearrange("p (t e) -> p t e", t=NT_F),
                        in_=b.i["rb1T"].rearrange("(t p) e -> p t e", p=128)).then_inc(BL.h, 16)
            g.dma_start(out=sgb3.rearrange("p (t e) -> p t e", t=NT_F),
                        in_=b.i["sgbT"].rearrange("(t p) e -> p t e", p=128)).then_inc(BL.h, 16)
            g.dma_start(out=sb13.rearrange("p (t e) -> p t e", t=NT_F),
                        in_=b.i["sb1T"].rearrange("(t p) e -> p t e", p=128)).then_inc(BL.h, 16)
            g.memset(rb2p, 0.0)
            mz = g.memset(sb2p, 0.0)
            mz.then_inc(MZ2.h, 1)
            g.wait_ge(MZ2.h, 1)
            g.wait_ge(BL.h, 64)
            g.dma_start(out=rb2p[0:16, :], in_=b.i["rb2"]).then_inc(BL.h, 16)
            g.dma_start(out=sb2p[0:2, :], in_=b.i["sb2"]).then_inc(BL.h, 16)
            g.dma_start(out=o2p, in_=b.ones2_d).then_inc(BL.h, 16)
            BL.v = 64 + 48
            MZ2.v = 1
            for e in range(NEX):
                ng, n1, n2, _, _, ei = eW(e)
                for sl in range(8):
                    ls = e * 8 + sl
                    slot = ls % 2
                    if ls >= 2:
                        g.wait_ge(H1.h, 2 * (ls - 1))
                    g.dma_start(out=wg[slot].rearrange("p (t o) -> p t o", t=NT_D),
                                in_=b.i[ng][ei][:, sl * 128:(sl + 1) * 128]
                                .rearrange("(t p) o -> p t o", p=128)).then_inc(WLG[slot].h, 16)
                    g.dma_start(out=w1b[slot].rearrange("p (t o) -> p t o", t=NT_D),
                                in_=b.i[n1][ei][:, sl * 128:(sl + 1) * 128]
                                .rearrange("(t p) o -> p t o", p=128)).then_inc(WLG[slot].h, 16)
                    WLG[slot].v += 32
                for sl in range(8):
                    ls = e * 8 + sl
                    slot = ls % 2
                    if ls >= 2:
                        g.wait_ge(W2M.h, 32 + 4 * (ls - 1))
                    g.dma_start(out=w2b[slot].rearrange("p (t o) -> p t o", t=NT_F),
                                in_=b.i[n2][ei][:, sl * 256:(sl + 1) * 256]
                                .rearrange("(t p) o -> p t o", p=128)).then_inc(WL2[slot].h, 16)
                    WL2[slot].v += 16

        @blk.tensor
        def _(t_):
            t_.wait_ge(CPX.h, 1)
            t_.wait_ge(BL.h, BL.v)
            t_.wait_ge(MZ2.h, 1)
            w3g = [w.rearrange("p (t o) -> p t o", t=NT_D) for w in wg]
            w31 = [w.rearrange("p (t o) -> p t o", t=NT_D) for w in w1b]
            w32 = [w.rearrange("p (t o) -> p t o", t=NT_F) for w in w2b]
            for gi in range(32):
                dt, ch = gi // 2, gi % 2
                if gi >= 2:
                    t_.wait_ge(AE.h, gi - 1)
                pp = b.P[4 + (gi % 2)]
                t_.matmul(pp, rb2p[:, dt * 128:(dt + 1) * 128],
                          b.combT[:, ch * 512:(ch + 1) * 512], start=True, stop=False)
                t_.matmul(pp, sb2p[:, dt * 128:(dt + 1) * 128],
                          o2p, start=False, stop=True).then_inc(W2M.h, 1)
                W2M.v += 1
            for e in range(NEX):
                if e < NE:
                    for ch in range(2):
                        if 2 * e + ch >= 1:
                            t_.wait_ge(CBE.h, 2 * e + ch)
                        if e >= 1 and ch == 0:
                            t_.wait_ge(YE.h, 16 * e)
                        t_.matmul(b.P[6], b._moe_es[:, e * 128:(e + 1) * 128],
                                  b.combT[:, ch * 512:(ch + 1) * 512],
                                  start=True, stop=True).then_inc(CB.h, 1)
                        CB.v += 1
                for gi in range(16):
                    ft, ch = gi // 2, gi % 2
                    ls = e * 8 + ft
                    slot = ls % 2
                    gg = e * 16 + gi
                    t_.wait_ge(WLG[slot].h, 32 * (ls // 2 + 1))
                    if gg >= 2:
                        t_.wait_ge(GE.h, gg - 1)
                        t_.wait_ge(YG.h, gg - 1)
                    pg = b.P[0 + (gg % 2)]
                    p1 = b.P[2 + (gg % 2)]
                    for dt in range(NT_D):
                        mmg = t_.matmul(pg, w3g[slot][:, dt],
                                        xc3[:, dt, ch * 512:(ch + 1) * 512],
                                        start=(dt == 0), stop=(dt == NT_D - 1))
                    mmg.then_inc(HG.h, 1)
                    HG.v += 1
                    for dt in range(NT_D):
                        mm1 = t_.matmul(p1, w31[slot][:, dt],
                                        xc3[:, dt, ch * 512:(ch + 1) * 512],
                                        start=(dt == 0), stop=(dt == NT_D - 1))
                    mm1.then_inc(H1.h, 1)
                    H1.v += 1
                t_.wait_ge(YE.h, 16 * (e + 1))
                for wgi in range(32):
                    dt, ch = wgi // 2, wgi % 2
                    sl = dt // 2
                    ls2 = e * 8 + sl
                    slot = ls2 % 2
                    gw = 32 + e * 32 + wgi
                    t_.wait_ge(WL2[slot].h, 16 * (ls2 // 2 + 1))
                    if gw >= 2:
                        t_.wait_ge(AE.h, gw - 1)
                    pp = b.P[4 + (gw % 2)]
                    for ft in range(NT_F):
                        mmw = t_.matmul(pp, w32[slot][:, ft, (dt % 2) * 128:(dt % 2 + 1) * 128],
                                        y3[:, ft, ch * 512:(ch + 1) * 512],
                                        start=(ft == 0), stop=(ft == NT_F - 1))
                    mmw.then_inc(W2M.h, 1)
                    W2M.v += 1

        @blk.scalar
        def _(s_):
            for e in range(NEX):
                _, _, _, gb3, b13, ei = eW(e)
                for gi in range(16):
                    ft, ch = gi // 2, gi % 2
                    gg = e * 16 + gi
                    s_.wait_ge(HG.h, gg + 1)
                    if gg >= 1:
                        s_.wait_ge(YG.h, gg)
                    pg = b.P[0 + (gg % 2)]
                    s_.activation(gbuf, pg, AF.Gelu,
                                  bias=gb3.rearrange("p (t e) -> p t e", t=NT_F)[:, ft, ei:ei + 1]
                                  ).then_inc(GE.h, 1)
                    GE.v += 1

        @blk.vector
        def _(v):
            for gi0 in range(32):
                v.wait_ge(W2M.h, gi0 + 1)
                dt, ch = gi0 // 2, gi0 % 2
                pp = b.P[4 + (gi0 % 2)]
                v.tensor_copy(a3[:, dt, ch * 512:(ch + 1) * 512], pp).then_inc(AE.h, 1)
                AE.v += 1
            for e in range(NEX):
                _, _, _, gb3, b13, ei = eW(e)
                if e < NE:
                    for ch in range(2):
                        v.wait_ge(CB.h, 2 * e + ch + 1)
                        v.tensor_copy(cbb[:, ch * 512:(ch + 1) * 512],
                                      b.P[6]).then_inc(CBE.h, 1)
                        CBE.v += 1
                for gi in range(16):
                    ft, ch = gi // 2, gi % 2
                    gg = e * 16 + gi
                    v.wait_ge(GE.h, gg + 1)
                    v.wait_ge(H1.h, gg + 1)
                    if gg >= 1:
                        v.wait_ge(YE.h, gg)
                    p1 = b.P[2 + (gg % 2)]
                    sv.step(v.scalar_tensor_tensor(
                        tmp2, p1, b13.rearrange("p (t e) -> p t e", t=NT_F)[:, ft, ei:ei + 1],
                        gbuf, ALU.add, ALU.mult), v)
                    nc.vector.engine_nop().then_inc(YG.h, 1)
                    YG.v += 1
                    if e < NE:
                        v.tensor_tensor(y3[:, ft, ch * 512:(ch + 1) * 512], tmp2,
                                        cbb[:, ch * 512:(ch + 1) * 512],
                                        ALU.mult).then_inc(YE.h, 1)
                    else:
                        v.tensor_copy(y3[:, ft, ch * 512:(ch + 1) * 512],
                                      tmp2).then_inc(YE.h, 1)
                    YE.v += 1
                for wgi in range(32):
                    gw = 32 + e * 32 + wgi
                    dt, ch = wgi // 2, wgi % 2
                    v.wait_ge(W2M.h, gw + 1)
                    pp = b.P[4 + (gw % 2)]
                    v.tensor_tensor(a3[:, dt, ch * 512:(ch + 1) * 512], pp,
                                    a3[:, dt, ch * 512:(ch + 1) * 512],
                                    ALU.add).then_inc(AE.h, 1)
                    AE.v += 1
    b.end_phase()

def _dump_acc(b):
    nc = b.nc
    d = b.dbg("acc", (128, NT_D * T_OWN))
    dc = b.dbg("combT", (128, T_OWN), BF16)
    ST = b.sem("daST")
    with nc.Block() as blk:
        @blk.sync
        def _(sync):
            sync.dma_start(out=d, in_=b.acc).then_inc(ST.h, 16)
            sync.dma_start(out=dc, in_=b.combT).then_inc(ST.h, 16)
            ST.v = 32
            sync.wait_ge(ST.h, ST.v)
    b.end_phase()

def emit_combine_ln1(b):
    nc = b.nc
    moff = b._moe_offS
    prod = b.salloc("cprod", [128, NT_D * T_OWN], BF16, at=moff)
    p3 = prod.rearrange("p (t c) -> p t c", t=NT_D)
    gtmp = b.salloc("cgt", [128, T_OWN], F32, at=moff + 32768)
    xnt = b.salloc("cxnt", [128, D], BF16, at=moff + 32768 + 4096)
    stt = b.salloc("cstat", [128, 16], F32, at=65536)
    ptm = b.salloc("cptm", [128, D], BF16, at=65536 + 128)
    x1t = b.salloc("cx1", [128, D], F32, at=65536 + 128 + 4096)
    assert 65536 + 128 + 4096 + 8192 <= 79072
    xt = [b.salloc(f"cx{k}", [128, D], F32, at=32768 + k * 8192) for k in range(2)]
    sqd = b.salloc("csq", [128, D], F32, at=49152)
    b.xn1 = [b.salloc(f"cxn1_{t}", [128, T_OWN], BF16, at=t * 2048) for t in range(NT_D)]
    assert moff + 32768 + 4096 + 4096 <= 206000

    XL = [b.sem("cXL0"), b.sem("cXL1")]
    PRD = b.sem("cPRD")
    TPD = b.sem("cTPD")
    EVD = b.sem("cEVD")
    X1R = b.sem("cX1R")
    X1S = [b.sem("cX1S0"), b.sem("cX1S1")]
    SQ2 = b.sem("cSQ2")
    VAR2 = b.sem("cVAR2")
    RSTD2 = b.sem("cRSTD2")
    XN2 = b.sem("cXN2")
    TP2 = b.sem("cTP2")
    EV2 = b.sem("cEV2")
    sv = Ser(b, "cSV")
    ss = Ser(b, "cSS")
    C1 = float(1.0 / D)
    xin = b.i["xh"].rearrange("(t p) d -> t p d", p=128)

    with nc.Block() as blk:
        @blk.scalar
        def _(s_):
            for dt in range(NT_D):
                s_.wait_ge(PRD.h, 3 * dt + 1)
                ss.step(s_.activation(gtmp, gtmp, AF.Silu), s_)
                s_.nop().then_inc(PRD.h, 1)
            for t in range(NT_OWN):
                s_.wait_ge(X1R.h, t + 1)
                if t > 0:
                    s_.wait_ge(VAR2.h, t)
                s_.activation(sqd, x1t, AF.Square, accum_out=stt[:, 1:2]).then_inc(SQ2.h, 1)
                s_.wait_ge(VAR2.h, t + 1)
                ss.step(s_.activation(stt[:, 5:6], stt[:, 4:5], AF.Ln, bias=1e-5), s_)
                s_.activation(stt[:, 6:7], stt[:, 5:6], AF.Exp, scale=-0.5).then_inc(RSTD2.h, 1)
            SQ2.v = RSTD2.v = NT_OWN

        @blk.vector
        def _(v):
            for dt in range(NT_D):
                if dt > 0:
                    v.wait_ge(PRD.h, 3 * dt)
                v.tensor_tensor(gtmp, b.acc.rearrange("p (t c) -> p t c", t=NT_D)[:, dt],
                                b.xnc.rearrange("p (t c) -> p t c", t=NT_D)[:, dt],
                                ALU.add).then_inc(PRD.h, 1)
                v.wait_ge(PRD.h, 3 * dt + 2)
                v.tensor_tensor(p3[:, dt], gtmp, b.s_fm.rearrange("p (t c) -> p t c", t=NT_D)[:, dt],
                                ALU.mult).then_inc(PRD.h, 1)
            PRD.v = 3 * NT_D
            for t in range(NT_OWN):
                for dt in range(NT_D):
                    v.wait_ge(TPD.h, t * NT_D + dt + 1)
                    pb = b.P[dt % 2].bitcast(BF16)
                    v.tensor_copy(ptm[:, dt * 128:(dt + 1) * 128], pb[:, 0:128]).then_inc(EVD.h, 1)
                v.wait_ge(EVD.h, (t + 1) * NT_D)
                v.wait_ge(XL[t % 2].h, 16 * (t // 2 + 1))
                if t >= 1:
                    v.wait_ge(X1S[0].h, 16 * t)
                sv.step(v.tensor_tensor(x1t, xt[t % 2], ptm, ALU.add), v)
                nc.vector.engine_nop().then_inc(X1R.h, 1)
                v.wait_ge(SQ2.h, t + 1)
                sv.step(v.reduce_sum(stt[:, 0:1], x1t, AX.X), v)
                sv.step(v.tensor_scalar_mul(stt[:, 2:3], stt[:, 0:1], C1), v)
                sv.step(v.tensor_scalar_mul(stt[:, 3:4], stt[:, 1:2], C1), v)
                sv.step(v.tensor_tensor(stt[:, 4:5], stt[:, 2:3], stt[:, 2:3], ALU.mult), v)
                v.tensor_tensor(stt[:, 4:5], stt[:, 3:4], stt[:, 4:5],
                                ALU.subtract).then_inc(VAR2.h, 1)
                v.wait_ge(RSTD2.h, t + 1)
                if t > 0:
                    v.wait_ge(TP2.h, t * NT_D)
                v.tensor_scalar(xnt, x1t, stt[:, 2:3], stt[:, 6:7],
                                ALU.subtract, ALU.mult).then_inc(XN2.h, 1)
                for dt in range(NT_D):
                    v.wait_ge(TP2.h, t * NT_D + dt + 1)
                    pb = b.P[2 + dt % 2].bitcast(BF16)
                    v.tensor_copy(b.xn1[dt][:, t * 128:(t + 1) * 128], pb[:, 0:128]).then_inc(EV2.h, 1)
            X1R.v = NT_OWN
            VAR2.v = XN2.v = NT_OWN
            EVD.v = EV2.v = NT_OWN * NT_D

        @blk.tensor
        def _(t_):
            t_.wait_ge(PRD.h, 3 * NT_D)
            for t in range(NT_OWN):
                for dt in range(NT_D):
                    li = t * NT_D + dt
                    if li >= 2:
                        t_.wait_ge(EVD.h, li - 1)
                    pb = b.P[dt % 2].bitcast(BF16)
                    t_.transpose(pb[:, 0:128], p3[:, dt, t * 128:(t + 1) * 128],
                                 b.ident).then_inc(TPD.h, 1)
                t_.wait_ge(XN2.h, t + 1)
                for dt in range(NT_D):
                    li = t * NT_D + dt
                    if li >= 2:
                        t_.wait_ge(EV2.h, li - 1)
                    pb = b.P[2 + dt % 2].bitcast(BF16)
                    t_.transpose(pb[:, 0:128], xnt[:, dt * 128:(dt + 1) * 128],
                                 b.ident).then_inc(TP2.h, 1)
            TPD.v = NT_OWN * NT_D
            TP2.v = NT_OWN * NT_D

        @blk.sync
        def _(sync):
            sync.wait_ge(PRD.h, 3 * NT_D)
            for t in range(NT_OWN):
                if t >= 2:
                    sync.wait_ge(XN2.h, t - 1)
                    sync.dma_start(out=b.o["x1"][(t - 2) * 128:(t - 1) * 128, :],
                                   in_=x1t).then_inc(X1S[0].h, 16)
                    X1S[0].v += 16
                sync.dma_start(out=xt[t % 2], in_=xin[4 + t]).then_inc(XL[t % 2].h, 16)
                XL[t % 2].v += 16
            for t in (NT_OWN - 2, NT_OWN - 1):
                sync.wait_ge(XN2.h, t + 1)
                sync.dma_start(out=b.o["x1"][t * 128:(t + 1) * 128, :],
                               in_=x1t).then_inc(X1S[0].h, 16)
                X1S[0].v += 16
            sync.wait_ge(X1S[0].h, X1S[0].v)
    b.end_phase()

def _dump_xn1(b):
    nc = b.nc
    d = b.dbg("xn1", (NT_D, 128, T_OWN), BF16)
    ST = b.sem("dx1ST")
    with nc.Block() as blk:
        @blk.sync
        def _(sync):
            for dt in range(NT_D):
                sync.dma_start(out=d[dt], in_=b.xn1[dt]).then_inc(ST.h, 16)
            ST.v = 16 * NT_D
            sync.wait_ge(ST.h, ST.v)



TWO_PI = float(2.0 * np.pi)
INV_2PI = float(1.0 / TWO_PI)
MAGIC_RND = 12582912.0
PI_F = float(np.float32(np.pi))
PI_HALF = float(np.float32(np.pi / 2))
N_WELEM = HID * D


def _cw_split():
    def chop(x):
        u = np.frombuffer(np.float32(x).tobytes(), np.uint32)[0] & 0xFFFFF000
        return float(np.frombuffer(np.uint32(u).tobytes(), np.float32)[0])
    hi = chop(TWO_PI)
    mid = chop(TWO_PI - hi)
    lo = float(np.float32(TWO_PI - hi - mid))
    return hi, mid, lo


C2PI_HI, C2PI_MID, C2PI_LO = _cw_split()


def emit_l1qkv(b):
    nc = b.nc
    wbuf = [b.salloc(f"jw{k}", [128, NT_D * 256], BF16, at=32768 + k * 8192) for k in range(2)]
    outb = [b.salloc(f"jo{k}", [128, 512], BF16, at=49152 + k * 1024) for k in range(2)]
    wvb = [b.salloc(f"jv{k}", [128, NT_D * 512], BF16, at=81920 + k * 16384) for k in range(2)]

    for which, colbase, dst in (("q", 0, b.qf_d), ("k", D, b.kf_d)):
        WL = [b.sem("j" + which + "WL0", g=True), b.sem("j" + which + "WL1", g=True)]
        MM = b.sem("j" + which + "MM")
        EV = b.sem("j" + which + "EV")
        SO = [b.sem("j" + which + "SO0"), b.sem("j" + which + "SO1")]
        gpo = 4
        ngroups = 8 * gpo
        with nc.Block() as blk:
            @blk.gpsimd
            def _(g):
                for og in range(8):
                    slot = og % 2
                    if og >= 2:
                        g.wait_ge(MM.h, (og - 1) * gpo)
                    g.dma_start(out=wbuf[slot].rearrange("p (t o) -> p t o", t=NT_D),
                                in_=b.i["wqkvT"][:, colbase + og * 256:colbase + (og + 1) * 256]
                                .rearrange("(t p) o -> p t o", p=128)).then_inc(WL[slot].h, 16)
                    WL[slot].v += 16

            @blk.tensor
            def _(t_):
                for gi in range(ngroups):
                    og, ot, ck = gi // gpo, (gi % gpo) // 2, gi % 2
                    slot = og % 2
                    if gi % gpo == 0:
                        t_.wait_ge(WL[slot].h, 16 * (og // 2 + 1))
                    if gi >= 2:
                        t_.wait_ge(EV.h, gi - 1)
                    pp = b.P[gi % 2]
                    w3 = wbuf[slot].rearrange("p (t o) -> p t o", t=NT_D)
                    for dt in range(NT_D):
                        mm = t_.matmul(pp, w3[:, dt, ot * 128:(ot + 1) * 128],
                                       b.xn1[dt][:, ck * 512:(ck + 1) * 512],
                                       start=(dt == 0), stop=(dt == NT_D - 1))
                    mm.then_inc(MM.h, 1)
                    MM.v += 1

            @blk.vector
            def _(v):
                for gi in range(ngroups):
                    slot = gi % 2
                    v.wait_ge(MM.h, gi + 1)
                    if gi >= 2:
                        v.wait_ge(SO[slot].h, 16 * (gi // 2))
                    v.tensor_copy(outb[slot], b.P[gi % 2]).then_inc(EV.h, 1)
                    EV.v += 1

            @blk.sync
            def _(sync):
                for gi in range(ngroups):
                    og, ot, ck = gi // gpo, (gi % gpo) // 2, gi % 2
                    head = og * 2 + ot
                    slot = gi % 2
                    sync.wait_ge(EV.h, gi + 1)
                    sync.dma_start(out=dst[head][:, ck * 512:(ck + 1) * 512],
                                   in_=outb[slot]).then_inc(SO[slot].h, 16)
                    SO[slot].v += 16
                sync.wait_ge(SO[0].h, SO[0].v)
                sync.wait_ge(SO[1].h, SO[1].v)
        b.end_phase()

    WLV = [b.sem("jvWL0", g=True), b.sem("jvWL1", g=True)]
    MMV = b.sem("jvMM")
    EVV = b.sem("jvEV")
    SOV = [b.sem("jvSO0"), b.sem("jvSO1")]
    ngroups = 4 * NT_OWN
    with nc.Block() as blk:
        @blk.gpsimd
        def _(g):
            for dvc in range(4):
                slot = dvc % 2
                if dvc >= 2:
                    g.wait_ge(MMV.h, (dvc - 1) * NT_OWN)
                g.dma_start(out=wvb[slot].rearrange("p (t o) -> p t o", t=NT_D),
                            in_=b.i["wqkvT"][:, 2 * D + dvc * 512:2 * D + (dvc + 1) * 512]
                            .rearrange("(t p) o -> p t o", p=128)).then_inc(WLV[slot].h, 16)
                WLV[slot].v += 16

        @blk.tensor
        def _(t_):
            for gi in range(ngroups):
                dvc, tt = gi // NT_OWN, gi % NT_OWN
                slot = dvc % 2
                if gi % NT_OWN == 0:
                    t_.wait_ge(WLV[slot].h, 16 * (dvc // 2 + 1))
                if gi >= 2:
                    t_.wait_ge(EVV.h, gi - 1)
                pp = b.P[gi % 2]
                w3 = wvb[slot].rearrange("p (t o) -> p t o", t=NT_D)
                for dt in range(NT_D):
                    mm = t_.matmul(pp, b.xn1[dt][:, tt * 128:(tt + 1) * 128],
                                   w3[:, dt], start=(dt == 0), stop=(dt == NT_D - 1))
                mm.then_inc(MMV.h, 1)
                MMV.v += 1

        @blk.vector
        def _(v):
            for gi in range(ngroups):
                slot = gi % 2
                v.wait_ge(MMV.h, gi + 1)
                if gi >= 2:
                    v.wait_ge(SOV[slot].h, 16 * (gi // 2))
                v.tensor_copy(outb[slot], b.P[gi % 2]).then_inc(EVV.h, 1)
                EVV.v += 1

        @blk.sync
        def _(sync):
            for gi in range(ngroups):
                dvc, tt = gi // NT_OWN, gi % NT_OWN
                slot = gi % 2
                sync.wait_ge(EVV.h, gi + 1)
                sync.dma_start(out=b.vt_d[tt][:, dvc * 512:(dvc + 1) * 512],
                               in_=outb[slot]).then_inc(SOV[slot].h, 16)
                SOV[slot].v += 16
            sync.wait_ge(SOV[0].h, SOV[0].v)
            sync.wait_ge(SOV[1].h, SOV[1].v)
    b.end_phase()


def emit_l1tables(b):
    nc = b.nc
    b.cosT = b.salloc("cosT", [128, H * T_OWN], BF16, at=114688)
    b.sinT = b.salloc("sinT", [128, H * T_OWN], BF16, at=147456)
    th_sb = b.salloc("tth", [1, H * 128], F32, at=32768)
    idx_sb = b.salloc("tidx", [1, T_OWN], F32, at=40960)
    t1 = b.salloc("tt1", [128, 512], F32, at=45056)
    t2 = b.salloc("tt2", [128, 512], F32, at=47104)
    LD = b.sem("tLD")
    ANG = b.sem("tANG")
    RED = b.sem("tRED")
    SIN = b.sem("tSIN")
    sv = Ser(b, "tSV")

    with nc.Block() as blk:
        @blk.sync
        def _(sync):
            sync.dma_start(out=th_sb, in_=b.i["thetarow"]).then_inc(LD.h, 16)
            sync.dma_start(out=idx_sb, in_=b.i["idxrow"]).then_inc(LD.h, 16)
            LD.v = 32

        @blk.tensor
        def _(t_):
            t_.wait_ge(LD.h, 32)
            for h in range(H):
                if h > 0:
                    t_.wait_ge(RED.h, 4 * h)
                t_.matmul(b.P[6], th_sb[:, h * 128:(h + 1) * 128],
                          idx_sb[:, 0:512], start=True, stop=True)
                t_.matmul(b.P[7], th_sb[:, h * 128:(h + 1) * 128],
                          idx_sb[:, 512:1024], start=True,
                          stop=True).then_inc(ANG.h, 1)
                ANG.v += 1

        @blk.vector
        def _(v):
            for h in range(H):
                v.wait_ge(ANG.h, h + 1)
                for trig in range(2):
                    for cc in range(2):
                        gidx = (h * 2 + trig) * 2 + cc
                        ps = b.P[6 + cc]
                        if gidx > 0:
                            v.wait_ge(SIN.h, gidx)
                        bias = 0.25 if trig == 1 else 0.0
                        sv.step(v.tensor_scalar(t1, ps, INV_2PI, bias,
                                                ALU.mult, ALU.add), v)
                        sv.step(v.tensor_scalar_add(t1, t1, MAGIC_RND), v)
                        sv.step(v.tensor_scalar_add(t1, t1, -MAGIC_RND), v)
                        sv.step(v.scalar_tensor_tensor(t2, t1, -C2PI_HI, ps,
                                                       ALU.mult, ALU.add), v)
                        sv.step(v.scalar_tensor_tensor(t2, t1, -C2PI_MID, t2,
                                                       ALU.mult, ALU.add), v)
                        sv.step(v.scalar_tensor_tensor(t2, t1, -C2PI_LO, t2,
                                                       ALU.mult, ALU.add), v)
                        if trig == 1:
                            sv.step(v.tensor_scalar_add(t2, t2, PI_HALF), v)
                        v.tensor_scalar(t2, t2, PI_F, -PI_F,
                                        ALU.min, ALU.max).then_inc(RED.h, 1)
                        RED.v += 1

        @blk.scalar
        def _(s_):
            for h in range(H):
                for trig in range(2):
                    for cc in range(2):
                        gidx = (h * 2 + trig) * 2 + cc
                        s_.wait_ge(RED.h, gidx + 1)
                        base = h * T_OWN + cc * 512
                        dstv = (b.sinT if trig == 0 else b.cosT)[:, base:base + 512]
                        s_.activation(dstv, t2, AF.Sin).then_inc(SIN.h, 1)
                        SIN.v += 1
    b.end_phase()


def emit_l1scan(b):
    nc = b.nc
    qh = b.salloc("sqh", [128, T_OWN], BF16, at=32768)
    kh = b.salloc("skh", [128, T_OWN], BF16, at=34816)
    vh = b.salloc("svh", [128, NCH * 128], BF16, at=36864)
    qdA = b.salloc("sqdA", [128, T_OWN], BF16, at=38912)
    qdB = b.salloc("sqdB", [128, T_OWN], BF16, at=40960)
    kdA = b.salloc("skdA", [128, T_OWN], BF16, at=43008)
    kdB = b.salloc("skdB", [128, T_OWN], BF16, at=45056)
    qwA = b.salloc("sqwA", [128, T_OWN], BF16, at=47104)
    qwB = b.salloc("sqwB", [128, T_OWN], BF16, at=49152)
    dec = b.salloc("sdec", [128, 128], F32, at=51200)
    qdecb = b.salloc("sqdecb", [128, 128], F32, at=51712)
    kdecb = b.salloc("skdecb", [128, 128], F32, at=52224)
    qrow = b.salloc("sqrow", [1, H * 128], F32, at=52736)
    krow = b.salloc("skrow", [1, H * 128], F32, at=60928)
    ones1f = b.salloc("sones1f", [1, 128], F32, at=69120)
    Pb = b.salloc("sPb", [128, 128], BF16, at=69632)
    kwA = b.salloc("skwA", [128, 128], BF16, at=69888)
    kwB = b.salloc("skwB", [128, 128], BF16, at=70144)
    kwAT = b.salloc("skwAT", [128, 128], BF16, at=70400)
    kwBT = b.salloc("skwBT", [128, 128], BF16, at=70656)
    stA = b.salloc("sstA", [128, 128], F32, at=70912)
    stB = b.salloc("sstB", [128, 128], F32, at=71424)
    stAb = b.salloc("sstAb", [128, 128], BF16, at=71936)
    stBb = b.salloc("sstBb", [128, 128], BF16, at=72192)
    yob = b.salloc("syob", [128, 128], BF16, at=72448)
    slopes = np.exp(-_slopes() * CHUNK)

    QHL = b.sem("scQHL", g=True)
    KHL = b.sem("scKHL", g=True)
    VHL = b.sem("scVHL", g=True)
    MS1 = b.sem("scMS1")
    CL = b.sem("scCL")
    DECL = b.sem("scDECL")
    QWS = b.sem("scQWS")
    SCST = b.sem("scSCST")
    YS = b.sem("scYS")
    QD = b.sem("scQD")
    SCP = b.sem("scSCP")
    PBE = b.sem("scPBE")
    KW = b.sem("scKW")
    TPK = b.sem("scTPK")
    KTE = b.sem("scKTE")
    OV = b.sem("scOV")
    YEv = b.sem("scYE")
    UPD = b.sem("scUPD")
    STB = b.sem("scSTB")
    SCS = b.sem("scSCS")
    sv = Ser(b, "scSV")

    with nc.Block() as blk:
        @blk.gpsimd
        def _(g):
            g.memset(ones1f, 1.0).then_inc(MS1.h, 1)
            MS1.v = 1
            for h in range(H):
                if h >= 1:
                    g.wait_ge(QD.h, 5 * h)
                g.dma_start(out=qh, in_=b.qf_d[h]).then_inc(QHL.h, 16)
                g.dma_start(out=kh, in_=b.kf_d[h][:, 0:T_OWN]).then_inc(KHL.h, 16)
                QHL.v += 16
                KHL.v += 16
                if h >= 1:
                    g.wait_ge(SCP.h, 17 * h)
                g.dma_start(out=vh.rearrange("p (t c) -> p t c", t=NCH),
                            in_=b.vt_d[0:NCH, :, h * 128:(h + 1) * 128]
                            .rearrange("t p c -> p t c")).then_inc(VHL.h, 16)
                VHL.v += 16

        @blk.sync
        def _(sync):
            sync.dma_start(out=qrow, in_=b.i["qdecay"]).then_inc(CL.h, 16)
            sync.dma_start(out=krow, in_=b.i["kdecay"]).then_inc(CL.h, 16)
            CL.v = 32
            for h in range(H):
                sync.dma_start(out=dec, in_=b.i["diagdecT"][h]).then_inc(DECL.h, 16)
                DECL.v += 16
                for j in range(NCH):
                    sync.wait_ge(YEv.h, h * NCH + j + 1)
                    sync.dma_start(out=b.o["ylocal"][h * 128:(h + 1) * 128,
                                                     j * 128:(j + 1) * 128],
                                   in_=yob).then_inc(YS.h, 16)
                    YS.v += 16
                sync.wait_ge(QD.h, 5 * (h + 1))
                sync.dma_start(out=b.o["qdw"][h * 256:h * 256 + 128, :],
                               in_=qwA).then_inc(QWS.h, 16)
                sync.dma_start(out=b.o["qdw"][h * 256 + 128:h * 256 + 256, :],
                               in_=qwB).then_inc(QWS.h, 16)
                QWS.v += 32
                sync.wait_ge(SCS.h, h + 1)
                sync.dma_start(out=b.o["S_c"][h * 256:h * 256 + 128, :],
                               in_=stA).then_inc(SCST.h, 16)
                sync.dma_start(out=b.o["S_c"][h * 256 + 128:h * 256 + 256, :],
                               in_=stB).then_inc(SCST.h, 16)
                SCST.v += 32
            sync.wait_ge(YS.h, YS.v)
            sync.wait_ge(QWS.h, QWS.v)
            sync.wait_ge(SCST.h, SCST.v)

        @blk.tensor
        def _(t_):
            pk = b.P[4].bitcast(BF16)
            t_.wait_ge(CL.h, 32)
            t_.wait_ge(MS1.h, 1)
            for h in range(H):
                if h:
                    t_.wait_ge(QD.h, 5 * h)
                t_.matmul(b.P[5][:, 0:128], ones1f, qrow[:, h * 128:(h + 1) * 128],
                          start=True, stop=True)
                t_.matmul(b.P[5][:, 128:256], ones1f, krow[:, h * 128:(h + 1) * 128],
                          start=True, stop=True).then_inc(SCP.h, 1)
                SCP.v += 1
                t_.wait_ge(QD.h, 5 * h + 5)
                t_.wait_ge(VHL.h, 16 * (h + 1))
                for j in range(NCH):
                    jj = h * NCH + j
                    sl = slice(j * 128, (j + 1) * 128)
                    if jj:
                        t_.wait_ge(PBE.h, jj)
                    t_.matmul(b.P[0][:, 0:128], kdA[:, sl], qdA[:, sl],
                              start=True, stop=False)
                    t_.matmul(b.P[0][:, 0:128], kdB[:, sl], qdB[:, sl],
                              start=False, stop=True).then_inc(SCP.h, 1)
                    SCP.v += 1
                    t_.wait_ge(KW.h, 2 * jj + 2)
                    if jj:
                        t_.wait_ge(KTE.h, 2 * jj)
                    t_.transpose(pk[:, 0:128], kwA, b.ident)
                    t_.transpose(pk[:, 128:256], kwB, b.ident).then_inc(TPK.h, 1)
                    TPK.v += 1
                    t_.wait_ge(PBE.h, jj + 1)
                    t_.wait_ge(STB.h, 25 * h + 3 * j + 1)
                    if jj:
                        t_.wait_ge(YEv.h, jj)
                    t_.matmul(b.P[1][:, 0:128], vh[:, sl], Pb, start=True, stop=False)
                    t_.matmul(b.P[1][:, 0:128], stAb, qwA[:, sl], start=False, stop=False)
                    t_.matmul(b.P[1][:, 0:128], stBb, qwB[:, sl],
                              start=False, stop=True).then_inc(OV.h, 1)
                    OV.v += 1
                    t_.wait_ge(KTE.h, 2 * jj + 2)
                    if jj:
                        t_.wait_ge(UPD.h, 2 * jj)
                    t_.matmul(b.P[2][:, 0:128], kwAT, vh[:, sl], start=True, stop=True)
                    t_.matmul(b.P[3][:, 0:128], kwBT, vh[:, sl],
                              start=True, stop=True).then_inc(SCP.h, 1)
                    SCP.v += 1

        @blk.vector
        def _(v):
            pk = b.P[4].bitcast(BF16)
            for h in range(H):
                bd = float(slopes[h])
                cs = b.cosT[:, h * T_OWN:(h + 1) * T_OWN]
                sn = b.sinT[:, h * T_OWN:(h + 1) * T_OWN]
                v.wait_ge(DECL.h, 16 * (h + 1))
                v.wait_ge(SCP.h, 17 * h + 1)
                sv.step(v.tensor_copy(qdecb, b.P[5][:, 0:128]), v)
                sv.step(v.tensor_copy(kdecb, b.P[5][:, 128:256]), v)
                v.wait_ge(QHL.h, 16 * (h + 1))
                v.tensor_tensor(qdA, qh, cs, ALU.mult).then_inc(QD.h, 1)
                v.tensor_tensor(qdB, qh, sn, ALU.mult).then_inc(QD.h, 1)
                v.wait_ge(KHL.h, 16 * (h + 1))
                v.tensor_tensor(kdA, kh, cs, ALU.mult).then_inc(QD.h, 1)
                sv.step(v.tensor_tensor(kdB, kh, sn, ALU.mult), v)
                if h:
                    v.wait_ge(QWS.h, 32 * h)
                for j in range(NCH):
                    sl = slice(j * 128, (j + 1) * 128)
                    v.tensor_tensor(qwA[:, sl], qdA[:, sl], qdecb, ALU.mult)
                    ins = v.tensor_tensor(qwB[:, sl], qdB[:, sl], qdecb, ALU.mult)
                    if j == NCH - 1:
                        sv.step(ins, v)
                v.engine_nop().then_inc(QD.h, 2)
                QD.v += 5
                if h:
                    v.wait_ge(SCST.h, 32 * h)
                v.memset(stA, 0.0)
                v.memset(stB, 0.0)
                v.memset(stAb, 0.0)
                sv.step(v.memset(stBb, 0.0), v)
                v.engine_nop().then_inc(STB.h, 1)
                STB.v += 1
                for j in range(NCH):
                    jj = h * NCH + j
                    sl = slice(j * 128, (j + 1) * 128)
                    v.wait_ge(SCP.h, 17 * h + 2 + 2 * j)
                    if jj:
                        v.wait_ge(OV.h, jj)
                    v.tensor_tensor(Pb, b.P[0][:, 0:128], dec, ALU.mult).then_inc(PBE.h, 1)
                    PBE.v += 1
                    sv.step(v.tensor_tensor(kwA, kdA[:, sl], kdecb, ALU.mult), v)
                    v.tensor_tensor(kwB, kdB[:, sl], kdecb, ALU.mult).then_inc(KW.h, 2)
                    KW.v += 2
                    v.wait_ge(TPK.h, jj + 1)
                    v.tensor_copy(kwAT, pk[:, 0:128])
                    v.tensor_copy(kwBT, pk[:, 128:256]).then_inc(KTE.h, 2)
                    KTE.v += 2
                    v.wait_ge(OV.h, jj + 1)
                    if jj:
                        v.wait_ge(YS.h, 16 * jj)
                    v.tensor_copy(yob, b.P[1][:, 0:128]).then_inc(YEv.h, 1)
                    YEv.v += 1
                    v.wait_ge(SCP.h, 17 * h + 3 + 2 * j)
                    sv.step(v.tensor_scalar_mul(stA, stA, bd), v)
                    sv.step(v.tensor_tensor(stA, b.P[2][:, 0:128], stA, ALU.add), v)
                    sv.step(v.tensor_scalar_mul(stB, stB, bd), v)
                    sv.step(v.tensor_tensor(stB, b.P[3][:, 0:128], stB, ALU.add), v)
                    v.engine_nop().then_inc(UPD.h, 2)
                    UPD.v += 2
                    if j < NCH - 1:
                        sv.step(v.tensor_copy(stAb, stA), v)
                        sv.step(v.tensor_copy(stBb, stB), v)
                        v.engine_nop().then_inc(STB.h, 3)
                        STB.v += 3
                    else:
                        v.engine_nop().then_inc(STB.h, 3)
                        STB.v += 3
                        v.engine_nop().then_inc(SCS.h, 1)
                        SCS.v += 1
    b.end_phase()


def emit_l1scanq(b):
    nc = b.nc
    qh = b.salloc("sqh", [128, T_OWN], BF16, at=32768)
    kh = b.salloc("skh", [128, T_OWN], BF16, at=34816)
    vh = b.salloc("svh", [128, NCH * 128], BF16, at=36864)
    qdA = b.salloc("sqdA", [128, T_OWN], BF16, at=38912)
    qdB = b.salloc("sqdB", [128, T_OWN], BF16, at=40960)
    kdA = b.salloc("skdA", [128, T_OWN], BF16, at=43008)
    kdB = b.salloc("skdB", [128, T_OWN], BF16, at=45056)
    qwA = b.salloc("sqwA", [128, T_OWN], BF16, at=47104)
    qwB = b.salloc("sqwB", [128, T_OWN], BF16, at=49152)
    dec = b.salloc("sdec", [128, 128], F32, at=51200)
    qdecb = b.salloc("sqdecb", [128, 128], F32, at=51712)
    kdecb = b.salloc("skdecb", [128, 128], F32, at=52224)
    qrow = b.salloc("sqrow", [1, H * 128], F32, at=52736)
    krow = b.salloc("skrow", [1, H * 128], F32, at=60928)
    ones1f = b.salloc("sones1f", [1, 128], F32, at=69120)
    Pb = b.salloc("sPb", [128, 128], BF16, at=69632)
    kwA = b.salloc("skwA", [128, 128], BF16, at=69888)
    kwB = b.salloc("skwB", [128, 128], BF16, at=70144)
    kwAT = b.salloc("skwAT", [128, 128], BF16, at=70400)
    kwBT = b.salloc("skwBT", [128, 128], BF16, at=70656)
    stA = b.salloc("sstA", [128, 128], F32, at=70912)
    stB = b.salloc("sstB", [128, 128], F32, at=71424)
    stAb = b.salloc("sstAb", [128, 128], BF16, at=71936)
    stBb = b.salloc("sstBb", [128, 128], BF16, at=72192)
    yob = b.salloc("syob", [128, 128], BF16, at=72448)
    t2a = b.salloc("ut2a", [128, 2048], BF16, at=72704)
    t2b = b.salloc("ut2b", [128, 2048], BF16, at=76800)
    wfA = [b.salloc(f"uwa{k}", [128, 2752], F32, at=81920 + k * 11008)
           for k in range(2)]
    accs = b.salloc("uaccs", [128, 4], F32, at=103936)
    tmpc = b.salloc("utmpc", [128, 2], F32, at=104096)
    onesc = b.salloc("uonesc", [128, 1], F32, at=104224)
    onesr = b.salloc("uonesr", [1, 128], F32, at=104288)
    tot = b.salloc("utot", [1, 4], F32, at=104832)
    b.sq_sb = b.salloc("usq", [128, 12], F32, at=180224)
    wfB = [b.salloc(f"uwb{k}", [128, 2048], F32, at=180288 + k * 8192)
           for k in range(2)]
    wqb = [b.salloc(f"uwqb{k}", [128, 2048], BF16, at=196672 + k * 4096)
           for k in range(2)]
    slopes = np.exp(-_slopes() * CHUNK)

    jobsA = []
    for m, nm in ((0, "w1T"), (1, "w3T")):
        for bt in range(NT_D):
            rs = b.i[nm][bt * 128:(bt + 1) * 128, :]
            jobsA.append((m, rs[:, 0:2752], 128, 2752))
            jobsA.append((m, rs[:, 2752:HID], 128, HID - 2752))
    for bt in range(NT_HID):
        r0 = bt * 128
        rows = min(128, HID - r0)
        jobsA.append((2, b.i["w2T"][r0:r0 + rows, :], rows, D))
    cumA = [32, 64, 107]
    jobsB = []
    for m, nm, dst in ((0, "w1T", b.w1q_d), (1, "w3T", b.w3q_d)):
        for bt in range(NT_D):
            for c in range(3):
                c0 = c * 2048
                dataw = min(2048, HID - c0)
                padw = 0 if c < 2 else HID_P - HID
                jobsB.append((m, b.i[nm][bt * 128:(bt + 1) * 128, c0:c0 + dataw],
                              dst[bt * 128:(bt + 1) * 128, c0:c0 + dataw + padw],
                              128, dataw, padw))
    for bt in range(NT_HID):
        r0 = bt * 128
        rows = min(128, HID - r0)
        jobsB.append((2, b.i["w2T"][r0:r0 + rows, :], b.w2q_d[r0:r0 + rows, :],
                      rows, D, 0))
    NB = len(jobsB)

    headA = {h: [] for h in range(H)}
    for j in range(32):
        headA[j // 8].append(j)
    for j in range(32, 64):
        headA[4 + (j - 32) // 8].append(j)
    for j in range(64, 107):
        headA[8 + min((j - 64) // 11, 3)].append(j)
    headB = {h: [] for h in range(H)}
    for k in range(48):
        headB[5 + k // 12].append(k)
    for k in range(48, 96):
        headB[9 + (k - 48) // 12].append(k)
    for k in range(96, NB):
        headB[13 + min((k - 96) // 15, 2)].append(k)
    tot_head = {3: 0, 7: 1, 11: 2}

    QHL = b.sem("scQHL", g=True)
    KHL = b.sem("scKHL", g=True)
    VHL = b.sem("scVHL", g=True)
    AWL = [b.sem("scAWL0", g=True), b.sem("scAWL1", g=True)]
    BWL = [b.sem("scBWL0", g=True), b.sem("scBWL1", g=True)]
    MS1 = b.sem("scMS1")
    CL = b.sem("scCL")
    DECL = b.sem("scDECL")
    QWS = b.sem("scQWS")
    SCST = b.sem("scSCST")
    YS = b.sem("scYS")
    QD = b.sem("scQD")
    SCP = b.sem("scSCP")
    PBE = b.sem("scPBE")
    KW = b.sem("scKW")
    TPK = b.sem("scTPK")
    KTE = b.sem("scKTE")
    OV = b.sem("scOV")
    YEv = b.sem("scYE")
    UPD = b.sem("scUPD")
    STB = b.sem("scSTB")
    SCS = b.sem("scSCS")
    ARD = b.sem("qARD")
    TOT = b.sem("qTOT")
    TCP = b.sem("qTCP")
    SQ = b.sem("qSQ")
    SGB = b.sem("qSGB")
    ADB = b.sem("qADB")
    BST = [b.sem("qBST0"), b.sem("qBST1")]
    ZP = b.sem("qZP")
    sv = Ser(b, "scSV")

    with nc.Block() as blk:
        @blk.gpsimd
        def _(g):
            g.memset(ones1f, 1.0)
            g.memset(accs, 0.0)
            g.memset(tmpc, 0.0)
            g.memset(onesc, 1.0)
            g.memset(onesr, 1.0).then_inc(MS1.h, 1)
            MS1.v = 1
            for h in range(H):
                if h >= 1:
                    g.wait_ge(QD.h, 5 * h)
                g.dma_start(out=qh, in_=b.qf_d[h]).then_inc(QHL.h, 16)
                g.dma_start(out=kh, in_=b.kf_d[h][:, 0:T_OWN]).then_inc(KHL.h, 16)
                QHL.v += 16
                KHL.v += 16
                if h >= 1:
                    g.wait_ge(SCP.h, 17 * h)
                g.dma_start(out=vh.rearrange("p (t c) -> p t c", t=NCH),
                            in_=b.vt_d[0:NCH, :, h * 128:(h + 1) * 128]
                            .rearrange("t p c -> p t c")).then_inc(VHL.h, 16)
                VHL.v += 16
                for j in headA[h]:
                    m, ap, rows, width = jobsA[j]
                    slot = j % 2
                    if j >= 2:
                        g.wait_ge(ARD.h, j - 1)
                    g.dma_start(out=wfA[slot][0:rows, 0:width],
                                in_=ap).then_inc(AWL[slot].h, 16)
                    AWL[slot].v += 16
                for k in headB[h]:
                    m, ap, dstp, rows, dataw, padw = jobsB[k]
                    slot = k % 2
                    if k >= 2:
                        g.wait_ge(SGB.h, 2 * (k - 1))
                    g.dma_start(out=wfB[slot][0:rows, 0:dataw],
                                in_=ap).then_inc(BWL[slot].h, 16)
                    BWL[slot].v += 16

        @blk.sync
        def _(sync):
            sync.dma_start(out=qrow, in_=b.i["qdecay"]).then_inc(CL.h, 16)
            sync.dma_start(out=krow, in_=b.i["kdecay"]).then_inc(CL.h, 16)
            CL.v = 32
            for h in range(H):
                sync.dma_start(out=dec, in_=b.i["diagdecT"][h]).then_inc(DECL.h, 16)
                DECL.v += 16
                for j in range(NCH):
                    sync.wait_ge(YEv.h, h * NCH + j + 1)
                    sync.dma_start(out=b.o["ylocal"][h * 128:(h + 1) * 128,
                                                     j * 128:(j + 1) * 128],
                                   in_=yob).then_inc(YS.h, 16)
                    YS.v += 16
                sync.wait_ge(QD.h, 5 * (h + 1))
                sync.dma_start(out=b.o["qdw"][h * 256:h * 256 + 128, :],
                               in_=qwA).then_inc(QWS.h, 16)
                sync.dma_start(out=b.o["qdw"][h * 256 + 128:h * 256 + 256, :],
                               in_=qwB).then_inc(QWS.h, 16)
                QWS.v += 32
                sync.wait_ge(SCS.h, h + 1)
                sync.dma_start(out=b.o["S_c"][h * 256:h * 256 + 128, :],
                               in_=stA).then_inc(SCST.h, 16)
                sync.dma_start(out=b.o["S_c"][h * 256 + 128:h * 256 + 256, :],
                               in_=stB).then_inc(SCST.h, 16)
                SCST.v += 32
                for k in headB[h]:
                    m, ap, dstp, rows, dataw, padw = jobsB[k]
                    pslot = k % 2
                    sync.wait_ge(ADB.h, k + 1)
                    sync.dma_start(out=dstp,
                                   in_=wqb[pslot][0:rows, 0:dataw + padw]
                                   ).then_inc(BST[pslot].h, 16)
                    BST[pslot].v += 16
            sync.wait_ge(ZP.h, 1)
            sync.dma_start(out=b.w2q_d[HID:HID_P, :],
                           in_=t2a[0:HID_P - HID, 0:D]).then_inc(BST[0].h, 16)
            BST[0].v += 16
            sync.wait_ge(YS.h, YS.v)
            sync.wait_ge(QWS.h, QWS.v)
            sync.wait_ge(SCST.h, SCST.v)
            sync.wait_ge(BST[0].h, BST[0].v)
            sync.wait_ge(BST[1].h, BST[1].v)

        @blk.tensor
        def _(t_):
            pk = b.P[4].bitcast(BF16)
            t_.wait_ge(CL.h, 32)
            t_.wait_ge(MS1.h, 1)
            for h in range(H):
                if h:
                    t_.wait_ge(QD.h, 5 * h)
                t_.matmul(b.P[5][:, 0:128], ones1f, qrow[:, h * 128:(h + 1) * 128],
                          start=True, stop=True)
                t_.matmul(b.P[5][:, 128:256], ones1f, krow[:, h * 128:(h + 1) * 128],
                          start=True, stop=True).then_inc(SCP.h, 1)
                SCP.v += 1
                t_.wait_ge(QD.h, 5 * h + 5)
                t_.wait_ge(VHL.h, 16 * (h + 1))
                for j in range(NCH):
                    jj = h * NCH + j
                    sl = slice(j * 128, (j + 1) * 128)
                    if jj:
                        t_.wait_ge(PBE.h, jj)
                    t_.matmul(b.P[0][:, 0:128], kdA[:, sl], qdA[:, sl],
                              start=True, stop=False)
                    t_.matmul(b.P[0][:, 0:128], kdB[:, sl], qdB[:, sl],
                              start=False, stop=True).then_inc(SCP.h, 1)
                    SCP.v += 1
                    t_.wait_ge(KW.h, 2 * jj + 2)
                    if jj:
                        t_.wait_ge(KTE.h, 2 * jj)
                    t_.transpose(pk[:, 0:128], kwA, b.ident)
                    t_.transpose(pk[:, 128:256], kwB, b.ident).then_inc(TPK.h, 1)
                    TPK.v += 1
                    t_.wait_ge(PBE.h, jj + 1)
                    t_.wait_ge(STB.h, 25 * h + 3 * j + 1)
                    if jj:
                        t_.wait_ge(YEv.h, jj)
                    t_.matmul(b.P[1][:, 0:128], vh[:, sl], Pb, start=True, stop=False)
                    t_.matmul(b.P[1][:, 0:128], stAb, qwA[:, sl], start=False, stop=False)
                    t_.matmul(b.P[1][:, 0:128], stBb, qwB[:, sl],
                              start=False, stop=True).then_inc(OV.h, 1)
                    OV.v += 1
                    t_.wait_ge(KTE.h, 2 * jj + 2)
                    if jj:
                        t_.wait_ge(UPD.h, 2 * jj)
                    t_.matmul(b.P[2][:, 0:128], kwAT, vh[:, sl], start=True, stop=True)
                    t_.matmul(b.P[3][:, 0:128], kwBT, vh[:, sl],
                              start=True, stop=True).then_inc(SCP.h, 1)
                    SCP.v += 1
                if h in tot_head:
                    m = tot_head[h]
                    t_.wait_ge(ARD.h, cumA[m])
                    t_.matmul(b.P[6][0:1, m:m + 1], accs[:, m:m + 1], onesc,
                              start=True, stop=True).then_inc(TOT.h, 1)
                    TOT.v += 1
                    t_.wait_ge(TCP.h, m + 1)
                    t_.matmul(b.P[7][:, m:m + 1], onesr, tot[:, m:m + 1],
                              start=True, stop=True).then_inc(TOT.h, 1)
                    TOT.v += 1

        @blk.scalar
        def _(s_):
            for h in range(H):
                for k in headB[h]:
                    m, ap, dstp, rows, dataw, padw = jobsB[k]
                    slot = k % 2
                    s_.wait_ge(SQ.h, m + 1)
                    s_.wait_ge(BWL[slot].h, 16 * (k // 2 + 1))
                    if k >= 1:
                        s_.wait_ge(ADB.h, k)
                    win = wfB[slot][0:rows, 0:dataw]
                    nb = b.sq_sb[0:rows, 3 + m:4 + m]
                    pb = b.sq_sb[0:rows, 6 + m:7 + m]
                    s_.activation(t2a[0:rows, 0:dataw], win, AF.Sign, bias=nb
                                  ).then_inc(SGB.h, 1)
                    s_.activation(t2b[0:rows, 0:dataw], win, AF.Sign, bias=pb
                                  ).then_inc(SGB.h, 1)
                    SGB.v += 2

        @blk.vector
        def _(v):
            pk = b.P[4].bitcast(BF16)
            half = 0.5 / N_WELEM
            for h in range(H):
                bd = float(slopes[h])
                cs = b.cosT[:, h * T_OWN:(h + 1) * T_OWN]
                sn = b.sinT[:, h * T_OWN:(h + 1) * T_OWN]
                if h in (4, 8, 12):
                    m = (h - 4) // 4
                    v.wait_ge(TOT.h, 2 * (m + 1))
                    pcol = b.P[7][:, m:m + 1]
                    v.tensor_scalar_mul(b.sq_sb[:, m:m + 1], pcol, half)
                    v.tensor_scalar_mul(b.sq_sb[:, 3 + m:4 + m], pcol, -half)
                    sv.step(v.tensor_scalar_mul(b.sq_sb[:, 6 + m:7 + m], pcol, half), v)
                    v.engine_nop().then_inc(SQ.h, 1)
                    SQ.v += 1
                v.wait_ge(DECL.h, 16 * (h + 1))
                v.wait_ge(SCP.h, 17 * h + 1)
                sv.step(v.tensor_copy(qdecb, b.P[5][:, 0:128]), v)
                sv.step(v.tensor_copy(kdecb, b.P[5][:, 128:256]), v)
                v.wait_ge(QHL.h, 16 * (h + 1))
                v.tensor_tensor(qdA, qh, cs, ALU.mult).then_inc(QD.h, 1)
                v.tensor_tensor(qdB, qh, sn, ALU.mult).then_inc(QD.h, 1)
                v.wait_ge(KHL.h, 16 * (h + 1))
                v.tensor_tensor(kdA, kh, cs, ALU.mult).then_inc(QD.h, 1)
                sv.step(v.tensor_tensor(kdB, kh, sn, ALU.mult), v)
                if h:
                    v.wait_ge(QWS.h, 32 * h)
                for j in range(NCH):
                    sl = slice(j * 128, (j + 1) * 128)
                    v.tensor_tensor(qwA[:, sl], qdA[:, sl], qdecb, ALU.mult)
                    ins = v.tensor_tensor(qwB[:, sl], qdB[:, sl], qdecb, ALU.mult)
                    if j == NCH - 1:
                        sv.step(ins, v)
                v.engine_nop().then_inc(QD.h, 2)
                QD.v += 5
                if h:
                    v.wait_ge(SCST.h, 32 * h)
                v.memset(stA, 0.0)
                v.memset(stB, 0.0)
                v.memset(stAb, 0.0)
                sv.step(v.memset(stBb, 0.0), v)
                v.engine_nop().then_inc(STB.h, 1)
                STB.v += 1
                for j in range(NCH):
                    jj = h * NCH + j
                    sl = slice(j * 128, (j + 1) * 128)
                    v.wait_ge(SCP.h, 17 * h + 2 + 2 * j)
                    if jj:
                        v.wait_ge(OV.h, jj)
                    v.tensor_tensor(Pb, b.P[0][:, 0:128], dec, ALU.mult).then_inc(PBE.h, 1)
                    PBE.v += 1
                    sv.step(v.tensor_tensor(kwA, kdA[:, sl], kdecb, ALU.mult), v)
                    v.tensor_tensor(kwB, kdB[:, sl], kdecb, ALU.mult).then_inc(KW.h, 2)
                    KW.v += 2
                    v.wait_ge(TPK.h, jj + 1)
                    v.tensor_copy(kwAT, pk[:, 0:128])
                    v.tensor_copy(kwBT, pk[:, 128:256]).then_inc(KTE.h, 2)
                    KTE.v += 2
                    v.wait_ge(OV.h, jj + 1)
                    if jj:
                        v.wait_ge(YS.h, 16 * jj)
                    v.tensor_copy(yob, b.P[1][:, 0:128]).then_inc(YEv.h, 1)
                    YEv.v += 1
                    v.wait_ge(SCP.h, 17 * h + 3 + 2 * j)
                    sv.step(v.tensor_scalar_mul(stA, stA, bd), v)
                    sv.step(v.tensor_tensor(stA, b.P[2][:, 0:128], stA, ALU.add), v)
                    sv.step(v.tensor_scalar_mul(stB, stB, bd), v)
                    sv.step(v.tensor_tensor(stB, b.P[3][:, 0:128], stB, ALU.add), v)
                    v.engine_nop().then_inc(UPD.h, 2)
                    UPD.v += 2
                    if j < NCH - 1:
                        sv.step(v.tensor_copy(stAb, stA), v)
                        sv.step(v.tensor_copy(stBb, stB), v)
                        v.engine_nop().then_inc(STB.h, 3)
                        STB.v += 3
                    else:
                        v.engine_nop().then_inc(STB.h, 3)
                        STB.v += 3
                        v.engine_nop().then_inc(SCS.h, 1)
                        SCS.v += 1
                for j in headA[h]:
                    m, ap, rows, width = jobsA[j]
                    slot = j % 2
                    v.wait_ge(AWL[slot].h, 16 * (j // 2 + 1))
                    tc = tmpc[:, j % 2:j % 2 + 1]
                    if rows < 128:
                        sv.step(v.memset(tc, 0.0), v)
                    sv.step(v.tensor_reduce(tc[0:rows, :], wfA[slot][0:rows, 0:width],
                                            AX.X, ALU.add,
                                            apply_absolute_value=True), v)
                    acol = accs[:, m:m + 1]
                    v.tensor_tensor(acol, tc, acol, ALU.add).then_inc(ARD.h, 1)
                    ARD.v += 1
                    v.wait_ge(ARD.h, ARD.v)
                if h in tot_head:
                    m = tot_head[h]
                    v.wait_ge(TOT.h, 2 * m + 1)
                    sv.step(v.tensor_copy(tot[:, m:m + 1], b.P[6][0:1, m:m + 1]), v)
                    v.engine_nop().then_inc(TCP.h, 1)
                    TCP.v += 1
                for k in headB[h]:
                    m, ap, dstp, rows, dataw, padw = jobsB[k]
                    pslot = k % 2
                    v.wait_ge(SGB.h, 2 * (k + 1))
                    if k >= 2:
                        v.wait_ge(BST[pslot].h, 16 * (k // 2))
                    if padw:
                        sv.step(v.memset(wqb[pslot][:, dataw:dataw + padw], 0.0), v)
                    v.tensor_tensor(wqb[pslot][0:rows, 0:dataw], t2a[0:rows, 0:dataw],
                                    t2b[0:rows, 0:dataw], ALU.add).then_inc(ADB.h, 1)
                    ADB.v += 1
            v.wait_ge(ADB.h, NB)
            sv.step(v.memset(t2a[0:HID_P - HID, 0:D], 0.0), v)
            v.engine_nop().then_inc(ZP.h, 1)
            ZP.v += 1
    b.end_phase()


def emit_l1quant(b):
    nc = b.nc
    wf = [b.salloc(f"uw{k}", [128, HID_P], F32, at=32768 + k * 22016) for k in range(2)]
    t2a = b.salloc("ut2a", [128, HID], BF16, at=81920)
    t2b = b.salloc("ut2b", [128, HID], BF16, at=92864)
    wqb = [b.salloc(f"uwqb{k}", [128, HID_P], BF16, at=103808 + k * 11008)
           for k in range(2)]
    accs = b.salloc("uaccs", [128, 4], F32, at=76800)
    tmpc = b.salloc("utmpc", [128, 1], F32, at=76960)
    onesc = b.salloc("uonesc", [128, 1], F32, at=77024)
    onesr = b.salloc("uonesr", [1, 128], F32, at=77088)
    tot = b.salloc("utot", [1, 4], F32, at=77600)
    b.sq_sb = b.salloc("usq", [128, 12], F32, at=180224)

    mats = []
    for mi, nm in enumerate(("w1T", "w3T", "w2T")):
        src = b.i[nm]
        blocks = []
        if nm != "w2T":
            for bt in range(NT_D):
                blocks.append((src[bt * 128:(bt + 1) * 128, :], 128, HID))
        else:
            for bt in range(NT_HID):
                r0 = bt * 128
                rows = min(128, HID - r0)
                blocks.append((src[r0:r0 + rows, :], rows, D))
        mats.append((mi, nm, blocks))

    MS = b.sem("uMS")
    WLD = [b.sem("uWLD0"), b.sem("uWLD1")]
    RD = b.sem("uRD")
    TOT = b.sem("uTOT")
    SQ = b.sem("uSQ")
    SG = b.sem("uSG")
    AD = b.sem("uAD")
    ST = [b.sem("uST0"), b.sem("uST1")]
    sv = Ser(b, "uSV")
    ss = Ser(b, "uSS")

    nblk = sum(len(m[2]) for m in mats)

    with nc.Block() as blk:
        @blk.gpsimd
        def _(g):
            g.memset(accs, 0.0)
            g.memset(onesc, 1.0)
            g.memset(tmpc, 0.0)
            g.memset(onesr, 1.0).then_inc(MS.h, 1)
            MS.v = 1

        @blk.sync
        def _(sync):
            bi = 0
            for mi, nm, blocks in mats:
                for (ap, rows, width) in blocks:
                    slot = bi % 2
                    if bi >= 2:
                        sync.wait_ge(RD.h, bi - 1)
                    sync.dma_start(out=wf[slot][0:rows, 0:width],
                                   in_=ap).then_inc(WLD[slot].h, 16)
                    WLD[slot].v += 16
                    bi += 1

        @blk.vector
        def _(v):
            v.wait_ge(MS.h, 1)
            bi = 0
            for mi, nm, blocks in mats:
                for (ap, rows, width) in blocks:
                    slot = bi % 2
                    v.wait_ge(WLD[slot].h, 16 * (bi // 2 + 1))
                    if rows < 128:
                        sv.step(v.memset(tmpc, 0.0), v)
                    sv.step(v.tensor_reduce(tmpc[0:rows, :], wf[slot][0:rows, 0:width],
                                            AX.X, ALU.add,
                                            apply_absolute_value=True), v)
                    acol = accs[:, mi:mi + 1]
                    v.tensor_tensor(acol, tmpc, acol, ALU.add).then_inc(RD.h, 1)
                    RD.v += 1
                    v.wait_ge(RD.h, RD.v)
                    bi += 1
            v.wait_ge(TOT.h, 1)
            sv.step(v.tensor_copy(tot[:, 0:3], b.P[6][0:1, 0:3]), v)
            v.engine_nop().then_inc(RD.h, 1)
            RD.v += 1
            v.wait_ge(TOT.h, 2)
            half = 0.5 / N_WELEM
            for mi in range(3):
                pcol = b.P[7][:, mi:mi + 1]
                v.tensor_scalar_mul(b.sq_sb[:, mi:mi + 1], pcol, half)
                v.tensor_scalar_mul(b.sq_sb[:, 3 + mi:4 + mi], pcol, -half)
                ins = v.tensor_scalar_mul(b.sq_sb[:, 6 + mi:7 + mi], pcol, half)
                sv.step(ins, v)
            v.engine_nop().then_inc(SQ.h, 1)
            SQ.v += 1

        @blk.tensor
        def _(t_):
            t_.wait_ge(RD.h, nblk)
            for mi in range(3):
                mm = t_.matmul(b.P[6][0:1, mi:mi + 1], accs[:, mi:mi + 1], onesc,
                               start=True, stop=True)
            mm.then_inc(TOT.h, 1)
            TOT.v += 1
            t_.wait_ge(RD.h, nblk + 1)
            t_.matmul(b.P[7][:, 0:3], onesr, tot[:, 0:3],
                      start=True, stop=True).then_inc(TOT.h, 1)
            TOT.v += 1
            t_.wait_ge(SQ.h, 1)
    b.end_phase()

    MSB = b.sem("uMSB")
    WLB = [b.sem("uWLB0"), b.sem("uWLB1")]
    SGB = b.sem("uSGB")
    ADB = b.sem("uADB")
    STB2 = [b.sem("uSTB0"), b.sem("uSTB1")]
    ZP = b.sem("uZP")
    svb = Ser(b, "uSVB")
    ssb = Ser(b, "uSSB")
    dsts = {"w1T": b.w1q_d, "w3T": b.w3q_d, "w2T": b.w2q_d}

    with nc.Block() as blk:
        @blk.gpsimd
        def _(g):
            g.memset(wqb[0][:, HID:HID_P], 0.0)
            g.memset(wqb[1][:, HID:HID_P], 0.0).then_inc(MSB.h, 1)
            MSB.v = 1

        @blk.sync
        def _(sync):
            flat = []
            for mi, nm, blocks in mats:
                r0 = 0
                for (ap, rows, width) in blocks:
                    if nm != "w2T":
                        dd = dsts[nm][r0:r0 + rows, :]
                    else:
                        dd = dsts[nm][r0:r0 + rows, 0:width]
                    flat.append((ap, dd, rows, width, nm))
                    r0 += rows

            def store(bi):
                ap, dd, rows, width, nm = flat[bi]
                pslot = bi % 2
                sync.wait_ge(ADB.h, bi + 1)
                srcq = (wqb[pslot][0:rows, :] if nm != "w2T"
                        else wqb[pslot][0:rows, 0:width])
                sync.dma_start(out=dd, in_=srcq).then_inc(STB2[pslot].h, 16)
                STB2[pslot].v += 16

            for bi, (ap, dd, rows, width, nm) in enumerate(flat):
                slot = bi % 2
                if bi >= 2:
                    sync.wait_ge(SGB.h, 2 * (bi - 1))
                sync.dma_start(out=wf[slot][0:rows, 0:width],
                               in_=ap).then_inc(WLB[slot].h, 16)
                WLB[slot].v += 16
                if bi >= 1:
                    store(bi - 1)
            store(len(flat) - 1)
            sync.wait_ge(ZP.h, 1)
            sync.dma_start(out=b.w2q_d[HID:HID_P, :],
                           in_=t2a[0:HID_P - HID, 0:D]).then_inc(STB2[0].h, 16)
            STB2[0].v += 16
            sync.wait_ge(STB2[0].h, STB2[0].v)
            sync.wait_ge(STB2[1].h, STB2[1].v)

        @blk.scalar
        def _(s_):
            s_.wait_ge(MSB.h, 1)
            bi = 0
            for mi, nm, blocks in mats:
                nb = b.sq_sb[:, 3 + mi:4 + mi]
                pb = b.sq_sb[:, 6 + mi:7 + mi]
                for (ap, rows, width) in blocks:
                    slot = bi % 2
                    s_.wait_ge(WLB[slot].h, 16 * (bi // 2 + 1))
                    if bi >= 1:
                        s_.wait_ge(ADB.h, bi)
                    ss_in = wf[slot][0:rows, 0:width]
                    ssb.step(s_.activation(t2a[0:rows, 0:width], ss_in, AF.Sign,
                                           bias=nb[0:rows, :]), s_)
                    s_.activation(t2b[0:rows, 0:width], ss_in, AF.Sign,
                                  bias=pb[0:rows, :]).then_inc(SGB.h, 2)
                    SGB.v += 2
                    bi += 1

        @blk.vector
        def _(v):
            bi = 0
            for mi, nm, blocks in mats:
                for (ap, rows, width) in blocks:
                    pslot = bi % 2
                    v.wait_ge(SGB.h, 2 * (bi + 1))
                    if bi >= 2:
                        v.wait_ge(STB2[pslot].h, 16 * (bi // 2))
                    v.tensor_tensor(wqb[pslot][0:rows, 0:width], t2a[0:rows, 0:width],
                                    t2b[0:rows, 0:width], ALU.add).then_inc(ADB.h, 1)
                    ADB.v += 1
                    bi += 1
            v.wait_ge(ADB.h, nblk)
            svb.step(v.memset(t2a[0:HID_P - HID, 0:D], 0.0), v)
            v.engine_nop().then_inc(ZP.h, 1)
            ZP.v += 1
    b.end_phase()


def emit_l1ffn(b):
    nc = b.nc
    wc1 = [b.salloc(f"fw1{k}", [128, D], BF16, at=32768 + k * 4096) for k in range(2)]
    wc3 = [b.salloc(f"fw3{k}", [128, D], BF16, at=40960 + k * 4096) for k in range(2)]
    gt = b.salloc("fgt", [128, NT_HID * T_OWN], BF16, at=81920)
    tg1 = b.salloc("ftg1", [128, 512], F32, at=49152)
    tg1s = b.salloc("ftg1s", [128, 512], F32, at=51200)
    tg3 = b.salloc("ftg3", [128, 512], F32, at=53248)
    w2s = b.salloc("fw2s", [128, NT_HID * 512], BF16, at=32768)
    tmpf = b.salloc("ftmpf", [128, 512], F32, at=76800)
    g2b = [b.salloc(f"fg2b{k}", [128, 512], BF16, at=78848 + k * 1024) for k in range(2)]
    gt3 = gt.rearrange("p (t c) -> p t c", t=NT_HID)

    WL = [b.sem("fWL0", g=True), b.sem("fWL1", g=True)]
    MM = b.sem("fMM")
    GE = b.sem("fGE")
    T1 = b.sem("fT1")
    T3 = b.sem("fT3")
    SIL = b.sem("fSIL")

    s1c = b.sq_sb[:, 0:1]
    s3c = b.sq_sb[:, 1:2]
    s2c = b.sq_sb[:, 2:3]

    with nc.Block() as blk:
        @blk.gpsimd
        def _(g):
            for ht in range(NT_HID):
                slot = ht % 2
                if ht >= 2:
                    g.wait_ge(MM.h, 4 * (ht - 1))
                g.dma_start(out=wc1[slot].rearrange("p (t c) -> p t c", t=NT_D),
                            in_=b.w1q_d[:, ht * 128:(ht + 1) * 128]
                            .rearrange("(t p) c -> p t c", p=128)).then_inc(WL[slot].h, 16)
                g.dma_start(out=wc3[slot].rearrange("p (t c) -> p t c", t=NT_D),
                            in_=b.w3q_d[:, ht * 128:(ht + 1) * 128]
                            .rearrange("(t p) c -> p t c", p=128)).then_inc(WL[slot].h, 16)
                WL[slot].v += 32

        @blk.tensor
        def _(t_):
            for ht in range(NT_HID):
                slot = ht % 2
                t_.wait_ge(WL[slot].h, 32 * (ht // 2 + 1))
                if ht >= 1:
                    t_.wait_ge(T1.h, 2 * ht)
                w13 = wc1[slot].rearrange("p (t c) -> p t c", t=NT_D)
                w33 = wc3[slot].rearrange("p (t c) -> p t c", t=NT_D)
                for half in range(2):
                    hs = slice(half * 512, (half + 1) * 512)
                    for dt in range(NT_D):
                        t_.matmul(b.P[half], w13[:, dt], b.xn1[dt][:, hs],
                                  start=(dt == 0), stop=(dt == NT_D - 1))
                if ht >= 1:
                    t_.wait_ge(T3.h, 2 * ht)
                for half in range(2):
                    hs = slice(half * 512, (half + 1) * 512)
                    for dt in range(NT_D):
                        mm = t_.matmul(b.P[2 + half], w33[:, dt], b.xn1[dt][:, hs],
                                       start=(dt == 0), stop=(dt == NT_D - 1))
                mm.then_inc(MM.h, 4)
                MM.v += 4

        @blk.vector
        def _(v):
            for ht in range(NT_HID):
                v.wait_ge(MM.h, 4 * (ht + 1))
                for half in range(2):
                    idx = 2 * ht + half
                    v.tensor_scalar_mul(tg1, b.P[half], s1c).then_inc(T1.h, 1)
                    T1.v += 1
                    v.tensor_scalar_mul(tg3, b.P[2 + half], s3c).then_inc(T3.h, 1)
                    T3.v += 1
                    v.wait_ge(T3.h, T3.v)
                    v.wait_ge(SIL.h, idx + 1)
                    v.tensor_tensor(gt3[:, ht, half * 512:(half + 1) * 512],
                                    tg1s, tg3, ALU.mult).then_inc(GE.h, 1)
                    GE.v += 1
                    v.wait_ge(GE.h, GE.v)

        @blk.scalar
        def _(s_):
            for ht in range(NT_HID):
                for half in range(2):
                    idx = 2 * ht + half
                    s_.wait_ge(T1.h, idx + 1)
                    s_.activation(tg1s, tg1, AF.Silu).then_inc(SIL.h, 1)
                    SIL.v += 1
    b.end_phase()

    WL2 = b.sem("f2WL", g=True)
    MM2 = b.sem("f2MM")
    EV2 = b.sem("f2EV")
    SL2 = b.sem("f2SL")
    SO2 = [b.sem("f2SO0"), b.sem("f2SO1")]
    sv2 = Ser(b, "f2SV")
    w2s3 = w2s.rearrange("p (t c) -> p t c", t=NT_HID)

    with nc.Block() as blk:
        @blk.gpsimd
        def _(g):
            for og in range(4):
                if og >= 1:
                    g.wait_ge(MM2.h, NT_OWN * og)
                g.dma_start(out=w2s3,
                            in_=b.w2q_d[:, og * 512:(og + 1) * 512]
                            .rearrange("(t p) c -> p t c", p=128)).then_inc(WL2.h, 16)
                WL2.v += 16

        @blk.tensor
        def _(t_):
            for og in range(4):
                t_.wait_ge(WL2.h, 16 * (og + 1))
                for tk in range(NT_OWN):
                    gi = og * NT_OWN + tk
                    if gi >= 2:
                        t_.wait_ge(EV2.h, gi - 1)
                    pp = b.P[gi % 2]
                    for ht in range(NT_HID):
                        mm = t_.matmul(pp, gt3[:, ht, tk * 128:(tk + 1) * 128],
                                       w2s3[:, ht], start=(ht == 0),
                                       stop=(ht == NT_HID - 1))
                    mm.then_inc(MM2.h, 1)
                    MM2.v += 1

        @blk.vector
        def _(v):
            for gi in range(4 * NT_OWN):
                v.wait_ge(MM2.h, gi + 1)
                if gi >= 1:
                    v.wait_ge(SL2.h, gi)
                v.tensor_scalar_mul(tmpf, b.P[gi % 2], s2c).then_inc(EV2.h, 1)
                EV2.v += 1

        @blk.scalar
        def _(s_):
            for gi in range(4 * NT_OWN):
                slot = gi % 2
                s_.wait_ge(EV2.h, gi + 1)
                if gi >= 2:
                    s_.wait_ge(SO2[slot].h, 16 * (gi // 2))
                s_.activation(g2b[slot], tmpf, AF.Silu).then_inc(SL2.h, 1)
                SL2.v += 1

        @blk.sync
        def _(sync):
            for gi in range(4 * NT_OWN):
                og, tk = gi // NT_OWN, gi % NT_OWN
                slot = gi % 2
                sync.wait_ge(SL2.h, gi + 1)
                sync.dma_start(out=b.o["g2"][tk * 128:(tk + 1) * 128,
                                             og * 512:(og + 1) * 512],
                               in_=g2b[slot]).then_inc(SO2[slot].h, 16)
                SO2[slot].v += 16
            sync.wait_ge(SO2[0].h, SO2[0].v)
            sync.wait_ge(SO2[1].h, SO2[1].v)
    b.end_phase()


def _dump_tables(b):
    nc = b.nc
    dc = b.dbg("cosT", (128, H * T_OWN), BF16)
    ds = b.dbg("sinT", (128, H * T_OWN), BF16)
    ST = b.sem("dtST")
    with nc.Block() as blk:
        @blk.sync
        def _(sync):
            sync.dma_start(out=dc, in_=b.cosT).then_inc(ST.h, 16)
            sync.dma_start(out=ds, in_=b.sinT).then_inc(ST.h, 16)
            sync.wait_ge(ST.h, 32)
    b.end_phase()


def _dump_quant(b):
    nc = b.nc
    d1 = b.dbg("w1q", (D, HID_P), BF16)
    d2 = b.dbg("w2q", (HID_P, D), BF16)
    ST = b.sem("dqqST")
    with nc.Block() as blk:
        @blk.sync
        def _(sync):
            sync.dma_start(out=d1, in_=b.w1q_d).then_inc(ST.h, 16)
            sync.dma_start(out=d2, in_=b.w2q_d).then_inc(ST.h, 16)
            sync.wait_ge(ST.h, 32)
    b.end_phase()



class BuilderB:
    SB_BASE = Builder.SB_BASE
    SB_TOP = Builder.SB_TOP
    salloc = Builder.salloc
    sem = Builder.sem
    end_phase = Builder.end_phase

    def __init__(self):
        nc = bass.Bass("TRN2", target_bir_lowering=False, debug=False)
        self.nc = nc
        self.sboff = 0
        self._pool = []
        self._pool_idx = 0
        self._phase_sems = []

        def inp(name, shape, dtype=F32):
            return nc.dram_tensor(name, list(shape), dtype, kind="ExternalInput").ap()

        i = {}
        i["S_all"] = inp("S_all", (8, H * 256, DH))
        i["Wc"] = inp("Wc", (1, 8 * H))
        i["x1d"] = inp("x1d", (T_OWN, D))
        i["g2d"] = inp("g2d", (T_OWN, D), BF16)
        i["yld"] = inp("yld", (D, T_OWN), BF16)
        i["qdwd"] = inp("qdwd", (H * 256, T_OWN), BF16)
        i["wprojT"] = inp("wprojT", (D, D))
        self.i = i
        self.o = {"out": nc.dram_tensor("out", [T_OWN, D], F32,
                                        kind="ExternalOutput").ap()}
        self.P = [nc.alloc_psum_tensor(f"P{k}", [128, 512], F32).ap() for k in range(8)]
        self.yfin = [self.salloc(f"byf{h}", [128, T_OWN], BF16, at=h * 2048)
                     for h in range(H)]


def emit_B1(b):
    nc = b.nc
    sS = b.salloc("bsS", [128, 2048], F32, at=32768)
    stiA = b.salloc("bstiA", [128, 128], F32, at=40960)
    stiB = b.salloc("bstiB", [128, 128], F32, at=41472)
    tmpt = b.salloc("btmpt", [128, 128], F32, at=41984)
    stiAb = b.salloc("bstiAb", [128, 128], BF16, at=42496)
    stiBb = b.salloc("bstiBb", [128, 128], BF16, at=42752)
    qdwA = b.salloc("bqdwA", [128, T_OWN], BF16, at=43008)
    qdwB = b.salloc("bqdwB", [128, T_OWN], BF16, at=45056)
    ylh = b.salloc("bylh", [128, T_OWN], BF16, at=47104)
    wcr = b.salloc("bwcr", [1, 8 * H], F32, at=49152)
    onesr = b.salloc("bonesr", [1, 128], F32, at=49664)
    slopes = np.exp(-_slopes() * CHUNK)

    MS = b.sem("bMS")
    WCL = b.sem("bWCL")
    BCT = b.sem("bBCT")
    SLD = b.sem("bSLD")
    QL = b.sem("bQL")
    YLL = b.sem("bYLL")
    STI = b.sem("bSTI")
    O2 = b.sem("bO2")
    YF = b.sem("bYF")
    sv = Ser(b, "bSV")

    with nc.Block() as blk:
        @blk.gpsimd
        def _(g):
            g.memset(onesr, 1.0).then_inc(MS.h, 1)
            MS.v = 1

        @blk.sync
        def _(sync):
            sync.dma_start(out=wcr, in_=b.i["Wc"]).then_inc(WCL.h, 16)
            WCL.v = 16
            for h in range(H):
                if h:
                    sync.wait_ge(STI.h, h)
                sync.dma_start(out=sS[:, 0:1024].rearrange("d (g v) -> d g v", g=8),
                               in_=b.i["S_all"][:, h * 256:h * 256 + 128, :]
                               .rearrange("b d v -> d b v")).then_inc(SLD.h, 16)
                sync.dma_start(out=sS[:, 1024:2048].rearrange("d (g v) -> d g v", g=8),
                               in_=b.i["S_all"][:, h * 256 + 128:h * 256 + 256, :]
                               .rearrange("b d v -> d b v")).then_inc(SLD.h, 16)
                SLD.v += 32
                if h:
                    sync.wait_ge(O2.h, 2 * h)
                sync.dma_start(out=qdwA,
                               in_=b.i["qdwd"][h * 256:h * 256 + 128, :]).then_inc(QL.h, 16)
                sync.dma_start(out=qdwB,
                               in_=b.i["qdwd"][h * 256 + 128:h * 256 + 256, :]).then_inc(QL.h, 16)
                QL.v += 32
                if h:
                    sync.wait_ge(YF.h, 8 * h)
                sync.dma_start(out=ylh,
                               in_=b.i["yld"][h * 128:(h + 1) * 128, :]).then_inc(YLL.h, 16)
                YLL.v += 16

        @blk.tensor
        def _(t_):
            t_.wait_ge(MS.h, 1)
            t_.wait_ge(WCL.h, 16)
            t_.matmul(b.P[6][:, 0:128], onesr, wcr,
                      start=True, stop=True).then_inc(BCT.h, 1)
            BCT.v += 1
            for h in range(H):
                t_.wait_ge(STI.h, h + 1)
                t_.wait_ge(QL.h, 32 * (h + 1))
                for nh in range(2):
                    if h + nh:
                        t_.wait_ge(YF.h, 8 * h + 4 * nh)
                    ts = slice(nh * 512, (nh + 1) * 512)
                    t_.matmul(b.P[nh], stiAb, qdwA[:, ts], start=True, stop=False)
                    t_.matmul(b.P[nh], stiBb, qdwB[:, ts],
                              start=False, stop=True).then_inc(O2.h, 1)
                    O2.v += 1

        @blk.vector
        def _(v):
            v.wait_ge(BCT.h, 1)
            for h in range(H):
                v.wait_ge(SLD.h, 32 * (h + 1))
                if h:
                    v.wait_ge(O2.h, 2 * h)
                v.memset(stiA, 0.0)
                sv.step(v.memset(stiB, 0.0), v)
                for b_ in range(8):
                    col = b.P[6][:, b_ * H + h:b_ * H + h + 1]
                    sv.step(v.tensor_scalar_mul(tmpt, sS[:, b_ * 128:(b_ + 1) * 128],
                                                col), v)
                    sv.step(v.tensor_tensor(stiA, tmpt, stiA, ALU.add), v)
                    sv.step(v.tensor_scalar_mul(
                        tmpt, sS[:, 1024 + b_ * 128:1024 + (b_ + 1) * 128], col), v)
                    sv.step(v.tensor_tensor(stiB, tmpt, stiB, ALU.add), v)
                v.tensor_copy(stiAb, stiA)
                sv.step(v.tensor_copy(stiBb, stiB), v)
                v.engine_nop().then_inc(STI.h, 1)
                STI.v += 1
                v.wait_ge(YLL.h, 16 * (h + 1))
                bd = float(slopes[h])
                for nh in range(2):
                    v.wait_ge(O2.h, 2 * h + nh + 1)
                    for jq in range(4):
                        j = nh * 4 + jq
                        v.scalar_tensor_tensor(
                            b.yfin[h][:, j * 128:(j + 1) * 128],
                            b.P[nh][:, jq * 128:(jq + 1) * 128], float(bd ** j),
                            ylh[:, j * 128:(j + 1) * 128],
                            ALU.mult, ALU.add).then_inc(YF.h, 1)
                        YF.v += 1
    b.end_phase()


def emit_B2(b):
    nc = b.nc
    wp = [b.salloc(f"bwp{k}", [128, H * 512], BF16, at=53248 + k * 16384)
          for k in range(2)]
    x1s = [b.salloc(f"bx1s{k}", [128, 512], F32, at=86016 + k * 2048) for k in range(2)]
    g2s = [b.salloc(f"bg2s{k}", [128, 512], BF16, at=90112 + k * 1024) for k in range(2)]
    outs = [b.salloc(f"bouts{k}", [128, 512], F32, at=92160 + k * 2048) for k in range(2)]

    WPL = [b.sem("b2WPL0", g=True), b.sem("b2WPL1", g=True)]
    XL = [b.sem("b2XL0"), b.sem("b2XL1")]
    GL = [b.sem("b2GL0"), b.sem("b2GL1")]
    MMp = b.sem("b2MM")
    EVp = b.sem("b2EV")
    OS = [b.sem("b2OS0"), b.sem("b2OS1")]
    sv = Ser(b, "b2SV")
    NGI = 4 * NT_OWN

    with nc.Block() as blk:
        @blk.gpsimd
        def _(g):
            for og in range(4):
                slot = og % 2
                if og >= 2:
                    g.wait_ge(MMp.h, NT_OWN * (og - 1))
                g.dma_start(out=wp[slot].rearrange("p (t c) -> p t c", t=NT_D),
                            in_=b.i["wprojT"][:, og * 512:(og + 1) * 512]
                            .rearrange("(t p) c -> p t c", p=128)).then_inc(WPL[slot].h, 16)
                WPL[slot].v += 16

        @blk.tensor
        def _(t_):
            for gi in range(NGI):
                og, tk = gi // NT_OWN, gi % NT_OWN
                wslot = og % 2
                if gi % NT_OWN == 0:
                    t_.wait_ge(WPL[wslot].h, 16 * (og // 2 + 1))
                if gi >= 2:
                    t_.wait_ge(EVp.h, gi - 1)
                pp = b.P[gi % 2]
                wp3 = wp[wslot].rearrange("p (t c) -> p t c", t=H)
                for h in range(H):
                    mm = t_.matmul(pp, b.yfin[h][:, tk * 128:(tk + 1) * 128],
                                   wp3[:, h], start=(h == 0), stop=(h == H - 1))
                mm.then_inc(MMp.h, 1)
                MMp.v += 1

        @blk.sync
        def _(sync):
            for gi in range(NGI):
                og, tk = gi // NT_OWN, gi % NT_OWN
                slot = gi % 2
                if gi >= 2:
                    sync.wait_ge(EVp.h, gi - 1)
                sync.dma_start(out=x1s[slot],
                               in_=b.i["x1d"][tk * 128:(tk + 1) * 128,
                                              og * 512:(og + 1) * 512]).then_inc(XL[slot].h, 16)
                XL[slot].v += 16
                sync.dma_start(out=g2s[slot],
                               in_=b.i["g2d"][tk * 128:(tk + 1) * 128,
                                              og * 512:(og + 1) * 512]).then_inc(GL[slot].h, 16)
                GL[slot].v += 16
                if gi >= 1:
                    pg = gi - 1
                    pog, ptk = pg // NT_OWN, pg % NT_OWN
                    sync.wait_ge(EVp.h, pg + 1)
                    sync.dma_start(out=b.o["out"][ptk * 128:(ptk + 1) * 128,
                                                  pog * 512:(pog + 1) * 512],
                                   in_=outs[pg % 2]).then_inc(OS[pg % 2].h, 16)
                    OS[pg % 2].v += 16
            pg = NGI - 1
            pog, ptk = pg // NT_OWN, pg % NT_OWN
            sync.wait_ge(EVp.h, pg + 1)
            sync.dma_start(out=b.o["out"][ptk * 128:(ptk + 1) * 128,
                                          pog * 512:(pog + 1) * 512],
                           in_=outs[pg % 2]).then_inc(OS[pg % 2].h, 16)
            OS[pg % 2].v += 16
            sync.wait_ge(OS[0].h, OS[0].v)
            sync.wait_ge(OS[1].h, OS[1].v)

        @blk.vector
        def _(v):
            for gi in range(NGI):
                slot = gi % 2
                v.wait_ge(MMp.h, gi + 1)
                v.wait_ge(XL[slot].h, 16 * (gi // 2 + 1))
                v.wait_ge(GL[slot].h, 16 * (gi // 2 + 1))
                if gi >= 2:
                    v.wait_ge(OS[slot].h, 16 * (gi // 2))
                sv.step(v.tensor_tensor(outs[slot], b.P[gi % 2], g2s[slot],
                                        ALU.mult), v)
                v.tensor_tensor(outs[slot], outs[slot], x1s[slot],
                                ALU.add).then_inc(EVp.h, 1)
                EVp.v += 1
    b.end_phase()


def build_devB():
    b = BuilderB()
    with b.nc.named_scope("B1"):
        emit_B1(b)
    with b.nc.named_scope("B2"):
        emit_B2(b)
    return b


def build_devA(upto="ln0", want_dbg=()):
    b = Builder(want_dbg=want_dbg)
    nc = b.nc
    with nc.named_scope("consts"):
        emit_consts(b)
    with nc.named_scope("ln0"):
        emit_ln0(b)
    if upto == "ln0":
        return b
    with nc.named_scope("projq"):
        emit_proj_qk(b, "q")
    with nc.named_scope("projk"):
        emit_proj_qk(b, "k")
    with nc.named_scope("projv"):
        emit_proj_v(b)
    if "qkv" in b.want_dbg:
        _dump_qkv(b)
    if upto == "qkv":
        return b
    with nc.named_scope("attn"):
        emit_attn(b)
    if "yat" in b.want_dbg:
        _dump_yattn(b)
    if upto == "attn":
        return b
    with nc.named_scope("oproj"):
        emit_oproj(b)
    if "s0" in b.want_dbg:
        _dump_s(b)
    if upto == "oproj":
        return b
    with nc.named_scope("router"):
        emit_router(b)
    with nc.named_scope("moe"):
        emit_moe(b)
    if "acc" in b.want_dbg:
        _dump_acc(b)
    if upto == "moe":
        return b
    with nc.named_scope("combine"):
        emit_combine_ln1(b)
    if "xn1" in b.want_dbg:
        _dump_xn1(b)
    if upto == "ln1":
        return b
    with nc.named_scope("l1qkv"):
        emit_l1qkv(b)
    if "jqkv" in b.want_dbg:
        _dump_qkv(b)
    if upto == "l1qkv":
        return b
    with nc.named_scope("l1tables"):
        emit_l1tables(b)
    if "tab" in b.want_dbg:
        _dump_tables(b)
    if upto == "tables":
        return b
    with nc.named_scope("l1scanq"):
        emit_l1scanq(b)
    if "wq" in b.want_dbg:
        _dump_quant(b)
    if upto in ("scan", "quant"):
        return b
    with nc.named_scope("l1ffn"):
        emit_l1ffn(b)
    if upto == "ffn":
        return b
    raise ValueError(upto)


def _dump_qkv(b, parts=("q", "k", "v")):
    nc = b.nc
    srcs = {"q": (b.qf_d, (H, 128, T_OWN)), "k": (b.kf_d, (H, 128, TH)),
            "v": (b.vt_d, (NT_TH, 128, D))}
    ST = b.sem("dqST")
    with nc.Block() as blk:
        @blk.sync
        def _(sync):
            for p in parts:
                src, shp = srcs[p]
                d = b.dbg(p + "f", shp, BF16)
                sync.dma_start(out=d, in_=src).then_inc(ST.h, 16)
                ST.v += 16
            sync.wait_ge(ST.h, ST.v)
    b.end_phase()

def kernel(**inputs):
    maps = shard_inputs(inputs)
    bA = build_devA(upto="ffn")
    resA = run_bass_kernel_spmd(bA.nc, maps, core_ids=list(range(NCORES)))
    S_all = np.ascontiguousarray(
        np.stack([np.asarray(resA.results[c]["S_c"], np.float32)
                  for c in range(NCORES)]))
    wp = maps[0]["wprojT"]
    mapsB = []
    for c in range(NCORES):
        r = resA.results[c]
        mapsB.append({"S_all": S_all, "Wc": host_consts(c)["Wc"],
                      "x1d": np.asarray(r["x1"]), "g2d": np.asarray(r["g2"]),
                      "yld": np.asarray(r["ylocal"]),
                      "qdwd": np.asarray(r["qdw"]), "wprojT": wp})
    bB = build_devB()
    resB = run_bass_kernel_spmd(bB.nc, mapsB, core_ids=list(range(NCORES)))
    out = np.concatenate([np.asarray(resB.results[c]["out"], np.float32)
                          for c in range(NCORES)], axis=0)
    return out.reshape(1, NCORES * T_OWN, D)


# version 05
# speedup vs baseline: 1.2508x; 1.1091x over previous
import numpy as np
import ml_dtypes
import concourse.bass as bass
import concourse.mybir as mybir
from concourse.bass_utils import run_bass_kernel_spmd

F32 = mybir.dt.float32
BF16 = mybir.dt.bfloat16
AF = mybir.ActivationFunctionType
ALU = mybir.AluOpType
AX = mybir.AxisListType

D = 2048
NT_D = 16
T_OWN = 1024
NT_OWN = 8
HALO = 512
TH = T_OWN + HALO
NT_TH = 12
H = 16
DH = 128
R = 16
F_MOE = 1024
NT_F = 8
NE, NSH = 16, 2
HID = 5461
HID_P = 5504
NT_HID = 43
CHUNK = 128
NCH = 8
NCORES = 8
INV_SQRT_DH = float(1.0 / np.sqrt(128.0))
LORA_SCALE = 1.0 / 16.0

BF = ml_dtypes.bfloat16
NO_LORA = False


def _slopes():
    start = 2.0 ** (-2.0 ** (-(np.log2(H) - 3)))
    return np.array([start ** (i + 1) for i in range(H)], dtype=np.float64)



def host_consts(core):
    c = {}
    rbase = core * T_OWN
    half = DH // 2
    th = 1.0 / 10000.0 ** (np.arange(half, dtype=np.float64) / half)
    pos = np.arange(rbase - HALO, rbase + T_OWN, dtype=np.float64)
    pos = np.clip(pos, 0, None)
    fr = pos[:, None] * th[None, :]
    emb = np.concatenate([fr, fr], -1)
    c["ropecos"] = np.ascontiguousarray(np.cos(emb).T.astype(BF))
    c["ropesin"] = np.ascontiguousarray(np.sin(emb).T.astype(BF))
    masks = np.zeros((5, 128, 640), np.float32)
    for slot in range(5):
        qt = slot if (core == 0 and slot < 4) else None
        bcol = np.arange(640)[None, :]
        arow = np.arange(128)[:, None]
        ok = (bcol - arow >= 1) & (bcol - arow <= 512)
        if qt is not None:
            gq = qt * 128 + arow
            gk = qt * 128 + bcol - 512
            ok = ok & (gk >= 0) & (gk <= gq)
        masks[slot] = np.where(ok, 0.0, -1e30)
    c["mask5"] = masks
    sl = _slopes()
    idc = np.arange(CHUNK, dtype=np.float64)
    diff = idc[:, None] - idc[None, :]
    dd = np.where(diff >= 0, np.exp(-sl[:, None, None] * diff), 0.0)
    c["diagdecT"] = np.ascontiguousarray(dd.transpose(0, 2, 1).astype(np.float32))
    c["qdecay"] = np.exp(-sl[:, None] * (idc + 1.0)[None, :]).astype(np.float32).reshape(1, H * CHUNK)
    c["kdecay"] = np.exp(-sl[:, None] * (CHUNK - 1.0 - idc)[None, :]).astype(np.float32).reshape(1, H * CHUNK)
    bd = np.exp(-sl * CHUNK)
    Wc = np.zeros((8, H), np.float32)
    for b_ in range(core):
        Wc[b_] = (bd ** (8 * (core - 1 - b_))).astype(np.float32)
    c["Wc"] = np.ascontiguousarray(Wc.reshape(1, 8 * H))
    c["idxrow"] = np.arange(rbase, rbase + T_OWN, dtype=np.float32).reshape(1, T_OWN)
    return c


def shard_inputs(inputs):
    x = np.ascontiguousarray(np.asarray(inputs["L0_attn_x"])[0])
    TT = lambda a: np.ascontiguousarray(np.asarray(a).T)
    base = {}
    base["wqT"] = TT(inputs["L0_attn_wq"])
    base["wkT"] = TT(inputs["L0_attn_wk"])
    base["wvT"] = TT(inputs["L0_attn_wv"])
    base["woT"] = TT(inputs["L0_attn_wo"])
    base["nwfm"] = np.ascontiguousarray(np.asarray(inputs["L0_attn_norm_w"]).reshape(NT_D, 128).T)
    for nm in ("qA", "kA", "vA", "oA"):
        base[nm + "T"] = TT(inputs["L0_attn_" + nm])
    for nm in ("qB", "kB", "vB", "oB"):
        base[nm + "T"] = TT(inputs["L0_attn_" + nm])
    base["wa"] = np.asarray(inputs["L0_ffn_wa"])
    for nm in ("rg", "rw1", "rw2"):
        base[nm] = np.asarray(inputs["L0_ffn_" + nm])
    for nm in ("sg", "sw1", "sw2"):
        base[nm] = np.asarray(inputs["L0_ffn_" + nm])
    base["rgbT"] = TT(inputs["L0_ffn_rgb"])
    base["rb1T"] = TT(inputs["L0_ffn_rb1"])
    base["rb2"] = np.asarray(inputs["L0_ffn_rb2"])
    base["sgbT"] = TT(inputs["L0_ffn_sgb"])
    base["sb1T"] = TT(inputs["L0_ffn_sb1"])
    base["sb2"] = np.asarray(inputs["L0_ffn_sb2"])
    base["wqkvT"] = TT(inputs["L1_attn_w_qkv"])
    base["wprojT"] = TT(inputs["L1_attn_w_proj"])
    base["thetarow"] = np.ascontiguousarray(np.asarray(inputs["L1_attn_theta"]).reshape(1, H * 128))
    base["w1T"] = TT(inputs["L1_ffn_w1"])
    base["w3T"] = TT(inputs["L1_ffn_w3"])
    base["w2T"] = TT(inputs["L1_ffn_w2"])
    maps = []
    for core in range(NCORES):
        m = dict(base)
        rb = core * T_OWN
        xh = np.zeros((TH, D), np.float32)
        lo = max(0, rb - HALO)
        xh[HALO - (rb - lo):] = x[lo:rb + T_OWN]
        m["xh"] = xh
        m.update(host_consts(core))
        maps.append(m)
    return maps



class Sem:
    def __init__(self, h):
        self.h = h
        self.v = 0


class Ser:

    def __init__(self, b, name):
        self.s = b.sem(name)

    def step(self, inst, eng):
        inst.then_inc(self.s.h, 1)
        self.s.v += 1
        eng.wait_ge(self.s.h, self.s.v)


class Builder:
    def __init__(self, want_dbg=()):
        nc = bass.Bass("TRN2", target_bir_lowering=False, debug=False)
        self.nc = nc
        self.sboff = 0
        self.want_dbg = set(want_dbg)
        self.dbg_outs = {}
        self._pool = []
        self._pool_idx = 0
        self._phase_sems = []

        def inp(name, shape, dtype=F32):
            return nc.dram_tensor(name, list(shape), dtype, kind="ExternalInput").ap()

        i = {}
        i["xh"] = inp("xh", (TH, D))
        for nm in ("wqT", "wkT", "wvT", "woT"):
            i[nm] = inp(nm, (D, D))
        i["nwfm"] = inp("nwfm", (128, NT_D))
        for nm in ("qAT", "kAT", "vAT", "oAT"):
            i[nm] = inp(nm, (D, R))
        for nm in ("qBT", "kBT", "vBT", "oBT"):
            i[nm] = inp(nm, (R, D))
        i["wa"] = inp("wa", (D, NE))
        for nm in ("rg", "rw1"):
            i[nm] = inp(nm, (NE, D, F_MOE))
        i["rw2"] = inp("rw2", (NE, F_MOE, D))
        for nm in ("sg", "sw1"):
            i[nm] = inp(nm, (NSH, D, F_MOE))
        i["sw2"] = inp("sw2", (NSH, F_MOE, D))
        i["rgbT"] = inp("rgbT", (F_MOE, NE))
        i["rb1T"] = inp("rb1T", (F_MOE, NE))
        i["rb2"] = inp("rb2", (NE, D))
        i["sgbT"] = inp("sgbT", (F_MOE, NSH))
        i["sb1T"] = inp("sb1T", (F_MOE, NSH))
        i["sb2"] = inp("sb2", (NSH, D))
        i["ropecos"] = inp("ropecos", (128, TH), BF16)
        i["ropesin"] = inp("ropesin", (128, TH), BF16)
        i["mask5"] = inp("mask5", (5, 128, 640))
        i["wqkvT"] = inp("wqkvT", (D, 3 * D))
        i["wprojT"] = inp("wprojT", (D, D))
        i["thetarow"] = inp("thetarow", (1, H * 128))
        i["diagdecT"] = inp("diagdecT", (H, CHUNK, CHUNK))
        i["qdecay"] = inp("qdecay", (1, H * CHUNK))
        i["kdecay"] = inp("kdecay", (1, H * CHUNK))
        i["idxrow"] = inp("idxrow", (1, T_OWN))
        i["w1T"] = inp("w1T", (D, HID))
        i["w3T"] = inp("w3T", (D, HID))
        i["w2T"] = inp("w2T", (HID, D))
        self.i = i

        o = {}
        o["x1"] = nc.dram_tensor("x1", [T_OWN, D], F32, kind="ExternalOutput").ap()
        o["g2"] = nc.dram_tensor("g2", [T_OWN, D], BF16, kind="ExternalOutput").ap()
        o["ylocal"] = nc.dram_tensor("ylocal", [D, T_OWN], BF16, kind="ExternalOutput").ap()
        o["qdw"] = nc.dram_tensor("qdw", [H * 256, T_OWN], BF16, kind="ExternalOutput").ap()
        o["S_c"] = nc.dram_tensor("S_c", [H * 256, DH], F32, kind="ExternalOutput").ap()
        self.o = o

        self.yattn_d = nc.dram_tensor("yattn_d", [NT_D, 128, T_OWN], BF16, kind="Internal").ap()
        self.qf_d = nc.dram_tensor("qf_d", [H, 128, T_OWN], BF16, kind="Internal").ap()
        self.kf_d = nc.dram_tensor("kf_d", [H, 128, TH], BF16, kind="Internal").ap()
        self.vt_d = nc.dram_tensor("vt_d", [NT_TH, 128, D], BF16, kind="Internal").ap()
        self.w1q_d = nc.dram_tensor("w1q_d", [D, HID_P], BF16, kind="Internal").ap()
        self.w3q_d = nc.dram_tensor("w3q_d", [D, HID_P], BF16, kind="Internal").ap()
        self.w2q_d = nc.dram_tensor("w2q_d", [HID_P, D], BF16, kind="Internal").ap()

        self.ident_d = nc.inline_tensor(np.asarray(np.eye(128, dtype=np.float32), dtype=BF), "identbf").ap()
        self.identf_d = nc.inline_tensor(np.eye(128, dtype=np.float32), "identf32").ap()
        self.ones1_d = nc.inline_tensor(np.ones((1, 128), dtype=BF), "ones1bf").ap()
        o2 = np.zeros((128, 512), np.float32)
        o2[0:2, :] = 1.0
        self.ones2_d = nc.inline_tensor(np.asarray(o2, dtype=BF), "ones2bf").ap()
        Rm = np.zeros((128, 128), np.float32)
        for d_ in range(64):
            Rm[d_, d_ + 64] = -1.0
            Rm[d_ + 64, d_] = 1.0
        self.RT_d = nc.inline_tensor(np.asarray(Rm.T, dtype=BF), "ropeRT").ap()
        sel16 = np.zeros((128, NE * 128), np.float32)
        for e in range(NE):
            sel16[e, e * 128:(e + 1) * 128] = 1.0
        self.esel_d = nc.inline_tensor(np.asarray(sel16, dtype=BF), "esel").ap()
        sel12 = np.zeros((NT_TH, NT_TH * 128), np.float32)
        for t in range(NT_TH):
            sel12[t, t * 128:(t + 1) * 128] = 1.0
        self.tsel_d = nc.inline_tensor(sel12, "tsel12").ap()

        self.P = [nc.alloc_psum_tensor(f"P{k}", [128, 512], F32).ap() for k in range(8)]

        self.xn = [self.salloc(f"xn{t}", [128, TH], BF16) for t in range(NT_D)]
        self.rrb = self.salloc("rrb", [128, TH], F32)
        self.rrc = self.salloc("rrc", [128, NT_TH], F32)
        self.nwf = self.salloc("nwf", [128, NT_D], F32)
        self.nwfq = self.salloc("nwfq", [128, NT_D], F32)
        self.smll = self.salloc("smll", [128, 2048], BF16)
        self.BTm = {nm: self.salloc("bt_" + nm, [128, D], BF16)
                    for nm in ("qB", "kB", "vB", "oB")}
        self.zb = self.salloc("zb", [128, TH], BF16)
        self.ztmp = self.salloc("ztmp", [128, 16], BF16)
        self.off_zb = self.sboff
        self.combT = self.salloc("combT", [128, T_OWN], BF16)
        self.ident = self.salloc("ident", [128, 128], BF16)
        self.identf = self.salloc("identf", [128, 128], F32)
        self.consts = self.salloc("consts", [128, 8], F32)
        self.PERSIST = self.sboff

        sm = self.smll
        self.AT = {nm: sm[:, k * 256:(k + 1) * 256]
                   for k, nm in enumerate(("qA", "kA", "vA", "oA"))}
        self.waf = sm[:, 1024:1280]

        nc.const_aps.register(F32, 1e-5, self.consts[:, 0:1])
        nc.const_aps.register(F32, 1e-6, self.consts[:, 1:2])
        self._const_vals = [(0, 1e-5), (1, 1e-6)]

    SB_BASE = 16512
    SB_TOP = 229344

    def salloc(self, name, shape, dtype, at=None):
        per = int(np.prod(shape[1:])) * mybir.dt.size(dtype)
        per = (per + 31) & ~31
        if at is None:
            at = self.sboff
            self.sboff = at + per
        assert self.SB_BASE + at + per <= self.SB_TOP, (name, at, per)
        ap = self.nc.alloc_sbuf_tensor_at(name, list(shape), dtype,
                                          offset=self.SB_BASE + at)
        return ap.ap()

    def sem(self, name, g=False):
        if g:
            return Sem(self.nc.alloc_semaphore(name))
        if self._pool_idx >= len(self._pool):
            self._pool.append(self.nc.alloc_semaphore(f"pool{len(self._pool)}"))
        h = self._pool[self._pool_idx]
        self._pool_idx += 1
        s = Sem(h)
        self._phase_sems.append(s)
        return s

    def end_phase(self):
        nc = self.nc
        sems = list(self._phase_sems)
        if sems:
            with nc.Block() as blk:
                @blk.gpsimd
                def _(g):
                    for s in sems:
                        g.sem_clear(s.h)
        self._phase_sems = []
        self._pool_idx = 0

    def dbg(self, name, shape, dtype=F32):
        if name in self.dbg_outs:
            return self.dbg_outs[name]
        ap = self.nc.dram_tensor("dbg_" + name, list(shape), dtype, kind="ExternalOutput").ap()
        self.dbg_outs[name] = ap
        return ap



def emit_consts(b):
    nc = b.nc
    ldg = b.sem("c_ldg", g=True)
    lds = b.sem("c_lds")
    mzs = b.sem("c_mzs")
    with nc.Block() as blk:
        @blk.gpsimd
        def _(g):
            for col, val in b._const_vals:
                g.memset(b.consts[:, col:col + 1], val)
            g.memset(b.zb, 0.0)
            mz = None
            for nm in ("qB", "kB", "vB", "oB"):
                mz = g.memset(b.BTm[nm], 0.0)
            mz.then_inc(mzs.h, 1)
            g.wait_ge(mzs.h, 1)
            for nm in ("qA", "kA", "vA", "oA"):
                g.dma_start(out=b.AT[nm].rearrange("p (t r) -> p t r", t=NT_D),
                            in_=b.i[nm + "T"].rearrange("(t p) r -> p t r", p=128)).then_inc(ldg.h, 16)
            for nm in ("qB", "kB", "vB", "oB"):
                g.dma_start(out=b.BTm[nm][0:16, :], in_=b.i[nm + "T"]).then_inc(ldg.h, 16)
            g.dma_start(out=b.waf.rearrange("p (t e) -> p t e", t=NT_D),
                        in_=b.i["wa"].rearrange("(t p) e -> p t e", p=128)).then_inc(ldg.h, 16)
            ldg.v += 16 * 9

        @blk.sync
        def _(sync):
            sync.dma_start(out=b.nwf, in_=b.i["nwfm"]).then_inc(lds.h, 16)
            sync.dma_start(out=b.ident, in_=b.ident_d).then_inc(lds.h, 16)
            sync.dma_start(out=b.identf, in_=b.identf_d).then_inc(lds.h, 16)
            lds.v += 48

        @blk.vector
        def _(v):
            v.wait_ge(ldg.h, ldg.v)
            v.wait_ge(lds.h, lds.v)
            for nm in ("qA", "kA", "vA"):
                a3 = b.AT[nm].rearrange("p (t r) -> p t r", t=NT_D)
                for dt in range(NT_D):
                    v.tensor_scalar_mul(a3[:, dt], a3[:, dt], b.nwf[:, dt:dt + 1])
            v.tensor_scalar_mul(b.BTm["qB"], b.BTm["qB"], LORA_SCALE * INV_SQRT_DH)
            for nm in ("kB", "vB", "oB"):
                v.tensor_scalar_mul(b.BTm[nm], b.BTm[nm], LORA_SCALE)
            v.tensor_scalar_mul(b.nwfq, b.nwf, INV_SQRT_DH)
    b.end_phase()

def emit_ln0(b):
    nc = b.nc
    LD0, LD1, LDC, SQ, VAR, RSTD, XN, Q2, RR, TP, EV = (b.sem("n" + s) for s in
        ("LD0", "LD1", "LDC", "SQ", "VAR", "RSTD", "XN", "Q2", "RR", "TP", "EV"))
    LDx = (LD0, LD1)
    off = b.PERSIST
    xt = [b.salloc(f"l0x{k}", [128, D], F32, at=off + k * 8192) for k in range(2)]
    sqd = b.salloc("l0sq", [128, D], F32, at=off + 16384)
    xnt = b.salloc("l0xnt", [128, D], BF16, at=off + 16384 + 8192)
    stt = b.salloc("l0stat", [128, 16], F32, at=off + 16384 + 8192 + 4096)
    rrT = b.salloc("l0rrT", [NT_TH, 128], F32, at=off + 16384 + 8192 + 4096 + 64)
    tsel = b.salloc("l0tsel", [NT_TH, NT_TH * 128], F32, at=off + 16384 + 8192 + 4096 + 64 + 512)

    xin = b.i["xh"].rearrange("(t p) d -> t p d", p=128)
    C1 = float(1.0 / D)
    NT = NT_TH
    sttrace = b.salloc("l0sttr", [128, NT_TH * 16], F32,
                       at=off + 16384 + 8192 + 4096 + 64 + 512 + 6144)

    with nc.Block() as blk:
        @blk.sync
        def _(sync):
            sync.dma_start(out=tsel, in_=b.tsel_d).then_inc(LDC.h, 16)
            LDC.v = 16
            for t in range(NT):
                if t >= 2:
                    sync.wait_ge(XN.h, t - 1)
                    sync.wait_ge(SQ.h, t - 1)
                sync.dma_start(out=xt[t % 2], in_=xin[t]).then_inc(LDx[t % 2].h, 16)
            LD0.v = 16 * ((NT + 1) // 2)
            LD1.v = 16 * (NT // 2)

        @blk.scalar
        def _(s):
            ss = Ser(b, "nSS")
            for t in range(NT):
                s.wait_ge(LDx[t % 2].h, 16 * (t // 2 + 1))
                if t > 0:
                    s.wait_ge(VAR.h, t)
                s.activation(sqd, xt[t % 2], AF.Square,
                             accum_out=stt[:, 1:2]).then_inc(SQ.h, 1)
                s.wait_ge(VAR.h, t + 1)
                ss.step(s.activation(stt[:, 5:6], stt[:, 4:5], AF.Ln, bias=1e-5), s)
                s.activation(stt[:, 6:7], stt[:, 5:6], AF.Exp,
                             scale=-0.5).then_inc(RSTD.h, 1)
                s.wait_ge(Q2.h, t + 1)
                ss.step(s.activation(stt[:, 9:10], stt[:, 8:9], AF.Ln, bias=1e-6), s)
                s.activation(b.rrc[:, t:t + 1], stt[:, 9:10], AF.Exp,
                             scale=-0.5).then_inc(RR.h, 1)
            SQ.v = RSTD.v = RR.v = NT

        @blk.vector
        def _(v):
            sv = Ser(b, "nSV")
            for t in range(NT):
                v.wait_ge(LDx[t % 2].h, 16 * (t // 2 + 1))
                v.wait_ge(SQ.h, t + 1)
                x_ = xt[t % 2]
                sv.step(v.reduce_sum(stt[:, 0:1], x_, AX.X), v)
                sv.step(v.tensor_scalar_mul(stt[:, 2:3], stt[:, 0:1], C1), v)
                sv.step(v.tensor_scalar_mul(stt[:, 3:4], stt[:, 1:2], C1), v)
                sv.step(v.tensor_tensor(stt[:, 4:5], stt[:, 2:3], stt[:, 2:3], ALU.mult), v)
                v.tensor_tensor(stt[:, 4:5], stt[:, 3:4], stt[:, 4:5],
                                ALU.subtract).then_inc(VAR.h, 1)
                v.wait_ge(RSTD.h, t + 1)
                if t > 0:
                    v.wait_ge(TP.h, t * NT_D)
                v.tensor_scalar(xnt, x_, stt[:, 2:3], stt[:, 6:7],
                                ALU.subtract, ALU.mult).then_inc(XN.h, 1)
                sv.step(v.tensor_tensor(stt[:, 7:8], stt[:, 6:7], stt[:, 6:7], ALU.mult), v)
                v.tensor_tensor(stt[:, 8:9], stt[:, 4:5], stt[:, 7:8],
                                ALU.mult).then_inc(Q2.h, 1)
                for dt in range(NT_D):
                    v.wait_ge(TP.h, t * NT_D + dt + 1)
                    pb = b.P[dt % 4].bitcast(BF16)
                    v.tensor_copy(b.xn[dt][:, t * 128:(t + 1) * 128], pb[:, 0:128]).then_inc(EV.h, 1)
            VAR.v = XN.v = Q2.v = NT
            EV.v = NT * NT_D

        @blk.tensor
        def _(t_):
            for t in range(NT):
                t_.wait_ge(XN.h, t + 1)
                for dt in range(NT_D):
                    if t * NT_D + dt >= 4:
                        t_.wait_ge(EV.h, t * NT_D + dt - 3)
                    pb = b.P[dt % 4].bitcast(BF16)
                    t_.transpose(pb[:, 0:128], xnt[:, dt * 128:(dt + 1) * 128],
                                 b.ident).then_inc(TP.h, 1)
            TP.v = NT * NT_D

    TP2, EV2 = b.sem("nTP2"), b.sem("nEV2")
    with nc.Block() as blk:
        @blk.tensor
        def _(t_):
            t_.transpose(b.P[4][0:NT_TH, 0:128], b.rrc[:, 0:NT_TH], b.identf).then_inc(TP2.h, 1)

        @blk.vector
        def _(v):
            v.wait_ge(TP2.h, 1)
            v.tensor_copy(rrT, b.P[4][0:NT_TH, 0:128]).then_inc(EV2.h, 1)

        @blk.tensor
        def _(t_):
            t_.wait_ge(EV2.h, 1)
            t_.wait_ge(LDC.h, 16)
            for t in range(NT_TH):
                t_.matmul(b.P[5][:, 0:128], tsel[:, t * 128:(t + 1) * 128], rrT,
                          start=True, stop=True).then_inc(TP2.h, 1)
                t_.wait_ge(EV2.h, 2 + t)

        @blk.vector
        def _(v):
            for t in range(NT_TH):
                v.wait_ge(TP2.h, 2 + t)
                v.tensor_copy(b.rrb[:, t * 128:(t + 1) * 128], b.P[5][:, 0:128]).then_inc(EV2.h, 1)
            EV2.v = 1 + NT_TH
            TP2.v = 1 + NT_TH

    if "xn0" in b.want_dbg:
        dxn = b.dbg("xn0", (NT_D, 128, TH), BF16)
        drr = b.dbg("rrb", (128, TH))
        ST = b.sem("nST")
        with nc.Block() as blk:
            @blk.sync
            def _(sync):
                for dt in range(NT_D):
                    sync.dma_start(out=dxn[dt], in_=b.xn[dt]).then_inc(ST.h, 16)
                sync.dma_start(out=drr, in_=b.rrb).then_inc(ST.h, 16)
                ST.v = 16 * (NT_D + 1)
                sync.wait_ge(ST.h, ST.v)
    b.end_phase()

def _emit_z(b, Anm, chunks, ZP, ZE1, TPz, ZE):
    subs = [(cb + st * 128) for cb in chunks for st in range(4)]

    def tensor_part(t_):
        pzT = b.P[7].bitcast(BF16)
        for si, tok0 in enumerate(subs):
            if si > 0:
                t_.wait_ge(ZE1.h, si)
            a3 = b.AT[Anm].rearrange("p (t r) -> p t r", t=NT_D)
            for dt in range(NT_D):
                mmz = t_.matmul(b.P[6][:, 0:16], b.xn[dt][:, tok0:tok0 + 128],
                                a3[:, dt], start=(dt == 0), stop=(dt == NT_D - 1))
            mmz.then_inc(ZP.h, 1)
            t_.wait_ge(ZE1.h, si + 1)
            if si > 0:
                t_.wait_ge(ZE.h, si)
            t_.transpose(pzT[0:16, 0:128], b.ztmp, b.ident).then_inc(TPz.h, 1)
        ZP.v += len(subs)
        TPz.v += len(subs)

    def vector_part(v):
        pzT = b.P[7].bitcast(BF16)
        for si, tok0 in enumerate(subs):
            v.wait_ge(ZP.h, si + 1)
            v.tensor_scalar_mul(b.ztmp, b.P[6][:, 0:16],
                                b.rrc[:, tok0 // 128:tok0 // 128 + 1]).then_inc(ZE1.h, 1)
            v.wait_ge(TPz.h, si + 1)
            v.tensor_copy(b.zb[0:16, tok0:tok0 + 128], pzT[0:16, 0:128]).then_inc(ZE.h, 1)
        ZE1.v += len(subs)
        ZE.v += len(subs)

    return tensor_part, vector_part, len(subs)


def emit_proj_qk(b, which):
    nc = b.nc
    off = b.PERSIST
    isq = which == "q"
    wT, Anm, Bnm = ("wqT", "qA", "qB") if isq else ("wkT", "kA", "kB")
    dst = b.qf_d if isq else b.kf_d
    nck = 2 if isq else 3
    cb0 = HALO if isq else 0
    nwcol = b.nwfq if isq else b.nwf

    wbuf = [b.salloc(f"{which}w{k}", [128, NT_D * 256], BF16, at=off + k * 8192)
            for k in range(2)]
    ropec = b.salloc(which + "ropec", [128, TH], BF16, at=off + 16384)
    ropes = b.salloc(which + "ropes", [128, TH], BF16, at=off + 16384 + 3072)
    usc = b.salloc(which + "usc", [128, 512], F32, at=off + 16384 + 6144)
    rotb = b.salloc(which + "rot", [128, 512], F32, at=off + 16384 + 6144 + 2048)
    tmA = b.salloc(which + "tmA", [128, 512], F32, at=off + 16384 + 6144 + 4096)
    tmB = b.salloc(which + "tmB", [128, 512], F32, at=off + 16384 + 6144 + 6144)
    outb = [b.salloc(f"{which}o{k}", [128, 512], BF16,
                     at=off + 16384 + 6144 + 8192 + k * 1024) for k in range(2)]

    WL = [b.sem(which + "WL0", g=True), b.sem(which + "WL1", g=True)]
    RL = b.sem(which + "RL")
    FD = b.sem(which + "FD")
    MM = b.sem(which + "MM")
    ZP, ZE1 = b.sem(which + "ZP"), b.sem(which + "ZE1")
    TPz, ZE = b.sem(which + "TPz"), b.sem(which + "ZE")
    EV = b.sem(which + "EV")
    OE = b.sem(which + "OE")
    SO = [b.sem(which + "SO0"), b.sem(which + "SO1")]
    sv = Ser(b, which + "SV")
    zt, zv, nsubs = _emit_z(b, Anm, [cb0 + c * 512 for c in range(nck)], ZP, ZE1, TPz, ZE)

    gpo = 2 * nck
    ngroups = 8 * gpo

    with nc.Block() as blk:
        @blk.sync
        def _(sync):
            sync.dma_start(out=ropec, in_=b.i["ropecos"]).then_inc(RL.h, 16)
            sync.dma_start(out=ropes, in_=b.i["ropesin"]).then_inc(RL.h, 16)
            RL.v = 32
            for gi in range(ngroups):
                og, ot, ck = gi // gpo, (gi % gpo) // nck, gi % nck
                head = og * 2 + ot
                slot = gi % 2
                sync.wait_ge(OE.h, gi + 1)
                sync.dma_start(out=dst[head][:, ck * 512:(ck + 1) * 512],
                               in_=outb[slot]).then_inc(SO[slot].h, 16)
                SO[slot].v += 16
            sync.wait_ge(SO[0].h, SO[0].v)
            sync.wait_ge(SO[1].h, SO[1].v)

        @blk.gpsimd
        def _(g):
            for og in range(8):
                slot = og % 2
                if og >= 2:
                    g.wait_ge(MM.h, (og - 1) * gpo)
                g.dma_start(out=wbuf[slot].rearrange("p (t o) -> p t o", t=NT_D),
                            in_=b.i[wT][:, og * 256:(og + 1) * 256]
                            .rearrange("(t p) o -> p t o", p=128)).then_inc(WL[slot].h, 16)
                WL[slot].v += 16

        @blk.scalar
        def _(s_):
            for og in range(8):
                slot = og % 2
                s_.wait_ge(WL[slot].h, 16 * (og // 2 + 1))
                w3 = wbuf[slot].rearrange("p (t o) -> p t o", t=NT_D)
                for dt in range(NT_D):
                    m = s_.mul(w3[:, dt], w3[:, dt], nwcol[:, dt:dt + 1])
                m.then_inc(FD.h, 1)
                FD.v += 1

        @blk.tensor
        def _(t_):
            if not NO_LORA:
                zt(t_)
                t_.wait_ge(ZE.h, nsubs)
            for gi in range(ngroups):
                og, ot, ck = gi // gpo, (gi % gpo) // nck, gi % nck
                head = og * 2 + ot
                slot = og % 2
                cb = cb0 + ck * 512
                if gi % gpo == 0:
                    t_.wait_ge(FD.h, og + 1)
                if gi >= 2:
                    t_.wait_ge(EV.h, gi - 1)
                pp = b.P[gi % 2]
                w3 = wbuf[slot].rearrange("p (t o) -> p t o", t=NT_D)
                for dt in range(NT_D):
                    t_.matmul(pp, w3[:, dt, ot * 128:(ot + 1) * 128],
                              b.xn[dt][:, cb:cb + 512], start=(dt == 0),
                              stop=(NO_LORA and dt == NT_D - 1))
                if not NO_LORA:
                    t_.matmul(pp, b.BTm[Bnm][:, head * 128:(head + 1) * 128],
                              b.zb[:, cb:cb + 512],
                              start=False, stop=True).then_inc(MM.h, 1)
                else:
                    t_.nop().then_inc(MM.h, 1)
                MM.v += 1

        @blk.vector
        def _(v):
            if not NO_LORA:
                zv(v)
            v.wait_ge(RL.h, 32)
            for gi in range(ngroups):
                og, ot, ck = gi // gpo, (gi % gpo) // nck, gi % nck
                slot = gi % 2
                cb = cb0 + ck * 512
                pp = b.P[gi % 2]
                v.wait_ge(MM.h, gi + 1)
                if gi >= 2:
                    v.wait_ge(SO[slot].h, 16 * (gi // 2))
                sv.step(v.tensor_tensor(usc, pp, b.rrb[:, cb:cb + 512], ALU.mult), v)
                nc.vector.engine_nop().then_inc(EV.h, 1)
                EV.v += 1
                v.tensor_scalar_mul(rotb[0:64, :], usc[64:128, :], -1.0)
                sv.step(v.tensor_scalar_mul(rotb[64:128, :], usc[0:64, :], 1.0), v)
                sv.step(v.tensor_tensor(tmA, usc, ropec[:, cb:cb + 512], ALU.mult), v)
                sv.step(v.tensor_tensor(tmB, rotb, ropes[:, cb:cb + 512], ALU.mult), v)
                v.tensor_tensor(outb[slot], tmA, tmB, ALU.add).then_inc(OE.h, 1)
                OE.v += 1
    assert MM.v == ngroups
    b.end_phase()

def emit_proj_v(b):
    nc = b.nc
    off = b.PERSIST
    wbuf = [b.salloc(f"vw{k}", [128, NT_D * 512], BF16, at=off + k * 16384) for k in range(2)]
    outb = [b.salloc(f"vo{k}", [128, 512], BF16, at=off + 32768 + k * 1024) for k in range(2)]
    WL = [b.sem("vWL0", g=True), b.sem("vWL1", g=True)]
    FD = b.sem("vFD")
    MM = b.sem("vMM")
    ZP, ZE1 = b.sem("vZP"), b.sem("vZE1")
    TPz, ZE = b.sem("vTPz"), b.sem("vZE")
    EVv = b.sem("vEV")
    OE = b.sem("vOE")
    SO = [b.sem("vSO0"), b.sem("vSO1")]
    zt, zv, nsubs = _emit_z(b, "vA", [c * 512 for c in range(3)], ZP, ZE1, TPz, ZE)
    ngroups = 4 * NT_TH

    with nc.Block() as blk:
        @blk.sync
        def _(sync):
            for gi in range(ngroups):
                dvc, tt = gi // NT_TH, gi % NT_TH
                slot = gi % 2
                sync.wait_ge(OE.h, gi + 1)
                sync.dma_start(out=b.vt_d[tt][:, dvc * 512:(dvc + 1) * 512],
                               in_=outb[slot]).then_inc(SO[slot].h, 16)
                SO[slot].v += 16
            sync.wait_ge(SO[0].h, SO[0].v)
            sync.wait_ge(SO[1].h, SO[1].v)

        @blk.gpsimd
        def _(g):
            for dvc in range(4):
                slot = dvc % 2
                if dvc >= 2:
                    g.wait_ge(MM.h, (dvc - 1) * NT_TH)
                g.dma_start(out=wbuf[slot].rearrange("p (t o) -> p t o", t=NT_D),
                            in_=b.i["wvT"][:, dvc * 512:(dvc + 1) * 512]
                            .rearrange("(t p) o -> p t o", p=128)).then_inc(WL[slot].h, 16)
                WL[slot].v += 16

        @blk.scalar
        def _(s_):
            for dvc in range(4):
                slot = dvc % 2
                s_.wait_ge(WL[slot].h, 16 * (dvc // 2 + 1))
                w3 = wbuf[slot].rearrange("p (t o) -> p t o", t=NT_D)
                for dt in range(NT_D):
                    m = s_.mul(w3[:, dt], w3[:, dt], b.nwf[:, dt:dt + 1])
                m.then_inc(FD.h, 1)
                FD.v += 1

        @blk.tensor
        def _(t_):
            zt(t_)
            t_.wait_ge(ZE.h, nsubs)
            for gi in range(ngroups):
                dvc, tt = gi // NT_TH, gi % NT_TH
                slot = dvc % 2
                if gi % NT_TH == 0:
                    t_.wait_ge(FD.h, dvc + 1)
                if gi >= 2:
                    t_.wait_ge(EVv.h, gi - 1)
                pp = b.P[gi % 2]
                w3 = wbuf[slot].rearrange("p (t o) -> p t o", t=NT_D)
                for dt in range(NT_D):
                    t_.matmul(pp, b.xn[dt][:, tt * 128:(tt + 1) * 128],
                              w3[:, dt], start=(dt == 0), stop=False)
                t_.matmul(pp, b.zb[:, tt * 128:(tt + 1) * 128],
                          b.BTm["vB"][:, dvc * 512:(dvc + 1) * 512],
                          start=False, stop=True).then_inc(MM.h, 1)
                MM.v += 1

        @blk.vector
        def _(v):
            zv(v)
            for gi in range(ngroups):
                dvc, tt = gi // NT_TH, gi % NT_TH
                slot = gi % 2
                pp = b.P[gi % 2]
                v.wait_ge(MM.h, gi + 1)
                if gi >= 2:
                    v.wait_ge(SO[slot].h, 16 * (gi // 2))
                v.tensor_scalar_mul(outb[slot], pp, b.rrc[:, tt:tt + 1]).then_inc(OE.h, 1)
                OE.v += 1
                nc.vector.engine_nop().then_inc(EVv.h, 1)
                EVv.v += 1
    assert MM.v == ngroups
    b.end_phase()

def emit_attn(b):
    nc = b.nc
    off = b.PERSIST
    Kh = [b.salloc(f"aK{k}", [128, TH], BF16, at=off + k * 3072) for k in range(2)]
    Vh = [b.salloc(f"aV{k}", [128, TH], BF16, at=off + 6144 + k * 3072) for k in range(2)]
    Qh = [b.salloc(f"aQ{k}", [128, T_OWN], BF16, at=off + 12288 + k * 2048) for k in range(2)]
    msk = b.salloc("amsk", [128, 5 * 640], F32, at=off + 16384)
    s_sb = b.salloc("assb", [128, 640], F32, at=off + 16384 + 12800)
    pf = b.salloc("apf", [128, 640], F32, at=off + 16384 + 12800 + 2560)
    pbf = b.salloc("apbf", [128, 640], BF16, at=off + 16384 + 12800 + 5120)
    PTs = b.salloc("aPT", [128, 640], BF16, at=off + 16384 + 12800 + 5120 + 1280)
    ysb = [b.salloc(f"ay{k}", [128, 128], BF16, at=off + 16384 + 12800 + 5120 + 2560 + k * 256)
           for k in range(2)]
    ast = b.salloc("aast", [128, 4], F32, at=off + 16384 + 12800 + 5120 + 2560 + 512)

    KL = [b.sem("aKL0", g=True), b.sem("aKL1", g=True)]
    ML = b.sem("aML")
    MMD = b.sem("aMMD")
    SCP = b.sem("aSCP")
    SCE = b.sem("aSCE")
    NM = b.sem("aNM")
    EXP = b.sem("aEXP")
    PBF = b.sem("aPBF")
    TPP = b.sem("aTPP")
    PTE = b.sem("aPTE")
    OV = b.sem("aOV")
    YE = b.sem("aYE")
    YS = [b.sem("aYS0"), b.sem("aYS1")]
    sv = Ser(b, "aSV")

    NP = H * NT_OWN

    with nc.Block() as blk:
        @blk.sync
        def _(sync):
            for ms in range(5):
                sync.dma_start(out=msk[:, ms * 640:(ms + 1) * 640],
                               in_=b.i["mask5"][ms]).then_inc(ML.h, 16)
            ML.v = 80
            for i in range(NP):
                h, qt = i // NT_OWN, i % NT_OWN
                slot = i % 2
                sync.wait_ge(YE.h, i + 1)
                sync.dma_start(out=b.yattn_d[h][:, qt * 128:(qt + 1) * 128],
                               in_=ysb[slot]).then_inc(YS[slot].h, 16)
                YS[slot].v += 16
            sync.wait_ge(YS[0].h, YS[0].v)
            sync.wait_ge(YS[1].h, YS[1].v)

        @blk.gpsimd
        def _(g):
            for h in range(H):
                slot = h % 2
                if h >= 2:
                    g.wait_ge(OV.h, NT_OWN * (h - 1))
                g.dma_start(out=Kh[slot], in_=b.kf_d[h]).then_inc(KL[slot].h, 16)
                g.dma_start(out=Vh[slot].rearrange("p (t c) -> p t c", t=NT_TH),
                            in_=b.vt_d[:, :, h * 128:(h + 1) * 128]
                            .rearrange("t p c -> p t c")).then_inc(KL[slot].h, 16)
                g.dma_start(out=Qh[slot], in_=b.qf_d[h]).then_inc(KL[slot].h, 16)
                KL[slot].v += 48

        @blk.tensor
        def _(t_):
            for i in range(NP):
                h, qt = i // NT_OWN, i % NT_OWN
                slot = h % 2
                if qt == 0:
                    t_.wait_ge(KL[slot].h, 48 * (h // 2 + 1))
                if i > 0:
                    t_.wait_ge(SCE.h, i)
                t_.matmul(b.P[0], Qh[slot][:, qt * 128:(qt + 1) * 128],
                          Kh[slot][:, qt * 128:qt * 128 + 512], start=True, stop=True)
                t_.matmul(b.P[1][:, 0:128], Qh[slot][:, qt * 128:(qt + 1) * 128],
                          Kh[slot][:, qt * 128 + 512:qt * 128 + 640],
                          start=True, stop=True).then_inc(SCP.h, 1)
                SCP.v += 1
                t_.wait_ge(PBF.h, i + 1)
                for j in range(5):
                    if 5 * i + j >= 2:
                        t_.wait_ge(PTE.h, 5 * i + j - 1)
                    pb = b.P[2 + (j % 2)].bitcast(BF16)
                    t_.transpose(pb[:, 0:128], pbf[:, j * 128:(j + 1) * 128],
                                 b.ident).then_inc(TPP.h, 1)
                    TPP.v += 1
                if i >= 2:
                    t_.wait_ge(YE.h, i - 1)
                t_.wait_ge(PTE.h, 5 * i + 5)
                po = b.P[4 + (i % 2)]
                for j in range(5):
                    mm = t_.matmul(po[:, 0:128], Vh[slot][:, (qt + j) * 128:(qt + j + 1) * 128],
                                   PTs[:, j * 128:(j + 1) * 128],
                                   start=(j == 0), stop=(j == 4))
                mm.then_inc(OV.h, 1)
                OV.v += 1

        @blk.vector
        def _(v):
            v.wait_ge(ML.h, 80)
            for i in range(NP):
                h, qt = i // NT_OWN, i % NT_OWN
                mslot = min(qt, 4)
                if i >= 1:
                    v.wait_ge(EXP.h, i)
                v.wait_ge(SCP.h, i + 1)
                v.tensor_tensor(s_sb[:, 0:512], b.P[0], msk[:, mslot * 640:mslot * 640 + 512],
                                ALU.add)
                v.tensor_tensor(s_sb[:, 512:640], b.P[1][:, 0:128],
                                msk[:, mslot * 640 + 512:(mslot + 1) * 640],
                                ALU.add).then_inc(SCE.h, 1)
                SCE.v += 1
                v.wait_ge(SCE.h, SCE.v)
                sv.step(v.reduce_max(ast[:, 0:1], s_sb, AX.X, negate=True), v)
                nc.vector.engine_nop().then_inc(NM.h, 1)
                NM.v += 1
                v.wait_ge(EXP.h, i + 1)
                sv.step(v.reciprocal(ast[:, 2:3], ast[:, 1:2]), v)
                v.tensor_scalar_mul(pbf, pf, ast[:, 2:3]).then_inc(PBF.h, 1)
                PBF.v += 1
                for j in range(5):
                    v.wait_ge(TPP.h, 5 * i + j + 1)
                    pb = b.P[2 + (j % 2)].bitcast(BF16)
                    v.tensor_copy(PTs[:, j * 128:(j + 1) * 128], pb[:, 0:128]).then_inc(PTE.h, 1)
                    PTE.v += 1
                v.wait_ge(OV.h, i + 1)
                slot = i % 2
                if i >= 2:
                    v.wait_ge(YS[slot].h, 16 * (i // 2))
                v.tensor_copy(ysb[slot], b.P[4 + (i % 2)][:, 0:128]).then_inc(YE.h, 1)
                YE.v += 1

        @blk.scalar
        def _(s_):
            for i in range(NP):
                s_.wait_ge(NM.h, i + 1)
                if i >= 1:
                    s_.wait_ge(PBF.h, i)
                s_.activation(pf, s_sb, AF.Exp, bias=ast[:, 0:1],
                              accum_out=ast[:, 1:2]).then_inc(EXP.h, 1)
                EXP.v += 1
    b.end_phase()

def _dump_yattn(b):
    nc = b.nc
    d = b.dbg("yat", (NT_D, 128, T_OWN), BF16)
    ST = b.sem("dyST")
    with nc.Block() as blk:
        @blk.sync
        def _(sync):
            sync.dma_start(out=d, in_=b.yattn_d).then_inc(ST.h, 16)
            ST.v = 16
            sync.wait_ge(ST.h, ST.v)
    b.end_phase()

def emit_oproj(b):
    nc = b.nc
    b.s_fm = b.salloc("s_fm", [128, NT_D * T_OWN], BF16)
    off = b.sboff
    ybuf = b.salloc("oyb", [128, NT_D * T_OWN], BF16, at=off)
    wbuf = [b.salloc(f"ow{k}", [128, NT_D * 256], BF16, at=off + 32768 + k * 8192)
            for k in range(2)]
    y3 = ybuf.rearrange("p (t c) -> p t c", t=NT_D)
    s3 = b.s_fm.rearrange("p (t c) -> p t c", t=NT_D)

    YL = b.sem("oYL", g=True)
    WL = [b.sem("oWL0", g=True), b.sem("oWL1", g=True)]
    MM = b.sem("oMM")
    ZP, ZE1 = b.sem("oZP"), b.sem("oZE1")
    TPz, ZE = b.sem("oTPz"), b.sem("oZE")
    EV = b.sem("oEV")
    ngroups = 8 * 2 * 2

    with nc.Block() as blk:
        @blk.gpsimd
        def _(g):
            g.dma_start(out=y3, in_=b.yattn_d.rearrange("t p c -> p t c")).then_inc(YL.h, 16)
            YL.v += 16
            for og in range(8):
                slot = og % 2
                if og >= 2:
                    g.wait_ge(MM.h, (og - 1) * 4)
                g.dma_start(out=wbuf[slot].rearrange("p (t o) -> p t o", t=NT_D),
                            in_=b.i["woT"][:, og * 256:(og + 1) * 256]
                            .rearrange("(t p) o -> p t o", p=128)).then_inc(WL[slot].h, 16)
                WL[slot].v += 16

        @blk.tensor
        def _(t_):
            t_.wait_ge(YL.h, 16)
            a3 = b.AT["oA"].rearrange("p (t r) -> p t r", t=NT_D)
            pzT = b.P[7].bitcast(BF16)
            for si in range(8):
                tok0 = si * 128
                if si > 0:
                    t_.wait_ge(ZE1.h, si)
                for dt in range(NT_D):
                    mmz = t_.matmul(b.P[6][:, 0:16], y3[:, dt, tok0:tok0 + 128],
                                    a3[:, dt], start=(dt == 0), stop=(dt == NT_D - 1))
                mmz.then_inc(ZP.h, 1)
                ZP.v += 1
                t_.wait_ge(ZE1.h, si + 1)
                if si > 0:
                    t_.wait_ge(ZE.h, si)
                t_.transpose(pzT[0:16, 0:128], b.ztmp, b.ident).then_inc(TPz.h, 1)
                TPz.v += 1
            t_.wait_ge(ZE.h, 8)
            for gi in range(ngroups):
                og, ot, ck = gi // 4, (gi % 4) // 2, gi % 2
                head = og * 2 + ot
                slot = og % 2
                if gi % 4 == 0:
                    t_.wait_ge(WL[slot].h, 16 * (og // 2 + 1))
                if gi >= 2:
                    t_.wait_ge(EV.h, gi - 1)
                pp = b.P[gi % 2]
                w3 = wbuf[slot].rearrange("p (t o) -> p t o", t=NT_D)
                for dt in range(NT_D):
                    t_.matmul(pp, w3[:, dt, ot * 128:(ot + 1) * 128],
                              y3[:, dt, ck * 512:(ck + 1) * 512], start=(dt == 0), stop=False)
                t_.matmul(pp, b.BTm["oB"][:, head * 128:(head + 1) * 128],
                          b.zb[:, ck * 512:(ck + 1) * 512],
                          start=False, stop=True).then_inc(MM.h, 1)
                MM.v += 1

        @blk.vector
        def _(v):
            pzT = b.P[7].bitcast(BF16)
            for si in range(8):
                v.wait_ge(ZP.h, si + 1)
                v.tensor_copy(b.ztmp, b.P[6][:, 0:16]).then_inc(ZE1.h, 1)
                ZE1.v += 1
                v.wait_ge(TPz.h, si + 1)
                v.tensor_copy(b.zb[0:16, si * 128:(si + 1) * 128],
                              pzT[0:16, 0:128]).then_inc(ZE.h, 1)
                ZE.v += 1
            for gi in range(ngroups):
                og, ot, ck = gi // 4, (gi % 4) // 2, gi % 2
                head = og * 2 + ot
                pp = b.P[gi % 2]
                v.wait_ge(MM.h, gi + 1)
                v.tensor_tensor(s3[:, head, ck * 512:(ck + 1) * 512], pp,
                                b.xn[head][:, HALO + ck * 512:HALO + (ck + 1) * 512],
                                ALU.add).then_inc(EV.h, 1)
                EV.v += 1
    b.end_phase()

def _dump_s(b):
    nc = b.nc
    d = b.dbg("s0", (128, NT_D * T_OWN), BF16)
    ST = b.sem("dsST")
    with nc.Block() as blk:
        @blk.sync
        def _(sync):
            sync.dma_start(out=d, in_=b.s_fm).then_inc(ST.h, 16)
            ST.v = 16
            sync.wait_ge(ST.h, ST.v)
    b.end_phase()

def emit_router(b):
    nc = b.nc
    off = 64 * 1024 + 512
    aff = b.salloc("raff", [128, 16], F32, at=off)
    eqt = b.salloc("reqt", [128, 16], F32, at=off + 64)
    tbd = b.salloc("rtbd", [128, 16], F32, at=off + 128)
    rst = b.salloc("rrst", [128, 16], F32, at=off + 192)
    ctm = b.salloc("rctm", [128, 16], BF16, at=off + 256)
    es = b.salloc("resel", [128, NE * 128], BF16)
    b._moe_es = es

    EL = b.sem("rEL")
    ELM = b.sem("rELM")
    AP_ = b.sem("rAP")
    SG = b.sem("rSG")
    NM1 = b.sem("rNM1")
    EQS = b.sem("rEQS")
    MS = b.sem("rMS")
    SG2 = b.sem("rSG2")
    CMB = b.sem("rCMB")
    TPC = b.sem("rTPC")
    CEV = b.sem("rCEV")
    sv = Ser(b, "rSV")
    ss = Ser(b, "rSS")

    with nc.Block() as blk:
        @blk.sync
        def _(sync):
            sync.dma_start(out=es, in_=b.esel_d).then_inc(EL.h, 16)
            EL.v = 16

        @blk.gpsimd
        def _(g):
            g.memset(b.combT, 0.0).then_inc(ELM.h, 1)
            ELM.v += 1

        @blk.tensor
        def _(t_):
            t_.wait_ge(EL.h, EL.v)
            t_.wait_ge(ELM.h, ELM.v)
            pcT = b.P[7].bitcast(BF16)
            for tt in range(NT_OWN):
                if tt > 0:
                    t_.wait_ge(SG.h, tt)
                for dt in range(NT_D):
                    mm = t_.matmul(b.P[5][:, 0:16],
                                   b.xn[dt][:, HALO + tt * 128:HALO + (tt + 1) * 128],
                                   b.waf.rearrange("p (t e) -> p t e", t=NT_D)[:, dt],
                                   start=(dt == 0), stop=(dt == NT_D - 1))
                mm.then_inc(AP_.h, 1)
                AP_.v += 1
                t_.wait_ge(CMB.h, tt + 1)
                if tt > 0:
                    t_.wait_ge(CEV.h, tt)
                t_.transpose(pcT[0:16, 0:128], ctm, b.ident).then_inc(TPC.h, 1)
                TPC.v += 1

        @blk.scalar
        def _(s_):
            for tt in range(NT_OWN):
                s_.wait_ge(AP_.h, tt + 1)
                if tt > 0:
                    s_.wait_ge(CMB.h, tt)
                ss.step(s_.activation(aff, b.P[5][:, 0:16], AF.Sigmoid), s_)
                s_.nop().then_inc(SG.h, 1)
                SG.v += 1
                s_.wait_ge(NM1.h, tt + 1)
                ss.step(s_.activation(eqt, aff, AF.Sign, bias=rst[:, 0:1]), s_)
                s_.nop().then_inc(EQS.h, 1)
                EQS.v += 1
                s_.wait_ge(MS.h, tt + 1)
                ss.step(s_.activation(tbd, aff, AF.Sign, bias=rst[:, 9:10]), s_)
                s_.nop().then_inc(SG2.h, 1)
                SG2.v += 1

        @blk.vector
        def _(v):
            for tt in range(NT_OWN):
                v.wait_ge(SG.h, tt + 1)
                sv.step(v.reduce_max(rst[:, 0:1], aff, AX.X, negate=True), v)
                v.reduce_max(rst[:, 1:2], aff, AX.X).then_inc(NM1.h, 1)
                NM1.v += 1
                v.wait_ge(EQS.h, tt + 1)
                sv.step(v.tensor_scalar_add(eqt, eqt, 1.0), v)
                sv.step(v.reduce_sum(rst[:, 5:6], eqt, AX.X), v)
                sv.step(v.scalar_tensor_tensor(tbd, eqt, 1e30, aff, ALU.mult, ALU.subtract), v)
                sv.step(v.tensor_reduce(rst[:, 6:7], tbd, AX.X, ALU.min, negate=True), v)
                sv.step(v.tensor_scalar(rst[:, 7:8], rst[:, 5:6], -1.5, 0.0, ALU.add, ALU.max), v)
                sv.step(v.tensor_scalar_min(rst[:, 7:8], rst[:, 7:8], 1.0), v)
                sv.step(v.tensor_tensor(rst[:, 8:9], rst[:, 1:2], rst[:, 6:7], ALU.subtract), v)
                sv.step(v.tensor_tensor(rst[:, 8:9], rst[:, 8:9], rst[:, 7:8], ALU.mult), v)
                sv.step(v.tensor_tensor(rst[:, 8:9], rst[:, 6:7], rst[:, 8:9], ALU.add), v)
                sv.step(v.tensor_scalar_mul(rst[:, 9:10], rst[:, 8:9], -1.0), v)
                sv.step(v.tensor_tensor(rst[:, 11:12], rst[:, 1:2], rst[:, 8:9], ALU.add), v)
                v.reciprocal(rst[:, 12:13], rst[:, 11:12]).then_inc(MS.h, 1)
                MS.v += 1
                v.wait_ge(SG2.h, tt + 1)
                sv.step(v.tensor_scalar(tbd, tbd, 1.0, 1.0, ALU.add, ALU.min), v)
                sv.step(v.tensor_tensor(tbd, aff, tbd, ALU.mult), v)
                v.tensor_scalar_mul(ctm, tbd, rst[:, 12:13]).then_inc(CMB.h, 1)
                CMB.v += 1
                v.wait_ge(TPC.h, tt + 1)
                pcT = b.P[7].bitcast(BF16)
                v.tensor_copy(b.combT[0:16, tt * 128:(tt + 1) * 128],
                              pcT[0:16, 0:128]).then_inc(CEV.h, 1)
                CEV.v += 1
    b.end_phase()

def emit_moe(b):
    nc = b.nc
    b.xnc = b.salloc("xnc", [128, NT_D * T_OWN], BF16)
    xc3 = b.xnc.rearrange("p (t c) -> p t c", t=NT_D)
    CPX = b.sem("mCPX")
    with nc.Block() as blk:
        @blk.vector
        def _(v):
            for dt in range(NT_D):
                cp = v.tensor_copy(xc3[:, dt], b.xn[dt][:, HALO:HALO + T_OWN])
            cp.then_inc(CPX.h, 1)
            CPX.v = 1

    acc = b.salloc("macc", [128, NT_D * T_OWN], F32, at=0)
    b.acc = acc
    a3 = acc.rearrange("p (t c) -> p t c", t=NT_D)
    off2 = 65536
    gbuf = b.salloc("mgel", [128, 512], F32, at=off2)
    tmp2 = b.salloc("mtmp2", [128, 512], F32, at=off2 + 2048)
    rgb3 = b.salloc("mrgb", [128, NT_F * NE], F32, at=off2 + 4096)
    rb13 = b.salloc("mrb1", [128, NT_F * NE], F32, at=off2 + 4608)
    sgb3 = b.salloc("msgb", [128, NT_F * NSH], F32, at=off2 + 5120)
    sb13 = b.salloc("msb1", [128, NT_F * NSH], F32, at=off2 + 5184)
    rb2p = b.salloc("mrb2p", [128, D], BF16, at=off2 + 5248)
    sb2p = b.salloc("msb2p", [128, D], BF16, at=off2 + 5248 + 4096)
    assert off2 + 5248 + 8192 <= 79072
    o2p = b.salloc("mo2p", [128, 512], BF16)
    cbb = b.salloc("mcbb", [128, T_OWN], BF16)
    b._moe_offS = b.sboff
    wg = [b.salloc(f"mwg{k}", [128, NT_D * 128], BF16) for k in range(2)]
    w1b = [b.salloc(f"mw1{k}", [128, NT_D * 128], BF16) for k in range(2)]
    w2b = [b.salloc(f"mw2{k}", [128, NT_F * 256], BF16) for k in range(2)]
    yb = b.salloc("myb", [128, NT_F * T_OWN], BF16)
    y3 = yb.rearrange("p (t c) -> p t c", t=NT_F)
    assert b.sboff <= 206000, b.sboff

    NEX = NE + NSH
    WLG = [b.sem("mWLG0", g=True), b.sem("mWLG1", g=True)]
    WL2 = [b.sem("mWL20", g=True), b.sem("mWL21", g=True)]
    BL = b.sem("mBL", g=True)
    CB = b.sem("mCB")
    CBE = b.sem("mCBE")
    HG = b.sem("mHG")
    H1 = b.sem("mH1")
    GE = b.sem("mGE")
    YG = b.sem("mYG")
    YE = b.sem("mYE")
    W2M = b.sem("mW2M")
    AE = b.sem("mAE")
    MZ2 = b.sem("mMZ2")
    sv = Ser(b, "mSV")

    def eW(e):
        if e < NE:
            return "rg", "rw1", "rw2", rgb3, rb13, e
        return "sg", "sw1", "sw2", sgb3, sb13, e - NE

    with nc.Block() as blk:
        @blk.gpsimd
        def _(g):
            g.dma_start(out=rgb3.rearrange("p (t e) -> p t e", t=NT_F),
                        in_=b.i["rgbT"].rearrange("(t p) e -> p t e", p=128)).then_inc(BL.h, 16)
            g.dma_start(out=rb13.rearrange("p (t e) -> p t e", t=NT_F),
                        in_=b.i["rb1T"].rearrange("(t p) e -> p t e", p=128)).then_inc(BL.h, 16)
            g.dma_start(out=sgb3.rearrange("p (t e) -> p t e", t=NT_F),
                        in_=b.i["sgbT"].rearrange("(t p) e -> p t e", p=128)).then_inc(BL.h, 16)
            g.dma_start(out=sb13.rearrange("p (t e) -> p t e", t=NT_F),
                        in_=b.i["sb1T"].rearrange("(t p) e -> p t e", p=128)).then_inc(BL.h, 16)
            g.memset(rb2p, 0.0)
            mz = g.memset(sb2p, 0.0)
            mz.then_inc(MZ2.h, 1)
            g.wait_ge(MZ2.h, 1)
            g.wait_ge(BL.h, 64)
            g.dma_start(out=rb2p[0:16, :], in_=b.i["rb2"]).then_inc(BL.h, 16)
            g.dma_start(out=sb2p[0:2, :], in_=b.i["sb2"]).then_inc(BL.h, 16)
            g.dma_start(out=o2p, in_=b.ones2_d).then_inc(BL.h, 16)
            BL.v = 64 + 48
            MZ2.v = 1
            for e in range(NEX):
                ng, n1, n2, _, _, ei = eW(e)
                for sl in range(8):
                    ls = e * 8 + sl
                    slot = ls % 2
                    if ls >= 2:
                        g.wait_ge(H1.h, 2 * (ls - 1))
                    g.dma_start(out=wg[slot].rearrange("p (t o) -> p t o", t=NT_D),
                                in_=b.i[ng][ei][:, sl * 128:(sl + 1) * 128]
                                .rearrange("(t p) o -> p t o", p=128)).then_inc(WLG[slot].h, 16)
                    g.dma_start(out=w1b[slot].rearrange("p (t o) -> p t o", t=NT_D),
                                in_=b.i[n1][ei][:, sl * 128:(sl + 1) * 128]
                                .rearrange("(t p) o -> p t o", p=128)).then_inc(WLG[slot].h, 16)
                    WLG[slot].v += 32
                for sl in range(8):
                    ls = e * 8 + sl
                    slot = ls % 2
                    if ls >= 2:
                        g.wait_ge(W2M.h, 32 + 4 * (ls - 1))
                    g.dma_start(out=w2b[slot].rearrange("p (t o) -> p t o", t=NT_F),
                                in_=b.i[n2][ei][:, sl * 256:(sl + 1) * 256]
                                .rearrange("(t p) o -> p t o", p=128)).then_inc(WL2[slot].h, 16)
                    WL2[slot].v += 16

        @blk.tensor
        def _(t_):
            t_.wait_ge(CPX.h, 1)
            t_.wait_ge(BL.h, BL.v)
            t_.wait_ge(MZ2.h, 1)
            w3g = [w.rearrange("p (t o) -> p t o", t=NT_D) for w in wg]
            w31 = [w.rearrange("p (t o) -> p t o", t=NT_D) for w in w1b]
            w32 = [w.rearrange("p (t o) -> p t o", t=NT_F) for w in w2b]
            for gi in range(32):
                dt, ch = gi // 2, gi % 2
                if gi >= 2:
                    t_.wait_ge(AE.h, gi - 1)
                pp = b.P[4 + (gi % 2)]
                t_.matmul(pp, rb2p[:, dt * 128:(dt + 1) * 128],
                          b.combT[:, ch * 512:(ch + 1) * 512], start=True, stop=False)
                t_.matmul(pp, sb2p[:, dt * 128:(dt + 1) * 128],
                          o2p, start=False, stop=True).then_inc(W2M.h, 1)
                W2M.v += 1
            for e in range(NEX):
                if e < NE:
                    for ch in range(2):
                        if 2 * e + ch >= 1:
                            t_.wait_ge(CBE.h, 2 * e + ch)
                        if e >= 1 and ch == 0:
                            t_.wait_ge(YE.h, 16 * e)
                        t_.matmul(b.P[6], b._moe_es[:, e * 128:(e + 1) * 128],
                                  b.combT[:, ch * 512:(ch + 1) * 512],
                                  start=True, stop=True).then_inc(CB.h, 1)
                        CB.v += 1
                for gi in range(16):
                    ft, ch = gi // 2, gi % 2
                    ls = e * 8 + ft
                    slot = ls % 2
                    gg = e * 16 + gi
                    t_.wait_ge(WLG[slot].h, 32 * (ls // 2 + 1))
                    if gg >= 2:
                        t_.wait_ge(GE.h, gg - 1)
                        t_.wait_ge(YG.h, gg - 1)
                    pg = b.P[0 + (gg % 2)]
                    p1 = b.P[2 + (gg % 2)]
                    for dt in range(NT_D):
                        mmg = t_.matmul(pg, w3g[slot][:, dt],
                                        xc3[:, dt, ch * 512:(ch + 1) * 512],
                                        start=(dt == 0), stop=(dt == NT_D - 1))
                    mmg.then_inc(HG.h, 1)
                    HG.v += 1
                    for dt in range(NT_D):
                        mm1 = t_.matmul(p1, w31[slot][:, dt],
                                        xc3[:, dt, ch * 512:(ch + 1) * 512],
                                        start=(dt == 0), stop=(dt == NT_D - 1))
                    mm1.then_inc(H1.h, 1)
                    H1.v += 1
                t_.wait_ge(YE.h, 16 * (e + 1))
                for wgi in range(32):
                    dt, ch = wgi // 2, wgi % 2
                    sl = dt // 2
                    ls2 = e * 8 + sl
                    slot = ls2 % 2
                    gw = 32 + e * 32 + wgi
                    t_.wait_ge(WL2[slot].h, 16 * (ls2 // 2 + 1))
                    if gw >= 2:
                        t_.wait_ge(AE.h, gw - 1)
                    pp = b.P[4 + (gw % 2)]
                    for ft in range(NT_F):
                        mmw = t_.matmul(pp, w32[slot][:, ft, (dt % 2) * 128:(dt % 2 + 1) * 128],
                                        y3[:, ft, ch * 512:(ch + 1) * 512],
                                        start=(ft == 0), stop=(ft == NT_F - 1))
                    mmw.then_inc(W2M.h, 1)
                    W2M.v += 1

        @blk.scalar
        def _(s_):
            for e in range(NEX):
                _, _, _, gb3, b13, ei = eW(e)
                for gi in range(16):
                    ft, ch = gi // 2, gi % 2
                    gg = e * 16 + gi
                    s_.wait_ge(HG.h, gg + 1)
                    if gg >= 1:
                        s_.wait_ge(YG.h, gg)
                    pg = b.P[0 + (gg % 2)]
                    s_.activation(gbuf, pg, AF.Gelu,
                                  bias=gb3.rearrange("p (t e) -> p t e", t=NT_F)[:, ft, ei:ei + 1]
                                  ).then_inc(GE.h, 1)
                    GE.v += 1

        @blk.vector
        def _(v):
            for gi0 in range(32):
                v.wait_ge(W2M.h, gi0 + 1)
                dt, ch = gi0 // 2, gi0 % 2
                pp = b.P[4 + (gi0 % 2)]
                v.tensor_copy(a3[:, dt, ch * 512:(ch + 1) * 512], pp).then_inc(AE.h, 1)
                AE.v += 1
            for e in range(NEX):
                _, _, _, gb3, b13, ei = eW(e)
                if e < NE:
                    for ch in range(2):
                        v.wait_ge(CB.h, 2 * e + ch + 1)
                        v.tensor_copy(cbb[:, ch * 512:(ch + 1) * 512],
                                      b.P[6]).then_inc(CBE.h, 1)
                        CBE.v += 1
                for gi in range(16):
                    ft, ch = gi // 2, gi % 2
                    gg = e * 16 + gi
                    v.wait_ge(GE.h, gg + 1)
                    v.wait_ge(H1.h, gg + 1)
                    if gg >= 1:
                        v.wait_ge(YE.h, gg)
                    p1 = b.P[2 + (gg % 2)]
                    sv.step(v.scalar_tensor_tensor(
                        tmp2, p1, b13.rearrange("p (t e) -> p t e", t=NT_F)[:, ft, ei:ei + 1],
                        gbuf, ALU.add, ALU.mult), v)
                    nc.vector.engine_nop().then_inc(YG.h, 1)
                    YG.v += 1
                    if e < NE:
                        v.tensor_tensor(y3[:, ft, ch * 512:(ch + 1) * 512], tmp2,
                                        cbb[:, ch * 512:(ch + 1) * 512],
                                        ALU.mult).then_inc(YE.h, 1)
                    else:
                        v.tensor_copy(y3[:, ft, ch * 512:(ch + 1) * 512],
                                      tmp2).then_inc(YE.h, 1)
                    YE.v += 1
                for wgi in range(32):
                    gw = 32 + e * 32 + wgi
                    dt, ch = wgi // 2, wgi % 2
                    v.wait_ge(W2M.h, gw + 1)
                    pp = b.P[4 + (gw % 2)]
                    v.tensor_tensor(a3[:, dt, ch * 512:(ch + 1) * 512], pp,
                                    a3[:, dt, ch * 512:(ch + 1) * 512],
                                    ALU.add).then_inc(AE.h, 1)
                    AE.v += 1
    b.end_phase()

def _dump_acc(b):
    nc = b.nc
    d = b.dbg("acc", (128, NT_D * T_OWN))
    dc = b.dbg("combT", (128, T_OWN), BF16)
    ST = b.sem("daST")
    with nc.Block() as blk:
        @blk.sync
        def _(sync):
            sync.dma_start(out=d, in_=b.acc).then_inc(ST.h, 16)
            sync.dma_start(out=dc, in_=b.combT).then_inc(ST.h, 16)
            ST.v = 32
            sync.wait_ge(ST.h, ST.v)
    b.end_phase()

def emit_combine_ln1(b):
    nc = b.nc
    moff = b._moe_offS
    prod = b.salloc("cprod", [128, NT_D * T_OWN], BF16, at=moff)
    p3 = prod.rearrange("p (t c) -> p t c", t=NT_D)
    gtmp = b.salloc("cgt", [128, T_OWN], F32, at=moff + 32768)
    xnt = b.salloc("cxnt", [128, D], BF16, at=moff + 32768 + 4096)
    stt = b.salloc("cstat", [128, 16], F32, at=65536)
    ptm = b.salloc("cptm", [128, D], BF16, at=65536 + 128)
    x1t = b.salloc("cx1", [128, D], F32, at=65536 + 128 + 4096)
    assert 65536 + 128 + 4096 + 8192 <= 79072
    xt = [b.salloc(f"cx{k}", [128, D], F32, at=32768 + k * 8192) for k in range(2)]
    sqd = b.salloc("csq", [128, D], F32, at=49152)
    b.xn1 = [b.salloc(f"cxn1_{t}", [128, T_OWN], BF16, at=t * 2048) for t in range(NT_D)]
    assert moff + 32768 + 4096 + 4096 <= 206000

    XL = [b.sem("cXL0"), b.sem("cXL1")]
    PRD = b.sem("cPRD")
    TPD = b.sem("cTPD")
    EVD = b.sem("cEVD")
    X1R = b.sem("cX1R")
    X1S = [b.sem("cX1S0"), b.sem("cX1S1")]
    SQ2 = b.sem("cSQ2")
    VAR2 = b.sem("cVAR2")
    RSTD2 = b.sem("cRSTD2")
    XN2 = b.sem("cXN2")
    TP2 = b.sem("cTP2")
    EV2 = b.sem("cEV2")
    sv = Ser(b, "cSV")
    ss = Ser(b, "cSS")
    C1 = float(1.0 / D)
    xin = b.i["xh"].rearrange("(t p) d -> t p d", p=128)

    with nc.Block() as blk:
        @blk.scalar
        def _(s_):
            for dt in range(NT_D):
                s_.wait_ge(PRD.h, 3 * dt + 1)
                ss.step(s_.activation(gtmp, gtmp, AF.Silu), s_)
                s_.nop().then_inc(PRD.h, 1)
            for t in range(NT_OWN):
                s_.wait_ge(X1R.h, t + 1)
                if t > 0:
                    s_.wait_ge(VAR2.h, t)
                s_.activation(sqd, x1t, AF.Square, accum_out=stt[:, 1:2]).then_inc(SQ2.h, 1)
                s_.wait_ge(VAR2.h, t + 1)
                ss.step(s_.activation(stt[:, 5:6], stt[:, 4:5], AF.Ln, bias=1e-5), s_)
                s_.activation(stt[:, 6:7], stt[:, 5:6], AF.Exp, scale=-0.5).then_inc(RSTD2.h, 1)
            SQ2.v = RSTD2.v = NT_OWN

        @blk.vector
        def _(v):
            for dt in range(NT_D):
                if dt > 0:
                    v.wait_ge(PRD.h, 3 * dt)
                v.tensor_tensor(gtmp, b.acc.rearrange("p (t c) -> p t c", t=NT_D)[:, dt],
                                b.xnc.rearrange("p (t c) -> p t c", t=NT_D)[:, dt],
                                ALU.add).then_inc(PRD.h, 1)
                v.wait_ge(PRD.h, 3 * dt + 2)
                v.tensor_tensor(p3[:, dt], gtmp, b.s_fm.rearrange("p (t c) -> p t c", t=NT_D)[:, dt],
                                ALU.mult).then_inc(PRD.h, 1)
            PRD.v = 3 * NT_D
            for t in range(NT_OWN):
                for dt in range(NT_D):
                    v.wait_ge(TPD.h, t * NT_D + dt + 1)
                    pb = b.P[dt % 2].bitcast(BF16)
                    v.tensor_copy(ptm[:, dt * 128:(dt + 1) * 128], pb[:, 0:128]).then_inc(EVD.h, 1)
                v.wait_ge(EVD.h, (t + 1) * NT_D)
                v.wait_ge(XL[t % 2].h, 16 * (t // 2 + 1))
                if t >= 1:
                    v.wait_ge(X1S[0].h, 16 * t)
                sv.step(v.tensor_tensor(x1t, xt[t % 2], ptm, ALU.add), v)
                nc.vector.engine_nop().then_inc(X1R.h, 1)
                v.wait_ge(SQ2.h, t + 1)
                sv.step(v.reduce_sum(stt[:, 0:1], x1t, AX.X), v)
                sv.step(v.tensor_scalar_mul(stt[:, 2:3], stt[:, 0:1], C1), v)
                sv.step(v.tensor_scalar_mul(stt[:, 3:4], stt[:, 1:2], C1), v)
                sv.step(v.tensor_tensor(stt[:, 4:5], stt[:, 2:3], stt[:, 2:3], ALU.mult), v)
                v.tensor_tensor(stt[:, 4:5], stt[:, 3:4], stt[:, 4:5],
                                ALU.subtract).then_inc(VAR2.h, 1)
                v.wait_ge(RSTD2.h, t + 1)
                if t > 0:
                    v.wait_ge(TP2.h, t * NT_D)
                v.tensor_scalar(xnt, x1t, stt[:, 2:3], stt[:, 6:7],
                                ALU.subtract, ALU.mult).then_inc(XN2.h, 1)
                for dt in range(NT_D):
                    v.wait_ge(TP2.h, t * NT_D + dt + 1)
                    pb = b.P[2 + dt % 2].bitcast(BF16)
                    v.tensor_copy(b.xn1[dt][:, t * 128:(t + 1) * 128], pb[:, 0:128]).then_inc(EV2.h, 1)
            X1R.v = NT_OWN
            VAR2.v = XN2.v = NT_OWN
            EVD.v = EV2.v = NT_OWN * NT_D

        @blk.tensor
        def _(t_):
            t_.wait_ge(PRD.h, 3 * NT_D)
            for t in range(NT_OWN):
                for dt in range(NT_D):
                    li = t * NT_D + dt
                    if li >= 2:
                        t_.wait_ge(EVD.h, li - 1)
                    pb = b.P[dt % 2].bitcast(BF16)
                    t_.transpose(pb[:, 0:128], p3[:, dt, t * 128:(t + 1) * 128],
                                 b.ident).then_inc(TPD.h, 1)
                t_.wait_ge(XN2.h, t + 1)
                for dt in range(NT_D):
                    li = t * NT_D + dt
                    if li >= 2:
                        t_.wait_ge(EV2.h, li - 1)
                    pb = b.P[2 + dt % 2].bitcast(BF16)
                    t_.transpose(pb[:, 0:128], xnt[:, dt * 128:(dt + 1) * 128],
                                 b.ident).then_inc(TP2.h, 1)
            TPD.v = NT_OWN * NT_D
            TP2.v = NT_OWN * NT_D

        @blk.sync
        def _(sync):
            sync.wait_ge(PRD.h, 3 * NT_D)
            for t in range(NT_OWN):
                if t >= 2:
                    sync.wait_ge(XN2.h, t - 1)
                    sync.dma_start(out=b.o["x1"][(t - 2) * 128:(t - 1) * 128, :],
                                   in_=x1t).then_inc(X1S[0].h, 16)
                    X1S[0].v += 16
                sync.dma_start(out=xt[t % 2], in_=xin[4 + t]).then_inc(XL[t % 2].h, 16)
                XL[t % 2].v += 16
            for t in (NT_OWN - 2, NT_OWN - 1):
                sync.wait_ge(XN2.h, t + 1)
                sync.dma_start(out=b.o["x1"][t * 128:(t + 1) * 128, :],
                               in_=x1t).then_inc(X1S[0].h, 16)
                X1S[0].v += 16
            sync.wait_ge(X1S[0].h, X1S[0].v)
    b.end_phase()

def _dump_xn1(b):
    nc = b.nc
    d = b.dbg("xn1", (NT_D, 128, T_OWN), BF16)
    ST = b.sem("dx1ST")
    with nc.Block() as blk:
        @blk.sync
        def _(sync):
            for dt in range(NT_D):
                sync.dma_start(out=d[dt], in_=b.xn1[dt]).then_inc(ST.h, 16)
            ST.v = 16 * NT_D
            sync.wait_ge(ST.h, ST.v)



TWO_PI = float(2.0 * np.pi)
INV_2PI = float(1.0 / TWO_PI)
MAGIC_RND = 12582912.0
PI_F = float(np.float32(np.pi))
PI_HALF = float(np.float32(np.pi / 2))
N_WELEM = HID * D


def _cw_split():
    def chop(x):
        u = np.frombuffer(np.float32(x).tobytes(), np.uint32)[0] & 0xFFFFF000
        return float(np.frombuffer(np.uint32(u).tobytes(), np.float32)[0])
    hi = chop(TWO_PI)
    mid = chop(TWO_PI - hi)
    lo = float(np.float32(TWO_PI - hi - mid))
    return hi, mid, lo


C2PI_HI, C2PI_MID, C2PI_LO = _cw_split()


def emit_l1qkv(b):
    nc = b.nc
    wbuf = [b.salloc(f"jw{k}", [128, NT_D * 256], BF16, at=32768 + k * 8192) for k in range(2)]
    outb = [b.salloc(f"jo{k}", [128, 512], BF16, at=49152 + k * 1024) for k in range(2)]
    wvb = [b.salloc(f"jv{k}", [128, NT_D * 512], BF16, at=81920 + k * 16384) for k in range(2)]

    for which, colbase, dst in (("q", 0, b.qf_d), ("k", D, b.kf_d)):
        WL = [b.sem("j" + which + "WL0", g=True), b.sem("j" + which + "WL1", g=True)]
        MM = b.sem("j" + which + "MM")
        EV = b.sem("j" + which + "EV")
        SO = [b.sem("j" + which + "SO0"), b.sem("j" + which + "SO1")]
        gpo = 4
        ngroups = 8 * gpo
        with nc.Block() as blk:
            @blk.gpsimd
            def _(g):
                for og in range(8):
                    slot = og % 2
                    if og >= 2:
                        g.wait_ge(MM.h, (og - 1) * gpo)
                    g.dma_start(out=wbuf[slot].rearrange("p (t o) -> p t o", t=NT_D),
                                in_=b.i["wqkvT"][:, colbase + og * 256:colbase + (og + 1) * 256]
                                .rearrange("(t p) o -> p t o", p=128)).then_inc(WL[slot].h, 16)
                    WL[slot].v += 16

            @blk.tensor
            def _(t_):
                for gi in range(ngroups):
                    og, ot, ck = gi // gpo, (gi % gpo) // 2, gi % 2
                    slot = og % 2
                    if gi % gpo == 0:
                        t_.wait_ge(WL[slot].h, 16 * (og // 2 + 1))
                    if gi >= 2:
                        t_.wait_ge(EV.h, gi - 1)
                    pp = b.P[gi % 2]
                    w3 = wbuf[slot].rearrange("p (t o) -> p t o", t=NT_D)
                    for dt in range(NT_D):
                        mm = t_.matmul(pp, w3[:, dt, ot * 128:(ot + 1) * 128],
                                       b.xn1[dt][:, ck * 512:(ck + 1) * 512],
                                       start=(dt == 0), stop=(dt == NT_D - 1))
                    mm.then_inc(MM.h, 1)
                    MM.v += 1

            @blk.vector
            def _(v):
                for gi in range(ngroups):
                    slot = gi % 2
                    v.wait_ge(MM.h, gi + 1)
                    if gi >= 2:
                        v.wait_ge(SO[slot].h, 16 * (gi // 2))
                    v.tensor_copy(outb[slot], b.P[gi % 2]).then_inc(EV.h, 1)
                    EV.v += 1

            @blk.sync
            def _(sync):
                for gi in range(ngroups):
                    og, ot, ck = gi // gpo, (gi % gpo) // 2, gi % 2
                    head = og * 2 + ot
                    slot = gi % 2
                    sync.wait_ge(EV.h, gi + 1)
                    sync.dma_start(out=dst[head][:, ck * 512:(ck + 1) * 512],
                                   in_=outb[slot]).then_inc(SO[slot].h, 16)
                    SO[slot].v += 16
                sync.wait_ge(SO[0].h, SO[0].v)
                sync.wait_ge(SO[1].h, SO[1].v)
        b.end_phase()

    WLV = [b.sem("jvWL0", g=True), b.sem("jvWL1", g=True)]
    MMV = b.sem("jvMM")
    EVV = b.sem("jvEV")
    SOV = [b.sem("jvSO0"), b.sem("jvSO1")]
    ngroups = 4 * NT_OWN
    with nc.Block() as blk:
        @blk.gpsimd
        def _(g):
            for dvc in range(4):
                slot = dvc % 2
                if dvc >= 2:
                    g.wait_ge(MMV.h, (dvc - 1) * NT_OWN)
                g.dma_start(out=wvb[slot].rearrange("p (t o) -> p t o", t=NT_D),
                            in_=b.i["wqkvT"][:, 2 * D + dvc * 512:2 * D + (dvc + 1) * 512]
                            .rearrange("(t p) o -> p t o", p=128)).then_inc(WLV[slot].h, 16)
                WLV[slot].v += 16

        @blk.tensor
        def _(t_):
            for gi in range(ngroups):
                dvc, tt = gi // NT_OWN, gi % NT_OWN
                slot = dvc % 2
                if gi % NT_OWN == 0:
                    t_.wait_ge(WLV[slot].h, 16 * (dvc // 2 + 1))
                if gi >= 2:
                    t_.wait_ge(EVV.h, gi - 1)
                pp = b.P[gi % 2]
                w3 = wvb[slot].rearrange("p (t o) -> p t o", t=NT_D)
                for dt in range(NT_D):
                    mm = t_.matmul(pp, b.xn1[dt][:, tt * 128:(tt + 1) * 128],
                                   w3[:, dt], start=(dt == 0), stop=(dt == NT_D - 1))
                mm.then_inc(MMV.h, 1)
                MMV.v += 1

        @blk.vector
        def _(v):
            for gi in range(ngroups):
                slot = gi % 2
                v.wait_ge(MMV.h, gi + 1)
                if gi >= 2:
                    v.wait_ge(SOV[slot].h, 16 * (gi // 2))
                v.tensor_copy(outb[slot], b.P[gi % 2]).then_inc(EVV.h, 1)
                EVV.v += 1

        @blk.sync
        def _(sync):
            for gi in range(ngroups):
                dvc, tt = gi // NT_OWN, gi % NT_OWN
                slot = gi % 2
                sync.wait_ge(EVV.h, gi + 1)
                sync.dma_start(out=b.vt_d[tt][:, dvc * 512:(dvc + 1) * 512],
                               in_=outb[slot]).then_inc(SOV[slot].h, 16)
                SOV[slot].v += 16
            sync.wait_ge(SOV[0].h, SOV[0].v)
            sync.wait_ge(SOV[1].h, SOV[1].v)
    b.end_phase()


def emit_l1tables(b):
    nc = b.nc
    b.cosT = b.salloc("cosT", [128, H * T_OWN], BF16, at=114688)
    b.sinT = b.salloc("sinT", [128, H * T_OWN], BF16, at=147456)
    th_sb = b.salloc("tth", [1, H * 128], F32, at=32768)
    idx_sb = b.salloc("tidx", [1, T_OWN], F32, at=40960)
    t1 = b.salloc("tt1", [128, 512], F32, at=45056)
    t2 = b.salloc("tt2", [128, 512], F32, at=47104)
    LD = b.sem("tLD")
    ANG = b.sem("tANG")
    RED = b.sem("tRED")
    SIN = b.sem("tSIN")
    sv = Ser(b, "tSV")

    with nc.Block() as blk:
        @blk.sync
        def _(sync):
            sync.dma_start(out=th_sb, in_=b.i["thetarow"]).then_inc(LD.h, 16)
            sync.dma_start(out=idx_sb, in_=b.i["idxrow"]).then_inc(LD.h, 16)
            LD.v = 32

        @blk.tensor
        def _(t_):
            t_.wait_ge(LD.h, 32)
            for h in range(H):
                if h > 0:
                    t_.wait_ge(RED.h, 4 * h)
                t_.matmul(b.P[6], th_sb[:, h * 128:(h + 1) * 128],
                          idx_sb[:, 0:512], start=True, stop=True)
                t_.matmul(b.P[7], th_sb[:, h * 128:(h + 1) * 128],
                          idx_sb[:, 512:1024], start=True,
                          stop=True).then_inc(ANG.h, 1)
                ANG.v += 1

        @blk.vector
        def _(v):
            for h in range(H):
                v.wait_ge(ANG.h, h + 1)
                for trig in range(2):
                    for cc in range(2):
                        gidx = (h * 2 + trig) * 2 + cc
                        ps = b.P[6 + cc]
                        if gidx > 0:
                            v.wait_ge(SIN.h, gidx)
                        bias = 0.25 if trig == 1 else 0.0
                        sv.step(v.tensor_scalar(t1, ps, INV_2PI, bias,
                                                ALU.mult, ALU.add), v)
                        sv.step(v.tensor_scalar_add(t1, t1, MAGIC_RND), v)
                        sv.step(v.tensor_scalar_add(t1, t1, -MAGIC_RND), v)
                        sv.step(v.scalar_tensor_tensor(t2, t1, -C2PI_HI, ps,
                                                       ALU.mult, ALU.add), v)
                        sv.step(v.scalar_tensor_tensor(t2, t1, -C2PI_MID, t2,
                                                       ALU.mult, ALU.add), v)
                        sv.step(v.scalar_tensor_tensor(t2, t1, -C2PI_LO, t2,
                                                       ALU.mult, ALU.add), v)
                        if trig == 1:
                            sv.step(v.tensor_scalar_add(t2, t2, PI_HALF), v)
                        v.tensor_scalar(t2, t2, PI_F, -PI_F,
                                        ALU.min, ALU.max).then_inc(RED.h, 1)
                        RED.v += 1

        @blk.scalar
        def _(s_):
            for h in range(H):
                for trig in range(2):
                    for cc in range(2):
                        gidx = (h * 2 + trig) * 2 + cc
                        s_.wait_ge(RED.h, gidx + 1)
                        base = h * T_OWN + cc * 512
                        dstv = (b.sinT if trig == 0 else b.cosT)[:, base:base + 512]
                        s_.activation(dstv, t2, AF.Sin).then_inc(SIN.h, 1)
                        SIN.v += 1
    b.end_phase()


def emit_l1scan(b):
    nc = b.nc
    qh = b.salloc("sqh", [128, T_OWN], BF16, at=32768)
    kh = b.salloc("skh", [128, T_OWN], BF16, at=34816)
    vh = b.salloc("svh", [128, NCH * 128], BF16, at=36864)
    qdA = b.salloc("sqdA", [128, T_OWN], BF16, at=38912)
    qdB = b.salloc("sqdB", [128, T_OWN], BF16, at=40960)
    kdA = b.salloc("skdA", [128, T_OWN], BF16, at=43008)
    kdB = b.salloc("skdB", [128, T_OWN], BF16, at=45056)
    qwA = b.salloc("sqwA", [128, T_OWN], BF16, at=47104)
    qwB = b.salloc("sqwB", [128, T_OWN], BF16, at=49152)
    dec = b.salloc("sdec", [128, 128], F32, at=51200)
    qdecb = b.salloc("sqdecb", [128, 128], F32, at=51712)
    kdecb = b.salloc("skdecb", [128, 128], F32, at=52224)
    qrow = b.salloc("sqrow", [1, H * 128], F32, at=52736)
    krow = b.salloc("skrow", [1, H * 128], F32, at=60928)
    ones1f = b.salloc("sones1f", [1, 128], F32, at=69120)
    Pb = b.salloc("sPb", [128, 128], BF16, at=69632)
    kwA = b.salloc("skwA", [128, 128], BF16, at=69888)
    kwB = b.salloc("skwB", [128, 128], BF16, at=70144)
    kwAT = b.salloc("skwAT", [128, 128], BF16, at=70400)
    kwBT = b.salloc("skwBT", [128, 128], BF16, at=70656)
    stA = b.salloc("sstA", [128, 128], F32, at=70912)
    stB = b.salloc("sstB", [128, 128], F32, at=71424)
    stAb = b.salloc("sstAb", [128, 128], BF16, at=71936)
    stBb = b.salloc("sstBb", [128, 128], BF16, at=72192)
    yob = b.salloc("syob", [128, 128], BF16, at=72448)
    slopes = np.exp(-_slopes() * CHUNK)

    QHL = b.sem("scQHL", g=True)
    KHL = b.sem("scKHL", g=True)
    VHL = b.sem("scVHL", g=True)
    MS1 = b.sem("scMS1")
    CL = b.sem("scCL")
    DECL = b.sem("scDECL")
    QWS = b.sem("scQWS")
    SCST = b.sem("scSCST")
    YS = b.sem("scYS")
    QD = b.sem("scQD")
    SCP = b.sem("scSCP")
    PBE = b.sem("scPBE")
    KW = b.sem("scKW")
    TPK = b.sem("scTPK")
    KTE = b.sem("scKTE")
    OV = b.sem("scOV")
    YEv = b.sem("scYE")
    UPD = b.sem("scUPD")
    STB = b.sem("scSTB")
    SCS = b.sem("scSCS")
    sv = Ser(b, "scSV")

    with nc.Block() as blk:
        @blk.gpsimd
        def _(g):
            g.memset(ones1f, 1.0).then_inc(MS1.h, 1)
            MS1.v = 1
            for h in range(H):
                if h >= 1:
                    g.wait_ge(QD.h, 5 * h)
                g.dma_start(out=qh, in_=b.qf_d[h]).then_inc(QHL.h, 16)
                g.dma_start(out=kh, in_=b.kf_d[h][:, 0:T_OWN]).then_inc(KHL.h, 16)
                QHL.v += 16
                KHL.v += 16
                if h >= 1:
                    g.wait_ge(SCP.h, 17 * h)
                g.dma_start(out=vh.rearrange("p (t c) -> p t c", t=NCH),
                            in_=b.vt_d[0:NCH, :, h * 128:(h + 1) * 128]
                            .rearrange("t p c -> p t c")).then_inc(VHL.h, 16)
                VHL.v += 16

        @blk.sync
        def _(sync):
            sync.dma_start(out=qrow, in_=b.i["qdecay"]).then_inc(CL.h, 16)
            sync.dma_start(out=krow, in_=b.i["kdecay"]).then_inc(CL.h, 16)
            CL.v = 32
            for h in range(H):
                sync.dma_start(out=dec, in_=b.i["diagdecT"][h]).then_inc(DECL.h, 16)
                DECL.v += 16
                for j in range(NCH):
                    sync.wait_ge(YEv.h, h * NCH + j + 1)
                    sync.dma_start(out=b.o["ylocal"][h * 128:(h + 1) * 128,
                                                     j * 128:(j + 1) * 128],
                                   in_=yob).then_inc(YS.h, 16)
                    YS.v += 16
                    for k in slotB.get((h, j), []):
                        m, ap, dstp, rows, dataw, padw = jobsB[k]
                        pslot = k % 2
                        sync.wait_ge(ADB.h, k + 1)
                        sync.dma_start(out=dstp,
                                       in_=wqb[pslot][0:rows, 0:dataw + padw]
                                       ).then_inc(BST[pslot].h, 16)
                        BST[pslot].v += 16
                sync.wait_ge(QD.h, 5 * (h + 1))
                sync.dma_start(out=b.o["qdw"][h * 256:h * 256 + 128, :],
                               in_=qwA).then_inc(QWS.h, 16)
                sync.dma_start(out=b.o["qdw"][h * 256 + 128:h * 256 + 256, :],
                               in_=qwB).then_inc(QWS.h, 16)
                QWS.v += 32
                sync.wait_ge(SCS.h, h + 1)
                sync.dma_start(out=b.o["S_c"][h * 256:h * 256 + 128, :],
                               in_=stA).then_inc(SCST.h, 16)
                sync.dma_start(out=b.o["S_c"][h * 256 + 128:h * 256 + 256, :],
                               in_=stB).then_inc(SCST.h, 16)
                SCST.v += 32
            sync.wait_ge(YS.h, YS.v)
            sync.wait_ge(QWS.h, QWS.v)
            sync.wait_ge(SCST.h, SCST.v)

        @blk.tensor
        def _(t_):
            pk = b.P[4].bitcast(BF16)
            t_.wait_ge(CL.h, 32)
            t_.wait_ge(MS1.h, 1)
            for h in range(H):
                if h:
                    t_.wait_ge(QD.h, 5 * h)
                t_.matmul(b.P[5][:, 0:128], ones1f, qrow[:, h * 128:(h + 1) * 128],
                          start=True, stop=True)
                t_.matmul(b.P[5][:, 128:256], ones1f, krow[:, h * 128:(h + 1) * 128],
                          start=True, stop=True).then_inc(SCP.h, 1)
                SCP.v += 1
                t_.wait_ge(QD.h, 5 * h + 5)
                t_.wait_ge(VHL.h, 16 * (h + 1))
                for j in range(NCH):
                    jj = h * NCH + j
                    sl = slice(j * 128, (j + 1) * 128)
                    if jj:
                        t_.wait_ge(PBE.h, jj)
                    t_.matmul(b.P[0][:, 0:128], kdA[:, sl], qdA[:, sl],
                              start=True, stop=False)
                    t_.matmul(b.P[0][:, 0:128], kdB[:, sl], qdB[:, sl],
                              start=False, stop=True).then_inc(SCP.h, 1)
                    SCP.v += 1
                    t_.wait_ge(KW.h, 2 * jj + 2)
                    if jj:
                        t_.wait_ge(KTE.h, 2 * jj)
                    t_.transpose(pk[:, 0:128], kwA, b.ident)
                    t_.transpose(pk[:, 128:256], kwB, b.ident).then_inc(TPK.h, 1)
                    TPK.v += 1
                    t_.wait_ge(PBE.h, jj + 1)
                    t_.wait_ge(STB.h, 25 * h + 3 * j + 1)
                    if jj:
                        t_.wait_ge(YEv.h, jj)
                    t_.matmul(b.P[1][:, 0:128], vh[:, sl], Pb, start=True, stop=False)
                    t_.matmul(b.P[1][:, 0:128], stAb, qwA[:, sl], start=False, stop=False)
                    t_.matmul(b.P[1][:, 0:128], stBb, qwB[:, sl],
                              start=False, stop=True).then_inc(OV.h, 1)
                    OV.v += 1
                    t_.wait_ge(KTE.h, 2 * jj + 2)
                    if jj:
                        t_.wait_ge(UPD.h, 2 * jj)
                    t_.matmul(b.P[2][:, 0:128], kwAT, vh[:, sl], start=True, stop=True)
                    t_.matmul(b.P[3][:, 0:128], kwBT, vh[:, sl],
                              start=True, stop=True).then_inc(SCP.h, 1)
                    SCP.v += 1

        @blk.vector
        def _(v):
            pk = b.P[4].bitcast(BF16)
            for h in range(H):
                bd = float(slopes[h])
                cs = b.cosT[:, h * T_OWN:(h + 1) * T_OWN]
                sn = b.sinT[:, h * T_OWN:(h + 1) * T_OWN]
                v.wait_ge(DECL.h, 16 * (h + 1))
                v.wait_ge(SCP.h, 17 * h + 1)
                sv.step(v.tensor_copy(qdecb, b.P[5][:, 0:128]), v)
                sv.step(v.tensor_copy(kdecb, b.P[5][:, 128:256]), v)
                v.wait_ge(QHL.h, 16 * (h + 1))
                v.tensor_tensor(qdA, qh, cs, ALU.mult).then_inc(QD.h, 1)
                v.tensor_tensor(qdB, qh, sn, ALU.mult).then_inc(QD.h, 1)
                v.wait_ge(KHL.h, 16 * (h + 1))
                v.tensor_tensor(kdA, kh, cs, ALU.mult).then_inc(QD.h, 1)
                sv.step(v.tensor_tensor(kdB, kh, sn, ALU.mult), v)
                if h:
                    v.wait_ge(QWS.h, 32 * h)
                for j in range(NCH):
                    sl = slice(j * 128, (j + 1) * 128)
                    v.tensor_tensor(qwA[:, sl], qdA[:, sl], qdecb, ALU.mult)
                    ins = v.tensor_tensor(qwB[:, sl], qdB[:, sl], qdecb, ALU.mult)
                    if j == NCH - 1:
                        sv.step(ins, v)
                v.engine_nop().then_inc(QD.h, 2)
                QD.v += 5
                if h:
                    v.wait_ge(SCST.h, 32 * h)
                v.memset(stA, 0.0)
                v.memset(stB, 0.0)
                v.memset(stAb, 0.0)
                sv.step(v.memset(stBb, 0.0), v)
                v.engine_nop().then_inc(STB.h, 1)
                STB.v += 1
                for j in range(NCH):
                    jj = h * NCH + j
                    sl = slice(j * 128, (j + 1) * 128)
                    v.wait_ge(SCP.h, 17 * h + 2 + 2 * j)
                    if jj:
                        v.wait_ge(OV.h, jj)
                    v.tensor_tensor(Pb, b.P[0][:, 0:128], dec, ALU.mult).then_inc(PBE.h, 1)
                    PBE.v += 1
                    sv.step(v.tensor_tensor(kwA, kdA[:, sl], kdecb, ALU.mult), v)
                    v.tensor_tensor(kwB, kdB[:, sl], kdecb, ALU.mult).then_inc(KW.h, 2)
                    KW.v += 2
                    v.wait_ge(TPK.h, jj + 1)
                    v.tensor_copy(kwAT, pk[:, 0:128])
                    v.tensor_copy(kwBT, pk[:, 128:256]).then_inc(KTE.h, 2)
                    KTE.v += 2
                    v.wait_ge(OV.h, jj + 1)
                    if jj:
                        v.wait_ge(YS.h, 16 * jj)
                    v.tensor_copy(yob, b.P[1][:, 0:128]).then_inc(YEv.h, 1)
                    YEv.v += 1
                    v.wait_ge(SCP.h, 17 * h + 3 + 2 * j)
                    sv.step(v.tensor_scalar_mul(stA, stA, bd), v)
                    sv.step(v.tensor_tensor(stA, b.P[2][:, 0:128], stA, ALU.add), v)
                    sv.step(v.tensor_scalar_mul(stB, stB, bd), v)
                    sv.step(v.tensor_tensor(stB, b.P[3][:, 0:128], stB, ALU.add), v)
                    v.engine_nop().then_inc(UPD.h, 2)
                    UPD.v += 2
                    if j < NCH - 1:
                        sv.step(v.tensor_copy(stAb, stA), v)
                        sv.step(v.tensor_copy(stBb, stB), v)
                        v.engine_nop().then_inc(STB.h, 3)
                        STB.v += 3
                    else:
                        v.engine_nop().then_inc(STB.h, 3)
                        STB.v += 3
                        v.engine_nop().then_inc(SCS.h, 1)
                        SCS.v += 1
    b.end_phase()


def emit_l1scanq(b):
    nc = b.nc
    qh = b.salloc("sqh", [128, T_OWN], BF16, at=32768)
    kh = b.salloc("skh", [128, T_OWN], BF16, at=34816)
    vh = b.salloc("svh", [128, NCH * 128], BF16, at=36864)
    qdA = b.salloc("sqdA", [128, T_OWN], BF16, at=38912)
    qdB = b.salloc("sqdB", [128, T_OWN], BF16, at=40960)
    kdA = b.salloc("skdA", [128, T_OWN], BF16, at=43008)
    kdB = b.salloc("skdB", [128, T_OWN], BF16, at=45056)
    qwA = b.salloc("sqwA", [128, T_OWN], BF16, at=47104)
    qwB = b.salloc("sqwB", [128, T_OWN], BF16, at=49152)
    dec = b.salloc("sdec", [128, 128], F32, at=51200)
    qdecb = b.salloc("sqdecb", [128, 128], F32, at=51712)
    kdecb = b.salloc("skdecb", [128, 128], F32, at=52224)
    qrow = b.salloc("sqrow", [1, H * 128], F32, at=52736)
    krow = b.salloc("skrow", [1, H * 128], F32, at=60928)
    ones1f = b.salloc("sones1f", [1, 128], F32, at=69120)
    Pb = b.salloc("sPb", [128, 128], BF16, at=69632)
    kwA = b.salloc("skwA", [128, 128], BF16, at=69888)
    kwB = b.salloc("skwB", [128, 128], BF16, at=70144)
    kwAT = b.salloc("skwAT", [128, 128], BF16, at=70400)
    kwBT = b.salloc("skwBT", [128, 128], BF16, at=70656)
    stA = b.salloc("sstA", [128, 128], F32, at=70912)
    stB = b.salloc("sstB", [128, 128], F32, at=71424)
    stAb = b.salloc("sstAb", [128, 128], BF16, at=71936)
    stBb = b.salloc("sstBb", [128, 128], BF16, at=72192)
    yob = b.salloc("syob", [128, 128], BF16, at=72448)
    t2a = b.salloc("ut2a", [128, 2048], BF16, at=72704)
    t2b = b.salloc("ut2b", [128, 2048], BF16, at=76800)
    wfA = [b.salloc(f"uwa{k}", [128, 2752], F32, at=81920 + k * 11008)
           for k in range(2)]
    accs = b.salloc("uaccs", [128, 4], F32, at=103936)
    tmpc = b.salloc("utmpc", [128, 2], F32, at=104096)
    onesc = b.salloc("uonesc", [128, 1], F32, at=104224)
    onesr = b.salloc("uonesr", [1, 128], F32, at=104288)
    tot = b.salloc("utot", [1, 4], F32, at=104832)
    dumpb = b.salloc("udump", [128, 2752], BF16, at=104896)
    b.sq_sb = b.salloc("usq", [128, 12], F32, at=180224)
    wfB = [b.salloc(f"uwb{k}", [128, 2048], F32, at=180288 + k * 8192)
           for k in range(2)]
    wqb = [b.salloc(f"uwqb{k}", [128, 2048], BF16, at=196672 + k * 4096)
           for k in range(2)]
    slopes = np.exp(-_slopes() * CHUNK)

    jobsA = []
    for m, nm in ((0, "w1T"), (1, "w3T")):
        for bt in range(NT_D):
            rs = b.i[nm][bt * 128:(bt + 1) * 128, :]
            jobsA.append((m, rs[:, 0:2752], 128, 2752))
            jobsA.append((m, rs[:, 2752:HID], 128, HID - 2752))
    for bt in range(NT_HID):
        r0 = bt * 128
        rows = min(128, HID - r0)
        jobsA.append((2, b.i["w2T"][r0:r0 + rows, :], rows, D))
    cumA = [32, 64, 107]
    jobsB = []
    for m, nm, dst in ((0, "w1T", b.w1q_d), (1, "w3T", b.w3q_d)):
        for bt in range(NT_D):
            for c in range(3):
                c0 = c * 2048
                dataw = min(2048, HID - c0)
                padw = 0 if c < 2 else HID_P - HID
                jobsB.append((m, b.i[nm][bt * 128:(bt + 1) * 128, c0:c0 + dataw],
                              dst[bt * 128:(bt + 1) * 128, c0:c0 + dataw + padw],
                              128, dataw, padw))
    for bt in range(NT_HID):
        r0 = bt * 128
        rows = min(128, HID - r0)
        jobsB.append((2, b.i["w2T"][r0:r0 + rows, :], b.w2q_d[r0:r0 + rows, :],
                      rows, D, 0))
    NB = len(jobsB)

    headA = {h: [] for h in range(H)}
    slotA = {}
    for jj in range(107):
        h, sl = jj // 9, jj % 9
        headA[h].append(jj)
        slotA.setdefault((h, sl), []).append(jj)
    headB = {h: [] for h in range(H)}
    slotB = {}

    def _assignB(k0, k1, h0, nh):
        n = k1 - k0
        for q in range(n):
            h = h0 + min(q * nh // n, nh - 1)
            qq = q - (h - h0) * ((n + nh - 1) // nh)
            headB[h].append(k0 + q)
    for k in range(48):
        headB[5 + k // 12].append(k)
    for k in range(48, 96):
        headB[9 + (k - 48) // 12].append(k)
    for k in range(96, NB):
        headB[13 + min((k - 96) // 15, 2)].append(k)
    for h in range(H):
        lst = headB[h]
        for q, k in enumerate(lst):
            sl = min(q * 9 // max(len(lst), 1), 8)
            slotB.setdefault((h, sl), []).append(k)
    tot_head = {3: 0, 7: 1, 11: 2}

    QHL = b.sem("scQHL", g=True)
    KHL = b.sem("scKHL", g=True)
    VHL = b.sem("scVHL", g=True)
    AWL = [b.sem("scAWL0", g=True), b.sem("scAWL1", g=True)]
    BWL = [b.sem("scBWL0", g=True), b.sem("scBWL1", g=True)]
    MS1 = b.sem("scMS1")
    CL = b.sem("scCL")
    DECL = b.sem("scDECL")
    QWS = b.sem("scQWS")
    SCST = b.sem("scSCST")
    YS = b.sem("scYS")
    QD = b.sem("scQD")
    SCP = b.sem("scSCP")
    PBE = b.sem("scPBE")
    KW = b.sem("scKW")
    TPK = b.sem("scTPK")
    KTE = b.sem("scKTE")
    OV = b.sem("scOV")
    YEv = b.sem("scYE")
    UPD = b.sem("scUPD")
    STB = b.sem("scSTB")
    SCS = b.sem("scSCS")
    ARD = b.sem("qARD")
    AAB = b.sem("qAAB")
    TOT = b.sem("qTOT")
    TCP = b.sem("qTCP")
    SQ = b.sem("qSQ")
    SGB = b.sem("qSGB")
    ADB = b.sem("qADB")
    BST = [b.sem("qBST0"), b.sem("qBST1")]
    ZP = b.sem("qZP")
    sv = Ser(b, "scSV")

    with nc.Block() as blk:
        @blk.gpsimd
        def _(g):
            g.memset(ones1f, 1.0)
            g.memset(accs, 0.0)
            g.memset(tmpc, 0.0)
            g.memset(onesc, 1.0)
            g.memset(onesr, 1.0).then_inc(MS1.h, 1)
            MS1.v = 1
            for h in range(H):
                if h >= 1:
                    g.wait_ge(QD.h, 5 * h)
                g.dma_start(out=qh, in_=b.qf_d[h]).then_inc(QHL.h, 16)
                g.dma_start(out=kh, in_=b.kf_d[h][:, 0:T_OWN]).then_inc(KHL.h, 16)
                QHL.v += 16
                KHL.v += 16
                if h >= 1:
                    g.wait_ge(SCP.h, 17 * h)
                g.dma_start(out=vh.rearrange("p (t c) -> p t c", t=NCH),
                            in_=b.vt_d[0:NCH, :, h * 128:(h + 1) * 128]
                            .rearrange("t p c -> p t c")).then_inc(VHL.h, 16)
                VHL.v += 16
                for sl in range(9):
                    for j in slotA.get((h, sl), []):
                        m, ap, rows, width = jobsA[j]
                        slot = j % 2
                        if j >= 2:
                            g.wait_ge(AAB.h, j - 1)
                        g.dma_start(out=wfA[slot][0:rows, 0:width],
                                    in_=ap).then_inc(AWL[slot].h, 16)
                        AWL[slot].v += 16
                    for k in slotB.get((h, sl), []):
                        m, ap, dstp, rows, dataw, padw = jobsB[k]
                        slot = k % 2
                        if k >= 2:
                            g.wait_ge(SGB.h, 2 * (k - 1))
                        g.dma_start(out=wfB[slot][0:rows, 0:dataw],
                                    in_=ap).then_inc(BWL[slot].h, 16)
                        BWL[slot].v += 16

        @blk.sync
        def _(sync):
            sync.dma_start(out=qrow, in_=b.i["qdecay"]).then_inc(CL.h, 16)
            sync.dma_start(out=krow, in_=b.i["kdecay"]).then_inc(CL.h, 16)
            CL.v = 32
            for h in range(H):
                sync.dma_start(out=dec, in_=b.i["diagdecT"][h]).then_inc(DECL.h, 16)
                DECL.v += 16
                for j in range(NCH):
                    sync.wait_ge(YEv.h, h * NCH + j + 1)
                    sync.dma_start(out=b.o["ylocal"][h * 128:(h + 1) * 128,
                                                     j * 128:(j + 1) * 128],
                                   in_=yob).then_inc(YS.h, 16)
                    YS.v += 16
                    for k in slotB.get((h, j), []):
                        m, ap, dstp, rows, dataw, padw = jobsB[k]
                        pslot = k % 2
                        sync.wait_ge(ADB.h, k + 1)
                        sync.dma_start(out=dstp,
                                       in_=wqb[pslot][0:rows, 0:dataw + padw]
                                       ).then_inc(BST[pslot].h, 16)
                        BST[pslot].v += 16
                sync.wait_ge(QD.h, 5 * (h + 1))
                sync.dma_start(out=b.o["qdw"][h * 256:h * 256 + 128, :],
                               in_=qwA).then_inc(QWS.h, 16)
                sync.dma_start(out=b.o["qdw"][h * 256 + 128:h * 256 + 256, :],
                               in_=qwB).then_inc(QWS.h, 16)
                QWS.v += 32
                sync.wait_ge(SCS.h, h + 1)
                sync.dma_start(out=b.o["S_c"][h * 256:h * 256 + 128, :],
                               in_=stA).then_inc(SCST.h, 16)
                sync.dma_start(out=b.o["S_c"][h * 256 + 128:h * 256 + 256, :],
                               in_=stB).then_inc(SCST.h, 16)
                SCST.v += 32
                for k in slotB.get((h, 8), []):
                    m, ap, dstp, rows, dataw, padw = jobsB[k]
                    pslot = k % 2
                    sync.wait_ge(ADB.h, k + 1)
                    sync.dma_start(out=dstp,
                                   in_=wqb[pslot][0:rows, 0:dataw + padw]
                                   ).then_inc(BST[pslot].h, 16)
                    BST[pslot].v += 16
            sync.wait_ge(ZP.h, 1)
            sync.dma_start(out=b.w2q_d[HID:HID_P, :],
                           in_=t2a[0:HID_P - HID, 0:D]).then_inc(BST[0].h, 16)
            BST[0].v += 16
            sync.wait_ge(YS.h, YS.v)
            sync.wait_ge(QWS.h, QWS.v)
            sync.wait_ge(SCST.h, SCST.v)
            sync.wait_ge(BST[0].h, BST[0].v)
            sync.wait_ge(BST[1].h, BST[1].v)

        @blk.tensor
        def _(t_):
            pk = b.P[4].bitcast(BF16)
            t_.wait_ge(CL.h, 32)
            t_.wait_ge(MS1.h, 1)
            for h in range(H):
                if h:
                    t_.wait_ge(QD.h, 5 * h)
                t_.matmul(b.P[5][:, 0:128], ones1f, qrow[:, h * 128:(h + 1) * 128],
                          start=True, stop=True)
                t_.matmul(b.P[5][:, 128:256], ones1f, krow[:, h * 128:(h + 1) * 128],
                          start=True, stop=True).then_inc(SCP.h, 1)
                SCP.v += 1
                t_.wait_ge(QD.h, 5 * h + 5)
                t_.wait_ge(VHL.h, 16 * (h + 1))
                for j in range(NCH):
                    jj = h * NCH + j
                    sl = slice(j * 128, (j + 1) * 128)
                    if jj:
                        t_.wait_ge(PBE.h, jj)
                    t_.matmul(b.P[0][:, 0:128], kdA[:, sl], qdA[:, sl],
                              start=True, stop=False)
                    t_.matmul(b.P[0][:, 0:128], kdB[:, sl], qdB[:, sl],
                              start=False, stop=True).then_inc(SCP.h, 1)
                    SCP.v += 1
                    t_.wait_ge(KW.h, 2 * jj + 2)
                    if jj:
                        t_.wait_ge(KTE.h, 2 * jj)
                    t_.transpose(pk[:, 0:128], kwA, b.ident)
                    t_.transpose(pk[:, 128:256], kwB, b.ident).then_inc(TPK.h, 1)
                    TPK.v += 1
                    t_.wait_ge(PBE.h, jj + 1)
                    t_.wait_ge(STB.h, 25 * h + 3 * j + 1)
                    if jj:
                        t_.wait_ge(YEv.h, jj)
                    t_.matmul(b.P[1][:, 0:128], vh[:, sl], Pb, start=True, stop=False)
                    t_.matmul(b.P[1][:, 0:128], stAb, qwA[:, sl], start=False, stop=False)
                    t_.matmul(b.P[1][:, 0:128], stBb, qwB[:, sl],
                              start=False, stop=True).then_inc(OV.h, 1)
                    OV.v += 1
                    t_.wait_ge(KTE.h, 2 * jj + 2)
                    if jj:
                        t_.wait_ge(UPD.h, 2 * jj)
                    t_.matmul(b.P[2][:, 0:128], kwAT, vh[:, sl], start=True, stop=True)
                    t_.matmul(b.P[3][:, 0:128], kwBT, vh[:, sl],
                              start=True, stop=True).then_inc(SCP.h, 1)
                    SCP.v += 1
                if h in tot_head:
                    m = tot_head[h]
                    t_.wait_ge(ARD.h, cumA[m])
                    t_.matmul(b.P[6][0:1, m:m + 1], accs[:, m:m + 1], onesc,
                              start=True, stop=True).then_inc(TOT.h, 1)
                    TOT.v += 1
                    t_.wait_ge(TCP.h, m + 1)
                    t_.matmul(b.P[7][:, m:m + 1], onesr, tot[:, m:m + 1],
                              start=True, stop=True).then_inc(TOT.h, 1)
                    TOT.v += 1

        @blk.scalar
        def _(s_):
            for h in range(H):
                for sl in range(9):
                    for jj in slotA.get((h, sl), []):
                        m, ap, rows, width = jobsA[jj]
                        slot = jj % 2
                        s_.wait_ge(AWL[slot].h, 16 * (jj // 2 + 1))
                        if jj >= 1:
                            s_.wait_ge(AAB.h, jj)
                        if jj >= 2:
                            s_.wait_ge(ARD.h, jj - 1)
                        s_.activation(dumpb[0:rows, 0:width],
                                      wfA[slot][0:rows, 0:width], AF.Abs,
                                      accum_out=tmpc[0:rows, jj % 2:jj % 2 + 1]
                                      ).then_inc(AAB.h, 1)
                        AAB.v += 1
                    for k in slotB.get((h, sl), []):
                        m, ap, dstp, rows, dataw, padw = jobsB[k]
                        slot = k % 2
                        s_.wait_ge(SQ.h, m + 1)
                        s_.wait_ge(BWL[slot].h, 16 * (k // 2 + 1))
                        if k >= 1:
                            s_.wait_ge(ADB.h, k)
                        win = wfB[slot][0:rows, 0:dataw]
                        nb = b.sq_sb[0:rows, 3 + m:4 + m]
                        pb = b.sq_sb[0:rows, 6 + m:7 + m]
                        s_.activation(t2a[0:rows, 0:dataw], win, AF.Sign, bias=nb
                                      ).then_inc(SGB.h, 1)
                        s_.activation(t2b[0:rows, 0:dataw], win, AF.Sign, bias=pb
                                      ).then_inc(SGB.h, 1)
                        SGB.v += 2

        @blk.vector
        def _(v):
            pk = b.P[4].bitcast(BF16)
            half = 0.5 / N_WELEM

            def _vq_slot(v, h, sl):
                for jj in slotA.get((h, sl), []):
                    m, ap, rows, width = jobsA[jj]
                    tc = tmpc[0:rows, jj % 2:jj % 2 + 1]
                    acol = accs[0:rows, m:m + 1]
                    v.wait_ge(AAB.h, jj + 1)
                    v.tensor_tensor(acol, tc, acol, ALU.add).then_inc(ARD.h, 1)
                    ARD.v += 1
                    v.wait_ge(ARD.h, ARD.v)
                for k in slotB.get((h, sl), []):
                    m, ap, dstp, rows, dataw, padw = jobsB[k]
                    pslot = k % 2
                    v.wait_ge(SGB.h, 2 * (k + 1))
                    if k >= 2:
                        v.wait_ge(BST[pslot].h, 16 * (k // 2))
                    if padw:
                        sv.step(v.memset(wqb[pslot][:, dataw:dataw + padw], 0.0), v)
                    v.tensor_tensor(wqb[pslot][0:rows, 0:dataw], t2a[0:rows, 0:dataw],
                                    t2b[0:rows, 0:dataw], ALU.add).then_inc(ADB.h, 1)
                    ADB.v += 1
            for h in range(H):
                bd = float(slopes[h])
                cs = b.cosT[:, h * T_OWN:(h + 1) * T_OWN]
                sn = b.sinT[:, h * T_OWN:(h + 1) * T_OWN]
                if h in (4, 8, 12):
                    m = (h - 4) // 4
                    v.wait_ge(TOT.h, 2 * (m + 1))
                    pcol = b.P[7][:, m:m + 1]
                    v.tensor_scalar_mul(b.sq_sb[:, m:m + 1], pcol, half)
                    v.tensor_scalar_mul(b.sq_sb[:, 3 + m:4 + m], pcol, -half)
                    sv.step(v.tensor_scalar_mul(b.sq_sb[:, 6 + m:7 + m], pcol, half), v)
                    v.engine_nop().then_inc(SQ.h, 1)
                    SQ.v += 1
                v.wait_ge(DECL.h, 16 * (h + 1))
                v.wait_ge(SCP.h, 17 * h + 1)
                sv.step(v.tensor_copy(qdecb, b.P[5][:, 0:128]), v)
                sv.step(v.tensor_copy(kdecb, b.P[5][:, 128:256]), v)
                v.wait_ge(QHL.h, 16 * (h + 1))
                v.tensor_tensor(qdA, qh, cs, ALU.mult).then_inc(QD.h, 1)
                v.tensor_tensor(qdB, qh, sn, ALU.mult).then_inc(QD.h, 1)
                v.wait_ge(KHL.h, 16 * (h + 1))
                v.tensor_tensor(kdA, kh, cs, ALU.mult).then_inc(QD.h, 1)
                sv.step(v.tensor_tensor(kdB, kh, sn, ALU.mult), v)
                if h:
                    v.wait_ge(QWS.h, 32 * h)
                for j in range(NCH):
                    sl = slice(j * 128, (j + 1) * 128)
                    v.tensor_tensor(qwA[:, sl], qdA[:, sl], qdecb, ALU.mult)
                    ins = v.tensor_tensor(qwB[:, sl], qdB[:, sl], qdecb, ALU.mult)
                    if j == NCH - 1:
                        sv.step(ins, v)
                v.engine_nop().then_inc(QD.h, 2)
                QD.v += 5
                if h:
                    v.wait_ge(SCST.h, 32 * h)
                v.memset(stA, 0.0)
                v.memset(stB, 0.0)
                v.memset(stAb, 0.0)
                sv.step(v.memset(stBb, 0.0), v)
                v.engine_nop().then_inc(STB.h, 1)
                STB.v += 1
                for j in range(NCH):
                    jj = h * NCH + j
                    sl = slice(j * 128, (j + 1) * 128)
                    v.wait_ge(SCP.h, 17 * h + 2 + 2 * j)
                    if jj:
                        v.wait_ge(OV.h, jj)
                    v.tensor_tensor(Pb, b.P[0][:, 0:128], dec, ALU.mult).then_inc(PBE.h, 1)
                    PBE.v += 1
                    sv.step(v.tensor_tensor(kwA, kdA[:, sl], kdecb, ALU.mult), v)
                    v.tensor_tensor(kwB, kdB[:, sl], kdecb, ALU.mult).then_inc(KW.h, 2)
                    KW.v += 2
                    v.wait_ge(TPK.h, jj + 1)
                    v.tensor_copy(kwAT, pk[:, 0:128])
                    v.tensor_copy(kwBT, pk[:, 128:256]).then_inc(KTE.h, 2)
                    KTE.v += 2
                    v.wait_ge(OV.h, jj + 1)
                    if jj:
                        v.wait_ge(YS.h, 16 * jj)
                    v.tensor_copy(yob, b.P[1][:, 0:128]).then_inc(YEv.h, 1)
                    YEv.v += 1
                    v.wait_ge(SCP.h, 17 * h + 3 + 2 * j)
                    sv.step(v.tensor_scalar_mul(stA, stA, bd), v)
                    sv.step(v.tensor_tensor(stA, b.P[2][:, 0:128], stA, ALU.add), v)
                    sv.step(v.tensor_scalar_mul(stB, stB, bd), v)
                    sv.step(v.tensor_tensor(stB, b.P[3][:, 0:128], stB, ALU.add), v)
                    v.engine_nop().then_inc(UPD.h, 2)
                    UPD.v += 2
                    if j < NCH - 1:
                        sv.step(v.tensor_copy(stAb, stA), v)
                        sv.step(v.tensor_copy(stBb, stB), v)
                        v.engine_nop().then_inc(STB.h, 3)
                        STB.v += 3
                    else:
                        v.engine_nop().then_inc(STB.h, 3)
                        STB.v += 3
                        v.engine_nop().then_inc(SCS.h, 1)
                        SCS.v += 1
                    _vq_slot(v, h, j)
                _vq_slot(v, h, 8)
                if h in tot_head:
                    m = tot_head[h]
                    v.wait_ge(TOT.h, 2 * m + 1)
                    sv.step(v.tensor_copy(tot[:, m:m + 1], b.P[6][0:1, m:m + 1]), v)
                    v.engine_nop().then_inc(TCP.h, 1)
                    TCP.v += 1
            v.wait_ge(ADB.h, NB)
            sv.step(v.memset(t2a[0:HID_P - HID, 0:D], 0.0), v)
            v.engine_nop().then_inc(ZP.h, 1)
            ZP.v += 1
    b.end_phase()


def emit_l1quant(b):
    nc = b.nc
    wf = [b.salloc(f"uw{k}", [128, HID_P], F32, at=32768 + k * 22016) for k in range(2)]
    t2a = b.salloc("ut2a", [128, HID], BF16, at=81920)
    t2b = b.salloc("ut2b", [128, HID], BF16, at=92864)
    wqb = [b.salloc(f"uwqb{k}", [128, HID_P], BF16, at=103808 + k * 11008)
           for k in range(2)]
    accs = b.salloc("uaccs", [128, 4], F32, at=76800)
    tmpc = b.salloc("utmpc", [128, 1], F32, at=76960)
    onesc = b.salloc("uonesc", [128, 1], F32, at=77024)
    onesr = b.salloc("uonesr", [1, 128], F32, at=77088)
    tot = b.salloc("utot", [1, 4], F32, at=77600)
    b.sq_sb = b.salloc("usq", [128, 12], F32, at=180224)

    mats = []
    for mi, nm in enumerate(("w1T", "w3T", "w2T")):
        src = b.i[nm]
        blocks = []
        if nm != "w2T":
            for bt in range(NT_D):
                blocks.append((src[bt * 128:(bt + 1) * 128, :], 128, HID))
        else:
            for bt in range(NT_HID):
                r0 = bt * 128
                rows = min(128, HID - r0)
                blocks.append((src[r0:r0 + rows, :], rows, D))
        mats.append((mi, nm, blocks))

    MS = b.sem("uMS")
    WLD = [b.sem("uWLD0"), b.sem("uWLD1")]
    RD = b.sem("uRD")
    TOT = b.sem("uTOT")
    SQ = b.sem("uSQ")
    SG = b.sem("uSG")
    AD = b.sem("uAD")
    ST = [b.sem("uST0"), b.sem("uST1")]
    sv = Ser(b, "uSV")
    ss = Ser(b, "uSS")

    nblk = sum(len(m[2]) for m in mats)

    with nc.Block() as blk:
        @blk.gpsimd
        def _(g):
            g.memset(accs, 0.0)
            g.memset(onesc, 1.0)
            g.memset(tmpc, 0.0)
            g.memset(onesr, 1.0).then_inc(MS.h, 1)
            MS.v = 1

        @blk.sync
        def _(sync):
            bi = 0
            for mi, nm, blocks in mats:
                for (ap, rows, width) in blocks:
                    slot = bi % 2
                    if bi >= 2:
                        sync.wait_ge(RD.h, bi - 1)
                    sync.dma_start(out=wf[slot][0:rows, 0:width],
                                   in_=ap).then_inc(WLD[slot].h, 16)
                    WLD[slot].v += 16
                    bi += 1

        @blk.vector
        def _(v):
            v.wait_ge(MS.h, 1)
            bi = 0
            for mi, nm, blocks in mats:
                for (ap, rows, width) in blocks:
                    slot = bi % 2
                    v.wait_ge(WLD[slot].h, 16 * (bi // 2 + 1))
                    if rows < 128:
                        sv.step(v.memset(tmpc, 0.0), v)
                    sv.step(v.tensor_reduce(tmpc[0:rows, :], wf[slot][0:rows, 0:width],
                                            AX.X, ALU.add,
                                            apply_absolute_value=True), v)
                    acol = accs[:, mi:mi + 1]
                    v.tensor_tensor(acol, tmpc, acol, ALU.add).then_inc(RD.h, 1)
                    RD.v += 1
                    v.wait_ge(RD.h, RD.v)
                    bi += 1
            v.wait_ge(TOT.h, 1)
            sv.step(v.tensor_copy(tot[:, 0:3], b.P[6][0:1, 0:3]), v)
            v.engine_nop().then_inc(RD.h, 1)
            RD.v += 1
            v.wait_ge(TOT.h, 2)
            half = 0.5 / N_WELEM
            for mi in range(3):
                pcol = b.P[7][:, mi:mi + 1]
                v.tensor_scalar_mul(b.sq_sb[:, mi:mi + 1], pcol, half)
                v.tensor_scalar_mul(b.sq_sb[:, 3 + mi:4 + mi], pcol, -half)
                ins = v.tensor_scalar_mul(b.sq_sb[:, 6 + mi:7 + mi], pcol, half)
                sv.step(ins, v)
            v.engine_nop().then_inc(SQ.h, 1)
            SQ.v += 1

        @blk.tensor
        def _(t_):
            t_.wait_ge(RD.h, nblk)
            for mi in range(3):
                mm = t_.matmul(b.P[6][0:1, mi:mi + 1], accs[:, mi:mi + 1], onesc,
                               start=True, stop=True)
            mm.then_inc(TOT.h, 1)
            TOT.v += 1
            t_.wait_ge(RD.h, nblk + 1)
            t_.matmul(b.P[7][:, 0:3], onesr, tot[:, 0:3],
                      start=True, stop=True).then_inc(TOT.h, 1)
            TOT.v += 1
            t_.wait_ge(SQ.h, 1)
    b.end_phase()

    MSB = b.sem("uMSB")
    WLB = [b.sem("uWLB0"), b.sem("uWLB1")]
    SGB = b.sem("uSGB")
    ADB = b.sem("uADB")
    STB2 = [b.sem("uSTB0"), b.sem("uSTB1")]
    ZP = b.sem("uZP")
    svb = Ser(b, "uSVB")
    ssb = Ser(b, "uSSB")
    dsts = {"w1T": b.w1q_d, "w3T": b.w3q_d, "w2T": b.w2q_d}

    with nc.Block() as blk:
        @blk.gpsimd
        def _(g):
            g.memset(wqb[0][:, HID:HID_P], 0.0)
            g.memset(wqb[1][:, HID:HID_P], 0.0).then_inc(MSB.h, 1)
            MSB.v = 1

        @blk.sync
        def _(sync):
            flat = []
            for mi, nm, blocks in mats:
                r0 = 0
                for (ap, rows, width) in blocks:
                    if nm != "w2T":
                        dd = dsts[nm][r0:r0 + rows, :]
                    else:
                        dd = dsts[nm][r0:r0 + rows, 0:width]
                    flat.append((ap, dd, rows, width, nm))
                    r0 += rows

            def store(bi):
                ap, dd, rows, width, nm = flat[bi]
                pslot = bi % 2
                sync.wait_ge(ADB.h, bi + 1)
                srcq = (wqb[pslot][0:rows, :] if nm != "w2T"
                        else wqb[pslot][0:rows, 0:width])
                sync.dma_start(out=dd, in_=srcq).then_inc(STB2[pslot].h, 16)
                STB2[pslot].v += 16

            for bi, (ap, dd, rows, width, nm) in enumerate(flat):
                slot = bi % 2
                if bi >= 2:
                    sync.wait_ge(SGB.h, 2 * (bi - 1))
                sync.dma_start(out=wf[slot][0:rows, 0:width],
                               in_=ap).then_inc(WLB[slot].h, 16)
                WLB[slot].v += 16
                if bi >= 1:
                    store(bi - 1)
            store(len(flat) - 1)
            sync.wait_ge(ZP.h, 1)
            sync.dma_start(out=b.w2q_d[HID:HID_P, :],
                           in_=t2a[0:HID_P - HID, 0:D]).then_inc(STB2[0].h, 16)
            STB2[0].v += 16
            sync.wait_ge(STB2[0].h, STB2[0].v)
            sync.wait_ge(STB2[1].h, STB2[1].v)

        @blk.scalar
        def _(s_):
            s_.wait_ge(MSB.h, 1)
            bi = 0
            for mi, nm, blocks in mats:
                nb = b.sq_sb[:, 3 + mi:4 + mi]
                pb = b.sq_sb[:, 6 + mi:7 + mi]
                for (ap, rows, width) in blocks:
                    slot = bi % 2
                    s_.wait_ge(WLB[slot].h, 16 * (bi // 2 + 1))
                    if bi >= 1:
                        s_.wait_ge(ADB.h, bi)
                    ss_in = wf[slot][0:rows, 0:width]
                    ssb.step(s_.activation(t2a[0:rows, 0:width], ss_in, AF.Sign,
                                           bias=nb[0:rows, :]), s_)
                    s_.activation(t2b[0:rows, 0:width], ss_in, AF.Sign,
                                  bias=pb[0:rows, :]).then_inc(SGB.h, 2)
                    SGB.v += 2
                    bi += 1

        @blk.vector
        def _(v):
            bi = 0
            for mi, nm, blocks in mats:
                for (ap, rows, width) in blocks:
                    pslot = bi % 2
                    v.wait_ge(SGB.h, 2 * (bi + 1))
                    if bi >= 2:
                        v.wait_ge(STB2[pslot].h, 16 * (bi // 2))
                    v.tensor_tensor(wqb[pslot][0:rows, 0:width], t2a[0:rows, 0:width],
                                    t2b[0:rows, 0:width], ALU.add).then_inc(ADB.h, 1)
                    ADB.v += 1
                    bi += 1
            v.wait_ge(ADB.h, nblk)
            svb.step(v.memset(t2a[0:HID_P - HID, 0:D], 0.0), v)
            v.engine_nop().then_inc(ZP.h, 1)
            ZP.v += 1
    b.end_phase()


def emit_l1ffn(b):
    nc = b.nc
    wc1 = [b.salloc(f"fw1{k}", [128, D], BF16, at=32768 + k * 4096) for k in range(2)]
    wc3 = [b.salloc(f"fw3{k}", [128, D], BF16, at=40960 + k * 4096) for k in range(2)]
    gt = b.salloc("fgt", [128, NT_HID * T_OWN], BF16, at=81920)
    tg1 = b.salloc("ftg1", [128, 512], F32, at=49152)
    tg1s = b.salloc("ftg1s", [128, 512], F32, at=51200)
    tg3 = b.salloc("ftg3", [128, 512], F32, at=53248)
    w2s = b.salloc("fw2s", [128, NT_HID * 512], BF16, at=32768)
    tmpf = b.salloc("ftmpf", [128, 512], F32, at=76800)
    g2b = [b.salloc(f"fg2b{k}", [128, 512], BF16, at=78848 + k * 1024) for k in range(2)]
    gt3 = gt.rearrange("p (t c) -> p t c", t=NT_HID)

    WL = [b.sem("fWL0", g=True), b.sem("fWL1", g=True)]
    MM = b.sem("fMM")
    GE = b.sem("fGE")
    T1 = b.sem("fT1")
    T3 = b.sem("fT3")
    SIL = b.sem("fSIL")

    s1c = b.sq_sb[:, 0:1]
    s3c = b.sq_sb[:, 1:2]
    s2c = b.sq_sb[:, 2:3]

    with nc.Block() as blk:
        @blk.gpsimd
        def _(g):
            for ht in range(NT_HID):
                slot = ht % 2
                if ht >= 2:
                    g.wait_ge(MM.h, 4 * (ht - 1))
                g.dma_start(out=wc1[slot].rearrange("p (t c) -> p t c", t=NT_D),
                            in_=b.w1q_d[:, ht * 128:(ht + 1) * 128]
                            .rearrange("(t p) c -> p t c", p=128)).then_inc(WL[slot].h, 16)
                g.dma_start(out=wc3[slot].rearrange("p (t c) -> p t c", t=NT_D),
                            in_=b.w3q_d[:, ht * 128:(ht + 1) * 128]
                            .rearrange("(t p) c -> p t c", p=128)).then_inc(WL[slot].h, 16)
                WL[slot].v += 32

        @blk.tensor
        def _(t_):
            for ht in range(NT_HID):
                slot = ht % 2
                t_.wait_ge(WL[slot].h, 32 * (ht // 2 + 1))
                if ht >= 1:
                    t_.wait_ge(T1.h, 2 * ht)
                w13 = wc1[slot].rearrange("p (t c) -> p t c", t=NT_D)
                w33 = wc3[slot].rearrange("p (t c) -> p t c", t=NT_D)
                for half in range(2):
                    hs = slice(half * 512, (half + 1) * 512)
                    for dt in range(NT_D):
                        t_.matmul(b.P[half], w13[:, dt], b.xn1[dt][:, hs],
                                  start=(dt == 0), stop=(dt == NT_D - 1))
                if ht >= 1:
                    t_.wait_ge(T3.h, 2 * ht)
                for half in range(2):
                    hs = slice(half * 512, (half + 1) * 512)
                    for dt in range(NT_D):
                        mm = t_.matmul(b.P[2 + half], w33[:, dt], b.xn1[dt][:, hs],
                                       start=(dt == 0), stop=(dt == NT_D - 1))
                mm.then_inc(MM.h, 4)
                MM.v += 4

        @blk.vector
        def _(v):
            for ht in range(NT_HID):
                v.wait_ge(MM.h, 4 * (ht + 1))
                for half in range(2):
                    idx = 2 * ht + half
                    v.tensor_scalar_mul(tg1, b.P[half], s1c).then_inc(T1.h, 1)
                    T1.v += 1
                    v.tensor_scalar_mul(tg3, b.P[2 + half], s3c).then_inc(T3.h, 1)
                    T3.v += 1
                    v.wait_ge(T3.h, T3.v)
                    v.wait_ge(SIL.h, idx + 1)
                    v.tensor_tensor(gt3[:, ht, half * 512:(half + 1) * 512],
                                    tg1s, tg3, ALU.mult).then_inc(GE.h, 1)
                    GE.v += 1
                    v.wait_ge(GE.h, GE.v)

        @blk.scalar
        def _(s_):
            for ht in range(NT_HID):
                for half in range(2):
                    idx = 2 * ht + half
                    s_.wait_ge(T1.h, idx + 1)
                    s_.activation(tg1s, tg1, AF.Silu).then_inc(SIL.h, 1)
                    SIL.v += 1
    b.end_phase()

    WL2 = b.sem("f2WL", g=True)
    MM2 = b.sem("f2MM")
    EV2 = b.sem("f2EV")
    SL2 = b.sem("f2SL")
    SO2 = [b.sem("f2SO0"), b.sem("f2SO1")]
    sv2 = Ser(b, "f2SV")
    w2s3 = w2s.rearrange("p (t c) -> p t c", t=NT_HID)

    with nc.Block() as blk:
        @blk.gpsimd
        def _(g):
            for og in range(4):
                if og >= 1:
                    g.wait_ge(MM2.h, NT_OWN * og)
                g.dma_start(out=w2s3,
                            in_=b.w2q_d[:, og * 512:(og + 1) * 512]
                            .rearrange("(t p) c -> p t c", p=128)).then_inc(WL2.h, 16)
                WL2.v += 16

        @blk.tensor
        def _(t_):
            for og in range(4):
                t_.wait_ge(WL2.h, 16 * (og + 1))
                for tk in range(NT_OWN):
                    gi = og * NT_OWN + tk
                    if gi >= 2:
                        t_.wait_ge(EV2.h, gi - 1)
                    pp = b.P[gi % 2]
                    for ht in range(NT_HID):
                        mm = t_.matmul(pp, gt3[:, ht, tk * 128:(tk + 1) * 128],
                                       w2s3[:, ht], start=(ht == 0),
                                       stop=(ht == NT_HID - 1))
                    mm.then_inc(MM2.h, 1)
                    MM2.v += 1

        @blk.vector
        def _(v):
            for gi in range(4 * NT_OWN):
                v.wait_ge(MM2.h, gi + 1)
                if gi >= 1:
                    v.wait_ge(SL2.h, gi)
                v.tensor_scalar_mul(tmpf, b.P[gi % 2], s2c).then_inc(EV2.h, 1)
                EV2.v += 1

        @blk.scalar
        def _(s_):
            for gi in range(4 * NT_OWN):
                slot = gi % 2
                s_.wait_ge(EV2.h, gi + 1)
                if gi >= 2:
                    s_.wait_ge(SO2[slot].h, 16 * (gi // 2))
                s_.activation(g2b[slot], tmpf, AF.Silu).then_inc(SL2.h, 1)
                SL2.v += 1

        @blk.sync
        def _(sync):
            for gi in range(4 * NT_OWN):
                og, tk = gi // NT_OWN, gi % NT_OWN
                slot = gi % 2
                sync.wait_ge(SL2.h, gi + 1)
                sync.dma_start(out=b.o["g2"][tk * 128:(tk + 1) * 128,
                                             og * 512:(og + 1) * 512],
                               in_=g2b[slot]).then_inc(SO2[slot].h, 16)
                SO2[slot].v += 16
            sync.wait_ge(SO2[0].h, SO2[0].v)
            sync.wait_ge(SO2[1].h, SO2[1].v)
    b.end_phase()


def _dump_tables(b):
    nc = b.nc
    dc = b.dbg("cosT", (128, H * T_OWN), BF16)
    ds = b.dbg("sinT", (128, H * T_OWN), BF16)
    ST = b.sem("dtST")
    with nc.Block() as blk:
        @blk.sync
        def _(sync):
            sync.dma_start(out=dc, in_=b.cosT).then_inc(ST.h, 16)
            sync.dma_start(out=ds, in_=b.sinT).then_inc(ST.h, 16)
            sync.wait_ge(ST.h, 32)
    b.end_phase()


def _dump_quant(b):
    nc = b.nc
    d1 = b.dbg("w1q", (D, HID_P), BF16)
    d2 = b.dbg("w2q", (HID_P, D), BF16)
    ST = b.sem("dqqST")
    with nc.Block() as blk:
        @blk.sync
        def _(sync):
            sync.dma_start(out=d1, in_=b.w1q_d).then_inc(ST.h, 16)
            sync.dma_start(out=d2, in_=b.w2q_d).then_inc(ST.h, 16)
            sync.wait_ge(ST.h, 32)
    b.end_phase()



class BuilderB:
    SB_BASE = Builder.SB_BASE
    SB_TOP = Builder.SB_TOP
    salloc = Builder.salloc
    sem = Builder.sem
    end_phase = Builder.end_phase

    def __init__(self):
        nc = bass.Bass("TRN2", target_bir_lowering=False, debug=False)
        self.nc = nc
        self.sboff = 0
        self._pool = []
        self._pool_idx = 0
        self._phase_sems = []

        def inp(name, shape, dtype=F32):
            return nc.dram_tensor(name, list(shape), dtype, kind="ExternalInput").ap()

        i = {}
        i["S_all"] = inp("S_all", (8, H * 256, DH))
        i["Wc"] = inp("Wc", (1, 8 * H))
        i["x1d"] = inp("x1d", (T_OWN, D))
        i["g2d"] = inp("g2d", (T_OWN, D), BF16)
        i["yld"] = inp("yld", (D, T_OWN), BF16)
        i["qdwd"] = inp("qdwd", (H * 256, T_OWN), BF16)
        i["wprojT"] = inp("wprojT", (D, D))
        self.i = i
        self.o = {"out": nc.dram_tensor("out", [T_OWN, D], F32,
                                        kind="ExternalOutput").ap()}
        self.P = [nc.alloc_psum_tensor(f"P{k}", [128, 512], F32).ap() for k in range(8)]
        self.yfin = [self.salloc(f"byf{h}", [128, T_OWN], BF16, at=h * 2048)
                     for h in range(H)]


def emit_B1(b):
    nc = b.nc
    sS = b.salloc("bsS", [128, 2048], F32, at=32768)
    stiA = b.salloc("bstiA", [128, 128], F32, at=40960)
    stiB = b.salloc("bstiB", [128, 128], F32, at=41472)
    tmpt = b.salloc("btmpt", [128, 128], F32, at=41984)
    stiAb = b.salloc("bstiAb", [128, 128], BF16, at=42496)
    stiBb = b.salloc("bstiBb", [128, 128], BF16, at=42752)
    qdwA = b.salloc("bqdwA", [128, T_OWN], BF16, at=43008)
    qdwB = b.salloc("bqdwB", [128, T_OWN], BF16, at=45056)
    ylh = b.salloc("bylh", [128, T_OWN], BF16, at=47104)
    wcr = b.salloc("bwcr", [1, 8 * H], F32, at=49152)
    onesr = b.salloc("bonesr", [1, 128], F32, at=49664)
    slopes = np.exp(-_slopes() * CHUNK)

    MS = b.sem("bMS")
    WCL = b.sem("bWCL")
    BCT = b.sem("bBCT")
    SLD = b.sem("bSLD")
    QL = b.sem("bQL")
    YLL = b.sem("bYLL")
    STI = b.sem("bSTI")
    O2 = b.sem("bO2")
    YF = b.sem("bYF")
    sv = Ser(b, "bSV")

    with nc.Block() as blk:
        @blk.gpsimd
        def _(g):
            g.memset(onesr, 1.0).then_inc(MS.h, 1)
            MS.v = 1

        @blk.sync
        def _(sync):
            sync.dma_start(out=wcr, in_=b.i["Wc"]).then_inc(WCL.h, 16)
            WCL.v = 16
            for h in range(H):
                if h:
                    sync.wait_ge(STI.h, h)
                sync.dma_start(out=sS[:, 0:1024].rearrange("d (g v) -> d g v", g=8),
                               in_=b.i["S_all"][:, h * 256:h * 256 + 128, :]
                               .rearrange("b d v -> d b v")).then_inc(SLD.h, 16)
                sync.dma_start(out=sS[:, 1024:2048].rearrange("d (g v) -> d g v", g=8),
                               in_=b.i["S_all"][:, h * 256 + 128:h * 256 + 256, :]
                               .rearrange("b d v -> d b v")).then_inc(SLD.h, 16)
                SLD.v += 32
                if h:
                    sync.wait_ge(O2.h, 2 * h)
                sync.dma_start(out=qdwA,
                               in_=b.i["qdwd"][h * 256:h * 256 + 128, :]).then_inc(QL.h, 16)
                sync.dma_start(out=qdwB,
                               in_=b.i["qdwd"][h * 256 + 128:h * 256 + 256, :]).then_inc(QL.h, 16)
                QL.v += 32
                if h:
                    sync.wait_ge(YF.h, 8 * h)
                sync.dma_start(out=ylh,
                               in_=b.i["yld"][h * 128:(h + 1) * 128, :]).then_inc(YLL.h, 16)
                YLL.v += 16

        @blk.tensor
        def _(t_):
            t_.wait_ge(MS.h, 1)
            t_.wait_ge(WCL.h, 16)
            t_.matmul(b.P[6][:, 0:128], onesr, wcr,
                      start=True, stop=True).then_inc(BCT.h, 1)
            BCT.v += 1
            for h in range(H):
                t_.wait_ge(STI.h, h + 1)
                t_.wait_ge(QL.h, 32 * (h + 1))
                for nh in range(2):
                    if h + nh:
                        t_.wait_ge(YF.h, 8 * h + 4 * nh)
                    ts = slice(nh * 512, (nh + 1) * 512)
                    t_.matmul(b.P[nh], stiAb, qdwA[:, ts], start=True, stop=False)
                    t_.matmul(b.P[nh], stiBb, qdwB[:, ts],
                              start=False, stop=True).then_inc(O2.h, 1)
                    O2.v += 1

        @blk.vector
        def _(v):
            v.wait_ge(BCT.h, 1)
            for h in range(H):
                v.wait_ge(SLD.h, 32 * (h + 1))
                if h:
                    v.wait_ge(O2.h, 2 * h)
                v.memset(stiA, 0.0)
                sv.step(v.memset(stiB, 0.0), v)
                for b_ in range(8):
                    col = b.P[6][:, b_ * H + h:b_ * H + h + 1]
                    sv.step(v.tensor_scalar_mul(tmpt, sS[:, b_ * 128:(b_ + 1) * 128],
                                                col), v)
                    sv.step(v.tensor_tensor(stiA, tmpt, stiA, ALU.add), v)
                    sv.step(v.tensor_scalar_mul(
                        tmpt, sS[:, 1024 + b_ * 128:1024 + (b_ + 1) * 128], col), v)
                    sv.step(v.tensor_tensor(stiB, tmpt, stiB, ALU.add), v)
                v.tensor_copy(stiAb, stiA)
                sv.step(v.tensor_copy(stiBb, stiB), v)
                v.engine_nop().then_inc(STI.h, 1)
                STI.v += 1
                v.wait_ge(YLL.h, 16 * (h + 1))
                bd = float(slopes[h])
                for nh in range(2):
                    v.wait_ge(O2.h, 2 * h + nh + 1)
                    for jq in range(4):
                        j = nh * 4 + jq
                        v.scalar_tensor_tensor(
                            b.yfin[h][:, j * 128:(j + 1) * 128],
                            b.P[nh][:, jq * 128:(jq + 1) * 128], float(bd ** j),
                            ylh[:, j * 128:(j + 1) * 128],
                            ALU.mult, ALU.add).then_inc(YF.h, 1)
                        YF.v += 1
    b.end_phase()


def emit_B2(b):
    nc = b.nc
    wp = [b.salloc(f"bwp{k}", [128, H * 512], BF16, at=53248 + k * 16384)
          for k in range(2)]
    x1s = [b.salloc(f"bx1s{k}", [128, 512], F32, at=86016 + k * 2048) for k in range(2)]
    g2s = [b.salloc(f"bg2s{k}", [128, 512], BF16, at=90112 + k * 1024) for k in range(2)]
    outs = [b.salloc(f"bouts{k}", [128, 512], F32, at=92160 + k * 2048) for k in range(2)]

    WPL = [b.sem("b2WPL0", g=True), b.sem("b2WPL1", g=True)]
    XL = [b.sem("b2XL0"), b.sem("b2XL1")]
    GL = [b.sem("b2GL0"), b.sem("b2GL1")]
    MMp = b.sem("b2MM")
    EVp = b.sem("b2EV")
    OS = [b.sem("b2OS0"), b.sem("b2OS1")]
    sv = Ser(b, "b2SV")
    NGI = 4 * NT_OWN

    with nc.Block() as blk:
        @blk.gpsimd
        def _(g):
            for og in range(4):
                slot = og % 2
                if og >= 2:
                    g.wait_ge(MMp.h, NT_OWN * (og - 1))
                g.dma_start(out=wp[slot].rearrange("p (t c) -> p t c", t=NT_D),
                            in_=b.i["wprojT"][:, og * 512:(og + 1) * 512]
                            .rearrange("(t p) c -> p t c", p=128)).then_inc(WPL[slot].h, 16)
                WPL[slot].v += 16

        @blk.tensor
        def _(t_):
            for gi in range(NGI):
                og, tk = gi // NT_OWN, gi % NT_OWN
                wslot = og % 2
                if gi % NT_OWN == 0:
                    t_.wait_ge(WPL[wslot].h, 16 * (og // 2 + 1))
                if gi >= 2:
                    t_.wait_ge(EVp.h, gi - 1)
                pp = b.P[gi % 2]
                wp3 = wp[wslot].rearrange("p (t c) -> p t c", t=H)
                for h in range(H):
                    mm = t_.matmul(pp, b.yfin[h][:, tk * 128:(tk + 1) * 128],
                                   wp3[:, h], start=(h == 0), stop=(h == H - 1))
                mm.then_inc(MMp.h, 1)
                MMp.v += 1

        @blk.sync
        def _(sync):
            for gi in range(NGI):
                og, tk = gi // NT_OWN, gi % NT_OWN
                slot = gi % 2
                if gi >= 2:
                    sync.wait_ge(EVp.h, gi - 1)
                sync.dma_start(out=x1s[slot],
                               in_=b.i["x1d"][tk * 128:(tk + 1) * 128,
                                              og * 512:(og + 1) * 512]).then_inc(XL[slot].h, 16)
                XL[slot].v += 16
                sync.dma_start(out=g2s[slot],
                               in_=b.i["g2d"][tk * 128:(tk + 1) * 128,
                                              og * 512:(og + 1) * 512]).then_inc(GL[slot].h, 16)
                GL[slot].v += 16
                if gi >= 1:
                    pg = gi - 1
                    pog, ptk = pg // NT_OWN, pg % NT_OWN
                    sync.wait_ge(EVp.h, pg + 1)
                    sync.dma_start(out=b.o["out"][ptk * 128:(ptk + 1) * 128,
                                                  pog * 512:(pog + 1) * 512],
                                   in_=outs[pg % 2]).then_inc(OS[pg % 2].h, 16)
                    OS[pg % 2].v += 16
            pg = NGI - 1
            pog, ptk = pg // NT_OWN, pg % NT_OWN
            sync.wait_ge(EVp.h, pg + 1)
            sync.dma_start(out=b.o["out"][ptk * 128:(ptk + 1) * 128,
                                          pog * 512:(pog + 1) * 512],
                           in_=outs[pg % 2]).then_inc(OS[pg % 2].h, 16)
            OS[pg % 2].v += 16
            sync.wait_ge(OS[0].h, OS[0].v)
            sync.wait_ge(OS[1].h, OS[1].v)

        @blk.vector
        def _(v):
            for gi in range(NGI):
                slot = gi % 2
                v.wait_ge(MMp.h, gi + 1)
                v.wait_ge(XL[slot].h, 16 * (gi // 2 + 1))
                v.wait_ge(GL[slot].h, 16 * (gi // 2 + 1))
                if gi >= 2:
                    v.wait_ge(OS[slot].h, 16 * (gi // 2))
                sv.step(v.tensor_tensor(outs[slot], b.P[gi % 2], g2s[slot],
                                        ALU.mult), v)
                v.tensor_tensor(outs[slot], outs[slot], x1s[slot],
                                ALU.add).then_inc(EVp.h, 1)
                EVp.v += 1
    b.end_phase()


def build_devB():
    b = BuilderB()
    with b.nc.named_scope("B1"):
        emit_B1(b)
    with b.nc.named_scope("B2"):
        emit_B2(b)
    return b


def build_devA(upto="ln0", want_dbg=()):
    b = Builder(want_dbg=want_dbg)
    nc = b.nc
    with nc.named_scope("consts"):
        emit_consts(b)
    with nc.named_scope("ln0"):
        emit_ln0(b)
    if upto == "ln0":
        return b
    with nc.named_scope("projq"):
        emit_proj_qk(b, "q")
    with nc.named_scope("projk"):
        emit_proj_qk(b, "k")
    with nc.named_scope("projv"):
        emit_proj_v(b)
    if "qkv" in b.want_dbg:
        _dump_qkv(b)
    if upto == "qkv":
        return b
    with nc.named_scope("attn"):
        emit_attn(b)
    if "yat" in b.want_dbg:
        _dump_yattn(b)
    if upto == "attn":
        return b
    with nc.named_scope("oproj"):
        emit_oproj(b)
    if "s0" in b.want_dbg:
        _dump_s(b)
    if upto == "oproj":
        return b
    with nc.named_scope("router"):
        emit_router(b)
    with nc.named_scope("moe"):
        emit_moe(b)
    if "acc" in b.want_dbg:
        _dump_acc(b)
    if upto == "moe":
        return b
    with nc.named_scope("combine"):
        emit_combine_ln1(b)
    if "xn1" in b.want_dbg:
        _dump_xn1(b)
    if upto == "ln1":
        return b
    with nc.named_scope("l1qkv"):
        emit_l1qkv(b)
    if "jqkv" in b.want_dbg:
        _dump_qkv(b)
    if upto == "l1qkv":
        return b
    with nc.named_scope("l1tables"):
        emit_l1tables(b)
    if "tab" in b.want_dbg:
        _dump_tables(b)
    if upto == "tables":
        return b
    with nc.named_scope("l1scanq"):
        emit_l1scanq(b)
    if "wq" in b.want_dbg:
        _dump_quant(b)
    if upto in ("scan", "quant"):
        return b
    with nc.named_scope("l1ffn"):
        emit_l1ffn(b)
    if upto == "ffn":
        return b
    raise ValueError(upto)


def _dump_qkv(b, parts=("q", "k", "v")):
    nc = b.nc
    srcs = {"q": (b.qf_d, (H, 128, T_OWN)), "k": (b.kf_d, (H, 128, TH)),
            "v": (b.vt_d, (NT_TH, 128, D))}
    ST = b.sem("dqST")
    with nc.Block() as blk:
        @blk.sync
        def _(sync):
            for p in parts:
                src, shp = srcs[p]
                d = b.dbg(p + "f", shp, BF16)
                sync.dma_start(out=d, in_=src).then_inc(ST.h, 16)
                ST.v += 16
            sync.wait_ge(ST.h, ST.v)
    b.end_phase()

def kernel(**inputs):
    maps = shard_inputs(inputs)
    bA = build_devA(upto="ffn")
    resA = run_bass_kernel_spmd(bA.nc, maps, core_ids=list(range(NCORES)))
    S_all = np.ascontiguousarray(
        np.stack([np.asarray(resA.results[c]["S_c"], np.float32)
                  for c in range(NCORES)]))
    wp = maps[0]["wprojT"]
    mapsB = []
    for c in range(NCORES):
        r = resA.results[c]
        mapsB.append({"S_all": S_all, "Wc": host_consts(c)["Wc"],
                      "x1d": np.asarray(r["x1"]), "g2d": np.asarray(r["g2"]),
                      "yld": np.asarray(r["ylocal"]),
                      "qdwd": np.asarray(r["qdw"]), "wprojT": wp})
    bB = build_devB()
    resB = run_bass_kernel_spmd(bB.nc, mapsB, core_ids=list(range(NCORES)))
    out = np.concatenate([np.asarray(resB.results[c]["out"], np.float32)
                          for c in range(NCORES)], axis=0)
    return out.reshape(1, NCORES * T_OWN, D)
